# version 06
# baseline (speedup 1.0000x reference)
import contextlib
import numpy as np
import ml_dtypes
import concourse.bass as bass
import concourse.mybir as mybir

F32 = mybir.dt.float32
BF16 = mybir.dt.bfloat16
AF = mybir.ActivationFunctionType
ALU = mybir.AluOpType
AX = mybir.AxisListType

D = 1024
L = 4096
NB = 2
DT = D // 128
NH = 16
DH = 64
U = 45
E = 8
HID = 2048
FS = 12288
IS = 4096
RNOV = 0.7
DBG_STAGES = False
SAME_ENGINE_EDGES = True



class Buf:
    __slots__ = ("name", "w", "readers")

    def __init__(self, name):
        self.name = name
        self.w = None
        self.readers = []


class _Op:
    __slots__ = ("stream", "fn", "waits", "ev", "inc_needed", "is_dma", "tag")

    def __init__(self, stream, fn, waits, ev, is_dma, tag):
        self.stream = stream
        self.fn = fn
        self.waits = waits
        self.ev = ev
        self.inc_needed = False
        self.is_dma = is_dma
        self.tag = tag


class Em:

    STREAMS = ("pe", "act", "dve", "pool", "sp")
    QUEUES = {"qsp": "sp", "qact": "act", "qpool": "pool"}
    NROT = 8

    def __init__(self, nc):
        self.nc = nc
        self.ops = {s: [] for s in self.STREAMS}
        self.counts = dict.fromkeys(self.STREAMS, 0)
        self.qcounts = dict.fromkeys(self.QUEUES, 0)
        self.ops_by_ev = {}
        self._last_ev = dict.fromkeys(self.STREAMS, None)

    def sem_keys(self):
        keys = list(self.STREAMS)
        for q in self.QUEUES:
            keys += [f"{q}{j}" for j in range(self.NROT)]
        return keys

    def _record(self, stream, fn, reads, writes, queue=None, tag=""):
        deps = set()
        for b in reads:
            if b.w is not None:
                deps.add(b.w)
        for b in writes:
            if b.w is not None:
                deps.add(b.w)
            for r in b.readers:
                deps.add(r)
        if queue is None:
            self.counts[stream] += 1
            ev = (stream, self.counts[stream])
        else:
            i = self.qcounts[queue]
            self.qcounts[queue] += 1
            j = i % self.NROT
            chain = i // self.NROT + 1
            ev = (f"{queue}{j}", 16 * chain)
            if chain > 1:
                deps.add((f"{queue}{j}", 16 * (chain - 1)))
        waits = []
        for (dstream, dval) in deps:
            if dstream == stream and queue is None and not SAME_ENGINE_EDGES:
                continue
            if dstream == stream == "pe" and queue is None:
                continue
            waits.append((dstream, dval))
            if (dstream, dval) in self.ops_by_ev:
                self.ops_by_ev[(dstream, dval)].inc_needed = True
        op = _Op(stream, fn, waits, ev, queue is not None, tag)
        self.ops[stream].append(op)
        self.ops_by_ev[ev] = op
        self._last_ev[stream] = ev
        for b in reads:
            b.readers.append(ev)
        for b in writes:
            b.w = ev
            b.readers = []
        return ev

    def op(self, stream, fn, reads=(), writes=(), tag=""):
        return self._record(stream, fn, reads, writes, None, tag)

    def dma(self, queue, fn, reads=(), writes=(), tag=""):
        return self._record(self.QUEUES[queue], fn, reads, writes, queue, tag)

    def barrier(self):
        lasts = {s: self._last_ev[s] for s in self.STREAMS}
        qlasts = []
        for q in self.QUEUES:
            n = self.qcounts[q]
            for j in range(self.NROT):
                cnt = (n - j + self.NROT - 1) // self.NROT
                if cnt > 0:
                    qlasts.append((f"{q}{j}", 16 * cnt))
        for s in self.STREAMS:
            deps = [ev for t, ev in lasts.items() if t != s and ev is not None]
            deps += qlasts
            for ev in deps:
                if ev in self.ops_by_ev:
                    self.ops_by_ev[ev].inc_needed = True
            op = _Op(s, lambda eng: None, [d for d in deps if d[0] != s], None, False, "barrier")
            self.ops[s].append(op)

    def finalize(self, block, sems):
        rank = {}
        for s in self.STREAMS:
            r = 0
            for op in self.ops[s]:
                if op.ev is not None and not op.is_dma:
                    if op.inc_needed:
                        r += 1
                    rank[op.ev] = r

        def mapval(ev):
            s, v = ev
            if s in self.STREAMS:
                return rank[ev]
            return v

        def emit_stream(eng, stream):
            waited = {}
            for op in self.ops[stream]:
                for dep in op.waits:
                    dval = mapval(dep)
                    dstream = dep[0]
                    if dval == 0 or waited.get(dstream, 0) >= dval:
                        continue
                    eng.wait_ge(sems[dstream], dval)
                    waited[dstream] = dval
                inst = op.fn(eng)
                if op.is_dma:
                    inst.then_inc(sems[op.ev[0]], 16)
                elif op.inc_needed:
                    if inst is None:
                        inst = eng.nop()
                    inst.then_inc(sems[op.ev[0]], 1)

        @block.tensor
        def _(t):
            emit_stream(t, "pe")

        @block.scalar
        def _(s):
            emit_stream(s, "act")

        @block.vector
        def _(v):
            emit_stream(v, "dve")

        @block.gpsimd
        def _(g):
            emit_stream(g, "pool")

        @block.sync
        def _(sy):
            emit_stream(sy, "sp")



def ceil_div(a, b):
    return -(-a // b)



def make_host_consts():
    return {
        "c_ones_col": np.ones((128, 1), np.float32),
        "c_ones_row": np.ones((1, 512), np.float32),
        "c_ident32": np.eye(128, dtype=np.float32),
        "c_ident16": np.eye(128).astype(ml_dtypes.bfloat16),
        "c_iota45": (np.arange(1, 46, dtype=np.float32)).reshape(45, 1),
        "c_ones16": np.ones((1, 512), ml_dtypes.bfloat16),
        "c_iotarow45": np.tile(np.arange(1, 46, dtype=np.float32), (128, 1)),
        "c_ones_col16": np.ones((128, 1), ml_dtypes.bfloat16),
        "c_lowtri": np.tril(np.ones((128, 128), np.float32), -1).T.copy(),
        "c_fid": (np.arange(FT)[None, :] * 128 + np.arange(128)[:, None]).astype(np.int32),
    }


class Consts:

    def __init__(self, nc, em, stk):
        self.ones_col = stk.enter_context(nc.sbuf_tensor("s_ones_col", [128, 1], F32))
        self.ones_row = stk.enter_context(nc.sbuf_tensor("s_ones_row", [1, 512], F32))
        self.ident32 = stk.enter_context(nc.sbuf_tensor("s_ident32", [128, 128], F32))
        self.ident16 = stk.enter_context(nc.sbuf_tensor("s_ident16", [128, 128], BF16))
        self.iota45 = stk.enter_context(nc.sbuf_tensor("s_iota45", [45, 1], F32))
        self.ones16 = stk.enter_context(nc.sbuf_tensor("s_ones16", [1, 512], BF16))
        self.iotarow45 = stk.enter_context(nc.sbuf_tensor("s_iotarow45", [128, 45], F32))
        self.ones_col16 = stk.enter_context(nc.sbuf_tensor("s_ones_col16", [128, 1], BF16))
        self.lowtri = stk.enter_context(nc.sbuf_tensor("s_lowtri", [128, 128], F32))
        self.fid = stk.enter_context(nc.sbuf_tensor("s_fid", [128, FT], mybir.dt.int32))
        self.buf = Buf("consts")
        self._cols = {}
        self._colbuf = stk.enter_context(nc.sbuf_tensor("s_colconsts", [128, 32], F32))
        self._ncols = 0
        names = ["c_ones_col", "c_ones_row", "c_ident32", "c_ident16", "c_iota45", "c_ones16",
                 "c_iotarow45", "c_ones_col16", "c_lowtri", "c_fid"]
        tiles = [self.ones_col, self.ones_row, self.ident32, self.ident16, self.iota45, self.ones16,
                 self.iotarow45, self.ones_col16, self.lowtri, self.fid]
        for name, tile in zip(names, tiles):
            ap = nc.dram_tensor(name, list(tile.ap().shape), tile.ap().dtype,
                                kind="ExternalInput").ap()
            em.dma(
                "qsp",
                (lambda t, a: lambda eng: eng.dma_start(out=t[:, :], in_=a[:, :]))(tile.ap(), ap),
                reads=(), writes=(self.buf,), tag=f"load {name}",
            )



def phase_in_transpose(nc, em, C, x_in, x0_f32, b):
    with contextlib.ExitStack() as stk:
        stage_tm = [stk.enter_context(nc.sbuf_tensor(f"p0tm{b}_{i}", [128, D], F32)) for i in range(2)]
        stage_fm = [stk.enter_context(nc.sbuf_tensor(f"p0fm{b}_{i}", [128, DT * 128], F32)) for i in range(2)]
        ps = [stk.enter_context(nc.psum_tensor(f"p0ps{b}_{i}", [128, 128], F32)) for i in range(2)]
        btm = [Buf("tm0"), Buf("tm1")]
        bfm = [Buf("fm0"), Buf("fm1")]
        bps = [Buf("ps0"), Buf("ps1")]
        for lt in range(L // 128):
            i = lt % 2
            em.dma("qsp", (lambda i=i, lt=lt: lambda eng: eng.dma_start(
                out=stage_tm[i][:, :], in_=x_in[b, lt * 128:(lt + 1) * 128, :]))(),
                reads=(), writes=(btm[i],), tag=f"p0 load {lt}")
            for dt in range(DT):
                j = dt % 2
                em.op("pe", (lambda i=i, j=j, dt=dt: lambda eng: eng.transpose(
                    ps[j][:, :], stage_tm[i][:, dt * 128:(dt + 1) * 128], C.ident32[:, :]))(),
                    reads=(btm[i], C.buf), writes=(bps[j],), tag="p0 tr")
                em.op("act", (lambda i=i, j=j, dt=dt: lambda eng: eng.copy(
                    stage_fm[i][:, dt * 128:(dt + 1) * 128], ps[j][:, :]))(),
                    reads=(bps[j],), writes=(bfm[i],), tag="p0 evac")
            em.dma("qact", (lambda i=i, lt=lt: lambda eng: eng.dma_start(
                out=x0_f32.rearrange("(dt p) l -> p dt l", p=128)[:, :, lt * 128:(lt + 1) * 128],
                in_=stage_fm[i][:, :].rearrange("p (dt c) -> p dt c", dt=DT)))(),
                reads=(bfm[i],), writes=(), tag=f"p0 store {lt}")



def phase_ln(nc, em, C, xprev, f_in, xout, b, tag="ln"):
    CH = 512
    NCH = L // CH
    with contextlib.ExitStack() as stk:
        zb = [stk.enter_context(nc.sbuf_tensor(f"{tag}z{i}", [128, DT * CH], F32)) for i in range(2)]
        fb = [stk.enter_context(nc.sbuf_tensor(f"{tag}f{i}", [128, DT * CH], F32)) for i in range(2)]
        z2 = stk.enter_context(nc.sbuf_tensor(f"{tag}z2", [128, DT * CH], F32))
        yb = [stk.enter_context(nc.sbuf_tensor(f"{tag}y{i}", [128, DT * CH], F32)) for i in range(2)]
        mrow = stk.enter_context(nc.sbuf_tensor(f"{tag}m", [1, CH], F32))
        vrow = stk.enter_context(nc.sbuf_tensor(f"{tag}v", [1, CH], F32))
        rrow = stk.enter_context(nc.sbuf_tensor(f"{tag}r", [1, CH], F32))
        m2row = stk.enter_context(nc.sbuf_tensor(f"{tag}m2", [1, CH], F32))
        pS = stk.enter_context(nc.psum_tensor(f"{tag}pS", [1, CH], F32))
        pQ = stk.enter_context(nc.psum_tensor(f"{tag}pQ", [1, CH], F32))
        pM = stk.enter_context(nc.psum_tensor(f"{tag}pM", [128, CH], F32))
        pR = stk.enter_context(nc.psum_tensor(f"{tag}pR", [128, CH], F32))
        bz = [Buf("z0"), Buf("z1")]
        bf_ = [Buf("f0"), Buf("f1")]
        bz2, by = Buf("z2"), [Buf("y0"), Buf("y1")]
        bm, bv, br, bm2 = Buf("m"), Buf("v"), Buf("r"), Buf("m2")
        bpS, bpQ, bpM, bpR = Buf("pS"), Buf("pQ"), Buf("pM"), Buf("pR")

        eps_ap = C.col(em, 1e-5)

        def ch_view(dram, c):
            return dram.rearrange("(dt p) l -> p dt l", p=128)[:, :, c * CH:(c + 1) * CH]

        for c in range(NCH):
            i = c % 2
            em.dma("qsp", (lambda i=i, c=c: lambda eng: eng.dma_start(
                out=zb[i][:, :].rearrange("p (dt c) -> p dt c", dt=DT), in_=ch_view(xprev, c)))(),
                reads=(), writes=(bz[i],), tag=f"{tag} ldx {c}")
            em.dma("qsp", (lambda i=i, c=c: lambda eng: eng.dma_start(
                out=fb[i][:, :].rearrange("p (dt c) -> p dt c", dt=DT), in_=ch_view(f_in, c)))(),
                reads=(), writes=(bf_[i],), tag=f"{tag} ldf {c}")
            em.op("dve", (lambda i=i: lambda eng: eng.tensor_add(zb[i][:, :], zb[i][:, :], fb[i][:, :]))(),
                  reads=(bz[i], bf_[i]), writes=(bz[i],), tag=f"{tag} add")
            em.op("act", (lambda i=i: lambda eng: eng.activation(z2[:, :], zb[i][:, :], AF.Square))(),
                  reads=(bz[i],), writes=(bz2,), tag=f"{tag} sq")
            for dt in range(DT):
                em.op("pe", (lambda i=i, dt=dt: lambda eng: eng.matmul(
                    pS[:, :], C.ones_col[:, :], zb[i][:, dt * CH:(dt + 1) * CH],
                    start=(dt == 0), stop=(dt == DT - 1)))(),
                    reads=(bz[i], C.buf), writes=(bpS,), tag=f"{tag} mmS")
            for dt in range(DT):
                em.op("pe", (lambda dt=dt: lambda eng: eng.matmul(
                    pQ[:, :], C.ones_col[:, :], z2[:, dt * CH:(dt + 1) * CH],
                    start=(dt == 0), stop=(dt == DT - 1)))(),
                    reads=(bz2, C.buf), writes=(bpQ,), tag=f"{tag} mmQ")
            em.op("dve", lambda eng: eng.tensor_scalar(mrow[:, :], pS[:, :], 1.0 / D, None, ALU.mult),
                  reads=(bpS,), writes=(bm,), tag=f"{tag} m")
            em.op("dve", lambda eng: eng.tensor_scalar(vrow[:, :], pQ[:, :], 1.0 / D, None, ALU.mult),
                  reads=(bpQ,), writes=(bv,), tag=f"{tag} q")
            em.op("act", lambda eng: eng.activation(m2row[:, :], mrow[:, :], AF.Square),
                  reads=(bm,), writes=(bm2,), tag=f"{tag} msq")
            em.op("dve", lambda eng: eng.tensor_sub(vrow[:, :], vrow[:, :], m2row[:, :]),
                  reads=(bv, bm2), writes=(bv,), tag=f"{tag} var")
            em.op("act", lambda eng: eng.activation(rrow[:, :], vrow[:, :], AF.Sqrt, bias=eps_ap[0:1, :]),
                  reads=(bv, C.buf), writes=(br,), tag=f"{tag} sqrt")
            em.op("dve", lambda eng: eng.reciprocal(rrow[:, :], rrow[:, :]),
                  reads=(br,), writes=(br,), tag=f"{tag} recip")
            em.op("pe", lambda eng: eng.matmul(pM[:, :], C.ones_row[0:1, 0:128], mrow[:, :],
                                               start=True, stop=True),
                  reads=(bm, C.buf), writes=(bpM,), tag=f"{tag} bcm")
            em.op("pe", lambda eng: eng.matmul(pR[:, :], C.ones_row[0:1, 0:128], rrow[:, :],
                                               start=True, stop=True),
                  reads=(br, C.buf), writes=(bpR,), tag=f"{tag} bcr")
            for dt in range(DT):
                em.op("dve", (lambda i=i, dt=dt: lambda eng: eng.tensor_sub(
                    yb[i][:, dt * CH:(dt + 1) * CH], zb[i][:, dt * CH:(dt + 1) * CH], pM[:, :]))(),
                    reads=(bz[i], bpM), writes=(by[i],), tag=f"{tag} sub")
            for dt in range(DT):
                em.op("dve", (lambda i=i, dt=dt: lambda eng: eng.tensor_mul(
                    yb[i][:, dt * CH:(dt + 1) * CH], yb[i][:, dt * CH:(dt + 1) * CH], pR[:, :]))(),
                    reads=(by[i], bpR), writes=(by[i],), tag=f"{tag} mul")
            em.dma("qact", (lambda i=i, c=c: lambda eng: eng.dma_start(
                out=ch_view(xout, c), in_=yb[i][:, :].rearrange("p (dt c) -> p dt c", dt=DT)))(),
                reads=(by[i],), writes=(), tag=f"{tag} st {c}")





def _consts_col(self, em, value):
    if value in self._cols:
        return self._cols[value]
    i = self._ncols
    assert i < 32
    self._ncols += 1
    ap = self._colbuf[:, i:i + 1]
    em.op("dve", (lambda ap=ap, value=value: lambda eng: eng.memset(ap, float(value)))(),
          reads=(), writes=(self.buf,), tag=f"const col {value}")
    self._cols[value] = ap
    return ap

Consts.col = _consts_col



def phase_moe_prep(nc, em, W, tag="moeprep"):
    w1bf = nc.dram_tensor(f"{tag}_w1", [E, D, HID], BF16, kind="Internal").ap()
    w2bf = nc.dram_tensor(f"{tag}_w2", [E, HID, D], BF16, kind="Internal").ap()
    with contextlib.ExitStack() as stk:
        tmp = [stk.enter_context(nc.sbuf_tensor(f"{tag}t{i}", [128, 16384], BF16)) for i in range(2)]
        bt = [Buf("t0"), Buf("t1")]
        for e in range(E):
            i = e % 2
            em.dma("qpool", (lambda e=e, i=i: lambda eng: eng.dma_start(
                out=tmp[i][:, :].rearrange("p (dt h) -> p dt h", dt=DT),
                in_=W["W1"][e].rearrange("(dt p) h -> p dt h", p=128)))(),
                reads=(), writes=(bt[i],), tag=f"{tag} in1")
            em.dma("qsp", (lambda e=e, i=i: lambda eng: eng.dma_start(
                out=w1bf[e].rearrange("(dt p) h -> p dt h", p=128),
                in_=tmp[i][:, :].rearrange("p (dt h) -> p dt h", dt=DT)))(),
                reads=(bt[i],), writes=(), tag=f"{tag} out1")
            em.dma("qpool", (lambda e=e, i=i: lambda eng: eng.dma_start(
                out=tmp[i][:, :].rearrange("p (ht d) -> p ht d", ht=HID // 128),
                in_=W["W2"][e].rearrange("(ht p) d -> p ht d", p=128)))(),
                reads=(bt[i],), writes=(bt[i],), tag=f"{tag} in2")
            em.dma("qsp", (lambda e=e, i=i: lambda eng: eng.dma_start(
                out=w2bf[e].rearrange("(ht p) d -> p ht d", p=128),
                in_=tmp[i][:, :].rearrange("p (ht d) -> p ht d", ht=HID // 128)))(),
                reads=(bt[i],), writes=(), tag=f"{tag} out2")
    return w1bf, w2bf


def phase_moe(nc, em, C, x_f32, W, w1bf, w2bf, out_f32, b, Lt=L, tag="moe"):
    CH = 512
    MAC = min(512, Lt)
    NM = Lt // MAC
    NCH = MAC // CH
    HT = HID // 128
    gs = nc.dram_tensor(f"{tag}{b}_gs", [8, Lt], F32, kind="Internal").ap()
    bgs = Buf(f"{tag}gs")

    with contextlib.ExitStack() as stk:
        sb = lambda name, shape, dt_: stk.enter_context(nc.sbuf_tensor(f"{tag}{b}_{name}", shape, dt_))
        gw = sb("gw", [128, DT * E], F32)
        xbg = sb("xbg", [128, DT * 512], F32)
        grows = sb("grows", [8, 512], F32)
        nmx = sb("nmx", [128, 1], F32)
        prob = sb("prob", [128, 8], F32)
        ssum = sb("ssum", [128, 1], F32)
        top8 = sb("top8", [128, 8], F32)
        wtm = sb("wtm", [128, 8], F32)
        pglog = stk.enter_context(nc.psum_tensor(f"{tag}{b}_pgl", [128, 8], F32))
        pgtr = stk.enter_context(nc.psum_tensor(f"{tag}{b}_pgt", [128, 128], F32))
        B = lambda n: Buf(f"{tag}{n}")
        bgw, bxb, bgrows, bsc = B("gw"), B("xbg"), B("grows"), B("sc")
        bpgl, bpgtr = B("pgl"), B("pgtr")
        em.dma("qsp", lambda eng: eng.dma_start(
            out=gw[:, :].rearrange("p (dt e) -> p dt e", dt=DT),
            in_=W["gate"].rearrange("(dt p) e -> p dt e", p=128)),
            reads=(), writes=(bgw,), tag=f"{tag} gw")
        GM = min(512, Lt)
        for m in range(Lt // GM):
            em.dma("qsp", (lambda m=m: lambda eng: eng.dma_start(
                out=xbg[:, 0:DT * GM].rearrange("p (dt c) -> p dt c", dt=DT),
                in_=x_f32.rearrange("(dt p) l -> p dt l", p=128)[:, :, m * GM:(m + 1) * GM]))(),
                reads=(), writes=(bxb,), tag=f"{tag} ldxg")
            for tt in range(GM // 128):
                for dt in range(DT):
                    em.op("pe", (lambda tt=tt, dt=dt: lambda eng: eng.matmul(
                        pglog[:, :], xbg[:, dt * GM + tt * 128: dt * GM + (tt + 1) * 128],
                        gw[:, dt * E:(dt + 1) * E], start=(dt == 0), stop=(dt == DT - 1)))(),
                        reads=(bxb, bgw), writes=(bpgl,), tag=f"{tag} glog")
                em.op("dve", lambda eng: eng.tensor_reduce(nmx[:, :], pglog[:, :], AX.X, ALU.max, negate=True),
                      reads=(bpgl,), writes=(bsc,), tag=f"{tag} gmax")
                em.op("act", lambda eng: eng.activation(prob[:, :], pglog[:, :], AF.Exp,
                                                        bias=nmx[:, :], accum_out=ssum[:, :]),
                      reads=(bpgl, bsc), writes=(bsc,), tag=f"{tag} gexp")
                em.op("dve", lambda eng: eng.reciprocal(ssum[:, :], ssum[:, :]),
                      reads=(bsc,), writes=(bsc,), tag=f"{tag} grecip")
                em.op("dve", lambda eng: eng.tensor_scalar(prob[:, :], prob[:, :], ssum[:, :], None, ALU.mult),
                      reads=(bsc,), writes=(bsc,), tag=f"{tag} gnorm")
                em.op("dve", lambda eng: eng.max(top8[:, :], prob[:, :]),
                      reads=(bsc,), writes=(bsc,), tag=f"{tag} gtop8")
                em.op("dve", lambda eng: eng.tensor_scalar(wtm[:, :], prob[:, :], top8[:, 1:2], None, ALU.is_ge),
                      reads=(bsc,), writes=(bsc,), tag=f"{tag} gmask")
                em.op("dve", lambda eng: eng.tensor_tensor(wtm[:, :], wtm[:, :], prob[:, :], ALU.mult),
                      reads=(bsc,), writes=(bsc,), tag=f"{tag} gmul")
                em.op("dve", lambda eng: eng.tensor_scalar(wtm[:, :], wtm[:, :], 1.0, None,
                                                           ALU.mult, ALU.add, accum_out=ssum[:, :]),
                      reads=(bsc,), writes=(bsc,), tag=f"{tag} gden")
                em.op("dve", lambda eng: eng.tensor_scalar_max(ssum[:, :], ssum[:, :], 1e-12),
                      reads=(bsc,), writes=(bsc,), tag=f"{tag} gdmax")
                em.op("dve", lambda eng: eng.reciprocal(ssum[:, :], ssum[:, :]),
                      reads=(bsc,), writes=(bsc,), tag=f"{tag} gdrecip")
                em.op("dve", lambda eng: eng.tensor_scalar(wtm[:, :], wtm[:, :], ssum[:, :], None, ALU.mult),
                      reads=(bsc,), writes=(bsc,), tag=f"{tag} gwn")
                em.op("pe", lambda eng: eng.transpose(pgtr[0:8, :], wtm[:, :], C.ident32[:, :]),
                      reads=(bsc, C.buf), writes=(bpgtr,), tag=f"{tag} gtr")
                em.op("dve", (lambda tt=tt: lambda eng: eng.tensor_copy(
                    grows[:, tt * 128:(tt + 1) * 128], pgtr[0:8, :]))(),
                    reads=(bpgtr,), writes=(bgrows,), tag=f"{tag} gcp")
            em.dma("qact", (lambda m=m: lambda eng: eng.dma_start(
                out=gs[:, m * GM:(m + 1) * GM], in_=grows[:, 0:GM]))(),
                reads=(bgrows,), writes=(bgs,), tag=f"{tag} gst")
    em.barrier()

    with contextlib.ExitStack() as stk:
        sb = lambda name, shape, dt_: stk.enter_context(nc.sbuf_tensor(f"{tag}{b}_{name}", shape, dt_))
        xb = sb("xb", [128, DT * MAC], BF16)
        xge = sb("xge", [128, DT * CH], BF16)
        hsb = sb("hsb", [128, HT * CH], BF16)
        acc = sb("acc", [128, DT * MAC], F32)
        w1 = [sb(f"w1_{i}", [128, DT * HID], BF16) for i in range(2)]
        w2 = [sb(f"w2_{i}", [128, HT * D], BF16) for i in range(2)]
        b1e = [sb(f"b1e{i}", [1, HID], BF16) for i in range(2)]
        b2e = [sb(f"b2e{i}", [1, D], BF16) for i in range(2)]
        gee = [sb(f"ge{i}", [1, MAC], BF16) for i in range(2)]
        pg = stk.enter_context(nc.psum_tensor(f"{tag}{b}_pg", [128, CH], F32))
        ph = [stk.enter_context(nc.psum_tensor(f"{tag}{b}_ph{i}", [128, CH], F32)) for i in range(2)]
        py = [stk.enter_context(nc.psum_tensor(f"{tag}{b}_py{i}", [128, CH], F32)) for i in range(2)]
        B = lambda n: Buf(f"{tag}{n}")
        bxb, bxge, bh, bacc = B("xb"), B("xge"), B("h"), B("acc")
        bw1 = [B("w1a"), B("w1b")]
        bw2 = [B("w2a"), B("w2b")]
        bb1 = [B("b1a"), B("b1b")]
        bb2 = [B("b2a"), B("b2b")]
        bge = [B("gea"), B("geb")]
        bpg = B("pg")
        bph = [B("pha"), B("phb")]
        bpy = [B("pya"), B("pyb")]
        for m in range(NM):
            em.dma("qpool", (lambda m=m: lambda eng: eng.dma_start(
                out=xb[:, :].rearrange("p (dt c) -> p dt c", dt=DT),
                in_=x_f32.rearrange("(dt p) l -> p dt l", p=128)[:, :, m * MAC:(m + 1) * MAC]))(),
                reads=(), writes=(bxb,), tag=f"{tag} ldx2")
            em.op("dve", lambda eng: eng.memset(acc[:, :], 0.0),
                  reads=(), writes=(bacc,), tag=f"{tag} acc0")
            for e in range(E):
                i = e % 2
                em.dma("qsp", (lambda e=e, i=i: lambda eng: eng.dma_start(
                    out=w1[i][:, :].rearrange("p (dt h) -> p dt h", dt=DT),
                    in_=w1bf[e].rearrange("(dt p) h -> p dt h", p=128)))(),
                    reads=(), writes=(bw1[i],), tag=f"{tag} w1 {e}")
                em.dma("qsp", (lambda e=e, i=i: lambda eng: eng.dma_start(
                    out=w2[i][:, :].rearrange("p (ht d) -> p ht d", ht=HT),
                    in_=w2bf[e].rearrange("(ht p) d -> p ht d", p=128)))(),
                    reads=(), writes=(bw2[i],), tag=f"{tag} w2 {e}")
                em.dma("qpool", (lambda e=e, i=i: lambda eng: eng.dma_start(
                    out=b1e[i][:, :], in_=W["b1"][e].unsqueeze(0)))(),
                    reads=(), writes=(bb1[i],), tag=f"{tag} b1 {e}")
                em.dma("qpool", (lambda e=e, i=i: lambda eng: eng.dma_start(
                    out=b2e[i][:, :], in_=W["b2"][e].unsqueeze(0)))(),
                    reads=(), writes=(bb2[i],), tag=f"{tag} b2 {e}")
                em.dma("qpool", (lambda e=e, i=i, m=m: lambda eng: eng.dma_start(
                    out=gee[i][:, :], in_=gs[e:e + 1, m * MAC:(m + 1) * MAC]))(),
                    reads=(bgs,), writes=(bge[i],), tag=f"{tag} ge {e}")
                for cc in range(NCH):
                    em.op("pe", (lambda i=i, cc=cc: lambda eng: eng.matmul(
                        pg[:, :], C.ones16[0:1, 0:128], gee[i][0:1, cc * CH:(cc + 1) * CH],
                        start=True, stop=True))(),
                        reads=(bge[i], C.buf), writes=(bpg,), tag=f"{tag} gbc")
                    for dt in range(DT):
                        em.op("dve", (lambda cc=cc, dt=dt: lambda eng: eng.tensor_tensor(
                            xge[:, dt * CH:(dt + 1) * CH],
                            xb[:, dt * MAC + cc * CH: dt * MAC + (cc + 1) * CH],
                            pg[:, :], ALU.mult))(),
                            reads=(bxb, bpg), writes=(bxge,), tag=f"{tag} xge")
                    for ht in range(HT):
                        j = ht % 2
                        for dt in range(DT):
                            em.op("pe", (lambda i=i, j=j, ht=ht, dt=dt: lambda eng: eng.matmul(
                                ph[j][:, :], w1[i][:, dt * HID + ht * 128: dt * HID + (ht + 1) * 128],
                                xge[:, dt * CH:(dt + 1) * CH], start=(dt == 0), stop=False))(),
                                reads=(bw1[i], bxge), writes=(bph[j],), tag=f"{tag} h mm")
                        em.op("pe", (lambda i=i, j=j, ht=ht, cc=cc: lambda eng: eng.matmul(
                            ph[j][:, :], b1e[i][0:1, ht * 128:(ht + 1) * 128],
                            gee[i][0:1, cc * CH:(cc + 1) * CH],
                            start=False, stop=True))(),
                            reads=(bb1[i], bge[i]), writes=(bph[j],), tag=f"{tag} h bias")
                        em.op("act", (lambda j=j, ht=ht: lambda eng: eng.activation(
                            hsb[:, ht * CH:(ht + 1) * CH], ph[j][:, :], AF.Relu))(),
                            reads=(bph[j],), writes=(bh,), tag=f"{tag} h relu")
                    for dt in range(DT):
                        j = dt % 2
                        for ht in range(HT):
                            em.op("pe", (lambda i=i, j=j, dt=dt, ht=ht: lambda eng: eng.matmul(
                                py[j][:, :], w2[i][:, ht * D + dt * 128: ht * D + (dt + 1) * 128],
                                hsb[:, ht * CH:(ht + 1) * CH], start=(ht == 0), stop=False))(),
                                reads=(bw2[i], bh), writes=(bpy[j],), tag=f"{tag} y mm")
                        em.op("pe", (lambda i=i, j=j, dt=dt, cc=cc: lambda eng: eng.matmul(
                            py[j][:, :], b2e[i][0:1, dt * 128:(dt + 1) * 128],
                            gee[i][0:1, cc * CH:(cc + 1) * CH],
                            start=False, stop=True))(),
                            reads=(bb2[i], bge[i]), writes=(bpy[j],), tag=f"{tag} y bias")
                        em.op("dve", (lambda j=j, dt=dt, cc=cc: lambda eng: eng.tensor_add(
                            acc[:, dt * MAC + cc * CH: dt * MAC + (cc + 1) * CH],
                            acc[:, dt * MAC + cc * CH: dt * MAC + (cc + 1) * CH], py[j][:, :]))(),
                            reads=(bpy[j], bacc), writes=(bacc,), tag=f"{tag} y acc")
            em.dma("qact", (lambda m=m: lambda eng: eng.dma_start(
                out=out_f32.rearrange("(dt p) l -> p dt l", p=128)[:, :, m * MAC:(m + 1) * MAC],
                in_=acc[:, :].rearrange("p (dt c) -> p dt c", dt=DT)))(),
                reads=(bacc,), writes=(), tag=f"{tag} store")



def phase_l1attn(nc, em, C, xs_f32, P, outs_f32, nb=NB, tag="phz"):
    TW = L + 1
    halfpi = C.col(em, float(np.pi / 2))
    with contextlib.ExitStack() as stk:
        sb = lambda name, shape, dt_: stk.enter_context(nc.sbuf_tensor(f"{tag}_{name}", shape, dt_))
        pcols = sb("pc", [128, 7 * DT], F32)
        dcols = sb("dc", [128, 16], F32)
        EC = sb("EC", [128, TW], F32)
        ESn = sb("ESn", [128, TW], F32)
        R = sb("R", [128, TW], F32)
        A = sb("A", [128, L], F32)
        Bt = sb("B", [128, L], F32)
        T = sb("T", [128, L], F32)
        rbc = sb("rbc", [128, L], F32)
        xr = sb("xr", [128, L], F32)
        ur = sb("ur", [128, L], F32)
        ui = sb("ui", [128, L], F32)
        vr = sb("vr", [128, L], F32)
        vi = sb("vi", [128, L], F32)
        bP, bD = Buf("pcols"), Buf("dcols")
        bEC, bES, bR = Buf("EC"), Buf("ESn"), Buf("R")
        bA, bB, bT, brbc = Buf("A"), Buf("B"), Buf("T"), Buf("rbc")
        bx, bur, bui, bvr, bvi = Buf("x"), Buf("ur"), Buf("ui"), Buf("vr"), Buf("vi")
        names = ["angle", "lnm", "scale", "pi_re", "pi_im", "lc_re", "lc_im"]
        for ip, pname in enumerate(names):
            em.dma("qsp", (lambda ip=ip, pname=pname: lambda eng: eng.dma_start(
                out=pcols[:, ip * DT:(ip + 1) * DT], in_=P[pname]))(),
                reads=(), writes=(bP,), tag=f"{tag} ld {pname}")
        coli = {n: ip for ip, n in enumerate(names)}

        def col(n, ct):
            return pcols[:, coli[n] * DT + ct: coli[n] * DT + ct + 1]

        cc = lambda j: dcols[:, j:j + 1]
        for ct in range(DT):
            em.op("act", (lambda ct=ct: lambda eng: eng.activation(cc(0), col("lnm", ct), AF.Exp))(),
                  reads=(bP, bD), writes=(bD,), tag=f"{tag} decay")
            em.op("dve", (lambda ct=ct: lambda eng: eng.tensor_tensor(cc(0), cc(0), col("scale", ct), ALU.mult))(),
                  reads=(bP, bD), writes=(bD,), tag=f"{tag} dscale")
            em.op("act", lambda eng: eng.activation(cc(1), cc(0), AF.Exp, scale=-1.0),
                  reads=(bD,), writes=(bD,), tag=f"{tag} r")
            em.op("dve", lambda eng: eng.tensor_scalar(cc(2), cc(1), -1.0, 1.0, ALU.mult, ALU.add),
                  reads=(bD,), writes=(bD,), tag=f"{tag} gate")
            em.op("dve", (lambda ct=ct: lambda eng: eng.tensor_tensor(cc(3), col("angle", ct), col("scale", ct), ALU.mult))(),
                  reads=(bP, bD), writes=(bD,), tag=f"{tag} theta")
            em.op("act", lambda eng: eng.activation(cc(4), cc(3), AF.Sin, bias=halfpi[0:128, :]),
                  reads=(bD, C.buf), writes=(bD,), tag=f"{tag} C1")
            em.op("act", lambda eng: eng.activation(cc(5), cc(3), AF.Sin),
                  reads=(bD,), writes=(bD,), tag=f"{tag} S1")
            em.op("dve", lambda eng: eng.memset(EC[:, 0:1], 1.0), reads=(bEC,), writes=(bEC,), tag=f"{tag} ec0")
            em.op("dve", lambda eng: eng.memset(ESn[:, 0:1], 0.0), reads=(bES,), writes=(bES,), tag=f"{tag} es0")
            em.op("dve", lambda eng: eng.memset(R[:, 0:1], 1.0), reads=(bR,), writes=(bR,), tag=f"{tag} r0")
            k = 1
            while k <= L:
                w = min(k, TW - k)
                em.op("dve", (lambda k=k: lambda eng: eng.tensor_tensor(cc(8), EC[:, k - 1:k], cc(4), ALU.mult))(),
                      reads=(bEC, bD), writes=(bD,), tag=f"{tag} ck1")
                em.op("dve", (lambda k=k: lambda eng: eng.scalar_tensor_tensor(
                    cc(8), ESn[:, k - 1:k], cc(5), cc(8), ALU.mult, ALU.add))(),
                    reads=(bES, bD), writes=(bD,), tag=f"{tag} ck2")
                em.op("dve", (lambda k=k: lambda eng: eng.tensor_tensor(cc(9), EC[:, k - 1:k], cc(5), ALU.mult))(),
                      reads=(bEC, bD), writes=(bD,), tag=f"{tag} sk1")
                em.op("dve", (lambda k=k: lambda eng: eng.scalar_tensor_tensor(
                    cc(11), ESn[:, k - 1:k], cc(4), cc(9), ALU.mult, ALU.subtract))(),
                    reads=(bES, bD), writes=(bD,), tag=f"{tag} sk2")
                em.op("dve", lambda eng: eng.tensor_scalar(cc(9), cc(11), -1.0, None, ALU.mult),
                      reads=(bD,), writes=(bD,), tag=f"{tag} sk3")
                em.op("dve", (lambda k=k: lambda eng: eng.tensor_tensor(cc(10), R[:, k - 1:k], cc(1), ALU.mult))(),
                      reads=(bR, bD), writes=(bD,), tag=f"{tag} rk")
                em.op("dve", (lambda k=k, w=w: lambda eng: eng.tensor_scalar(
                    EC[:, k:k + w], EC[:, 0:w], cc(8), None, ALU.mult))(),
                    reads=(bEC, bD), writes=(bEC,), tag=f"{tag} ecd1")
                em.op("dve", (lambda k=k, w=w: lambda eng: eng.scalar_tensor_tensor(
                    EC[:, k:k + w], ESn[:, 0:w], cc(9), EC[:, k:k + w], ALU.mult, ALU.add))(),
                    reads=(bES, bD, bEC), writes=(bEC,), tag=f"{tag} ecd2")
                em.op("dve", (lambda k=k, w=w: lambda eng: eng.tensor_scalar(
                    ESn[:, k:k + w], ESn[:, 0:w], cc(8), None, ALU.mult))(),
                    reads=(bES, bD), writes=(bES,), tag=f"{tag} esd1")
                em.op("dve", (lambda k=k, w=w: lambda eng: eng.scalar_tensor_tensor(
                    ESn[:, k:k + w], EC[:, 0:w], cc(9), ESn[:, k:k + w], ALU.mult, ALU.subtract))(),
                    reads=(bEC, bD, bES), writes=(bES,), tag=f"{tag} esd2")
                em.op("dve", (lambda k=k, w=w: lambda eng: eng.tensor_scalar(
                    ESn[:, k:k + w], ESn[:, k:k + w], -1.0, None, ALU.mult))(),
                    reads=(bES,), writes=(bES,), tag=f"{tag} esd3")
                em.op("dve", (lambda k=k, w=w: lambda eng: eng.tensor_scalar(
                    R[:, k:k + w], R[:, 0:w], cc(10), None, ALU.mult))(),
                    reads=(bR, bD), writes=(bR,), tag=f"{tag} rd")
                k *= 2
            em.op("dve", (lambda ct=ct: lambda eng: eng.tensor_scalar(
                A[:, :], EC[:, 0:L], col("pi_re", ct), None, ALU.mult))(),
                reads=(bEC, bP), writes=(bA,), tag=f"{tag} A1")
            em.op("dve", (lambda ct=ct: lambda eng: eng.scalar_tensor_tensor(
                A[:, :], ESn[:, 0:L], col("pi_im", ct), A[:, :], ALU.mult, ALU.add))(),
                reads=(bES, bP, bA), writes=(bA,), tag=f"{tag} A2")
            em.op("dve", (lambda ct=ct: lambda eng: eng.tensor_scalar(
                Bt[:, :], ESn[:, 0:L], col("pi_re", ct), None, ALU.mult))(),
                reads=(bES, bP), writes=(bB,), tag=f"{tag} B1")
            em.op("dve", (lambda ct=ct: lambda eng: eng.scalar_tensor_tensor(
                Bt[:, :], EC[:, 0:L], col("pi_im", ct), Bt[:, :], ALU.mult, ALU.subtract))(),
                reads=(bEC, bP, bB), writes=(bB,), tag=f"{tag} B2")
            em.op("dve", (lambda ct=ct: lambda eng: eng.tensor_scalar(
                T[:, :], EC[:, 1:TW], col("lc_re", ct), None, ALU.mult))(),
                reads=(bEC, bP), writes=(bT,), tag=f"{tag} T1")
            em.op("dve", (lambda ct=ct: lambda eng: eng.scalar_tensor_tensor(
                T[:, :], ESn[:, 1:TW], col("lc_im", ct), T[:, :], ALU.mult, ALU.add))(),
                reads=(bES, bP, bT), writes=(bT,), tag=f"{tag} T2")
            em.op("dve", lambda eng: eng.tensor_tensor(T[:, :], T[:, :], R[:, 1:TW], ALU.mult),
                  reads=(bR, bT), writes=(bT,), tag=f"{tag} T3")
            em.op("dve", lambda eng: eng.tensor_scalar(rbc[:, :], EC[:, 0:L], 0.0, cc(1), ALU.mult, ALU.add),
                  reads=(bEC, bD, brbc), writes=(brbc,), tag=f"{tag} rbc")

            for b in range(nb):
                em.dma("qsp", (lambda ct=ct, b=b: lambda eng: eng.dma_start(
                    out=xr[:, :], in_=xs_f32[b][ct * 128:(ct + 1) * 128, :]))(),
                    reads=(bx,), writes=(bx,), tag=f"{tag} ldx")
                em.op("dve", lambda eng: eng.tensor_mul(ur[:, :], xr[:, :], EC[:, 0:L]),
                      reads=(bx, bEC, bur), writes=(bur,), tag=f"{tag} ur")
                em.op("dve", lambda eng: eng.tensor_mul(ui[:, :], xr[:, :], ESn[:, 0:L]),
                      reads=(bx, bES, bui), writes=(bui,), tag=f"{tag} ui")
                em.op("dve", lambda eng: eng.tensor_tensor_scan(vr[:, :], rbc[:, :], ur[:, :], 0.0, ALU.mult, ALU.add),
                      reads=(brbc, bur, bvr), writes=(bvr,), tag=f"{tag} scanr")
                em.op("dve", lambda eng: eng.tensor_tensor_scan(vi[:, :], rbc[:, :], ui[:, :], 0.0, ALU.mult, ALU.add),
                      reads=(brbc, bui, bvi), writes=(bvi,), tag=f"{tag} scani")
                em.op("dve", lambda eng: eng.tensor_mul(ur[:, :], vr[:, :], A[:, :]),
                      reads=(bvr, bA, bur), writes=(bur,), tag=f"{tag} y1")
                em.op("dve", lambda eng: eng.tensor_mul(ui[:, :], vi[:, :], Bt[:, :]),
                      reads=(bvi, bB, bui), writes=(bui,), tag=f"{tag} y2")
                em.op("dve", lambda eng: eng.tensor_sub(ur[:, :], ur[:, :], ui[:, :]),
                      reads=(bur, bui), writes=(bur,), tag=f"{tag} y3")
                em.op("dve", lambda eng: eng.tensor_add(ur[:, :], ur[:, :], T[:, :]),
                      reads=(bur, bT), writes=(bur,), tag=f"{tag} y4")
                em.op("dve", lambda eng: eng.tensor_scalar(ur[:, :], ur[:, :], cc(2), None, ALU.mult),
                      reads=(bur, bD), writes=(bur,), tag=f"{tag} y5")
                em.dma("qact", (lambda ct=ct, b=b: lambda eng: eng.dma_start(
                    out=outs_f32[b][ct * 128:(ct + 1) * 128, :], in_=ur[:, :]))(),
                    reads=(bur,), writes=(), tag=f"{tag} sty")



FT = FS // 128
BASE_FT = IS // 128


def phase_l1ffn_prep(nc, em, C, W, tag="l1prep", subset=("Wm", "Wp", "Wc1", "Wc2")):
    WmR = nc.dram_tensor(f"{tag}_WmR", [FS, D], BF16, kind="Internal").ap()
    WmT = nc.dram_tensor(f"{tag}_WmT", [D, FS], BF16, kind="Internal").ap()
    WpT = nc.dram_tensor(f"{tag}_WpT", [FS, D], BF16, kind="Internal").ap()
    Wc1T = nc.dram_tensor(f"{tag}_Wc1T", [D, 256], BF16, kind="Internal").ap()
    Wc2T = nc.dram_tensor(f"{tag}_Wc2T", [256, FS], BF16, kind="Internal").ap()
    with contextlib.ExitStack() as stk:
        tin = [stk.enter_context(nc.sbuf_tensor(f"{tag}_i{i}", [128, FS], BF16)) for i in range(2)]
        tout = [stk.enter_context(nc.sbuf_tensor(f"{tag}_o{i}", [128, FS], BF16)) for i in range(2)]
        ps = [stk.enter_context(nc.psum_tensor(f"{tag}_p{i}", [128, 128], BF16)) for i in range(2)]
        bi = [Buf("i0"), Buf("i1")]
        bo = [Buf("o0"), Buf("o1")]
        bp = [Buf("p0"), Buf("p1")]

        def transpose_mat(src, rows, cols, dst, tg, rowcopy=None):
            nr, ncol = rows // 128, cols // 128
            for rt in range(nr):
                i = rt % 2
                em.dma("qpool", (lambda rt=rt, i=i: lambda eng: eng.dma_start(
                    out=tin[i][:, 0:cols], in_=src[rt * 128:(rt + 1) * 128, :]))(),
                    reads=(bo[i],), writes=(bi[i],), tag=f"{tg} in")
                for ct_ in range(ncol):
                    j = ct_ % 2
                    em.op("pe", (lambda i=i, j=j, ct_=ct_: lambda eng: eng.transpose(
                        ps[j][:, :], tin[i][:, ct_ * 128:(ct_ + 1) * 128], C.ident16[:, :]))(),
                        reads=(bi[i], bp[j], C.buf), writes=(bp[j],), tag=f"{tg} tr")
                    em.op("dve", (lambda i=i, j=j, ct_=ct_: lambda eng: eng.tensor_copy(
                        tout[i][:, ct_ * 128:(ct_ + 1) * 128], ps[j][:, :]))(),
                        reads=(bp[j], bo[i]), writes=(bo[i],), tag=f"{tg} ev")
                em.dma("qact", (lambda rt=rt, i=i, ncol=ncol: lambda eng: eng.dma_start(
                    out=dst.rearrange("(ct p) r -> p ct r", p=128)[:, :, rt * 128:(rt + 1) * 128],
                    in_=tout[i][:, 0:cols].rearrange("p (ct c) -> p ct c", ct=ncol)))(),
                    reads=(bo[i],), writes=(), tag=f"{tg} out")
                if rowcopy is not None:
                    em.dma("qact", (lambda rt=rt, i=i, cols=cols: lambda eng: eng.dma_start(
                        out=rowcopy[rt * 128:(rt + 1) * 128, :], in_=tin[i][:, 0:cols]))(),
                        reads=(bi[i],), writes=(), tag=f"{tg} rowc")

        if "Wm" in subset:
            transpose_mat(W["Wm"], FS, D, WmT, f"{tag}m", rowcopy=WmR)
        if "Wp" in subset:
            transpose_mat(W["Wp"], D, FS, WpT, f"{tag}p")
        if "Wc1" in subset:
            transpose_mat(W["Wc1"], 256, D, Wc1T, f"{tag}c1")
        if "Wc2" in subset:
            transpose_mat(W["Wc2"], FS, 256, Wc2T, f"{tag}c2")
    return dict(WmT=WmT, WpT=WpT, Wc1T=Wc1T, Wc2T=Wc2T, WmR=WmR)


def emit_bisect(nc, em, C, stk, ck_sb, bck, mask_sb, bmask, k_count, tag, nit=45):
    sc = stk.enter_context(nc.sbuf_tensor(f"{tag}_bs", [1, 8], F32))
    colb = stk.enter_context(nc.sbuf_tensor(f"{tag}_bc", [128, 3], F32))
    scr = stk.enter_context(nc.sbuf_tensor(f"{tag}_bscr", [128, FT], F32))
    ptr1 = stk.enter_context(nc.psum_tensor(f"{tag}_bp1", [1, 128], F32))
    pbc = stk.enter_context(nc.psum_tensor(f"{tag}_bp2", [128, 1], F32))
    pcnt = stk.enter_context(nc.psum_tensor(f"{tag}_bp3", [1, 1], F32))
    bs, bcol, bscr = Buf("bs"), Buf("bcol"), Buf("bscr")
    bp1, bp2, bp3 = Buf("bp1"), Buf("bp2"), Buf("bp3")
    em.op("dve", lambda eng: eng.tensor_reduce(colb[:, 0:1], ck_sb[:, :], AX.X, ALU.max),
          reads=(bck, bcol), writes=(bcol,), tag=f"{tag} mx1")
    em.op("dve", lambda eng: eng.tensor_reduce(colb[:, 1:2], ck_sb[:, :], AX.X, ALU.min),
          reads=(bck, bcol), writes=(bcol,), tag=f"{tag} mn1")
    em.op("pe", lambda eng: eng.transpose(ptr1[0:1, :], colb[:, 0:1], C.ident32[:, :]),
          reads=(bcol, C.buf, bp1), writes=(bp1,), tag=f"{tag} tr1")
    em.op("dve", lambda eng: eng.tensor_reduce(sc[:, 1:2], ptr1[0:1, :], AX.X, ALU.max),
          reads=(bp1, bs), writes=(bs,), tag=f"{tag} mx2")
    em.op("pe", lambda eng: eng.transpose(ptr1[0:1, :], colb[:, 1:2], C.ident32[:, :]),
          reads=(bcol, C.buf, bp1, bs), writes=(bp1,), tag=f"{tag} tr2")
    em.op("dve", lambda eng: eng.tensor_reduce(sc[:, 0:1], ptr1[0:1, :], AX.X, ALU.min),
          reads=(bp1, bs), writes=(bs,), tag=f"{tag} mn2")
    em.op("dve", lambda eng: eng.tensor_scalar(sc[:, 0:1], sc[:, 0:1], 1.0, None, ALU.subtract),
          reads=(bs,), writes=(bs,), tag=f"{tag} lo-")
    em.op("dve", lambda eng: eng.tensor_scalar(sc[:, 1:2], sc[:, 1:2], 1.0, None, ALU.add),
          reads=(bs,), writes=(bs,), tag=f"{tag} hi+")
    for it in range(nit):
        em.op("dve", lambda eng: eng.tensor_add(sc[:, 2:3], sc[:, 0:1], sc[:, 1:2]),
              reads=(bs,), writes=(bs,), tag=f"{tag} mid1")
        em.op("dve", lambda eng: eng.tensor_scalar(sc[:, 2:3], sc[:, 2:3], 0.5, None, ALU.mult),
              reads=(bs,), writes=(bs,), tag=f"{tag} mid2")
        em.op("pe", lambda eng: eng.matmul(pbc[:, :], C.ones_row[0:1, 0:128], sc[0:1, 2:3],
                                           start=True, stop=True),
              reads=(bs, C.buf, bp2), writes=(bp2,), tag=f"{tag} bcmid")
        em.op("dve", lambda eng: eng.tensor_copy(colb[:, 2:3], pbc[:, :]),
              reads=(bp2, bcol), writes=(bcol,), tag=f"{tag} midcol")
        em.op("dve", lambda eng: eng.tensor_scalar(scr[:, :], ck_sb[:, :], colb[:, 2:3], None,
                                                   ALU.is_ge, ALU.add, accum_out=colb[:, 0:1]),
              reads=(bck, bcol, bscr), writes=(bscr, bcol), tag=f"{tag} cmp")
        em.op("pe", lambda eng: eng.matmul(pcnt[:, :], C.ones_col[:, :], colb[:, 0:1],
                                           start=True, stop=True),
              reads=(bcol, C.buf, bp3), writes=(bp3,), tag=f"{tag} cnt")
        em.op("dve", (lambda k_count=k_count: lambda eng: eng.tensor_scalar(
            sc[:, 3:4], pcnt[0:1, 0:1], float(k_count), None, ALU.is_gt))(),
            reads=(bp3, bs), writes=(bs,), tag=f"{tag} gt")
        em.op("dve", lambda eng: eng.tensor_sub(sc[:, 4:5], sc[:, 2:3], sc[:, 0:1]),
              reads=(bs,), writes=(bs,), tag=f"{tag} lod")
        em.op("dve", lambda eng: eng.tensor_scalar(sc[:, 4:5], sc[:, 4:5], sc[0:1, 3:4], None, ALU.mult),
              reads=(bs,), writes=(bs,), tag=f"{tag} lom")
        em.op("dve", lambda eng: eng.tensor_add(sc[:, 0:1], sc[:, 0:1], sc[:, 4:5]),
              reads=(bs,), writes=(bs,), tag=f"{tag} lo")
        em.op("dve", lambda eng: eng.tensor_sub(sc[:, 4:5], sc[:, 1:2], sc[:, 2:3]),
              reads=(bs,), writes=(bs,), tag=f"{tag} hid")
        em.op("dve", lambda eng: eng.tensor_scalar(sc[:, 4:5], sc[:, 4:5], sc[0:1, 3:4], None, ALU.mult),
              reads=(bs,), writes=(bs,), tag=f"{tag} him")
        em.op("dve", lambda eng: eng.tensor_add(sc[:, 1:2], sc[:, 2:3], sc[:, 4:5]),
              reads=(bs,), writes=(bs,), tag=f"{tag} hi")
    em.op("pe", lambda eng: eng.matmul(pbc[:, :], C.ones_row[0:1, 0:128], sc[0:1, 1:2],
                                       start=True, stop=True),
          reads=(bs, C.buf, bp2), writes=(bp2,), tag=f"{tag} bchi")
    em.op("dve", lambda eng: eng.tensor_copy(colb[:, 2:3], pbc[:, :]),
          reads=(bp2, bcol), writes=(bcol,), tag=f"{tag} hicol")
    em.op("dve", lambda eng: eng.tensor_scalar(mask_sb[:, :], ck_sb[:, :], colb[:, 2:3], None, ALU.is_ge),
          reads=(bck, bcol, bmask), writes=(bmask,), tag=f"{tag} mask")


def phase_l1ffn_step(nc, em, C, xin_bf, Wsc, xout_bf, xout_f32, b, step,
                     active_dram=None, mask_new_dram=None, pooled_dram=None,
                     Lt=L, tag=None):
    import os
    _off = set(os.environ.get("L1F_OFF", "").split(","))
    """One scan step of L1 ffn for batch b.
    xin_bf: DRAM [D, L] bf16 input activations (f-major).
    xout_bf/xout_f32: DRAM [D, L] outputs (x_new).
    step==1: active = first IS features (tile-aligned); else active_dram [128, FT].
    If mask_new_dram: compute pooled->ck->bisect mask + novelty counts."""
    tag = tag or f"l1s{step}b{b}"
    CH = 512
    NCH = Lt // CH
    do_ck = pooled_dram is not None
    with contextlib.ExitStack() as stk:
        sb = lambda name, shape, dt_: stk.enter_context(nc.sbuf_tensor(f"{tag}_{name}", shape, dt_))
        xcb = sb("xcb", [128, DT * CH], BF16)
        hs = sb("hs", [128, FT * CH], BF16)
        wm = [sb(f"wm{i}", [128, DT * 512], BF16) for i in range(2)]
        wp = [sb(f"wp{i}", [128, FT * 128], BF16) for i in range(2)]
        xnf = sb("xnf", [128, DT * CH], F32)
        xnb = sb("xnb", [128, DT * CH], BF16)
        amask = sb("amask", [128, FT], F32)
        pooled = sb("pooled", [128, 2 * DT], F32)
        ph = [stk.enter_context(nc.psum_tensor(f"{tag}_ph{i}", [128, CH], F32)) for i in range(2)]
        px = [stk.enter_context(nc.psum_tensor(f"{tag}_px{i}", [128, CH], F32)) for i in range(2)]
        bxcb, bhs, bxnf, bxnb = Buf("xcb"), Buf("hs"), Buf("xnf"), Buf("xnb")
        bwm = [Buf("wm0"), Buf("wm1")]
        bwp = [Buf("wp0"), Buf("wp1")]
        bam, bpool = Buf("amask"), Buf("pooled")
        bph = [Buf("ph0"), Buf("ph1")]
        bpx = [Buf("px0"), Buf("px1")]
        if step == 1:
            ftl = list(range(BASE_FT))
        else:
            ftl = list(range(FT))
            em.dma("qsp", lambda eng: eng.dma_start(out=amask[:, :], in_=active_dram),
                   reads=(), writes=(bam,), tag=f"{tag} ldmask")
        em.op("dve", lambda eng: eng.memset(pooled[:, :], 0.0),
              reads=(), writes=(bpool,), tag=f"{tag} pool0")
        for c in range(NCH):
            em.dma("qsp", (lambda c=c: lambda eng: eng.dma_start(
                out=xcb[:, :].rearrange("p (dt t) -> p dt t", dt=DT),
                in_=xin_bf.rearrange("(dt p) l -> p dt l", p=128)[:, :, c * CH:(c + 1) * CH]))(),
                reads=(bxcb,), writes=(bxcb,), tag=f"{tag} ldx")
            for fqi, fq in enumerate(range(0, len(ftl), 4)):
                i = fqi % 2
                f0 = ftl[fq]
                em.dma("qsp", (lambda f0=f0, i=i: lambda eng: eng.dma_start(
                    out=wm[i][:, :].rearrange("p (dt f) -> p dt f", dt=DT),
                    in_=Wsc["WmT"].rearrange("(dt p) f -> p dt f", p=128)[:, :, f0 * 128:(f0 + 4) * 128]))(),
                    reads=(bwm[i],), writes=(bwm[i],), tag=f"{tag} wm")
                for fi in range(4):
                    ft = ftl[fq + fi]
                    j = ft % 2
                    for dt in range(DT):
                        em.op("pe", (lambda i=i, j=j, fi=fi, dt=dt: lambda eng: eng.matmul(
                            ph[j][:, :], wm[i][:, dt * 512 + fi * 128: dt * 512 + (fi + 1) * 128],
                            xcb[:, dt * CH:(dt + 1) * CH], start=(dt == 0), stop=(dt == DT - 1)))(),
                            reads=(bwm[i], bxcb), writes=(bph[j],), tag=f"{tag} hmm")
                    if step == 1:
                        em.op("act", (lambda j=j, ft=ft: lambda eng: eng.activation(
                            hs[:, ft * CH:(ft + 1) * CH], ph[j][:, :], AF.Relu))(),
                            reads=(bph[j], bhs), writes=(bhs,), tag=f"{tag} relu")
                    else:
                        em.op("act", (lambda j=j, ft=ft: lambda eng: eng.activation(
                            hs[:, ft * CH:(ft + 1) * CH], ph[j][:, :], AF.Relu,
                            scale=amask[:, ft:ft + 1]))(),
                            reads=(bph[j], bam, bhs), writes=(bhs,), tag=f"{tag} relum")
            for ct in range(DT if "xphase" not in _off else 0):
                i = ct % 2
                FQ = FT // 4
                for q in range(4):
                    em.dma("qact", (lambda ct=ct, i=i, q=q, FQ=FQ: lambda eng: eng.dma_start(
                        out=wp[i][:, q * FQ * 128:(q + 1) * FQ * 128].rearrange("p (ft c) -> p ft c", ft=FQ),
                        in_=Wsc["WpT"].rearrange("(ft p) c -> p ft c", p=128)[:, q * FQ:(q + 1) * FQ, ct * 128:(ct + 1) * 128]))(),
                        reads=(bwp[i],), writes=(bwp[i],), tag=f"{tag} wp{q}")
                for fi, ft in enumerate(ftl):
                    em.op("pe", (lambda i=i, ft=ft: lambda eng: eng.matmul(
                        px[i][:, :], wp[i][:, ft * 128:(ft + 1) * 128],
                        hs[:, ft * CH:(ft + 1) * CH], start=(ft == ftl[0]), stop=(ft == ftl[-1])))(),
                        reads=(bwp[i], bhs), writes=(bpx[i],), tag=f"{tag} xmm")
                if "evf" not in _off:
                    em.op("dve", (lambda i=i, ct=ct: lambda eng: eng.tensor_copy(
                        xnf[:, ct * CH:(ct + 1) * CH], px[i][:, :]))(),
                        reads=(bpx[i], bxnf), writes=(bxnf,), tag=f"{tag} evf")
                if "evb" not in _off:
                    em.op("act", (lambda i=i, ct=ct: lambda eng: eng.copy(
                        xnb[:, ct * CH:(ct + 1) * CH], xnf[:, ct * CH:(ct + 1) * CH]))(),
                        reads=(bxnf, bxnb), writes=(bxnb,), tag=f"{tag} evb")
                if do_ck and "pool" not in _off:
                    em.op("dve", (lambda ct=ct: lambda eng: eng.tensor_reduce(
                        pooled[:, DT + ct:DT + ct + 1],
                        xnf[:, ct * CH:(ct + 1) * CH], AX.X, ALU.add))(),
                        reads=(bxnf, bpool), writes=(bpool,), tag=f"{tag} poolred")
                    em.op("dve", (lambda ct=ct: lambda eng: eng.tensor_add(
                        pooled[:, ct:ct + 1], pooled[:, ct:ct + 1],
                        pooled[:, DT + ct:DT + ct + 1]))(),
                        reads=(bpool,), writes=(bpool,), tag=f"{tag} pooladd")
            if "stf" not in _off and "xphase" not in _off:
                em.dma("qact", (lambda c=c: lambda eng: eng.dma_start(
                    out=xout_f32.rearrange("(dt p) l -> p dt l", p=128)[:, :, c * CH:(c + 1) * CH],
                    in_=xnf[:, :].rearrange("p (dt t) -> p dt t", dt=DT)))(),
                    reads=(bxnf,), writes=(), tag=f"{tag} stf")
            if "stb" not in _off and "xphase" not in _off:
                em.dma("qact", (lambda c=c: lambda eng: eng.dma_start(
                    out=xout_bf.rearrange("(dt p) l -> p dt l", p=128)[:, :, c * CH:(c + 1) * CH],
                    in_=xnb[:, :].rearrange("p (dt t) -> p dt t", dt=DT)))(),
                    reads=(bxnb,), writes=(), tag=f"{tag} stb")
        if do_ck and "pool" not in _off:
            em.op("dve", lambda eng: eng.tensor_scalar(pooled[:, 0:DT], pooled[:, 0:DT],
                                                       1.0 / Lt, None, ALU.mult),
                  reads=(bpool,), writes=(bpool,), tag=f"{tag} poolavg")
            em.dma("qact", lambda eng: eng.dma_start(out=pooled_dram, in_=pooled[:, 0:DT]),
                   reads=(bpool,), writes=(), tag=f"{tag} poolst")


def phase_l1ffn_ck(nc, em, C, pooled_dram, Wsc, mask_new_dram, hist_in_dram,
                   hist_out_dram, nov_out_dram, tag):
    with contextlib.ExitStack() as stk:
        sb = lambda name, shape, dt_: stk.enter_context(nc.sbuf_tensor(f"{tag}_{name}", shape, dt_))
        pooled = sb("pooled", [128, DT], F32)
        pooled_bf = sb("poolbf", [128, DT], BF16)
        wc1 = sb("wc1", [128, DT * 256], BF16)
        wc2 = sb("wc2", [128, 2 * (FS // 2)], BF16)
        ubf = sb("ubf", [128, 2], BF16)
        cks = sb("cks", [128, FT], F32)
        msk = sb("msk", [128, FT], F32)
        hist = sb("hist", [128, FT], F32)
        scr = sb("scr", [128, FT], F32)
        nov = sb("nov", [1, 1], F32)
        pu = stk.enter_context(nc.psum_tensor(f"{tag}_pu", [128, 2], F32))
        pck = stk.enter_context(nc.psum_tensor(f"{tag}_pck", [128, FT], F32))
        pnv = stk.enter_context(nc.psum_tensor(f"{tag}_pnv", [1, 1], F32))
        bpool, bpbf, bwc1, bwc2 = Buf("pool"), Buf("pbf"), Buf("wc1"), Buf("wc2")
        bu, bck, bmsk, bhist, bscr, bnov = Buf("u"), Buf("ck"), Buf("msk"), Buf("hist"), Buf("scr"), Buf("nov")
        bpu, bpck, bpnv = Buf("pu"), Buf("pck"), Buf("pnv")
        em.dma("qsp", lambda eng: eng.dma_start(out=pooled[:, :], in_=pooled_dram),
               reads=(), writes=(bpool,), tag=f"{tag} ldpool")
        em.op("dve", lambda eng: eng.tensor_copy(pooled_bf[:, :], pooled[:, :]),
              reads=(bpool,), writes=(bpbf,), tag=f"{tag} poolbf")
        em.dma("qsp", lambda eng: eng.dma_start(
            out=wc1[:, :].rearrange("p (ct r) -> p ct r", ct=DT),
            in_=Wsc["Wc1T"].rearrange("(ct p) r -> p ct r", p=128)),
            reads=(), writes=(bwc1,), tag=f"{tag} ldwc1")
        for rt in range(2):
            for ct in range(DT):
                em.op("pe", (lambda rt=rt, ct=ct: lambda eng: eng.matmul(
                    pu[:, rt:rt + 1], wc1[:, ct * 256 + rt * 128: ct * 256 + (rt + 1) * 128],
                    pooled_bf[:, ct:ct + 1],
                    start=(rt == 0 and ct == 0), stop=(rt == 1 and ct == DT - 1)))(),
                    reads=(bwc1, bpbf), writes=(bpu,), tag=f"{tag} umm")
        em.op("act", lambda eng: eng.activation(ubf[:, :], pu[:, :], AF.Relu),
              reads=(bpu,), writes=(bu,), tag=f"{tag} urelu")
        FH = FT // 2
        for half in range(2):
            em.dma("qsp", (lambda half=half: lambda eng: eng.dma_start(
                out=wc2[:, :].rearrange("p (rt f) -> p rt f", rt=2),
                in_=Wsc["Wc2T"].rearrange("(rt p) f -> p rt f", p=128)[:, :, half * (FS // 2):(half + 1) * (FS // 2)]))(),
                reads=(bwc2,), writes=(bwc2,), tag=f"{tag} ldwc2")
            for fi in range(FH):
                ft = half * FH + fi
                for rt in range(2):
                    em.op("pe", (lambda fi=fi, ft=ft, rt=rt, half=half: lambda eng: eng.matmul(
                        pck[:, ft:ft + 1], wc2[:, rt * (FS // 2) + fi * 128: rt * (FS // 2) + (fi + 1) * 128],
                        ubf[:, rt:rt + 1],
                        start=(ft == 0 and rt == 0), stop=(ft == FT - 1 and rt == 1)))(),
                        reads=(bwc2, bu), writes=(bpck,), tag=f"{tag} ckmm")
        em.op("dve", lambda eng: eng.tensor_copy(cks[:, :], pck[:, :]),
              reads=(bpck,), writes=(bck,), tag=f"{tag} ckev")
        emit_bisect(nc, em, C, stk, cks, bck, msk, bmsk, IS, tag)
        em.dma("qact", lambda eng: eng.dma_start(out=mask_new_dram, in_=msk[:, :]),
               reads=(bmsk,), writes=(), tag=f"{tag} stmask")
        if hist_in_dram is None:
            em.op("dve", lambda eng: eng.memset(hist[:, 0:BASE_FT], 1.0),
                  reads=(), writes=(bhist,), tag=f"{tag} hbase1")
            em.op("dve", lambda eng: eng.memset(hist[:, BASE_FT:FT], 0.0),
                  reads=(bhist,), writes=(bhist,), tag=f"{tag} hbase0")
        else:
            em.dma("qsp", lambda eng: eng.dma_start(out=hist[:, :], in_=hist_in_dram),
                   reads=(), writes=(bhist,), tag=f"{tag} ldhist")
        em.op("dve", lambda eng: eng.tensor_max(scr[:, :], hist[:, :], msk[:, :]),
              reads=(bhist, bmsk), writes=(bscr,), tag=f"{tag} comb")
        if hist_out_dram is not None:
            em.dma("qact", lambda eng: eng.dma_start(out=hist_out_dram, in_=scr[:, :]),
                   reads=(bscr,), writes=(), tag=f"{tag} sthist")
        em.op("dve", lambda eng: eng.tensor_sub(hist[:, :], scr[:, :], hist[:, :]),
              reads=(bscr, bhist), writes=(bhist,), tag=f"{tag} delta")
        em.op("dve", lambda eng: eng.tensor_scalar(hist[:, :], hist[:, :], 1.0, None,
                                                   ALU.mult, ALU.add, accum_out=scr[:, 0:1]),
              reads=(bhist, bscr), writes=(bhist, bscr), tag=f"{tag} dcount")
        em.op("pe", lambda eng: eng.matmul(pnv[:, :], C.ones_col[:, :], scr[:, 0:1],
                                           start=True, stop=True),
              reads=(bscr, C.buf, bpnv), writes=(bpnv,), tag=f"{tag} dsum")
        em.op("dve", lambda eng: eng.tensor_copy(nov[:, :], pnv[:, :]),
              reads=(bpnv, bnov), writes=(bnov,), tag=f"{tag} novev")
        em.dma("qact", lambda eng: eng.dma_start(out=nov_out_dram, in_=nov[:, :]),
               reads=(bnov,), writes=(), tag=f"{tag} stnov")


def phase_l1ffn_done(nc, em, nov_all_dram, n_scalars_per_step, d_dram, tag="done"):
    thr = RNOV * 16 * IS
    n = n_scalars_per_step
    with contextlib.ExitStack() as stk:
        novr = stk.enter_context(nc.sbuf_tensor(f"{tag}_n", [1, 2 * n], F32))
        dd = stk.enter_context(nc.sbuf_tensor(f"{tag}_d", [1, 4], F32))
        bn, bd = Buf("novr"), Buf("dd")
        em.dma("qsp", lambda eng: eng.dma_start(
            out=novr[:, :], in_=nov_all_dram.rearrange("a b -> (a b)").unsqueeze(0)),
            reads=(), writes=(bn,), tag=f"{tag} ld")
        em.op("dve", lambda eng: eng.tensor_reduce(dd[:, 0:1], novr[0:1, 0:n], AX.X, ALU.add),
              reads=(bn, bd), writes=(bd,), tag=f"{tag} s1")
        em.op("dve", lambda eng: eng.tensor_reduce(dd[:, 1:2], novr[0:1, n:2 * n], AX.X, ALU.add),
              reads=(bn, bd), writes=(bd,), tag=f"{tag} s2")
        em.op("dve", (lambda thr=thr: lambda eng: eng.tensor_scalar(
            dd[:, 0:2], dd[:, 0:2], float(thr), None, ALU.is_lt))(),
            reads=(bd,), writes=(bd,), tag=f"{tag} lt")
        em.op("dve", lambda eng: eng.tensor_max(dd[:, 1:2], dd[:, 1:2], dd[:, 0:1]),
              reads=(bd,), writes=(bd,), tag=f"{tag} or")
        em.dma("qact", lambda eng: eng.dma_start(out=d_dram, in_=dd[:, 0:2]),
               reads=(bd,), writes=(), tag=f"{tag} st")


def phase_l1ffn_active3(nc, em, C, mask1_dram, mask2_dram, d_dram, active3_dram, tag="act3"):
    with contextlib.ExitStack() as stk:
        m1 = stk.enter_context(nc.sbuf_tensor(f"{tag}_m1", [128, FT], F32))
        m2 = stk.enter_context(nc.sbuf_tensor(f"{tag}_m2", [128, FT], F32))
        dd = stk.enter_context(nc.sbuf_tensor(f"{tag}_d", [1, 2], F32))
        dcol = stk.enter_context(nc.sbuf_tensor(f"{tag}_dc", [128, 1], F32))
        pb = stk.enter_context(nc.psum_tensor(f"{tag}_pb", [128, 1], F32))
        bm1, bm2, bd, bdc, bpb = Buf("m1"), Buf("m2"), Buf("d"), Buf("dc"), Buf("pb")
        em.dma("qsp", lambda eng: eng.dma_start(out=m1[:, :], in_=mask1_dram),
               reads=(), writes=(bm1,), tag=f"{tag} l1")
        em.dma("qsp", lambda eng: eng.dma_start(out=m2[:, :], in_=mask2_dram),
               reads=(), writes=(bm2,), tag=f"{tag} l2")
        em.dma("qsp", lambda eng: eng.dma_start(out=dd[:, :], in_=d_dram),
               reads=(), writes=(bd,), tag=f"{tag} ld")
        em.op("pe", lambda eng: eng.matmul(pb[:, :], C.ones_row[0:1, 0:128], dd[0:1, 0:1],
                                           start=True, stop=True),
              reads=(bd, C.buf, bpb), writes=(bpb,), tag=f"{tag} bc")
        em.op("dve", lambda eng: eng.tensor_copy(dcol[:, :], pb[:, :]),
              reads=(bpb, bdc), writes=(bdc,), tag=f"{tag} dcol")
        em.op("dve", lambda eng: eng.tensor_sub(m1[:, :], m1[:, :], m2[:, :]),
              reads=(bm1, bm2), writes=(bm1,), tag=f"{tag} sub")
        em.op("dve", lambda eng: eng.tensor_scalar(m1[:, :], m1[:, :], dcol[:, :], None, ALU.mult),
              reads=(bm1, bdc), writes=(bm1,), tag=f"{tag} mul")
        em.op("dve", lambda eng: eng.tensor_add(m1[:, :], m1[:, :], m2[:, :]),
              reads=(bm1, bm2), writes=(bm1,), tag=f"{tag} add")
        em.dma("qact", lambda eng: eng.dma_start(out=active3_dram, in_=m1[:, :]),
               reads=(bm1,), writes=(), tag=f"{tag} st")


def phase_l1ffn_blend(nc, em, C, xn1, xn2, xn3, d_dram, xfin_f32, Lt=L, tag="blend"):
    CH = 512
    with contextlib.ExitStack() as stk:
        sb = lambda name, shape, dt_: stk.enter_context(nc.sbuf_tensor(f"{tag}_{name}", shape, dt_))
        t1 = sb("t1", [128, DT * CH], F32)
        t2 = sb("t2", [128, DT * CH], F32)
        t3 = sb("t3", [128, DT * CH], F32)
        dd = sb("d", [1, 2], F32)
        dcol = sb("dc", [128, 2], F32)
        pb = stk.enter_context(nc.psum_tensor(f"{tag}_pb", [128, 2], F32))
        b1, b2, b3, bd, bdc, bpb = Buf("t1"), Buf("t2"), Buf("t3"), Buf("d"), Buf("dc"), Buf("pb")
        em.dma("qsp", lambda eng: eng.dma_start(out=dd[:, :], in_=d_dram),
               reads=(), writes=(bd,), tag=f"{tag} ld")
        em.op("pe", lambda eng: eng.matmul(pb[:, 0:1], C.ones_row[0:1, 0:128], dd[0:1, 0:1],
                                           start=True, stop=False),
              reads=(bd, C.buf, bpb), writes=(bpb,), tag=f"{tag} bc1")
        em.op("pe", lambda eng: eng.matmul(pb[:, 1:2], C.ones_row[0:1, 0:128], dd[0:1, 1:2],
                                           start=False, stop=True),
              reads=(bd, C.buf, bpb), writes=(bpb,), tag=f"{tag} bc2")
        em.op("dve", lambda eng: eng.tensor_copy(dcol[:, :], pb[:, :]),
              reads=(bpb, bdc), writes=(bdc,), tag=f"{tag} dcol")
        for c in range(Lt // CH):
            for (t, bt, src) in ((t1, b1, xn1), (t2, b2, xn2), (t3, b3, xn3)):
                em.dma("qsp", (lambda t=t, src=src, c=c: lambda eng: eng.dma_start(
                    out=t[:, :].rearrange("p (dt w) -> p dt w", dt=DT),
                    in_=src.rearrange("(dt p) l -> p dt l", p=128)[:, :, c * CH:(c + 1) * CH]))(),
                    reads=(bt,), writes=(bt,), tag=f"{tag} ldx")
            em.op("dve", lambda eng: eng.tensor_sub(t1[:, :], t1[:, :], t2[:, :]),
                  reads=(b1, b2), writes=(b1,), tag=f"{tag} s1")
            em.op("dve", lambda eng: eng.tensor_scalar(t1[:, :], t1[:, :], dcol[:, 0:1], None, ALU.mult),
                  reads=(b1, bdc), writes=(b1,), tag=f"{tag} m1")
            em.op("dve", lambda eng: eng.tensor_add(t1[:, :], t1[:, :], t2[:, :]),
                  reads=(b1, b2), writes=(b1,), tag=f"{tag} a1")
            em.op("dve", lambda eng: eng.tensor_sub(t1[:, :], t1[:, :], t3[:, :]),
                  reads=(b1, b3), writes=(b1,), tag=f"{tag} s2")
            em.op("dve", lambda eng: eng.tensor_scalar(t1[:, :], t1[:, :], dcol[:, 1:2], None, ALU.mult),
                  reads=(b1, bdc), writes=(b1,), tag=f"{tag} m2")
            em.op("dve", lambda eng: eng.tensor_add(t1[:, :], t1[:, :], t3[:, :]),
                  reads=(b1, b3), writes=(b1,), tag=f"{tag} a2")
            em.dma("qact", (lambda c=c: lambda eng: eng.dma_start(
                out=xfin_f32.rearrange("(dt p) l -> p dt l", p=128)[:, :, c * CH:(c + 1) * CH],
                in_=t1[:, :].rearrange("p (dt w) -> p dt w", dt=DT)))(),
                reads=(b1,), writes=(), tag=f"{tag} st")



def l0_generalize_bisect():
    pass


def emit_bisect2(nc, em, C, stk, vals, bvals, mask_sb, bmask, P, W, k_count, tag, nit=45):
    sc = stk.enter_context(nc.sbuf_tensor(f"{tag}_bs", [1, 8], F32))
    colb = stk.enter_context(nc.sbuf_tensor(f"{tag}_bc", [128, 3], F32))
    scr = stk.enter_context(nc.sbuf_tensor(f"{tag}_bscr", [P, W], F32))
    ptr1 = stk.enter_context(nc.psum_tensor(f"{tag}_bp1", [1, 128], F32))
    pbc = stk.enter_context(nc.psum_tensor(f"{tag}_bp2", [128, 1], F32))
    pcnt = stk.enter_context(nc.psum_tensor(f"{tag}_bp3", [1, 1], F32))
    bs, bcol, bscr = Buf("bs"), Buf("bcol"), Buf("bscr")
    bp1, bp2, bp3 = Buf("bp1"), Buf("bp2"), Buf("bp3")
    em.op("dve", lambda eng: eng.memset(colb[:, :], 0.0),
          reads=(bcol,), writes=(bcol,), tag=f"{tag} colz")
    em.op("dve", lambda eng: eng.tensor_reduce(colb[0:P, 0:1], vals[:, :], AX.X, ALU.max),
          reads=(bvals, bcol), writes=(bcol,), tag=f"{tag} mx1")
    em.op("dve", lambda eng: eng.tensor_reduce(colb[0:P, 1:2], vals[:, :], AX.X, ALU.min),
          reads=(bvals, bcol), writes=(bcol,), tag=f"{tag} mn1")
    if P < 128:
        em.op("dve", (lambda: lambda eng: eng.tensor_copy(colb[P:128, 0:2], colb[0:1, 0:2].partition_broadcast(128 - P)))() if False else
              (lambda: lambda eng: eng.memset(colb[P:128, 0:1], -3e38))(),
              reads=(bcol,), writes=(bcol,), tag=f"{tag} padmx")
        em.op("dve", lambda eng: eng.memset(colb[P:128, 1:2], 3e38),
              reads=(bcol,), writes=(bcol,), tag=f"{tag} padmn")
    em.op("pe", lambda eng: eng.transpose(ptr1[0:1, :], colb[:, 0:1], C.ident32[:, :]),
          reads=(bcol, C.buf, bp1), writes=(bp1,), tag=f"{tag} tr1")
    em.op("dve", lambda eng: eng.tensor_reduce(sc[:, 1:2], ptr1[0:1, :], AX.X, ALU.max),
          reads=(bp1, bs), writes=(bs,), tag=f"{tag} mx2")
    em.op("pe", lambda eng: eng.transpose(ptr1[0:1, :], colb[:, 1:2], C.ident32[:, :]),
          reads=(bcol, C.buf, bp1, bs), writes=(bp1,), tag=f"{tag} tr2")
    em.op("dve", lambda eng: eng.tensor_reduce(sc[:, 0:1], ptr1[0:1, :], AX.X, ALU.min),
          reads=(bp1, bs), writes=(bs,), tag=f"{tag} mn2")
    em.op("dve", lambda eng: eng.tensor_scalar(sc[:, 0:1], sc[:, 0:1], 1.0, None, ALU.subtract),
          reads=(bs,), writes=(bs,), tag=f"{tag} lo-")
    em.op("dve", lambda eng: eng.tensor_scalar(sc[:, 1:2], sc[:, 1:2], 1.0, None, ALU.add),
          reads=(bs,), writes=(bs,), tag=f"{tag} hi+")
    for it in range(nit):
        em.op("dve", lambda eng: eng.tensor_add(sc[:, 2:3], sc[:, 0:1], sc[:, 1:2]),
              reads=(bs,), writes=(bs,), tag=f"{tag} mid1")
        em.op("dve", lambda eng: eng.tensor_scalar(sc[:, 2:3], sc[:, 2:3], 0.5, None, ALU.mult),
              reads=(bs,), writes=(bs,), tag=f"{tag} mid2")
        em.op("pe", lambda eng: eng.matmul(pbc[:, :], C.ones_row[0:1, 0:128], sc[0:1, 2:3],
                                           start=True, stop=True),
              reads=(bs, C.buf, bp2), writes=(bp2,), tag=f"{tag} bcmid")
        em.op("dve", lambda eng: eng.tensor_copy(colb[:, 2:3], pbc[:, :]),
              reads=(bp2, bcol), writes=(bcol,), tag=f"{tag} midcol")
        em.op("dve", lambda eng: eng.tensor_scalar(scr[:, :], vals[:, :], colb[0:P, 2:3], None,
                                                   ALU.is_ge, ALU.add, accum_out=colb[0:P, 0:1]),
              reads=(bvals, bcol, bscr), writes=(bscr, bcol), tag=f"{tag} cmp")
        if P < 128:
            em.op("dve", lambda eng: eng.memset(colb[P:128, 0:1], 0.0),
                  reads=(bcol,), writes=(bcol,), tag=f"{tag} padc")
        em.op("pe", lambda eng: eng.matmul(pcnt[:, :], C.ones_col[:, :], colb[:, 0:1],
                                           start=True, stop=True),
              reads=(bcol, C.buf, bp3), writes=(bp3,), tag=f"{tag} cnt")
        em.op("dve", (lambda k_count=k_count: lambda eng: eng.tensor_scalar(
            sc[:, 3:4], pcnt[0:1, 0:1], float(k_count), None, ALU.is_gt))(),
            reads=(bp3, bs), writes=(bs,), tag=f"{tag} gt")
        em.op("dve", lambda eng: eng.tensor_sub(sc[:, 4:5], sc[:, 2:3], sc[:, 0:1]),
              reads=(bs,), writes=(bs,), tag=f"{tag} lod")
        em.op("dve", lambda eng: eng.tensor_scalar(sc[:, 4:5], sc[:, 4:5], sc[0:1, 3:4], None, ALU.mult),
              reads=(bs,), writes=(bs,), tag=f"{tag} lom")
        em.op("dve", lambda eng: eng.tensor_add(sc[:, 0:1], sc[:, 0:1], sc[:, 4:5]),
              reads=(bs,), writes=(bs,), tag=f"{tag} lo")
        em.op("dve", lambda eng: eng.tensor_sub(sc[:, 4:5], sc[:, 1:2], sc[:, 2:3]),
              reads=(bs,), writes=(bs,), tag=f"{tag} hid")
        em.op("dve", lambda eng: eng.tensor_scalar(sc[:, 4:5], sc[:, 4:5], sc[0:1, 3:4], None, ALU.mult),
              reads=(bs,), writes=(bs,), tag=f"{tag} him")
        em.op("dve", lambda eng: eng.tensor_add(sc[:, 1:2], sc[:, 2:3], sc[:, 4:5]),
              reads=(bs,), writes=(bs,), tag=f"{tag} hi")
    em.op("pe", lambda eng: eng.matmul(pbc[:, :], C.ones_row[0:1, 0:128], sc[0:1, 1:2],
                                       start=True, stop=True),
          reads=(bs, C.buf, bp2), writes=(bp2,), tag=f"{tag} bchi")
    em.op("dve", lambda eng: eng.tensor_copy(colb[:, 2:3], pbc[:, :]),
          reads=(bp2, bcol), writes=(bcol,), tag=f"{tag} hicol")
    em.op("dve", lambda eng: eng.tensor_scalar(mask_sb[:, :], vals[:, :], colb[0:P, 2:3], None, ALU.is_ge),
          reads=(bvals, bcol, bmask), writes=(bmask,), tag=f"{tag} mask")


def phase_l0attn_A(nc, em, C, xs_f32, W, sample_idx, Mrows, nb=NB, tag="a0A"):
    NP = NH // 2
    with contextlib.ExitStack() as stk:
        sb = lambda name, shape, dt_: stk.enter_context(nc.sbuf_tensor(f"{tag}_{name}", shape, dt_))
        xch = sb("xch", [128, DT * 512], F32)
        xsamp = sb("xs", [128, DT * U], F32)
        wq = sb("wq", [128, DT * 128], F32)
        wk = sb("wk", [128, DT * 128], F32)
        bqk = sb("bqk", [1, 2 * D], F32)
        qfm = sb("qfm", [128, L], F32)
        ks = sb("ks", [128, U], F32)
        msb = sb("msb", [128, 32], F32)
        mcol = sb("mcol", [128, 2], F32)
        pq = [stk.enter_context(nc.psum_tensor(f"{tag}_pq{i}", [128, 512], F32)) for i in range(2)]
        pm = [stk.enter_context(nc.psum_tensor(f"{tag}_pm{i}", [128, 45], F32)) for i in range(2)]
        bxc, bxs, bwq, bwk, bbqk = Buf("xch"), Buf("xs"), Buf("wq"), Buf("wk"), Buf("bqk")
        bqfm, bks, bmsb, bmcol = Buf("qfm"), Buf("ks"), Buf("msb"), Buf("mcol")
        bpq = [Buf("pq0"), Buf("pq1")]
        bpm = [Buf("pm0"), Buf("pm1")]
        em.dma("qsp", lambda eng: eng.dma_start(out=bqk[:, 0:D], in_=W["bq"].unsqueeze(0)),
               reads=(), writes=(bbqk,), tag=f"{tag} bq")
        em.dma("qsp", lambda eng: eng.dma_start(out=bqk[:, D:2 * D], in_=W["bk"].unsqueeze(0)),
               reads=(), writes=(bbqk,), tag=f"{tag} bk")
        samp_by_chunk = {}
        for j, idx in enumerate(sample_idx):
            samp_by_chunk.setdefault(int(idx) // 512, []).append((j, int(idx)))
        for b in range(nb):
            for c in range(L // 512):
                em.dma("qsp", (lambda b=b, c=c: lambda eng: eng.dma_start(
                    out=xch[:, :].rearrange("p (dt w) -> p dt w", dt=DT),
                    in_=xs_f32[b].rearrange("(dt p) l -> p dt l", p=128)[:, :, c * 512:(c + 1) * 512]))(),
                    reads=(bxc,), writes=(bxc,), tag=f"{tag} gldx")
                for (j, idx) in samp_by_chunk.get(c, []):
                    em.op("dve", (lambda j=j, idx=idx, c=c: lambda eng: eng.tensor_copy(
                        xsamp[:, :].rearrange("p (dt u) -> p dt u", dt=DT)[:, :, j:j + 1],
                        xch[:, :].rearrange("p (dt w) -> p dt w", dt=DT)[:, :, idx - c * 512: idx - c * 512 + 1]))(),
                        reads=(bxc, bxs), writes=(bxs,), tag=f"{tag} xs{j}")
            for hp in range(NP):
                em.dma("qsp", (lambda hp=hp: lambda eng: eng.dma_start(
                    out=wq[:, :].rearrange("p (dt c) -> p dt c", dt=DT),
                    in_=W["Wq"].rearrange("(dt p) e -> p dt e", p=128)[:, :, hp * 128:(hp + 1) * 128]))(),
                    reads=(bwq,), writes=(bwq,), tag=f"{tag} wq")
                em.dma("qsp", (lambda hp=hp: lambda eng: eng.dma_start(
                    out=wk[:, :].rearrange("p (dt c) -> p dt c", dt=DT),
                    in_=W["Wk"].rearrange("(dt p) e -> p dt e", p=128)[:, :, hp * 128:(hp + 1) * 128]))(),
                    reads=(bwk,), writes=(bwk,), tag=f"{tag} wk")
                for c in range(L // 512):
                    i = c % 2
                    em.dma("qsp", (lambda b=b, c=c: lambda eng: eng.dma_start(
                        out=xch[:, :].rearrange("p (dt w) -> p dt w", dt=DT),
                        in_=xs_f32[b].rearrange("(dt p) l -> p dt l", p=128)[:, :, c * 512:(c + 1) * 512]))(),
                        reads=(bxc,), writes=(bxc,), tag=f"{tag} ldx")
                    for dt in range(DT):
                        em.op("pe", (lambda i=i, dt=dt: lambda eng: eng.matmul(
                            pq[i][:, :], wq[:, dt * 128:(dt + 1) * 128],
                            xch[:, dt * 512:(dt + 1) * 512],
                            start=(dt == 0), stop=False))(),
                            reads=(bwq, bxc), writes=(bpq[i],), tag=f"{tag} qmm")
                    em.op("pe", (lambda i=i, hp=hp: lambda eng: eng.matmul(
                        pq[i][:, :], bqk[0:1, hp * 128:(hp + 1) * 128], C.ones_row[0:1, :],
                        start=False, stop=True))(),
                        reads=(bbqk, C.buf), writes=(bpq[i],), tag=f"{tag} qb")
                    em.op("act", (lambda i=i, c=c: lambda eng: eng.copy(
                        qfm[:, c * 512:(c + 1) * 512], pq[i][:, :]))(),
                        reads=(bpq[i], bqfm), writes=(bqfm,), tag=f"{tag} qev")
                for dt in range(DT):
                    em.op("pe", (lambda dt=dt: lambda eng: eng.matmul(
                        pm[0][:, :], wk[:, dt * 128:(dt + 1) * 128],
                        xsamp[:, dt * U:(dt + 1) * U], start=(dt == 0), stop=False))(),
                        reads=(bwk, bxs), writes=(bpm[0],), tag=f"{tag} kmm")
                em.op("pe", (lambda hp=hp: lambda eng: eng.matmul(
                    pm[0][:, :], bqk[0:1, D + hp * 128: D + (hp + 1) * 128], C.ones_row[0:1, 0:U],
                    start=False, stop=True))(),
                    reads=(bbqk, C.buf), writes=(bpm[0],), tag=f"{tag} kb")
                em.op("act", lambda eng: eng.copy(ks[:, :], pm[0][:, :]),
                      reads=(bpm[0], bks), writes=(bks,), tag=f"{tag} kev")
                for h in range(2):
                    for lt in range(L // 128):
                        i = lt % 2
                        em.op("pe", (lambda i=i, h=h, lt=lt: lambda eng: eng.matmul(
                            pm[i][:, :], qfm[h * 64:(h + 1) * 64, lt * 128:(lt + 1) * 128],
                            ks[h * 64:(h + 1) * 64, :], start=True, stop=True))(),
                            reads=(bqfm, bks), writes=(bpm[i],), tag=f"{tag} qk")
                        em.op("dve", (lambda i=i, lt=lt: lambda eng: eng.tensor_reduce(
                            mcol[:, 0:1], pm[i][:, :], AX.X, ALU.max))(),
                            reads=(bpm[i], bmcol), writes=(bmcol,), tag=f"{tag} mmax")
                        em.op("dve", (lambda i=i, lt=lt: lambda eng: eng.tensor_reduce(
                            mcol[:, 1:2], pm[i][:, :], AX.X, ALU.add))(),
                            reads=(bpm[i], bmcol), writes=(bmcol,), tag=f"{tag} msum")
                        em.op("dve", (lambda lt=lt: lambda eng: eng.scalar_tensor_tensor(
                            msb[:, lt:lt + 1], mcol[:, 1:2], -1.0 / U, mcol[:, 0:1],
                            ALU.mult, ALU.add))(),
                            reads=(bmcol, bmsb), writes=(bmsb,), tag=f"{tag} mval")
                    bh = b * NH + hp * 2 + h
                    em.dma("qact", (lambda bh=bh: lambda eng: eng.dma_start(
                        out=Mrows[bh].rearrange("(p lt) -> p lt", p=128), in_=msb[:, :]))(),
                        reads=(bmsb,), writes=(), tag=f"{tag} mst")


def phase_l0attn_sel(nc, em, C, Mrows, RMrows, nrows, tag="a0S"):
    with contextlib.ExitStack() as stk:
        sb = lambda name, shape, dt_: stk.enter_context(nc.sbuf_tensor(f"{tag}_{name}", shape, dt_))
        mr = sb("mr", [nrows, L], F32)
        ones = sb("ones", [nrows, L], F32)
        msk = sb("msk", [nrows, L], F32)
        rk = sb("rk", [nrows, L], F32)
        his = sb("his", [128, 1], F32)
        pbc = stk.enter_context(nc.psum_tensor(f"{tag}_pbc", [128, 1], F32))
        bmr, bones, bmsk, brk, bhis, bpbc = Buf("mr"), Buf("ones"), Buf("msk"), Buf("rk"), Buf("his"), Buf("pbc")
        em.dma("qsp", lambda eng: eng.dma_start(out=mr[:, :], in_=Mrows),
               reads=(), writes=(bmr,), tag=f"{tag} ld")
        em.op("dve", lambda eng: eng.memset(ones[:, :], 1.0),
              reads=(), writes=(bones,), tag=f"{tag} ones")
        emit_bisect_rows(nc, em, C, stk, mr, bmr, msk, bmsk, his, bhis, pbc, bpbc, nrows, tag)
        em.op("dve", lambda eng: eng.tensor_tensor_scan(rk[:, :], ones[:, :], msk[:, :], 0.0,
                                                        ALU.mult, ALU.add),
              reads=(bones, bmsk, brk), writes=(brk,), tag=f"{tag} cumsum")
        em.op("dve", lambda eng: eng.tensor_mul(rk[:, :], rk[:, :], msk[:, :]),
              reads=(brk, bmsk), writes=(brk,), tag=f"{tag} rm")
        em.dma("qact", lambda eng: eng.dma_start(out=RMrows, in_=rk[:, :]),
               reads=(brk,), writes=(), tag=f"{tag} st")


def emit_bisect_rows(nc, em, C, stk, vals, bvals, mask_sb, bmask, his, bhis, pbc, bpbc, P, tag):
    cols = stk.enter_context(nc.sbuf_tensor(f"{tag}_cols", [128, 6], F32))
    scr = stk.enter_context(nc.sbuf_tensor(f"{tag}_scr", [P, L], F32))
    bcols, bscr = Buf("cols"), Buf("scr")
    em.op("dve", lambda eng: eng.tensor_reduce(cols[0:P, 1:2], vals[:, :], AX.X, ALU.max),
          reads=(bvals, bcols), writes=(bcols,), tag=f"{tag} mx")
    em.op("dve", lambda eng: eng.tensor_reduce(cols[0:P, 0:1], vals[:, :], AX.X, ALU.min),
          reads=(bvals, bcols), writes=(bcols,), tag=f"{tag} mn")
    em.op("dve", lambda eng: eng.tensor_scalar(cols[0:P, 0:1], cols[0:P, 0:1], 1.0, None, ALU.subtract),
          reads=(bcols,), writes=(bcols,), tag=f"{tag} lo-")
    em.op("dve", lambda eng: eng.tensor_scalar(cols[0:P, 1:2], cols[0:P, 1:2], 1.0, None, ALU.add),
          reads=(bcols,), writes=(bcols,), tag=f"{tag} hi+")
    for it in range(42):
        em.op("dve", lambda eng: eng.tensor_add(cols[0:P, 2:3], cols[0:P, 0:1], cols[0:P, 1:2]),
              reads=(bcols,), writes=(bcols,), tag=f"{tag} mid")
        em.op("dve", lambda eng: eng.tensor_scalar(cols[0:P, 2:3], cols[0:P, 2:3], 0.5, None, ALU.mult),
              reads=(bcols,), writes=(bcols,), tag=f"{tag} mid2")
        em.op("dve", lambda eng: eng.tensor_scalar(scr[:, :], vals[:, :], cols[0:P, 2:3], None,
                                                   ALU.is_ge, ALU.add, accum_out=cols[0:P, 3:4]),
              reads=(bvals, bcols, bscr), writes=(bscr, bcols), tag=f"{tag} cnt")
        em.op("dve", lambda eng: eng.tensor_scalar(cols[0:P, 4:5], cols[0:P, 3:4], float(U), None, ALU.is_gt),
              reads=(bcols,), writes=(bcols,), tag=f"{tag} gt")
        em.op("dve", lambda eng: eng.tensor_sub(cols[0:P, 5:6], cols[0:P, 2:3], cols[0:P, 0:1]),
              reads=(bcols,), writes=(bcols,), tag=f"{tag} lod")
        em.op("dve", lambda eng: eng.tensor_tensor(cols[0:P, 5:6], cols[0:P, 5:6], cols[0:P, 4:5], ALU.mult),
              reads=(bcols,), writes=(bcols,), tag=f"{tag} lom")
        em.op("dve", lambda eng: eng.tensor_add(cols[0:P, 0:1], cols[0:P, 0:1], cols[0:P, 5:6]),
              reads=(bcols,), writes=(bcols,), tag=f"{tag} lo")
        em.op("dve", lambda eng: eng.tensor_sub(cols[0:P, 5:6], cols[0:P, 1:2], cols[0:P, 2:3]),
              reads=(bcols,), writes=(bcols,), tag=f"{tag} hid")
        em.op("dve", lambda eng: eng.tensor_tensor(cols[0:P, 5:6], cols[0:P, 5:6], cols[0:P, 4:5], ALU.mult),
              reads=(bcols,), writes=(bcols,), tag=f"{tag} him")
        em.op("dve", lambda eng: eng.tensor_add(cols[0:P, 1:2], cols[0:P, 2:3], cols[0:P, 5:6]),
              reads=(bcols,), writes=(bcols,), tag=f"{tag} hi")
    em.op("dve", lambda eng: eng.tensor_scalar(mask_sb[:, :], vals[:, :], cols[0:P, 1:2], None, ALU.is_ge),
          reads=(bvals, bcols, bmask), writes=(bmask,), tag=f"{tag} mask")


def phase_l0attn_B(nc, em, C, xs_f32, W, RMrows, outs_f32, nb=NB, tag="a0B"):
    NP = NH // 2
    GP = 2
    OA = [nc.dram_tensor(f"{tag}_oa{b}", [D, L], BF16,
                         kind="ExternalOutput" if DBG_STAGES else "Internal").ap() for b in range(nb)]
    bOA = [Buf(f"{tag}_oaD{b}") for b in range(nb)]
    with contextlib.ExitStack() as stk:
        sb = lambda name, shape, dt_: stk.enter_context(nc.sbuf_tensor(f"{tag}_{name}", shape, dt_))
        xb = sb("xb", [128, DT * L], BF16)
        qtm = sb("qtm", [128, 32 * GP * 128], BF16)
        vtm = sb("vtm", [128, 32 * GP * 128], BF16)
        wpair = sb("wpair", [128, 3 * DT * 128], BF16)
        brow = sb("brow", [1, 4 * D], BF16)
        qfm = sb("qfm", [128, L], BF16)
        kfm = sb("kfm", [128, L], BF16)
        rmc = sb("rmc", [128, 32], F32)
        stt_ = sb("stt", [128, 32 * 45], BF16)
        ssb = sb("ssb", [45, L], BF16)
        scosb = sb("sco", [45, L], F32)
        attn = sb("attn", [45, L], BF16)
        attnT = sb("attnT", [128, 32 * 45], BF16)
        qred = sb("qred", [128, 45], BF16)
        vmr = sb("vmr", [1, 64], F32)
        vmrb = sb("vmrb", [1, 64], BF16)
        dsc = sb("dsc", [45, 64], F32)
        delta = sb("delta", [45, 64], BF16)
        rcol = sb("rcol", [45, 2], F32)
        oastage = sb("oast", [64, 512], BF16)
        pA = stk.enter_context(nc.psum_tensor(f"{tag}_pA", [128, 512], F32))
        pB = stk.enter_context(nc.psum_tensor(f"{tag}_pB", [128, 512], F32))
        pE = stk.enter_context(nc.psum_tensor(f"{tag}_pE", [128, 512], F32))
        pT = stk.enter_context(nc.psum_tensor(f"{tag}_pT", [128, 128], BF16))
        B_ = lambda n: Buf(f"{tag}{n}")
        bxb, bqtm, bvtm, bwp, bbr = B_("xb"), B_("qtm"), B_("vtm"), B_("wp"), B_("br")
        bqfm, bkfm, brm = B_("qfm"), B_("kfm"), B_("rm")
        bst, bss, bsco, batn, batnT = B_("st"), B_("ss"), B_("sco"), B_("atn"), B_("atnT")
        bqred, bvm, bdsc, bdelta, brc, boa = B_("qred"), B_("vm"), B_("dsc"), B_("delta"), B_("rc"), B_("oa")
        bpA, bpB, bpE, bpT = B_("pA"), B_("pB"), B_("pE"), B_("pT")
        for j, nm in enumerate(["bq", "bk", "bv", "bo"]):
            em.dma("qpool", (lambda j=j, nm=nm: lambda eng: eng.dma_start(
                out=brow[:, j * D:(j + 1) * D], in_=W[nm].unsqueeze(0)))(),
                reads=(), writes=(bbr,), tag=f"{tag} {nm}")
        for b in range(nb):
            for c in range(L // 512):
                em.dma("qpool", (lambda b=b, c=c: lambda eng: eng.dma_start(
                    out=xb[:, :].rearrange("p (dt l) -> p dt l", dt=DT)[:, :, c * 512:(c + 1) * 512],
                    in_=xs_f32[b].rearrange("(dt p) l -> p dt l", p=128)[:, :, c * 512:(c + 1) * 512]))(),
                    reads=(bxb,), writes=(bxb,), tag=f"{tag} ldx")
            for grp in range(NP // GP):
                GW = GP * 128
                for wsel, dst, bdst in (("Wq", qtm, bqtm), ("Wv", vtm, bvtm)):
                    em.dma("qpool", (lambda wsel=wsel, grp=grp, GW=GW: lambda eng: eng.dma_start(
                        out=wpair[:, 0:DT * GW].rearrange("p (dt c) -> p dt c", dt=DT),
                        in_=W[wsel].rearrange("(dt p) e -> p dt e", p=128)[:, :, grp * GW:(grp + 1) * GW]))(),
                        reads=(bwp,), writes=(bwp,), tag=f"{tag} w{wsel}")
                    for lt in range(L // 128):
                        i = lt % 2
                        ps = pA if i == 0 else pB
                        bps = bpA if i == 0 else bpB
                        for dt in range(DT):
                            em.op("pe", (lambda ps=ps, lt=lt, dt=dt, GW=GW: lambda eng: eng.matmul(
                                ps[:, 0:GW], xb[:, dt * L + lt * 128: dt * L + (lt + 1) * 128],
                                wpair[:, dt * GW:(dt + 1) * GW], start=(dt == 0), stop=False))(),
                                reads=(bxb, bwp), writes=(bps,), tag=f"{tag} tmm")
                        em.op("pe", (lambda ps=ps, wsel=wsel, grp=grp, GW=GW: lambda eng: eng.matmul(
                            ps[:, 0:GW], C.ones16[0:1, 0:128],
                            brow[0:1, (0 if wsel == "Wq" else 2) * D + grp * GW:
                                 (0 if wsel == "Wq" else 2) * D + (grp + 1) * GW],
                            start=False, stop=True))(),
                            reads=(bbr, C.buf), writes=(bps,), tag=f"{tag} tmb")
                        em.op("act", (lambda ps=ps, dst=dst, lt=lt, GW=GW: lambda eng: eng.copy(
                            dst[:, lt * GW:(lt + 1) * GW], ps[:, 0:GW]))(),
                            reads=(bps, bdst), writes=(bdst,), tag=f"{tag} tmev")
                for pp in range(GP):
                    hp = grp * GP + pp
                    em.dma("qpool", (lambda hp=hp: lambda eng: eng.dma_start(
                        out=wpair[:, 0:DT * 128].rearrange("p (dt c) -> p dt c", dt=DT),
                        in_=W["Wq"].rearrange("(dt p) e -> p dt e", p=128)[:, :, hp * 128:(hp + 1) * 128]))(),
                        reads=(bwp,), writes=(bwp,), tag=f"{tag} pwq")
                    em.dma("qpool", (lambda hp=hp: lambda eng: eng.dma_start(
                        out=wpair[:, DT * 128:2 * DT * 128].rearrange("p (dt c) -> p dt c", dt=DT),
                        in_=W["Wk"].rearrange("(dt p) e -> p dt e", p=128)[:, :, hp * 128:(hp + 1) * 128]))(),
                        reads=(bwp,), writes=(bwp,), tag=f"{tag} pwk")
                    for (area, wb, dst, bdst) in ((0, 0, qfm, bqfm), (1, 1, kfm, bkfm)):
                        for c in range(L // 512):
                            i = c % 2
                            ps = pA if i == 0 else pB
                            bps = bpA if i == 0 else bpB
                            for dt in range(DT):
                                em.op("pe", (lambda ps=ps, area=area, c=c, dt=dt: lambda eng: eng.matmul(
                                    ps[:, :], wpair[:, area * DT * 128 + dt * 128: area * DT * 128 + (dt + 1) * 128],
                                    xb[:, dt * L + c * 512: dt * L + (c + 1) * 512],
                                    start=(dt == 0), stop=False))(),
                                    reads=(bwp, bxb), writes=(bps,), tag=f"{tag} fmm")
                            em.op("pe", (lambda ps=ps, wb=wb, hp=hp: lambda eng: eng.matmul(
                                ps[:, :], brow[0:1, wb * D + hp * 128: wb * D + (hp + 1) * 128],
                                C.ones16[0:1, :], start=False, stop=True))(),
                                reads=(bbr, C.buf), writes=(bps,), tag=f"{tag} fmb")
                            em.op("act", (lambda ps=ps, dst=dst, c=c: lambda eng: eng.copy(
                                dst[:, c * 512:(c + 1) * 512], ps[:, :]))(),
                                reads=(bps, bdst), writes=(bdst,), tag=f"{tag} fmev")
                    for h in range(2):
                        hh = hp * 2 + h
                        bh = b * NH + hh
                        hc = pp * 128 + h * 64
                        em.dma("qsp", (lambda bh=bh: lambda eng: eng.dma_start(
                            out=rmc[:, :], in_=RMrows[bh].rearrange("(p lt) -> p lt", p=128)))(),
                            reads=(brm,), writes=(brm,), tag=f"{tag} rmc")
                        for lt in range(L // 128):
                            em.op("dve", (lambda lt=lt: lambda eng: eng.tensor_scalar(
                                stt_[:, lt * 45:(lt + 1) * 45], C.iotarow45[:, :],
                                rmc[:, lt:lt + 1], None, ALU.is_equal))(),
                                reads=(brm, C.buf, bst), writes=(bst,), tag=f"{tag} stb")
                        for lt in range(L // 128):
                            em.op("pe", (lambda lt=lt: lambda eng: eng.transpose(
                                pT[0:45, :], stt_[:, lt * 45:(lt + 1) * 45], C.ident16[:, :]))(),
                                reads=(bst, C.buf, bpT), writes=(bpT,), tag=f"{tag} str")
                            em.op("dve", (lambda lt=lt: lambda eng: eng.tensor_copy(
                                ssb[:, lt * 128:(lt + 1) * 128], pT[0:45, :]))(),
                                reads=(bpT, bss), writes=(bss,), tag=f"{tag} strev")
                        for lt in range(L // 128):
                            em.op("pe", (lambda lt=lt, hc=hc: lambda eng: eng.matmul(
                                pE[0:64, 0:45], qtm[:, lt * (GP * 128) + hc: lt * (GP * 128) + hc + 64],
                                stt_[:, lt * 45:(lt + 1) * 45],
                                start=(lt == 0), stop=(lt == 31)))(),
                                reads=(bqtm, bst, bpE), writes=(bpE,), tag=f"{tag} qred")
                        em.op("act", (lambda h=h: lambda eng: eng.copy(
                            qred[h * 64:(h + 1) * 64, :], pE[0:64, 0:45]))(),
                            reads=(bpE, bqred), writes=(bqred,), tag=f"{tag} qrev")
                        for c in range(L // 512):
                            i = c % 2
                            ps = pA if i == 0 else pB
                            bps = bpA if i == 0 else bpB
                            em.op("pe", (lambda ps=ps, h=h, c=c: lambda eng: eng.matmul(
                                ps[0:45, :], qred[h * 64:(h + 1) * 64, :],
                                kfm[h * 64:(h + 1) * 64, c * 512:(c + 1) * 512],
                                start=True, stop=True))(),
                                reads=(bqred, bkfm, bps), writes=(bps,), tag=f"{tag} sc")
                            em.op("dve", (lambda ps=ps, c=c: lambda eng: eng.tensor_copy(
                                scosb[:, c * 512:(c + 1) * 512], ps[0:45, :]))(),
                                reads=(bps, bsco), writes=(bsco,), tag=f"{tag} scev")
                        em.op("dve", lambda eng: eng.tensor_reduce(rcol[:, 0:1], scosb[:, :], AX.X, ALU.max),
                              reads=(bsco, brc), writes=(brc,), tag=f"{tag} smax")
                        em.op("dve", lambda eng: eng.tensor_scalar(rcol[:, 0:1], rcol[:, 0:1],
                                                                   -0.125, None, ALU.mult),
                              reads=(brc,), writes=(brc,), tag=f"{tag} snm")
                        em.op("act", lambda eng: eng.activation(attn[:, :], scosb[:, :], AF.Exp,
                                                                bias=rcol[:, 0:1], scale=0.125,
                                                                accum_out=rcol[:, 1:2]),
                              reads=(bsco, brc), writes=(batn, brc), tag=f"{tag} sexp")
                        em.op("dve", lambda eng: eng.reciprocal(rcol[:, 1:2], rcol[:, 1:2]),
                              reads=(brc,), writes=(brc,), tag=f"{tag} srec")
                        for lt in range(L // 128):
                            em.op("pe", (lambda lt=lt: lambda eng: eng.transpose(
                                pT[:, 0:45], attn[:, lt * 128:(lt + 1) * 128], C.ident16[0:45, 0:45]))(),
                                reads=(batn, C.buf, bpT), writes=(bpT,), tag=f"{tag} atr")
                            em.op("dve", (lambda lt=lt: lambda eng: eng.tensor_copy(
                                attnT[:, lt * 45:(lt + 1) * 45], pT[:, 0:45]))(),
                                reads=(bpT, batnT), writes=(batnT,), tag=f"{tag} atrev")
                        for lt in range(L // 128):
                            em.op("pe", (lambda lt=lt, hc=hc: lambda eng: eng.matmul(
                                pE[0:1, 0:64], C.ones_col16[:, :],
                                vtm[:, lt * (GP * 128) + hc: lt * (GP * 128) + hc + 64],
                                start=(lt == 0), stop=(lt == 31)))(),
                                reads=(bvtm, C.buf, bpE), writes=(bpE,), tag=f"{tag} vm")
                        em.op("dve", lambda eng: eng.tensor_scalar(vmr[:, :], pE[0:1, 0:64],
                                                                   1.0 / L, None, ALU.mult),
                              reads=(bpE, bvm), writes=(bvm,), tag=f"{tag} vmev")
                        em.op("act", lambda eng: eng.copy(vmrb[:, :], vmr[:, :]),
                              reads=(bvm,), writes=(bvm,), tag=f"{tag} vmbf")
                        for lt in range(L // 128):
                            em.op("pe", (lambda lt=lt, hc=hc: lambda eng: eng.matmul(
                                pE[0:45, 64:128], attnT[:, lt * 45:(lt + 1) * 45],
                                vtm[:, lt * (GP * 128) + hc: lt * (GP * 128) + hc + 64],
                                start=(lt == 0), stop=(lt == 31)))(),
                                reads=(batnT, bvtm, bpE), writes=(bpE,), tag=f"{tag} ored")
                        em.op("dve", lambda eng: eng.tensor_scalar(dsc[:, :], pE[0:45, 64:128],
                                                                   rcol[:, 1:2], None, ALU.mult),
                              reads=(bpE, brc, bdsc), writes=(bdsc,), tag=f"{tag} dnorm")
                        em.op("pe", lambda eng: eng.matmul(pE[0:45, 128:192], C.ones_row[0:1, 0:45],
                                                           vmr[:, :], start=True, stop=True),
                              reads=(bvm, C.buf, bpE), writes=(bpE,), tag=f"{tag} vbc")
                        em.op("dve", lambda eng: eng.tensor_sub(delta[:, :], dsc[:, :], pE[0:45, 128:192]),
                              reads=(bdsc, bpE, bdelta), writes=(bdelta,), tag=f"{tag} dsub")
                        for c in range(L // 512):
                            i = c % 2
                            ps = pA if i == 0 else pB
                            bps = bpA if i == 0 else bpB
                            em.op("pe", (lambda ps=ps, c=c: lambda eng: eng.matmul(
                                ps[0:64, :], delta[:, :], ssb[:, c * 512:(c + 1) * 512],
                                start=True, stop=False))(),
                                reads=(bdelta, bss, bps), writes=(bps,), tag=f"{tag} scat")
                            em.op("pe", (lambda ps=ps, c=c: lambda eng: eng.matmul(
                                ps[0:64, :], vmrb[:, :], C.ones16[0:1, :],
                                start=False, stop=True))(),
                                reads=(bvm, C.buf, bps), writes=(bps,), tag=f"{tag} fill")
                            em.op("act", (lambda ps=ps: lambda eng: eng.copy(
                                oastage[:, :], ps[0:64, :]))(),
                                reads=(bps, boa), writes=(boa,), tag=f"{tag} oaev")
                            em.dma("qact", (lambda b=b, hh=hh, c=c: lambda eng: eng.dma_start(
                                out=OA[b][hh * 64:(hh + 1) * 64, c * 512:(c + 1) * 512],
                                in_=oastage[:, :]))(),
                                reads=(boa,), writes=(bOA[b],), tag=f"{tag} oast")
        for b in range(nb):
            with contextlib.ExitStack() as stk2:
                oach = stk2.enter_context(nc.sbuf_tensor(f"{tag}_oach{b}", [128, DT * 512], BF16))
                wo = [stk2.enter_context(nc.sbuf_tensor(f"{tag}_wo{b}{i}", [128, DT * 128], BF16))
                      for i in range(2)]
                ost = stk2.enter_context(nc.sbuf_tensor(f"{tag}_ost{b}", [128, DT * 512], F32))
                boach, bost = B_("oach"), B_("ost")
                bwo = [B_("wo0"), B_("wo1")]
                for c in range(L // 512):
                    if c == 0:
                        em.barrier()
                    em.dma("qsp", (lambda b=b, c=c: lambda eng: eng.dma_start(
                        out=oach[:, :].rearrange("p (dt l) -> p dt l", dt=DT),
                        in_=OA[b].rearrange("(dt p) l -> p dt l", p=128)[:, :, c * 512:(c + 1) * 512]))(),
                        reads=(boach, bOA[b]), writes=(boach,), tag=f"{tag} oald")
                    for et in range(DT):
                        i = et % 2
                        ps = pA if i == 0 else pB
                        bps = bpA if i == 0 else bpB
                        em.dma("qpool", (lambda et=et, i=i: lambda eng: eng.dma_start(
                            out=wo[i][:, :].rearrange("p (dt c) -> p dt c", dt=DT),
                            in_=W["Wo"].rearrange("(dt p) e -> p dt e", p=128)[:, :, et * 128:(et + 1) * 128]))(),
                            reads=(bwo[i],), writes=(bwo[i],), tag=f"{tag} wold")
                        for dt in range(DT):
                            em.op("pe", (lambda ps=ps, i=i, dt=dt, c=c: lambda eng: eng.matmul(
                                ps[:, :], wo[i][:, dt * 128:(dt + 1) * 128],
                                oach[:, dt * 512:(dt + 1) * 512], start=(dt == 0), stop=False))(),
                                reads=(bwo[i], boach), writes=(bps,), tag=f"{tag} womm")
                        em.op("pe", (lambda ps=ps, et=et: lambda eng: eng.matmul(
                            ps[:, :], brow[0:1, 3 * D + et * 128: 3 * D + (et + 1) * 128],
                            C.ones16[0:1, :], start=False, stop=True))(),
                            reads=(bbr, C.buf), writes=(bps,), tag=f"{tag} wob")
                        em.op("act", (lambda ps=ps, et=et: lambda eng: eng.copy(
                            ost[:, et * 512:(et + 1) * 512], ps[:, :]))(),
                            reads=(bps, bost), writes=(bost,), tag=f"{tag} woev")
                    em.dma("qact", (lambda b=b, c=c: lambda eng: eng.dma_start(
                        out=outs_f32[b].rearrange("(dt p) l -> p dt l", p=128)[:, :, c * 512:(c + 1) * 512],
                        in_=ost[:, :].rearrange("p (dt l) -> p dt l", dt=DT)))(),
                        reads=(bost,), writes=(), tag=f"{tag} wost")



def phase_cast_bf16(nc, em, src_f32, dst_bf16, tag):
    with contextlib.ExitStack() as stk:
        t = stk.enter_context(nc.sbuf_tensor(f"{tag}_cb", [128, DT * 512], BF16))
        bt = Buf(f"{tag}cb")
        for c in range(L // 512):
            em.dma("qpool", (lambda c=c: lambda eng: eng.dma_start(
                out=t[:, :].rearrange("p (dt w) -> p dt w", dt=DT),
                in_=src_f32.rearrange("(dt p) l -> p dt l", p=128)[:, :, c * 512:(c + 1) * 512]))(),
                reads=(bt,), writes=(bt,), tag=f"{tag} in")
            em.dma("qsp", (lambda c=c: lambda eng: eng.dma_start(
                out=dst_bf16.rearrange("(dt p) l -> p dt l", p=128)[:, :, c * 512:(c + 1) * 512],
                in_=t[:, :].rearrange("p (dt w) -> p dt w", dt=DT)))(),
                reads=(bt,), writes=(), tag=f"{tag} out")


def phase_out_transpose(nc, em, C, src_f32, y_out, b, tag):
    with contextlib.ExitStack() as stk:
        fin = [stk.enter_context(nc.sbuf_tensor(f"{tag}_f{i}", [128, DT * 128], F32)) for i in range(2)]
        tmo = [stk.enter_context(nc.sbuf_tensor(f"{tag}_t{i}", [128, D], F32)) for i in range(2)]
        ps = [stk.enter_context(nc.psum_tensor(f"{tag}_p{i}", [128, 128], F32)) for i in range(2)]
        bf = [Buf("f0"), Buf("f1")]
        bt = [Buf("t0"), Buf("t1")]
        bp = [Buf("p0"), Buf("p1")]
        for lt in range(L // 128):
            i = lt % 2
            em.dma("qsp", (lambda lt=lt, i=i: lambda eng: eng.dma_start(
                out=fin[i][:, :].rearrange("p (dt c) -> p dt c", dt=DT),
                in_=src_f32.rearrange("(dt p) l -> p dt l", p=128)[:, :, lt * 128:(lt + 1) * 128]))(),
                reads=(bf[i], bt[i]), writes=(bf[i],), tag=f"{tag} ld")
            for dt in range(DT):
                j = dt % 2
                em.op("pe", (lambda i=i, j=j, dt=dt: lambda eng: eng.transpose(
                    ps[j][:, :], fin[i][:, dt * 128:(dt + 1) * 128], C.ident32[:, :]))(),
                    reads=(bf[i], C.buf, bp[j]), writes=(bp[j],), tag=f"{tag} tr")
                em.op("act", (lambda i=i, j=j, dt=dt: lambda eng: eng.copy(
                    tmo[i][:, dt * 128:(dt + 1) * 128], ps[j][:, :]))(),
                    reads=(bp[j], bt[i]), writes=(bt[i],), tag=f"{tag} ev")
            em.dma("qact", (lambda b=b, lt=lt, i=i: lambda eng: eng.dma_start(
                out=y_out[b, lt * 128:(lt + 1) * 128, :], in_=tmo[i][:, :]))(),
                reads=(bt[i],), writes=(), tag=f"{tag} st")



from concourse.bass_utils import run_bass_kernel_spmd

PNAMES = ["angle", "lnm", "scale", "pi_re", "pi_im", "lc_re", "lc_im"]


def _ext(nc, name, shape, dt_, kind="ExternalInput"):
    return nc.dram_tensor(name, list(shape), dt_, kind=kind).ap()


def _weight_tensors(nc, which):
    W = {}
    if "l0attn" in which:
        for n in ["Wq", "Wk", "Wv", "Wo"]:
            W[n] = _ext(nc, f"L0_attn_{n}", [D, D], F32)
        for n in ["bq", "bk", "bv", "bo"]:
            W[n] = _ext(nc, f"L0_attn_{n}", [D], F32)
    if "l0ffn" in which:
        W["gate"] = _ext(nc, "L0_ffn_gate", [D, E], F32)
        W["W1"] = _ext(nc, "L0_ffn_W1", [E, D, HID], F32)
        W["b1"] = _ext(nc, "L0_ffn_b1", [E, HID], F32)
        W["W2"] = _ext(nc, "L0_ffn_W2", [E, HID, D], F32)
        W["b2"] = _ext(nc, "L0_ffn_b2", [E, D], F32)
    if "l1attn" in which:
        for n in PNAMES:
            W[f"p_{n}"] = _ext(nc, f"p_{n}", [128, DT], F32)
    if "l1ffn" in which:
        W["Wm"] = _ext(nc, "L1_ffn_Wm", [FS, D], F32)
        W["Wp"] = _ext(nc, "L1_ffn_Wp", [D, FS], F32)
    if "l1ffn_c" in which:
        W["Wc1"] = _ext(nc, "L1_ffn_Wc1", [256, D], F32)
        W["Wc2"] = _ext(nc, "L1_ffn_Wc2", [FS, 256], F32)
    return W


def build_launch1(sample_idx, max_phases=99):
    import os
    max_phases = int(os.environ.get("MAXPH", max_phases))
    nc = bass.Bass("TRN2", target_bir_lowering=False)
    em = Em(nc)
    x_in = _ext(nc, "x_in", [NB, L, D], F32)
    W = _weight_tensors(nc, ("l0attn", "l0ffn", "l1attn", "l1ffn", "l1ffn_c"))
    X3 = _ext(nc, "X3", [NB * D, L], F32, "ExternalOutput")
    XN1 = _ext(nc, "XN1", [NB * D, L], F32, "ExternalOutput")
    XN2 = _ext(nc, "XN2", [NB * D, L], F32, "ExternalOutput")
    MASK1 = _ext(nc, "MASK1", [NB * 128, FT], F32, "ExternalOutput")
    MASK2 = _ext(nc, "MASK2", [NB * 128, FT], F32, "ExternalOutput")
    NOV = _ext(nc, "NOV", [2, NB], F32, "ExternalOutput")
    dbgk = "ExternalOutput" if DBG_STAGES else "Internal"
    X0 = nc.dram_tensor("X0", [NB * D, L], F32, kind=dbgk).ap()
    A0 = nc.dram_tensor("A0", [NB * D, L], F32, kind=dbgk).ap()
    X1 = nc.dram_tensor("X1", [NB * D, L], F32, kind=dbgk).ap()
    M1o = nc.dram_tensor("M1o", [NB * D, L], F32, kind=dbgk).ap()
    X2 = nc.dram_tensor("X2", [NB * D, L], F32, kind=dbgk).ap()
    C2 = nc.dram_tensor("C2", [NB * D, L], F32, kind=dbgk).ap()
    X3BF = nc.dram_tensor("X3BF", [NB * D, L], BF16, kind="Internal").ap()
    XN1BF = nc.dram_tensor("XN1BF", [NB * D, L], BF16, kind="Internal").ap()
    XN2BF = nc.dram_tensor("XN2BF", [NB * D, L], BF16, kind="Internal").ap()
    HIST = nc.dram_tensor("HIST", [NB * 128, FT], F32, kind="Internal").ap()
    POOL = nc.dram_tensor("POOL", [2 * NB * 128, DT], F32, kind="Internal").ap()
    Mrows = nc.dram_tensor("Mrows", [NB * NH, L], F32, kind="Internal").ap()
    RMrows = nc.dram_tensor("RMrows", [NB * NH, L], F32, kind="Internal").ap()
    IDX = nc.dram_tensor("IDX", [IS, 1], mybir.dt.int32, kind="Internal").ap()
    WGT = nc.dram_tensor("WGT", [D, IS], BF16, kind="Internal").ap()
    WPG = nc.dram_tensor("WPG", [IS, D], BF16, kind="Internal").ap()
    bWGT, bWPG = Buf("WGTd"), Buf("WPGd")
    bs = lambda T, b: T[b * D:(b + 1) * D, :]
    bm = lambda T, b: T[b * 128:(b + 1) * 128, :]
    with contextlib.ExitStack() as stk:
        C = Consts(nc, em, stk)
        sems = {k: stk.enter_context(nc.semaphore(f"sem_{k}")) for k in em.sem_keys()}
        block = stk.enter_context(nc.Block())
        for b in range(NB):
            phase_in_transpose(nc, em, C, x_in, bs(X0, b), b)
            em.barrier()
        if max_phases < 1:
            em.finalize(block, sems)
            return nc
        phase_l0attn_A(nc, em, C, [bs(X0, b) for b in range(NB)], W, sample_idx, Mrows)
        em.barrier()
        phase_l0attn_sel(nc, em, C, Mrows, RMrows, NB * NH)
        em.barrier()
        if max_phases < 2:
            em.finalize(block, sems)
            return nc
        phase_l0attn_B(nc, em, C, [bs(X0, b) for b in range(NB)], W, RMrows,
                       [bs(A0, b) for b in range(NB)])
        em.barrier()
        for b in range(NB):
            phase_ln(nc, em, C, bs(X0, b), bs(A0, b), bs(X1, b), b, tag=f"ln1b{b}")
            em.barrier()
        if max_phases < 3:
            em.finalize(block, sems)
            return nc
        w1bf, w2bf = phase_moe_prep(nc, em, W)
        em.barrier()
        if max_phases < 4:
            em.finalize(block, sems)
            return nc
        for b in range(NB):
            phase_moe(nc, em, C, bs(X1, b), W, w1bf, w2bf, bs(M1o, b), b)
            em.barrier()
            phase_ln(nc, em, C, bs(X1, b), bs(M1o, b), bs(X2, b), b, tag=f"ln2b{b}")
            em.barrier()
        if max_phases < 5:
            em.finalize(block, sems)
            return nc
        P = {n: W[f"p_{n}"] for n in PNAMES}
        phase_l1attn(nc, em, C, [bs(X2, b) for b in range(NB)], P,
                     [bs(C2, b) for b in range(NB)])
        em.barrier()
        for b in range(NB):
            phase_ln(nc, em, C, bs(X2, b), bs(C2, b), bs(X3, b), b, tag=f"ln3b{b}")
            em.barrier()
        if max_phases < 6:
            em.finalize(block, sems)
            return nc
        Wsc = phase_l1ffn_prep(nc, em, C, W)
        em.barrier()
        if max_phases < 7:
            em.finalize(block, sems)
            return nc
        for b in range(NB):
            phase_cast_bf16(nc, em, bs(X3, b), bs(X3BF, b), f"cx3b{b}")
            em.barrier()
            phase_l1ffn_step(nc, em, C, bs(X3BF, b), Wsc, bs(XN1BF, b), bs(XN1, b),
                             b, 1, pooled_dram=bm(POOL, b))
            em.barrier()
            phase_l1ffn_ck(nc, em, C, bm(POOL, b), Wsc, bm(MASK1, b), None,
                           bm(HIST, b), NOV[0:1, b:b + 1], f"ck1b{b}")
            em.barrier()
            if max_phases < 8:
                continue
            bidx = phase_l1ffn_idx(nc, em, C, bm(MASK1, b), IDX, f"idx2b{b}")
            em.barrier()
            phase_l1ffn_gather(nc, em, C, Wsc, IDX, bidx, WGT, WPG, bWGT, bWPG, f"g2b{b}")
            em.barrier()
            phase_l1ffn_step_sparse(nc, em, C, bs(XN1BF, b), WGT, bWGT, WPG, bWPG,
                                    bs(XN2BF, b), bs(XN2, b), b, 2,
                                    pooled_dram=bm(POOL, NB + b))
            em.barrier()
            phase_l1ffn_ck(nc, em, C, bm(POOL, NB + b), Wsc, bm(MASK2, b), bm(HIST, b),
                           None, NOV[1:2, b:b + 1], f"ck2b{b}")
            em.barrier()
        em.finalize(block, sems)
    return nc


def build_launch2():
    nc = bass.Bass("TRN2", target_bir_lowering=False)
    em = Em(nc)
    W = _weight_tensors(nc, ("l1ffn",))
    X3 = _ext(nc, "X3", [NB * D, L], F32)
    XN1 = _ext(nc, "XN1", [NB * D, L], F32)
    XN2 = _ext(nc, "XN2", [NB * D, L], F32)
    MASK1 = _ext(nc, "MASK1", [NB * 128, FT], F32)
    MASK2 = _ext(nc, "MASK2", [NB * 128, FT], F32)
    NOVALL = _ext(nc, "NOVALL", [2, 16], F32)
    y_out = _ext(nc, "y_out", [NB, L, D], F32, "ExternalOutput")
    XN2BF = nc.dram_tensor("XN2BF", [NB * D, L], BF16, kind="Internal").ap()
    XN3BF = nc.dram_tensor("XN3BF", [NB * D, L], BF16, kind="Internal").ap()
    XN3 = nc.dram_tensor("XN3", [NB * D, L], F32, kind="Internal").ap()
    XFIN = nc.dram_tensor("XFIN", [NB * D, L], F32, kind="Internal").ap()
    XOUT = nc.dram_tensor("XOUT", [NB * D, L], F32, kind="Internal").ap()
    DFLAG = nc.dram_tensor("DFLAG", [1, 2], F32, kind="Internal").ap()
    ACT3 = nc.dram_tensor("ACT3", [NB * 128, FT], F32, kind="Internal").ap()
    IDX = nc.dram_tensor("IDX", [IS, 1], mybir.dt.int32, kind="Internal").ap()
    WGT = nc.dram_tensor("WGT", [D, IS], BF16, kind="Internal").ap()
    WPG = nc.dram_tensor("WPG", [IS, D], BF16, kind="Internal").ap()
    bWGT, bWPG = Buf("WGTd"), Buf("WPGd")
    bs = lambda T, b: T[b * D:(b + 1) * D, :]
    bm = lambda T, b: T[b * 128:(b + 1) * 128, :]
    with contextlib.ExitStack() as stk:
        C = Consts(nc, em, stk)
        sems = {k: stk.enter_context(nc.semaphore(f"sem_{k}")) for k in em.sem_keys()}
        block = stk.enter_context(nc.Block())
        Wsc = phase_l1ffn_prep(nc, em, C, W, subset=("Wm", "Wp"))
        em.barrier()
        phase_l1ffn_done(nc, em, NOVALL, 16, DFLAG)
        em.barrier()
        for b in range(NB):
            phase_l1ffn_active3(nc, em, C, bm(MASK1, b), bm(MASK2, b), DFLAG,
                                bm(ACT3, b), tag=f"act3b{b}")
            em.barrier()
            phase_cast_bf16(nc, em, bs(XN2, b), bs(XN2BF, b), f"cx2b{b}")
            em.barrier()
            bidx = phase_l1ffn_idx(nc, em, C, bm(ACT3, b), IDX, f"idx3b{b}")
            em.barrier()
            phase_l1ffn_gather(nc, em, C, Wsc, IDX, bidx, WGT, WPG, bWGT, bWPG, f"g3b{b}")
            em.barrier()
            phase_l1ffn_step_sparse(nc, em, C, bs(XN2BF, b), WGT, bWGT, WPG, bWPG,
                                    bs(XN3BF, b), bs(XN3, b), b, 3)
            em.barrier()
            phase_l1ffn_blend(nc, em, C, bs(XN1, b), bs(XN2, b), bs(XN3, b), DFLAG,
                              bs(XFIN, b), tag=f"blendb{b}")
            em.barrier()
            phase_ln(nc, em, C, bs(X3, b), bs(XFIN, b), bs(XOUT, b), b, tag=f"ln4b{b}")
            em.barrier()
            phase_out_transpose(nc, em, C, bs(XOUT, b), y_out, b, f"otb{b}")
            em.barrier()
        em.finalize(block, sems)
    return nc


def kernel(**inputs):
    NCORES = 8
    x = np.ascontiguousarray(inputs["L0_attn_x"])
    sample_idx = [int(v) for v in np.asarray(inputs["L0_attn_sample_idx"])]
    consts = make_host_consts()
    wmap = {}
    for k, v in inputs.items():
        if k not in ("L0_attn_x", "L0_attn_sample_idx"):
            wmap[k] = np.ascontiguousarray(v)
    for n in PNAMES:
        key = {"angle": "L1_attn_phazor_angle", "lnm": "L1_attn_ln_minus_ln_phazor_abs",
               "scale": "L1_attn_ln_phazor_scale", "pi_re": "L1_attn_phazor_init_re",
               "pi_im": "L1_attn_phazor_init_im", "lc_re": "L1_attn_last_conv_re",
               "lc_im": "L1_attn_last_conv_im"}[n]
        wmap[f"p_{n}"] = np.ascontiguousarray(np.asarray(inputs[key]).reshape(DT, 128).T)
    for k in ["L1_attn_phazor_angle", "L1_attn_ln_minus_ln_phazor_abs",
              "L1_attn_ln_phazor_scale", "L1_attn_phazor_init_re",
              "L1_attn_phazor_init_im", "L1_attn_last_conv_re", "L1_attn_last_conv_im"]:
        wmap.pop(k, None)

    nc1 = build_launch1(sample_idx)
    in_maps1 = []
    for ci in range(NCORES):
        m = {"x_in": x[ci * NB:(ci + 1) * NB]}
        m.update(wmap)
        m.update(consts)
        in_maps1.append(m)
    res1 = run_bass_kernel_spmd(nc1, in_maps1, core_ids=list(range(NCORES)))
    nov_all = np.concatenate([res1.results[ci]["NOV"] for ci in range(NCORES)], axis=1)
    nov_all = np.ascontiguousarray(nov_all.astype(np.float32))

    nc2 = build_launch2()
    in_maps2 = []
    for ci in range(NCORES):
        r = res1.results[ci]
        m = {"X3": r["X3"], "XN1": r["XN1"], "XN2": r["XN2"],
             "MASK1": r["MASK1"], "MASK2": r["MASK2"], "NOVALL": nov_all,
             "L1_ffn_Wm": wmap["L1_ffn_Wm"], "L1_ffn_Wp": wmap["L1_ffn_Wp"]}
        m.update(consts)
        in_maps2.append(m)
    res2 = run_bass_kernel_spmd(nc2, in_maps2, core_ids=list(range(NCORES)))
    out = np.concatenate([res2.results[ci]["y_out"] for ci in range(NCORES)], axis=0)
    return out



def phase_l1ffn_idx(nc, em, C, mask_dram, IDX, tag):
    from concourse.bass_core_rust import IndirectOffsetOnAxis
    I32 = mybir.dt.int32
    with contextlib.ExitStack() as stk:
        sb = lambda name, shape, dt_: stk.enter_context(nc.sbuf_tensor(f"{tag}_{name}", shape, dt_))
        msk = sb("m", [128, FT], F32)
        ones = sb("o", [128, FT], F32)
        rank = sb("r", [128, FT], F32)
        pen = sb("pen", [128, FT], F32)
        ji = sb("ji", [128, FT], I32)
        zz = sb("z", [128, IS // 128], I32)
        ppre = stk.enter_context(nc.psum_tensor(f"{tag}_pp", [128, 1], F32))
        bm, bo, br, bpen, bji, bz, bpp = (Buf(x) for x in
                                          ("m", "o", "r", "pen", "ji", "z", "pp"))
        em.dma("qsp", lambda eng: eng.dma_start(out=msk[:, :], in_=mask_dram),
               reads=(), writes=(bm,), tag=f"{tag} ldm")
        em.op("dve", lambda eng: eng.memset(ones[:, :], 1.0),
              reads=(), writes=(bo,), tag=f"{tag} ones")
        em.op("dve", lambda eng: eng.memset(zz[:, :], 0),
              reads=(), writes=(bz,), tag=f"{tag} zz")
        em.op("dve", lambda eng: eng.tensor_tensor_scan(rank[:, :], ones[:, :], msk[:, :],
                                                        0.0, ALU.mult, ALU.add),
              reads=(bo, bm, br), writes=(br,), tag=f"{tag} scan")
        em.op("pe", lambda eng: eng.matmul(ppre[:, :], C.lowtri[:, :], rank[:, FT - 1:FT],
                                           start=True, stop=True),
              reads=(br, C.buf, bpp), writes=(bpp,), tag=f"{tag} pre")
        em.op("dve", lambda eng: eng.tensor_scalar(pen[:, :], msk[:, :], -1e9, 1e9, ALU.mult, ALU.add),
              reads=(bm, bpen), writes=(bpen,), tag=f"{tag} pen")
        em.op("dve", lambda eng: eng.tensor_copy(ones[:, 0:1], ppre[:, :]),
              reads=(bpp, bo), writes=(bo,), tag=f"{tag} precp")
        em.op("dve", lambda eng: eng.tensor_scalar(rank[:, :], rank[:, :], ones[:, 0:1], -1.0,
                                                   ALU.add, ALU.add),
              reads=(br, bo), writes=(br,), tag=f"{tag} jall")
        em.op("dve", lambda eng: eng.tensor_add(rank[:, :], rank[:, :], pen[:, :]),
              reads=(br, bpen), writes=(br,), tag=f"{tag} jm")
        em.op("dve", lambda eng: eng.tensor_copy(ji[:, :], rank[:, :]),
              reads=(br, bji), writes=(bji,), tag=f"{tag} jint")
        em.dma("qpool", lambda eng: eng.dma_start(
            out=IDX.rearrange("(p t) a -> p (t a)", p=128), in_=zz[:, :]),
            reads=(bz,), writes=(), tag=f"{tag} zero")
        bidx = Buf(f"{tag}_idxD")
        for ft in range(FT):
            em.dma("qpool", (lambda ft=ft: lambda eng: eng.indirect_dma_start(
                IDX, IndirectOffsetOnAxis(ap=ji[:, ft:ft + 1], axis=0),
                C.fid[:, ft:ft + 1], None))(),
                reads=(bji, C.buf), writes=(bidx,), tag=f"{tag} sc{ft}")
    return bidx


def phase_l1ffn_gather(nc, em, C, Wsc, IDX, bidx, WGT, WPG, bWGT, bWPG, tag):
    from concourse.bass_core_rust import IndirectOffsetOnAxis
    I32 = mybir.dt.int32
    NT = IS // 128
    with contextlib.ExitStack() as stk:
        sb = lambda name, shape, dt_: stk.enter_context(nc.sbuf_tensor(f"{tag}_{name}", shape, dt_))
        idxc = sb("idxc", [128, NT], I32)
        wg = [sb(f"wg{i}", [128, D], BF16) for i in range(2)]
        wt = [sb(f"wt{i}", [128, D], BF16) for i in range(2)]
        ps = [stk.enter_context(nc.psum_tensor(f"{tag}_ps{i}", [128, 128], BF16)) for i in range(2)]
        bic = Buf("idxc")
        bwg = [Buf("wg0"), Buf("wg1")]
        bwt = [Buf("wt0"), Buf("wt1")]
        bps = [Buf("ps0"), Buf("ps1")]
        em.dma("qsp", lambda eng: eng.dma_start(
            out=idxc[:, :], in_=IDX.rearrange("(p t) a -> p (t a)", p=128)),
            reads=(bidx,), writes=(bic,), tag=f"{tag} ldi")
        for t in range(NT):
            i = t % 2
            em.dma("qpool", (lambda t=t, i=i: lambda eng: eng.indirect_dma_start(
                wt[i][:, :], None, Wsc["WpT"], IndirectOffsetOnAxis(ap=idxc[:, t:t + 1], axis=0)))(),
                reads=(bic, bwt[i]), writes=(bwt[i],), tag=f"{tag} gwp")
            em.dma("qact", (lambda t=t, i=i: lambda eng: eng.dma_start(
                out=WPG[t * 128:(t + 1) * 128, :], in_=wt[i][:, :]))(),
                reads=(bwt[i],), writes=(bWPG,), tag=f"{tag} swp")
            em.dma("qpool", (lambda t=t, i=i: lambda eng: eng.indirect_dma_start(
                wg[i][:, :], None, Wsc["WmR"], IndirectOffsetOnAxis(ap=idxc[:, t:t + 1], axis=0)))(),
                reads=(bic, bwg[i]), writes=(bwg[i],), tag=f"{tag} gwm")
            for dt in range(DT):
                j = dt % 2
                em.op("pe", (lambda i=i, j=j, dt=dt: lambda eng: eng.transpose(
                    ps[j][:, :], wg[i][:, dt * 128:(dt + 1) * 128], C.ident16[:, :]))(),
                    reads=(bwg[i], C.buf, bps[j]), writes=(bps[j],), tag=f"{tag} tr")
                em.op("dve", (lambda i=i, j=j, dt=dt: lambda eng: eng.tensor_copy(
                    wt[i][:, dt * 128:(dt + 1) * 128], ps[j][:, :]))(),
                    reads=(bps[j], bwt[i]), writes=(bwt[i],), tag=f"{tag} trev")
            em.dma("qact", (lambda t=t, i=i: lambda eng: eng.dma_start(
                out=WGT.rearrange("(dt p) f -> p dt f", p=128)[:, :, t * 128:(t + 1) * 128],
                in_=wt[i][:, :].rearrange("p (dt c) -> p dt c", dt=DT)))(),
                reads=(bwt[i],), writes=(bWGT,), tag=f"{tag} swm")


def phase_l1ffn_step_sparse(nc, em, C, xin_bf, WGT, bWGT, WPG, bWPG, xout_bf, xout_f32,
                            b, step, pooled_dram=None, Lt=L, tag=None):
    tag = tag or f"l1sp{step}b{b}"
    CH = 512
    NCH = Lt // CH
    NT = IS // 128
    do_ck = pooled_dram is not None
    with contextlib.ExitStack() as stk:
        sb = lambda name, shape, dt_: stk.enter_context(nc.sbuf_tensor(f"{tag}_{name}", shape, dt_))
        xcb = sb("xcb", [128, DT * CH], BF16)
        hs = sb("hs", [128, NT * CH], BF16)
        wm = [sb(f"wm{i}", [128, DT * 512], BF16) for i in range(2)]
        wp = [sb(f"wp{i}", [128, NT * 128], BF16) for i in range(2)]
        xnf = sb("xnf", [128, DT * CH], F32)
        xnb = sb("xnb", [128, DT * CH], BF16)
        pooled = sb("pooled", [128, 2 * DT], F32)
        ph = [stk.enter_context(nc.psum_tensor(f"{tag}_ph{i}", [128, CH], F32)) for i in range(2)]
        px = [stk.enter_context(nc.psum_tensor(f"{tag}_px{i}", [128, CH], F32)) for i in range(2)]
        bxcb, bhs, bxnf, bxnb = Buf("xcb"), Buf("hs"), Buf("xnf"), Buf("xnb")
        bwm = [Buf("wm0"), Buf("wm1")]
        bwp = [Buf("wp0"), Buf("wp1")]
        bpool = Buf("pooled")
        bph = [Buf("ph0"), Buf("ph1")]
        bpx = [Buf("px0"), Buf("px1")]
        em.op("dve", lambda eng: eng.memset(pooled[:, :], 0.0),
              reads=(), writes=(bpool,), tag=f"{tag} pool0")
        for c in range(NCH):
            em.dma("qsp", (lambda c=c: lambda eng: eng.dma_start(
                out=xcb[:, :].rearrange("p (dt t) -> p dt t", dt=DT),
                in_=xin_bf.rearrange("(dt p) l -> p dt l", p=128)[:, :, c * CH:(c + 1) * CH]))(),
                reads=(bxcb,), writes=(bxcb,), tag=f"{tag} ldx")
            for fq in range(NT // 4):
                i = fq % 2
                em.dma("qsp", (lambda fq=fq, i=i: lambda eng: eng.dma_start(
                    out=wm[i][:, :].rearrange("p (dt f) -> p dt f", dt=DT),
                    in_=WGT.rearrange("(dt p) f -> p dt f", p=128)[:, :, fq * 512:(fq + 1) * 512]))(),
                    reads=(bwm[i], bWGT), writes=(bwm[i],), tag=f"{tag} wm")
                for fi in range(4):
                    ft = fq * 4 + fi
                    j = ft % 2
                    for dt in range(DT):
                        em.op("pe", (lambda i=i, j=j, fi=fi, dt=dt: lambda eng: eng.matmul(
                            ph[j][:, :], wm[i][:, dt * 512 + fi * 128: dt * 512 + (fi + 1) * 128],
                            xcb[:, dt * CH:(dt + 1) * CH], start=(dt == 0), stop=(dt == DT - 1)))(),
                            reads=(bwm[i], bxcb), writes=(bph[j],), tag=f"{tag} hmm")
                    em.op("act", (lambda j=j, ft=ft: lambda eng: eng.activation(
                        hs[:, ft * CH:(ft + 1) * CH], ph[j][:, :], AF.Relu))(),
                        reads=(bph[j], bhs), writes=(bhs,), tag=f"{tag} relu")
            for ct in range(DT):
                i = ct % 2
                em.dma("qact", (lambda ct=ct, i=i: lambda eng: eng.dma_start(
                    out=wp[i][:, :].rearrange("p (ft c) -> p ft c", ft=NT),
                    in_=WPG.rearrange("(ft p) c -> p ft c", p=128)[:, :, ct * 128:(ct + 1) * 128]))(),
                    reads=(bwp[i], bWPG), writes=(bwp[i],), tag=f"{tag} wp")
                for ft in range(NT):
                    em.op("pe", (lambda i=i, ft=ft: lambda eng: eng.matmul(
                        px[i][:, :], wp[i][:, ft * 128:(ft + 1) * 128],
                        hs[:, ft * CH:(ft + 1) * CH], start=(ft == 0), stop=(ft == NT - 1)))(),
                        reads=(bwp[i], bhs), writes=(bpx[i],), tag=f"{tag} xmm")
                em.op("dve", (lambda i=i, ct=ct: lambda eng: eng.tensor_copy(
                    xnf[:, ct * CH:(ct + 1) * CH], px[i][:, :]))(),
                    reads=(bpx[i], bxnf), writes=(bxnf,), tag=f"{tag} evf")
                em.op("act", (lambda i=i, ct=ct: lambda eng: eng.copy(
                    xnb[:, ct * CH:(ct + 1) * CH], xnf[:, ct * CH:(ct + 1) * CH]))(),
                    reads=(bxnf, bxnb), writes=(bxnb,), tag=f"{tag} evb")
                if do_ck:
                    em.op("dve", (lambda ct=ct: lambda eng: eng.tensor_reduce(
                        pooled[:, DT + ct:DT + ct + 1],
                        xnf[:, ct * CH:(ct + 1) * CH], AX.X, ALU.add))(),
                        reads=(bxnf, bpool), writes=(bpool,), tag=f"{tag} poolred")
                    em.op("dve", (lambda ct=ct: lambda eng: eng.tensor_add(
                        pooled[:, ct:ct + 1], pooled[:, ct:ct + 1],
                        pooled[:, DT + ct:DT + ct + 1]))(),
                        reads=(bpool,), writes=(bpool,), tag=f"{tag} pooladd")
            em.dma("qact", (lambda c=c: lambda eng: eng.dma_start(
                out=xout_f32.rearrange("(dt p) l -> p dt l", p=128)[:, :, c * CH:(c + 1) * CH],
                in_=xnf[:, :].rearrange("p (dt t) -> p dt t", dt=DT)))(),
                reads=(bxnf,), writes=(), tag=f"{tag} stf")
            em.dma("qact", (lambda c=c: lambda eng: eng.dma_start(
                out=xout_bf.rearrange("(dt p) l -> p dt l", p=128)[:, :, c * CH:(c + 1) * CH],
                in_=xnb[:, :].rearrange("p (dt t) -> p dt t", dt=DT)))(),
                reads=(bxnb,), writes=(), tag=f"{tag} stb")
        if do_ck:
            em.op("dve", (lambda Lt=Lt: lambda eng: eng.tensor_scalar(
                pooled[:, 0:DT], pooled[:, 0:DT], 1.0 / Lt, None, ALU.mult))(),
                reads=(bpool,), writes=(bpool,), tag=f"{tag} poolavg")
            em.dma("qact", lambda eng: eng.dma_start(out=pooled_dram, in_=pooled[:, 0:DT]),
                   reads=(bpool,), writes=(), tag=f"{tag} poolst")


# version 07
# speedup vs baseline: 1.1566x; 1.1566x over previous
import contextlib
import numpy as np
import ml_dtypes
import concourse.bass as bass
import concourse.mybir as mybir

F32 = mybir.dt.float32
BF16 = mybir.dt.bfloat16
AF = mybir.ActivationFunctionType
ALU = mybir.AluOpType
AX = mybir.AxisListType

D = 1024
L = 4096
NB = 2
DT = D // 128
NH = 16
DH = 64
U = 45
E = 8
HID = 2048
FS = 12288
IS = 4096
RNOV = 0.7
DBG_STAGES = False
SAME_ENGINE_EDGES = True



class Buf:
    __slots__ = ("name", "w", "readers")

    def __init__(self, name):
        self.name = name
        self.w = None
        self.readers = []


class _Op:
    __slots__ = ("stream", "fn", "waits", "ev", "inc_needed", "is_dma", "tag")

    def __init__(self, stream, fn, waits, ev, is_dma, tag):
        self.stream = stream
        self.fn = fn
        self.waits = waits
        self.ev = ev
        self.inc_needed = False
        self.is_dma = is_dma
        self.tag = tag


class Em:

    STREAMS = ("pe", "act", "dve", "pool", "sp")
    QUEUES = {"qsp": "sp", "qact": "act", "qpool": "pool"}
    NROT = 8

    def __init__(self, nc):
        self.nc = nc
        self.ops = {s: [] for s in self.STREAMS}
        self.counts = dict.fromkeys(self.STREAMS, 0)
        self.qcounts = dict.fromkeys(self.QUEUES, 0)
        self.ops_by_ev = {}
        self._last_ev = dict.fromkeys(self.STREAMS, None)

    def sem_keys(self):
        keys = list(self.STREAMS)
        for q in self.QUEUES:
            keys += [f"{q}{j}" for j in range(self.NROT)]
        return keys

    def _record(self, stream, fn, reads, writes, queue=None, tag=""):
        deps = set()
        for b in reads:
            if b.w is not None:
                deps.add(b.w)
        for b in writes:
            if b.w is not None:
                deps.add(b.w)
            for r in b.readers:
                deps.add(r)
        if queue is None:
            self.counts[stream] += 1
            ev = (stream, self.counts[stream])
        else:
            i = self.qcounts[queue]
            self.qcounts[queue] += 1
            j = i % self.NROT
            chain = i // self.NROT + 1
            ev = (f"{queue}{j}", 16 * chain)
            if chain > 1:
                deps.add((f"{queue}{j}", 16 * (chain - 1)))
        waits = []
        for (dstream, dval) in deps:
            if dstream == stream and queue is None and not SAME_ENGINE_EDGES:
                continue
            if dstream == stream == "pe" and queue is None:
                continue
            waits.append((dstream, dval))
            if (dstream, dval) in self.ops_by_ev:
                self.ops_by_ev[(dstream, dval)].inc_needed = True
        op = _Op(stream, fn, waits, ev, queue is not None, tag)
        self.ops[stream].append(op)
        self.ops_by_ev[ev] = op
        self._last_ev[stream] = ev
        for b in reads:
            b.readers.append(ev)
        for b in writes:
            b.w = ev
            b.readers = []
        return ev

    def op(self, stream, fn, reads=(), writes=(), tag=""):
        return self._record(stream, fn, reads, writes, None, tag)

    def dma(self, queue, fn, reads=(), writes=(), tag=""):
        return self._record(self.QUEUES[queue], fn, reads, writes, queue, tag)

    def barrier(self):
        lasts = {s: self._last_ev[s] for s in self.STREAMS}
        qlasts = []
        for q in self.QUEUES:
            n = self.qcounts[q]
            for j in range(self.NROT):
                cnt = (n - j + self.NROT - 1) // self.NROT
                if cnt > 0:
                    qlasts.append((f"{q}{j}", 16 * cnt))
        for s in self.STREAMS:
            deps = [ev for t, ev in lasts.items() if t != s and ev is not None]
            deps += qlasts
            for ev in deps:
                if ev in self.ops_by_ev:
                    self.ops_by_ev[ev].inc_needed = True
            op = _Op(s, lambda eng: None, [d for d in deps if d[0] != s], None, False, "barrier")
            self.ops[s].append(op)

    def finalize(self, block, sems):
        rank = {}
        for s in self.STREAMS:
            r = 0
            for op in self.ops[s]:
                if op.ev is not None and not op.is_dma:
                    if op.inc_needed:
                        r += 1
                    rank[op.ev] = r

        def mapval(ev):
            s, v = ev
            if s in self.STREAMS:
                return rank[ev]
            return v

        def emit_stream(eng, stream):
            waited = {}
            for op in self.ops[stream]:
                for dep in op.waits:
                    dval = mapval(dep)
                    dstream = dep[0]
                    if dval == 0 or waited.get(dstream, 0) >= dval:
                        continue
                    eng.wait_ge(sems[dstream], dval)
                    waited[dstream] = dval
                inst = op.fn(eng)
                if op.is_dma:
                    inst.then_inc(sems[op.ev[0]], 16)
                elif op.inc_needed:
                    if inst is None:
                        inst = eng.nop()
                    inst.then_inc(sems[op.ev[0]], 1)

        @block.tensor
        def _(t):
            emit_stream(t, "pe")

        @block.scalar
        def _(s):
            emit_stream(s, "act")

        @block.vector
        def _(v):
            emit_stream(v, "dve")

        @block.gpsimd
        def _(g):
            emit_stream(g, "pool")

        @block.sync
        def _(sy):
            emit_stream(sy, "sp")



def ceil_div(a, b):
    return -(-a // b)



def make_host_consts():
    return {
        "c_ones_col": np.ones((128, 1), np.float32),
        "c_ones_row": np.ones((1, 512), np.float32),
        "c_ident32": np.eye(128, dtype=np.float32),
        "c_ident16": np.eye(128).astype(ml_dtypes.bfloat16),
        "c_iota45": (np.arange(1, 46, dtype=np.float32)).reshape(45, 1),
        "c_ones16": np.ones((1, 512), ml_dtypes.bfloat16),
        "c_iotarow45": np.tile(np.arange(1, 46, dtype=np.float32), (128, 1)),
        "c_ones_col16": np.ones((128, 1), ml_dtypes.bfloat16),
        "c_lowtri": np.tril(np.ones((128, 128), np.float32), -1).T.copy(),
        "c_fid": (np.arange(FT)[None, :] * 128 + np.arange(128)[:, None]).astype(np.int32),
    }


class Consts:

    def __init__(self, nc, em, stk):
        self.ones_col = stk.enter_context(nc.sbuf_tensor("s_ones_col", [128, 1], F32))
        self.ones_row = stk.enter_context(nc.sbuf_tensor("s_ones_row", [1, 512], F32))
        self.ident32 = stk.enter_context(nc.sbuf_tensor("s_ident32", [128, 128], F32))
        self.ident16 = stk.enter_context(nc.sbuf_tensor("s_ident16", [128, 128], BF16))
        self.iota45 = stk.enter_context(nc.sbuf_tensor("s_iota45", [45, 1], F32))
        self.ones16 = stk.enter_context(nc.sbuf_tensor("s_ones16", [1, 512], BF16))
        self.iotarow45 = stk.enter_context(nc.sbuf_tensor("s_iotarow45", [128, 45], F32))
        self.ones_col16 = stk.enter_context(nc.sbuf_tensor("s_ones_col16", [128, 1], BF16))
        self.lowtri = stk.enter_context(nc.sbuf_tensor("s_lowtri", [128, 128], F32))
        self.fid = stk.enter_context(nc.sbuf_tensor("s_fid", [128, FT], mybir.dt.int32))
        self.buf = Buf("consts")
        self._cols = {}
        self._colbuf = stk.enter_context(nc.sbuf_tensor("s_colconsts", [128, 32], F32))
        self._ncols = 0
        names = ["c_ones_col", "c_ones_row", "c_ident32", "c_ident16", "c_iota45", "c_ones16",
                 "c_iotarow45", "c_ones_col16", "c_lowtri", "c_fid"]
        tiles = [self.ones_col, self.ones_row, self.ident32, self.ident16, self.iota45, self.ones16,
                 self.iotarow45, self.ones_col16, self.lowtri, self.fid]
        for name, tile in zip(names, tiles):
            ap = nc.dram_tensor(name, list(tile.ap().shape), tile.ap().dtype,
                                kind="ExternalInput").ap()
            em.dma(
                "qsp",
                (lambda t, a: lambda eng: eng.dma_start(out=t[:, :], in_=a[:, :]))(tile.ap(), ap),
                reads=(), writes=(self.buf,), tag=f"load {name}",
            )



def phase_in_transpose(nc, em, C, x_in, x0_f32, b):
    with contextlib.ExitStack() as stk:
        stage_tm = [stk.enter_context(nc.sbuf_tensor(f"p0tm{b}_{i}", [128, D], F32)) for i in range(2)]
        stage_fm = [stk.enter_context(nc.sbuf_tensor(f"p0fm{b}_{i}", [128, DT * 128], F32)) for i in range(2)]
        ps = [stk.enter_context(nc.psum_tensor(f"p0ps{b}_{i}", [128, 128], F32)) for i in range(2)]
        btm = [Buf("tm0"), Buf("tm1")]
        bfm = [Buf("fm0"), Buf("fm1")]
        bps = [Buf("ps0"), Buf("ps1")]
        for lt in range(L // 128):
            i = lt % 2
            em.dma("qsp", (lambda i=i, lt=lt: lambda eng: eng.dma_start(
                out=stage_tm[i][:, :], in_=x_in[b, lt * 128:(lt + 1) * 128, :]))(),
                reads=(), writes=(btm[i],), tag=f"p0 load {lt}")
            for dt in range(DT):
                j = dt % 2
                em.op("pe", (lambda i=i, j=j, dt=dt: lambda eng: eng.transpose(
                    ps[j][:, :], stage_tm[i][:, dt * 128:(dt + 1) * 128], C.ident32[:, :]))(),
                    reads=(btm[i], C.buf), writes=(bps[j],), tag="p0 tr")
                em.op("act", (lambda i=i, j=j, dt=dt: lambda eng: eng.copy(
                    stage_fm[i][:, dt * 128:(dt + 1) * 128], ps[j][:, :]))(),
                    reads=(bps[j],), writes=(bfm[i],), tag="p0 evac")
            em.dma("qact", (lambda i=i, lt=lt: lambda eng: eng.dma_start(
                out=x0_f32.rearrange("(dt p) l -> p dt l", p=128)[:, :, lt * 128:(lt + 1) * 128],
                in_=stage_fm[i][:, :].rearrange("p (dt c) -> p dt c", dt=DT)))(),
                reads=(bfm[i],), writes=(), tag=f"p0 store {lt}")



def phase_ln(nc, em, C, xprev, f_in, xout, b, tag="ln"):
    CH = 512
    NCH = L // CH
    with contextlib.ExitStack() as stk:
        zb = [stk.enter_context(nc.sbuf_tensor(f"{tag}z{i}", [128, DT * CH], F32)) for i in range(2)]
        fb = [stk.enter_context(nc.sbuf_tensor(f"{tag}f{i}", [128, DT * CH], F32)) for i in range(2)]
        z2 = stk.enter_context(nc.sbuf_tensor(f"{tag}z2", [128, DT * CH], F32))
        yb = [stk.enter_context(nc.sbuf_tensor(f"{tag}y{i}", [128, DT * CH], F32)) for i in range(2)]
        mrow = stk.enter_context(nc.sbuf_tensor(f"{tag}m", [1, CH], F32))
        vrow = stk.enter_context(nc.sbuf_tensor(f"{tag}v", [1, CH], F32))
        rrow = stk.enter_context(nc.sbuf_tensor(f"{tag}r", [1, CH], F32))
        m2row = stk.enter_context(nc.sbuf_tensor(f"{tag}m2", [1, CH], F32))
        pS = stk.enter_context(nc.psum_tensor(f"{tag}pS", [1, CH], F32))
        pQ = stk.enter_context(nc.psum_tensor(f"{tag}pQ", [1, CH], F32))
        pM = stk.enter_context(nc.psum_tensor(f"{tag}pM", [128, CH], F32))
        pR = stk.enter_context(nc.psum_tensor(f"{tag}pR", [128, CH], F32))
        bz = [Buf("z0"), Buf("z1")]
        bf_ = [Buf("f0"), Buf("f1")]
        bz2, by = Buf("z2"), [Buf("y0"), Buf("y1")]
        bm, bv, br, bm2 = Buf("m"), Buf("v"), Buf("r"), Buf("m2")
        bpS, bpQ, bpM, bpR = Buf("pS"), Buf("pQ"), Buf("pM"), Buf("pR")

        eps_ap = C.col(em, 1e-5)

        def ch_view(dram, c):
            return dram.rearrange("(dt p) l -> p dt l", p=128)[:, :, c * CH:(c + 1) * CH]

        for c in range(NCH):
            i = c % 2
            em.dma("qsp", (lambda i=i, c=c: lambda eng: eng.dma_start(
                out=zb[i][:, :].rearrange("p (dt c) -> p dt c", dt=DT), in_=ch_view(xprev, c)))(),
                reads=(), writes=(bz[i],), tag=f"{tag} ldx {c}")
            em.dma("qact", (lambda i=i, c=c: lambda eng: eng.dma_start(
                out=fb[i][:, :].rearrange("p (dt c) -> p dt c", dt=DT), in_=ch_view(f_in, c)))(),
                reads=(), writes=(bf_[i],), tag=f"{tag} ldf {c}")
            em.op("dve", (lambda i=i: lambda eng: eng.tensor_add(zb[i][:, :], zb[i][:, :], fb[i][:, :]))(),
                  reads=(bz[i], bf_[i]), writes=(bz[i],), tag=f"{tag} add")
            em.op("act", (lambda i=i: lambda eng: eng.activation(z2[:, :], zb[i][:, :], AF.Square))(),
                  reads=(bz[i],), writes=(bz2,), tag=f"{tag} sq")
            for dt in range(DT):
                em.op("pe", (lambda i=i, dt=dt: lambda eng: eng.matmul(
                    pS[:, :], C.ones_col[:, :], zb[i][:, dt * CH:(dt + 1) * CH],
                    start=(dt == 0), stop=(dt == DT - 1)))(),
                    reads=(bz[i], C.buf), writes=(bpS,), tag=f"{tag} mmS")
            for dt in range(DT):
                em.op("pe", (lambda dt=dt: lambda eng: eng.matmul(
                    pQ[:, :], C.ones_col[:, :], z2[:, dt * CH:(dt + 1) * CH],
                    start=(dt == 0), stop=(dt == DT - 1)))(),
                    reads=(bz2, C.buf), writes=(bpQ,), tag=f"{tag} mmQ")
            em.op("dve", lambda eng: eng.tensor_scalar(mrow[:, :], pS[:, :], 1.0 / D, None, ALU.mult),
                  reads=(bpS,), writes=(bm,), tag=f"{tag} m")
            em.op("dve", lambda eng: eng.tensor_scalar(vrow[:, :], pQ[:, :], 1.0 / D, None, ALU.mult),
                  reads=(bpQ,), writes=(bv,), tag=f"{tag} q")
            em.op("act", lambda eng: eng.activation(m2row[:, :], mrow[:, :], AF.Square),
                  reads=(bm,), writes=(bm2,), tag=f"{tag} msq")
            em.op("dve", lambda eng: eng.tensor_sub(vrow[:, :], vrow[:, :], m2row[:, :]),
                  reads=(bv, bm2), writes=(bv,), tag=f"{tag} var")
            em.op("act", lambda eng: eng.activation(rrow[:, :], vrow[:, :], AF.Sqrt, bias=eps_ap[0:1, :]),
                  reads=(bv, C.buf), writes=(br,), tag=f"{tag} sqrt")
            em.op("dve", lambda eng: eng.reciprocal(rrow[:, :], rrow[:, :]),
                  reads=(br,), writes=(br,), tag=f"{tag} recip")
            em.op("pe", lambda eng: eng.matmul(pM[:, :], C.ones_row[0:1, 0:128], mrow[:, :],
                                               start=True, stop=True),
                  reads=(bm, C.buf), writes=(bpM,), tag=f"{tag} bcm")
            em.op("pe", lambda eng: eng.matmul(pR[:, :], C.ones_row[0:1, 0:128], rrow[:, :],
                                               start=True, stop=True),
                  reads=(br, C.buf), writes=(bpR,), tag=f"{tag} bcr")
            for dt in range(DT):
                em.op("dve", (lambda i=i, dt=dt: lambda eng: eng.tensor_sub(
                    yb[i][:, dt * CH:(dt + 1) * CH], zb[i][:, dt * CH:(dt + 1) * CH], pM[:, :]))(),
                    reads=(bz[i], bpM), writes=(by[i],), tag=f"{tag} sub")
            for dt in range(DT):
                em.op("dve", (lambda i=i, dt=dt: lambda eng: eng.tensor_mul(
                    yb[i][:, dt * CH:(dt + 1) * CH], yb[i][:, dt * CH:(dt + 1) * CH], pR[:, :]))(),
                    reads=(by[i], bpR), writes=(by[i],), tag=f"{tag} mul")
            em.dma("qact", (lambda i=i, c=c: lambda eng: eng.dma_start(
                out=ch_view(xout, c), in_=yb[i][:, :].rearrange("p (dt c) -> p dt c", dt=DT)))(),
                reads=(by[i],), writes=(), tag=f"{tag} st {c}")





def _consts_col(self, em, value):
    if value in self._cols:
        return self._cols[value]
    i = self._ncols
    assert i < 32
    self._ncols += 1
    ap = self._colbuf[:, i:i + 1]
    em.op("dve", (lambda ap=ap, value=value: lambda eng: eng.memset(ap, float(value)))(),
          reads=(), writes=(self.buf,), tag=f"const col {value}")
    self._cols[value] = ap
    return ap

Consts.col = _consts_col



def phase_moe_prep(nc, em, W, tag="moeprep"):
    w1bf = nc.dram_tensor(f"{tag}_w1", [E, D, HID], BF16, kind="Internal").ap()
    w2bf = nc.dram_tensor(f"{tag}_w2", [E, HID, D], BF16, kind="Internal").ap()
    with contextlib.ExitStack() as stk:
        tmp = [stk.enter_context(nc.sbuf_tensor(f"{tag}t{i}", [128, 16384], BF16)) for i in range(2)]
        bt = [Buf("t0"), Buf("t1")]
        for e in range(E):
            i = e % 2
            em.dma("qpool", (lambda e=e, i=i: lambda eng: eng.dma_start(
                out=tmp[i][:, :].rearrange("p (dt h) -> p dt h", dt=DT),
                in_=W["W1"][e].rearrange("(dt p) h -> p dt h", p=128)))(),
                reads=(), writes=(bt[i],), tag=f"{tag} in1")
            em.dma("qsp", (lambda e=e, i=i: lambda eng: eng.dma_start(
                out=w1bf[e].rearrange("(dt p) h -> p dt h", p=128),
                in_=tmp[i][:, :].rearrange("p (dt h) -> p dt h", dt=DT)))(),
                reads=(bt[i],), writes=(), tag=f"{tag} out1")
            em.dma("qpool", (lambda e=e, i=i: lambda eng: eng.dma_start(
                out=tmp[i][:, :].rearrange("p (ht d) -> p ht d", ht=HID // 128),
                in_=W["W2"][e].rearrange("(ht p) d -> p ht d", p=128)))(),
                reads=(bt[i],), writes=(bt[i],), tag=f"{tag} in2")
            em.dma("qsp", (lambda e=e, i=i: lambda eng: eng.dma_start(
                out=w2bf[e].rearrange("(ht p) d -> p ht d", p=128),
                in_=tmp[i][:, :].rearrange("p (ht d) -> p ht d", ht=HID // 128)))(),
                reads=(bt[i],), writes=(), tag=f"{tag} out2")
    return w1bf, w2bf


def phase_moe(nc, em, C, x_f32, W, w1bf, w2bf, out_f32, b, Lt=L, tag="moe"):
    CH = 512
    MAC = min(512, Lt)
    NM = Lt // MAC
    NCH = MAC // CH
    HT = HID // 128
    gs = nc.dram_tensor(f"{tag}{b}_gs", [8, Lt], F32, kind="Internal").ap()
    bgs = Buf(f"{tag}gs")

    with contextlib.ExitStack() as stk:
        sb = lambda name, shape, dt_: stk.enter_context(nc.sbuf_tensor(f"{tag}{b}_{name}", shape, dt_))
        gw = sb("gw", [128, DT * E], F32)
        xbg = sb("xbg", [128, DT * 512], F32)
        grows = sb("grows", [8, 512], F32)
        nmx = sb("nmx", [128, 1], F32)
        prob = sb("prob", [128, 8], F32)
        ssum = sb("ssum", [128, 1], F32)
        top8 = sb("top8", [128, 8], F32)
        wtm = sb("wtm", [128, 8], F32)
        pglog = stk.enter_context(nc.psum_tensor(f"{tag}{b}_pgl", [128, 8], F32))
        pgtr = stk.enter_context(nc.psum_tensor(f"{tag}{b}_pgt", [128, 128], F32))
        B = lambda n: Buf(f"{tag}{n}")
        bgw, bxb, bgrows, bsc = B("gw"), B("xbg"), B("grows"), B("sc")
        bpgl, bpgtr = B("pgl"), B("pgtr")
        em.dma("qsp", lambda eng: eng.dma_start(
            out=gw[:, :].rearrange("p (dt e) -> p dt e", dt=DT),
            in_=W["gate"].rearrange("(dt p) e -> p dt e", p=128)),
            reads=(), writes=(bgw,), tag=f"{tag} gw")
        GM = min(512, Lt)
        for m in range(Lt // GM):
            em.dma("qsp", (lambda m=m: lambda eng: eng.dma_start(
                out=xbg[:, 0:DT * GM].rearrange("p (dt c) -> p dt c", dt=DT),
                in_=x_f32.rearrange("(dt p) l -> p dt l", p=128)[:, :, m * GM:(m + 1) * GM]))(),
                reads=(), writes=(bxb,), tag=f"{tag} ldxg")
            for tt in range(GM // 128):
                for dt in range(DT):
                    em.op("pe", (lambda tt=tt, dt=dt: lambda eng: eng.matmul(
                        pglog[:, :], xbg[:, dt * GM + tt * 128: dt * GM + (tt + 1) * 128],
                        gw[:, dt * E:(dt + 1) * E], start=(dt == 0), stop=(dt == DT - 1)))(),
                        reads=(bxb, bgw), writes=(bpgl,), tag=f"{tag} glog")
                em.op("dve", lambda eng: eng.tensor_reduce(nmx[:, :], pglog[:, :], AX.X, ALU.max, negate=True),
                      reads=(bpgl,), writes=(bsc,), tag=f"{tag} gmax")
                em.op("act", lambda eng: eng.activation(prob[:, :], pglog[:, :], AF.Exp,
                                                        bias=nmx[:, :], accum_out=ssum[:, :]),
                      reads=(bpgl, bsc), writes=(bsc,), tag=f"{tag} gexp")
                em.op("dve", lambda eng: eng.reciprocal(ssum[:, :], ssum[:, :]),
                      reads=(bsc,), writes=(bsc,), tag=f"{tag} grecip")
                em.op("dve", lambda eng: eng.tensor_scalar(prob[:, :], prob[:, :], ssum[:, :], None, ALU.mult),
                      reads=(bsc,), writes=(bsc,), tag=f"{tag} gnorm")
                em.op("dve", lambda eng: eng.max(top8[:, :], prob[:, :]),
                      reads=(bsc,), writes=(bsc,), tag=f"{tag} gtop8")
                em.op("dve", lambda eng: eng.tensor_scalar(wtm[:, :], prob[:, :], top8[:, 1:2], None, ALU.is_ge),
                      reads=(bsc,), writes=(bsc,), tag=f"{tag} gmask")
                em.op("dve", lambda eng: eng.tensor_tensor(wtm[:, :], wtm[:, :], prob[:, :], ALU.mult),
                      reads=(bsc,), writes=(bsc,), tag=f"{tag} gmul")
                em.op("dve", lambda eng: eng.tensor_scalar(wtm[:, :], wtm[:, :], 1.0, None,
                                                           ALU.mult, ALU.add, accum_out=ssum[:, :]),
                      reads=(bsc,), writes=(bsc,), tag=f"{tag} gden")
                em.op("dve", lambda eng: eng.tensor_scalar_max(ssum[:, :], ssum[:, :], 1e-12),
                      reads=(bsc,), writes=(bsc,), tag=f"{tag} gdmax")
                em.op("dve", lambda eng: eng.reciprocal(ssum[:, :], ssum[:, :]),
                      reads=(bsc,), writes=(bsc,), tag=f"{tag} gdrecip")
                em.op("dve", lambda eng: eng.tensor_scalar(wtm[:, :], wtm[:, :], ssum[:, :], None, ALU.mult),
                      reads=(bsc,), writes=(bsc,), tag=f"{tag} gwn")
                em.op("pe", lambda eng: eng.transpose(pgtr[0:8, :], wtm[:, :], C.ident32[:, :]),
                      reads=(bsc, C.buf), writes=(bpgtr,), tag=f"{tag} gtr")
                em.op("dve", (lambda tt=tt: lambda eng: eng.tensor_copy(
                    grows[:, tt * 128:(tt + 1) * 128], pgtr[0:8, :]))(),
                    reads=(bpgtr,), writes=(bgrows,), tag=f"{tag} gcp")
            em.dma("qact", (lambda m=m: lambda eng: eng.dma_start(
                out=gs[:, m * GM:(m + 1) * GM], in_=grows[:, 0:GM]))(),
                reads=(bgrows,), writes=(bgs,), tag=f"{tag} gst")
    em.barrier()

    with contextlib.ExitStack() as stk:
        sb = lambda name, shape, dt_: stk.enter_context(nc.sbuf_tensor(f"{tag}{b}_{name}", shape, dt_))
        xb = sb("xb", [128, DT * MAC], BF16)
        xge = sb("xge", [128, DT * CH], BF16)
        hsb = sb("hsb", [128, HT * CH], BF16)
        acc = sb("acc", [128, DT * MAC], F32)
        w1 = [sb(f"w1_{i}", [128, DT * HID], BF16) for i in range(2)]
        w2 = [sb(f"w2_{i}", [128, HT * D], BF16) for i in range(2)]
        b1e = [sb(f"b1e{i}", [1, HID], BF16) for i in range(2)]
        b2e = [sb(f"b2e{i}", [1, D], BF16) for i in range(2)]
        gee = [sb(f"ge{i}", [1, MAC], BF16) for i in range(2)]
        pg = stk.enter_context(nc.psum_tensor(f"{tag}{b}_pg", [128, CH], F32))
        ph = [stk.enter_context(nc.psum_tensor(f"{tag}{b}_ph{i}", [128, CH], F32)) for i in range(2)]
        py = [stk.enter_context(nc.psum_tensor(f"{tag}{b}_py{i}", [128, CH], F32)) for i in range(2)]
        B = lambda n: Buf(f"{tag}{n}")
        bxb, bxge, bh, bacc = B("xb"), B("xge"), B("h"), B("acc")
        bw1 = [B("w1a"), B("w1b")]
        bw2 = [B("w2a"), B("w2b")]
        bb1 = [B("b1a"), B("b1b")]
        bb2 = [B("b2a"), B("b2b")]
        bge = [B("gea"), B("geb")]
        bpg = B("pg")
        bph = [B("pha"), B("phb")]
        bpy = [B("pya"), B("pyb")]
        for m in range(NM):
            em.dma("qpool", (lambda m=m: lambda eng: eng.dma_start(
                out=xb[:, :].rearrange("p (dt c) -> p dt c", dt=DT),
                in_=x_f32.rearrange("(dt p) l -> p dt l", p=128)[:, :, m * MAC:(m + 1) * MAC]))(),
                reads=(), writes=(bxb,), tag=f"{tag} ldx2")
            em.op("dve", lambda eng: eng.memset(acc[:, :], 0.0),
                  reads=(), writes=(bacc,), tag=f"{tag} acc0")
            for e in range(E):
                i = e % 2
                em.dma("qsp", (lambda e=e, i=i: lambda eng: eng.dma_start(
                    out=w1[i][:, :].rearrange("p (dt h) -> p dt h", dt=DT),
                    in_=w1bf[e].rearrange("(dt p) h -> p dt h", p=128)))(),
                    reads=(), writes=(bw1[i],), tag=f"{tag} w1 {e}")
                em.dma("qsp", (lambda e=e, i=i: lambda eng: eng.dma_start(
                    out=w2[i][:, :].rearrange("p (ht d) -> p ht d", ht=HT),
                    in_=w2bf[e].rearrange("(ht p) d -> p ht d", p=128)))(),
                    reads=(), writes=(bw2[i],), tag=f"{tag} w2 {e}")
                em.dma("qpool", (lambda e=e, i=i: lambda eng: eng.dma_start(
                    out=b1e[i][:, :], in_=W["b1"][e].unsqueeze(0)))(),
                    reads=(), writes=(bb1[i],), tag=f"{tag} b1 {e}")
                em.dma("qpool", (lambda e=e, i=i: lambda eng: eng.dma_start(
                    out=b2e[i][:, :], in_=W["b2"][e].unsqueeze(0)))(),
                    reads=(), writes=(bb2[i],), tag=f"{tag} b2 {e}")
                em.dma("qpool", (lambda e=e, i=i, m=m: lambda eng: eng.dma_start(
                    out=gee[i][:, :], in_=gs[e:e + 1, m * MAC:(m + 1) * MAC]))(),
                    reads=(bgs,), writes=(bge[i],), tag=f"{tag} ge {e}")
                for cc in range(NCH):
                    em.op("pe", (lambda i=i, cc=cc: lambda eng: eng.matmul(
                        pg[:, :], C.ones16[0:1, 0:128], gee[i][0:1, cc * CH:(cc + 1) * CH],
                        start=True, stop=True))(),
                        reads=(bge[i], C.buf), writes=(bpg,), tag=f"{tag} gbc")
                    for dt in range(DT):
                        em.op("dve", (lambda cc=cc, dt=dt: lambda eng: eng.tensor_tensor(
                            xge[:, dt * CH:(dt + 1) * CH],
                            xb[:, dt * MAC + cc * CH: dt * MAC + (cc + 1) * CH],
                            pg[:, :], ALU.mult))(),
                            reads=(bxb, bpg), writes=(bxge,), tag=f"{tag} xge")
                    for ht in range(HT):
                        j = ht % 2
                        for dt in range(DT):
                            em.op("pe", (lambda i=i, j=j, ht=ht, dt=dt: lambda eng: eng.matmul(
                                ph[j][:, :], w1[i][:, dt * HID + ht * 128: dt * HID + (ht + 1) * 128],
                                xge[:, dt * CH:(dt + 1) * CH], start=(dt == 0), stop=False))(),
                                reads=(bw1[i], bxge), writes=(bph[j],), tag=f"{tag} h mm")
                        em.op("pe", (lambda i=i, j=j, ht=ht, cc=cc: lambda eng: eng.matmul(
                            ph[j][:, :], b1e[i][0:1, ht * 128:(ht + 1) * 128],
                            gee[i][0:1, cc * CH:(cc + 1) * CH],
                            start=False, stop=True))(),
                            reads=(bb1[i], bge[i]), writes=(bph[j],), tag=f"{tag} h bias")
                        em.op("act", (lambda j=j, ht=ht: lambda eng: eng.activation(
                            hsb[:, ht * CH:(ht + 1) * CH], ph[j][:, :], AF.Relu))(),
                            reads=(bph[j],), writes=(bh,), tag=f"{tag} h relu")
                    for dt in range(DT):
                        j = dt % 2
                        for ht in range(HT):
                            em.op("pe", (lambda i=i, j=j, dt=dt, ht=ht: lambda eng: eng.matmul(
                                py[j][:, :], w2[i][:, ht * D + dt * 128: ht * D + (dt + 1) * 128],
                                hsb[:, ht * CH:(ht + 1) * CH], start=(ht == 0), stop=False))(),
                                reads=(bw2[i], bh), writes=(bpy[j],), tag=f"{tag} y mm")
                        em.op("pe", (lambda i=i, j=j, dt=dt, cc=cc: lambda eng: eng.matmul(
                            py[j][:, :], b2e[i][0:1, dt * 128:(dt + 1) * 128],
                            gee[i][0:1, cc * CH:(cc + 1) * CH],
                            start=False, stop=True))(),
                            reads=(bb2[i], bge[i]), writes=(bpy[j],), tag=f"{tag} y bias")
                        em.op("dve", (lambda j=j, dt=dt, cc=cc: lambda eng: eng.tensor_add(
                            acc[:, dt * MAC + cc * CH: dt * MAC + (cc + 1) * CH],
                            acc[:, dt * MAC + cc * CH: dt * MAC + (cc + 1) * CH], py[j][:, :]))(),
                            reads=(bpy[j], bacc), writes=(bacc,), tag=f"{tag} y acc")
            em.dma("qact", (lambda m=m: lambda eng: eng.dma_start(
                out=out_f32.rearrange("(dt p) l -> p dt l", p=128)[:, :, m * MAC:(m + 1) * MAC],
                in_=acc[:, :].rearrange("p (dt c) -> p dt c", dt=DT)))(),
                reads=(bacc,), writes=(), tag=f"{tag} store")



def phase_l1attn(nc, em, C, xs_f32, P, outs_f32, nb=NB, tag="phz"):
    TW = L + 1
    halfpi = C.col(em, float(np.pi / 2))
    with contextlib.ExitStack() as stk:
        sb = lambda name, shape, dt_: stk.enter_context(nc.sbuf_tensor(f"{tag}_{name}", shape, dt_))
        pcols = sb("pc", [128, 7 * DT], F32)
        dcols = sb("dc", [128, 16], F32)
        EC = sb("EC", [128, TW], F32)
        ESn = sb("ESn", [128, TW], F32)
        R = sb("R", [128, TW], F32)
        A = sb("A", [128, L], F32)
        Bt = sb("B", [128, L], F32)
        T = sb("T", [128, L], F32)
        rbc = sb("rbc", [128, L], F32)
        xr = sb("xr", [128, L], F32)
        ur = sb("ur", [128, L], F32)
        ui = sb("ui", [128, L], F32)
        vr = sb("vr", [128, L], F32)
        vi = sb("vi", [128, L], F32)
        bP, bD = Buf("pcols"), Buf("dcols")
        bEC, bES, bR = Buf("EC"), Buf("ESn"), Buf("R")
        bA, bB, bT, brbc = Buf("A"), Buf("B"), Buf("T"), Buf("rbc")
        bx, bur, bui, bvr, bvi = Buf("x"), Buf("ur"), Buf("ui"), Buf("vr"), Buf("vi")
        names = ["angle", "lnm", "scale", "pi_re", "pi_im", "lc_re", "lc_im"]
        for ip, pname in enumerate(names):
            em.dma("qsp", (lambda ip=ip, pname=pname: lambda eng: eng.dma_start(
                out=pcols[:, ip * DT:(ip + 1) * DT], in_=P[pname]))(),
                reads=(), writes=(bP,), tag=f"{tag} ld {pname}")
        coli = {n: ip for ip, n in enumerate(names)}

        def col(n, ct):
            return pcols[:, coli[n] * DT + ct: coli[n] * DT + ct + 1]

        cc = lambda j: dcols[:, j:j + 1]
        for ct in range(DT):
            em.op("act", (lambda ct=ct: lambda eng: eng.activation(cc(0), col("lnm", ct), AF.Exp))(),
                  reads=(bP, bD), writes=(bD,), tag=f"{tag} decay")
            em.op("dve", (lambda ct=ct: lambda eng: eng.tensor_tensor(cc(0), cc(0), col("scale", ct), ALU.mult))(),
                  reads=(bP, bD), writes=(bD,), tag=f"{tag} dscale")
            em.op("act", lambda eng: eng.activation(cc(1), cc(0), AF.Exp, scale=-1.0),
                  reads=(bD,), writes=(bD,), tag=f"{tag} r")
            em.op("dve", lambda eng: eng.tensor_scalar(cc(2), cc(1), -1.0, 1.0, ALU.mult, ALU.add),
                  reads=(bD,), writes=(bD,), tag=f"{tag} gate")
            em.op("dve", (lambda ct=ct: lambda eng: eng.tensor_tensor(cc(3), col("angle", ct), col("scale", ct), ALU.mult))(),
                  reads=(bP, bD), writes=(bD,), tag=f"{tag} theta")
            em.op("act", lambda eng: eng.activation(cc(4), cc(3), AF.Sin, bias=halfpi[0:128, :]),
                  reads=(bD, C.buf), writes=(bD,), tag=f"{tag} C1")
            em.op("act", lambda eng: eng.activation(cc(5), cc(3), AF.Sin),
                  reads=(bD,), writes=(bD,), tag=f"{tag} S1")
            em.op("dve", lambda eng: eng.memset(EC[:, 0:1], 1.0), reads=(bEC,), writes=(bEC,), tag=f"{tag} ec0")
            em.op("dve", lambda eng: eng.memset(ESn[:, 0:1], 0.0), reads=(bES,), writes=(bES,), tag=f"{tag} es0")
            em.op("dve", lambda eng: eng.memset(R[:, 0:1], 1.0), reads=(bR,), writes=(bR,), tag=f"{tag} r0")
            k = 1
            while k <= L:
                w = min(k, TW - k)
                em.op("dve", (lambda k=k: lambda eng: eng.tensor_tensor(cc(8), EC[:, k - 1:k], cc(4), ALU.mult))(),
                      reads=(bEC, bD), writes=(bD,), tag=f"{tag} ck1")
                em.op("dve", (lambda k=k: lambda eng: eng.scalar_tensor_tensor(
                    cc(8), ESn[:, k - 1:k], cc(5), cc(8), ALU.mult, ALU.add))(),
                    reads=(bES, bD), writes=(bD,), tag=f"{tag} ck2")
                em.op("dve", (lambda k=k: lambda eng: eng.tensor_tensor(cc(9), EC[:, k - 1:k], cc(5), ALU.mult))(),
                      reads=(bEC, bD), writes=(bD,), tag=f"{tag} sk1")
                em.op("dve", (lambda k=k: lambda eng: eng.scalar_tensor_tensor(
                    cc(11), ESn[:, k - 1:k], cc(4), cc(9), ALU.mult, ALU.subtract))(),
                    reads=(bES, bD), writes=(bD,), tag=f"{tag} sk2")
                em.op("dve", lambda eng: eng.tensor_scalar(cc(9), cc(11), -1.0, None, ALU.mult),
                      reads=(bD,), writes=(bD,), tag=f"{tag} sk3")
                em.op("dve", (lambda k=k: lambda eng: eng.tensor_tensor(cc(10), R[:, k - 1:k], cc(1), ALU.mult))(),
                      reads=(bR, bD), writes=(bD,), tag=f"{tag} rk")
                em.op("dve", (lambda k=k, w=w: lambda eng: eng.tensor_scalar(
                    EC[:, k:k + w], EC[:, 0:w], cc(8), None, ALU.mult))(),
                    reads=(bEC, bD), writes=(bEC,), tag=f"{tag} ecd1")
                em.op("dve", (lambda k=k, w=w: lambda eng: eng.scalar_tensor_tensor(
                    EC[:, k:k + w], ESn[:, 0:w], cc(9), EC[:, k:k + w], ALU.mult, ALU.add))(),
                    reads=(bES, bD, bEC), writes=(bEC,), tag=f"{tag} ecd2")
                em.op("dve", (lambda k=k, w=w: lambda eng: eng.tensor_scalar(
                    ESn[:, k:k + w], ESn[:, 0:w], cc(8), None, ALU.mult))(),
                    reads=(bES, bD), writes=(bES,), tag=f"{tag} esd1")
                em.op("dve", (lambda k=k, w=w: lambda eng: eng.scalar_tensor_tensor(
                    ESn[:, k:k + w], EC[:, 0:w], cc(9), ESn[:, k:k + w], ALU.mult, ALU.subtract))(),
                    reads=(bEC, bD, bES), writes=(bES,), tag=f"{tag} esd2")
                em.op("dve", (lambda k=k, w=w: lambda eng: eng.tensor_scalar(
                    ESn[:, k:k + w], ESn[:, k:k + w], -1.0, None, ALU.mult))(),
                    reads=(bES,), writes=(bES,), tag=f"{tag} esd3")
                em.op("dve", (lambda k=k, w=w: lambda eng: eng.tensor_scalar(
                    R[:, k:k + w], R[:, 0:w], cc(10), None, ALU.mult))(),
                    reads=(bR, bD), writes=(bR,), tag=f"{tag} rd")
                k *= 2
            em.op("dve", (lambda ct=ct: lambda eng: eng.tensor_scalar(
                A[:, :], EC[:, 0:L], col("pi_re", ct), None, ALU.mult))(),
                reads=(bEC, bP), writes=(bA,), tag=f"{tag} A1")
            em.op("dve", (lambda ct=ct: lambda eng: eng.scalar_tensor_tensor(
                A[:, :], ESn[:, 0:L], col("pi_im", ct), A[:, :], ALU.mult, ALU.add))(),
                reads=(bES, bP, bA), writes=(bA,), tag=f"{tag} A2")
            em.op("dve", (lambda ct=ct: lambda eng: eng.tensor_scalar(
                Bt[:, :], ESn[:, 0:L], col("pi_re", ct), None, ALU.mult))(),
                reads=(bES, bP), writes=(bB,), tag=f"{tag} B1")
            em.op("dve", (lambda ct=ct: lambda eng: eng.scalar_tensor_tensor(
                Bt[:, :], EC[:, 0:L], col("pi_im", ct), Bt[:, :], ALU.mult, ALU.subtract))(),
                reads=(bEC, bP, bB), writes=(bB,), tag=f"{tag} B2")
            em.op("dve", (lambda ct=ct: lambda eng: eng.tensor_scalar(
                T[:, :], EC[:, 1:TW], col("lc_re", ct), None, ALU.mult))(),
                reads=(bEC, bP), writes=(bT,), tag=f"{tag} T1")
            em.op("dve", (lambda ct=ct: lambda eng: eng.scalar_tensor_tensor(
                T[:, :], ESn[:, 1:TW], col("lc_im", ct), T[:, :], ALU.mult, ALU.add))(),
                reads=(bES, bP, bT), writes=(bT,), tag=f"{tag} T2")
            em.op("dve", lambda eng: eng.tensor_tensor(T[:, :], T[:, :], R[:, 1:TW], ALU.mult),
                  reads=(bR, bT), writes=(bT,), tag=f"{tag} T3")
            em.op("dve", lambda eng: eng.tensor_scalar(rbc[:, :], EC[:, 0:L], 0.0, cc(1), ALU.mult, ALU.add),
                  reads=(bEC, bD, brbc), writes=(brbc,), tag=f"{tag} rbc")

            for b in range(nb):
                em.dma("qsp", (lambda ct=ct, b=b: lambda eng: eng.dma_start(
                    out=xr[:, :], in_=xs_f32[b][ct * 128:(ct + 1) * 128, :]))(),
                    reads=(bx,), writes=(bx,), tag=f"{tag} ldx")
                em.op("dve", lambda eng: eng.tensor_mul(ur[:, :], xr[:, :], EC[:, 0:L]),
                      reads=(bx, bEC, bur), writes=(bur,), tag=f"{tag} ur")
                em.op("dve", lambda eng: eng.tensor_mul(ui[:, :], xr[:, :], ESn[:, 0:L]),
                      reads=(bx, bES, bui), writes=(bui,), tag=f"{tag} ui")
                em.op("dve", lambda eng: eng.tensor_tensor_scan(vr[:, :], rbc[:, :], ur[:, :], 0.0, ALU.mult, ALU.add),
                      reads=(brbc, bur, bvr), writes=(bvr,), tag=f"{tag} scanr")
                em.op("dve", lambda eng: eng.tensor_tensor_scan(vi[:, :], rbc[:, :], ui[:, :], 0.0, ALU.mult, ALU.add),
                      reads=(brbc, bui, bvi), writes=(bvi,), tag=f"{tag} scani")
                em.op("dve", lambda eng: eng.tensor_mul(ur[:, :], vr[:, :], A[:, :]),
                      reads=(bvr, bA, bur), writes=(bur,), tag=f"{tag} y1")
                em.op("dve", lambda eng: eng.tensor_mul(ui[:, :], vi[:, :], Bt[:, :]),
                      reads=(bvi, bB, bui), writes=(bui,), tag=f"{tag} y2")
                em.op("dve", lambda eng: eng.tensor_sub(ur[:, :], ur[:, :], ui[:, :]),
                      reads=(bur, bui), writes=(bur,), tag=f"{tag} y3")
                em.op("dve", lambda eng: eng.tensor_add(ur[:, :], ur[:, :], T[:, :]),
                      reads=(bur, bT), writes=(bur,), tag=f"{tag} y4")
                em.op("dve", lambda eng: eng.tensor_scalar(ur[:, :], ur[:, :], cc(2), None, ALU.mult),
                      reads=(bur, bD), writes=(bur,), tag=f"{tag} y5")
                em.dma("qact", (lambda ct=ct, b=b: lambda eng: eng.dma_start(
                    out=outs_f32[b][ct * 128:(ct + 1) * 128, :], in_=ur[:, :]))(),
                    reads=(bur,), writes=(), tag=f"{tag} sty")



FT = FS // 128
BASE_FT = IS // 128


def phase_l1ffn_prep(nc, em, C, W, tag="l1prep", subset=("Wm", "Wp", "Wc1", "Wc2")):
    WmR = nc.dram_tensor(f"{tag}_WmR", [FS, D], BF16, kind="Internal").ap()
    WmT = nc.dram_tensor(f"{tag}_WmT", [D, FS], BF16, kind="Internal").ap()
    WpT = nc.dram_tensor(f"{tag}_WpT", [FS, D], BF16, kind="Internal").ap()
    Wc1T = nc.dram_tensor(f"{tag}_Wc1T", [D, 256], BF16, kind="Internal").ap()
    Wc2T = nc.dram_tensor(f"{tag}_Wc2T", [256, FS], BF16, kind="Internal").ap()
    with contextlib.ExitStack() as stk:
        tin = [stk.enter_context(nc.sbuf_tensor(f"{tag}_i{i}", [128, FS], BF16)) for i in range(2)]
        tout = [stk.enter_context(nc.sbuf_tensor(f"{tag}_o{i}", [128, FS], BF16)) for i in range(2)]
        ps = [stk.enter_context(nc.psum_tensor(f"{tag}_p{i}", [128, 128], BF16)) for i in range(2)]
        bi = [Buf("i0"), Buf("i1")]
        bo = [Buf("o0"), Buf("o1")]
        bp = [Buf("p0"), Buf("p1")]

        def transpose_mat(src, rows, cols, dst, tg, rowcopy=None):
            nr, ncol = rows // 128, cols // 128
            for rt in range(nr):
                i = rt % 2
                em.dma("qpool", (lambda rt=rt, i=i: lambda eng: eng.dma_start(
                    out=tin[i][:, 0:cols], in_=src[rt * 128:(rt + 1) * 128, :]))(),
                    reads=(bo[i],), writes=(bi[i],), tag=f"{tg} in")
                for ct_ in range(ncol):
                    j = ct_ % 2
                    em.op("pe", (lambda i=i, j=j, ct_=ct_: lambda eng: eng.transpose(
                        ps[j][:, :], tin[i][:, ct_ * 128:(ct_ + 1) * 128], C.ident16[:, :]))(),
                        reads=(bi[i], bp[j], C.buf), writes=(bp[j],), tag=f"{tg} tr")
                    em.op("dve", (lambda i=i, j=j, ct_=ct_: lambda eng: eng.tensor_copy(
                        tout[i][:, ct_ * 128:(ct_ + 1) * 128], ps[j][:, :]))(),
                        reads=(bp[j], bo[i]), writes=(bo[i],), tag=f"{tg} ev")
                em.dma("qact", (lambda rt=rt, i=i, ncol=ncol: lambda eng: eng.dma_start(
                    out=dst.rearrange("(ct p) r -> p ct r", p=128)[:, :, rt * 128:(rt + 1) * 128],
                    in_=tout[i][:, 0:cols].rearrange("p (ct c) -> p ct c", ct=ncol)))(),
                    reads=(bo[i],), writes=(), tag=f"{tg} out")
                if rowcopy is not None:
                    em.dma("qact", (lambda rt=rt, i=i, cols=cols: lambda eng: eng.dma_start(
                        out=rowcopy[rt * 128:(rt + 1) * 128, :], in_=tin[i][:, 0:cols]))(),
                        reads=(bi[i],), writes=(), tag=f"{tg} rowc")

        if "Wm" in subset:
            transpose_mat(W["Wm"], FS, D, WmT, f"{tag}m", rowcopy=WmR)
        if "Wp" in subset:
            transpose_mat(W["Wp"], D, FS, WpT, f"{tag}p")
        if "Wc1" in subset:
            transpose_mat(W["Wc1"], 256, D, Wc1T, f"{tag}c1")
        if "Wc2" in subset:
            transpose_mat(W["Wc2"], FS, 256, Wc2T, f"{tag}c2")
    return dict(WmT=WmT, WpT=WpT, Wc1T=Wc1T, Wc2T=Wc2T, WmR=WmR)


def emit_bisect(nc, em, C, stk, ck_sb, bck, mask_sb, bmask, k_count, tag, nit=45):
    sc = stk.enter_context(nc.sbuf_tensor(f"{tag}_bs", [1, 8], F32))
    colb = stk.enter_context(nc.sbuf_tensor(f"{tag}_bc", [128, 3], F32))
    scr = stk.enter_context(nc.sbuf_tensor(f"{tag}_bscr", [128, FT], F32))
    ptr1 = stk.enter_context(nc.psum_tensor(f"{tag}_bp1", [1, 128], F32))
    pbc = stk.enter_context(nc.psum_tensor(f"{tag}_bp2", [128, 1], F32))
    pcnt = stk.enter_context(nc.psum_tensor(f"{tag}_bp3", [1, 1], F32))
    bs, bcol, bscr = Buf("bs"), Buf("bcol"), Buf("bscr")
    bp1, bp2, bp3 = Buf("bp1"), Buf("bp2"), Buf("bp3")
    em.op("dve", lambda eng: eng.tensor_reduce(colb[:, 0:1], ck_sb[:, :], AX.X, ALU.max),
          reads=(bck, bcol), writes=(bcol,), tag=f"{tag} mx1")
    em.op("dve", lambda eng: eng.tensor_reduce(colb[:, 1:2], ck_sb[:, :], AX.X, ALU.min),
          reads=(bck, bcol), writes=(bcol,), tag=f"{tag} mn1")
    em.op("pe", lambda eng: eng.transpose(ptr1[0:1, :], colb[:, 0:1], C.ident32[:, :]),
          reads=(bcol, C.buf, bp1), writes=(bp1,), tag=f"{tag} tr1")
    em.op("dve", lambda eng: eng.tensor_reduce(sc[:, 1:2], ptr1[0:1, :], AX.X, ALU.max),
          reads=(bp1, bs), writes=(bs,), tag=f"{tag} mx2")
    em.op("pe", lambda eng: eng.transpose(ptr1[0:1, :], colb[:, 1:2], C.ident32[:, :]),
          reads=(bcol, C.buf, bp1, bs), writes=(bp1,), tag=f"{tag} tr2")
    em.op("dve", lambda eng: eng.tensor_reduce(sc[:, 0:1], ptr1[0:1, :], AX.X, ALU.min),
          reads=(bp1, bs), writes=(bs,), tag=f"{tag} mn2")
    em.op("dve", lambda eng: eng.tensor_scalar(sc[:, 0:1], sc[:, 0:1], 1.0, None, ALU.subtract),
          reads=(bs,), writes=(bs,), tag=f"{tag} lo-")
    em.op("dve", lambda eng: eng.tensor_scalar(sc[:, 1:2], sc[:, 1:2], 1.0, None, ALU.add),
          reads=(bs,), writes=(bs,), tag=f"{tag} hi+")
    for it in range(nit):
        em.op("dve", lambda eng: eng.tensor_add(sc[:, 2:3], sc[:, 0:1], sc[:, 1:2]),
              reads=(bs,), writes=(bs,), tag=f"{tag} mid1")
        em.op("dve", lambda eng: eng.tensor_scalar(sc[:, 2:3], sc[:, 2:3], 0.5, None, ALU.mult),
              reads=(bs,), writes=(bs,), tag=f"{tag} mid2")
        em.op("pe", lambda eng: eng.matmul(pbc[:, :], C.ones_row[0:1, 0:128], sc[0:1, 2:3],
                                           start=True, stop=True),
              reads=(bs, C.buf, bp2), writes=(bp2,), tag=f"{tag} bcmid")
        em.op("dve", lambda eng: eng.tensor_copy(colb[:, 2:3], pbc[:, :]),
              reads=(bp2, bcol), writes=(bcol,), tag=f"{tag} midcol")
        em.op("dve", lambda eng: eng.tensor_scalar(scr[:, :], ck_sb[:, :], colb[:, 2:3], None,
                                                   ALU.is_ge, ALU.add, accum_out=colb[:, 0:1]),
              reads=(bck, bcol, bscr), writes=(bscr, bcol), tag=f"{tag} cmp")
        em.op("pe", lambda eng: eng.matmul(pcnt[:, :], C.ones_col[:, :], colb[:, 0:1],
                                           start=True, stop=True),
              reads=(bcol, C.buf, bp3), writes=(bp3,), tag=f"{tag} cnt")
        em.op("dve", (lambda k_count=k_count: lambda eng: eng.tensor_scalar(
            sc[:, 3:4], pcnt[0:1, 0:1], float(k_count), None, ALU.is_gt))(),
            reads=(bp3, bs), writes=(bs,), tag=f"{tag} gt")
        em.op("dve", lambda eng: eng.tensor_sub(sc[:, 4:5], sc[:, 2:3], sc[:, 0:1]),
              reads=(bs,), writes=(bs,), tag=f"{tag} lod")
        em.op("dve", lambda eng: eng.tensor_scalar(sc[:, 4:5], sc[:, 4:5], sc[0:1, 3:4], None, ALU.mult),
              reads=(bs,), writes=(bs,), tag=f"{tag} lom")
        em.op("dve", lambda eng: eng.tensor_add(sc[:, 0:1], sc[:, 0:1], sc[:, 4:5]),
              reads=(bs,), writes=(bs,), tag=f"{tag} lo")
        em.op("dve", lambda eng: eng.tensor_sub(sc[:, 4:5], sc[:, 1:2], sc[:, 2:3]),
              reads=(bs,), writes=(bs,), tag=f"{tag} hid")
        em.op("dve", lambda eng: eng.tensor_scalar(sc[:, 4:5], sc[:, 4:5], sc[0:1, 3:4], None, ALU.mult),
              reads=(bs,), writes=(bs,), tag=f"{tag} him")
        em.op("dve", lambda eng: eng.tensor_add(sc[:, 1:2], sc[:, 2:3], sc[:, 4:5]),
              reads=(bs,), writes=(bs,), tag=f"{tag} hi")
    em.op("pe", lambda eng: eng.matmul(pbc[:, :], C.ones_row[0:1, 0:128], sc[0:1, 1:2],
                                       start=True, stop=True),
          reads=(bs, C.buf, bp2), writes=(bp2,), tag=f"{tag} bchi")
    em.op("dve", lambda eng: eng.tensor_copy(colb[:, 2:3], pbc[:, :]),
          reads=(bp2, bcol), writes=(bcol,), tag=f"{tag} hicol")
    em.op("dve", lambda eng: eng.tensor_scalar(mask_sb[:, :], ck_sb[:, :], colb[:, 2:3], None, ALU.is_ge),
          reads=(bck, bcol, bmask), writes=(bmask,), tag=f"{tag} mask")


def phase_l1ffn_step(nc, em, C, xin_bf, Wsc, xout_bf, xout_f32, b, step,
                     active_dram=None, mask_new_dram=None, pooled_dram=None,
                     Lt=L, tag=None):
    import os
    _off = set(os.environ.get("L1F_OFF", "").split(","))
    """One scan step of L1 ffn for batch b.
    xin_bf: DRAM [D, L] bf16 input activations (f-major).
    xout_bf/xout_f32: DRAM [D, L] outputs (x_new).
    step==1: active = first IS features (tile-aligned); else active_dram [128, FT].
    If mask_new_dram: compute pooled->ck->bisect mask + novelty counts."""
    tag = tag or f"l1s{step}b{b}"
    CH = 512
    NCH = Lt // CH
    do_ck = pooled_dram is not None
    with contextlib.ExitStack() as stk:
        sb = lambda name, shape, dt_: stk.enter_context(nc.sbuf_tensor(f"{tag}_{name}", shape, dt_))
        xcb = sb("xcb", [128, DT * CH], BF16)
        hs = sb("hs", [128, FT * CH], BF16)
        wm = [sb(f"wm{i}", [128, DT * 512], BF16) for i in range(2)]
        wp = [sb(f"wp{i}", [128, FT * 128], BF16) for i in range(2)]
        xnf = sb("xnf", [128, DT * CH], F32)
        xnb = sb("xnb", [128, DT * CH], BF16)
        amask = sb("amask", [128, FT], F32)
        pooled = sb("pooled", [128, 2 * DT], F32)
        ph = [stk.enter_context(nc.psum_tensor(f"{tag}_ph{i}", [128, CH], F32)) for i in range(2)]
        px = [stk.enter_context(nc.psum_tensor(f"{tag}_px{i}", [128, CH], F32)) for i in range(2)]
        bxcb, bhs, bxnf, bxnb = Buf("xcb"), Buf("hs"), Buf("xnf"), Buf("xnb")
        bwm = [Buf("wm0"), Buf("wm1")]
        bwp = [Buf("wp0"), Buf("wp1")]
        bam, bpool = Buf("amask"), Buf("pooled")
        bph = [Buf("ph0"), Buf("ph1")]
        bpx = [Buf("px0"), Buf("px1")]
        if step == 1:
            ftl = list(range(BASE_FT))
        else:
            ftl = list(range(FT))
            em.dma("qsp", lambda eng: eng.dma_start(out=amask[:, :], in_=active_dram),
                   reads=(), writes=(bam,), tag=f"{tag} ldmask")
        em.op("dve", lambda eng: eng.memset(pooled[:, :], 0.0),
              reads=(), writes=(bpool,), tag=f"{tag} pool0")
        for c in range(NCH):
            em.dma("qsp", (lambda c=c: lambda eng: eng.dma_start(
                out=xcb[:, :].rearrange("p (dt t) -> p dt t", dt=DT),
                in_=xin_bf.rearrange("(dt p) l -> p dt l", p=128)[:, :, c * CH:(c + 1) * CH]))(),
                reads=(bxcb,), writes=(bxcb,), tag=f"{tag} ldx")
            for fqi, fq in enumerate(range(0, len(ftl), 4)):
                i = fqi % 2
                f0 = ftl[fq]
                em.dma("qsp", (lambda f0=f0, i=i: lambda eng: eng.dma_start(
                    out=wm[i][:, :].rearrange("p (dt f) -> p dt f", dt=DT),
                    in_=Wsc["WmT"].rearrange("(dt p) f -> p dt f", p=128)[:, :, f0 * 128:(f0 + 4) * 128]))(),
                    reads=(bwm[i],), writes=(bwm[i],), tag=f"{tag} wm")
                for fi in range(4):
                    ft = ftl[fq + fi]
                    j = ft % 2
                    for dt in range(DT):
                        em.op("pe", (lambda i=i, j=j, fi=fi, dt=dt: lambda eng: eng.matmul(
                            ph[j][:, :], wm[i][:, dt * 512 + fi * 128: dt * 512 + (fi + 1) * 128],
                            xcb[:, dt * CH:(dt + 1) * CH], start=(dt == 0), stop=(dt == DT - 1)))(),
                            reads=(bwm[i], bxcb), writes=(bph[j],), tag=f"{tag} hmm")
                    if step == 1:
                        em.op("act", (lambda j=j, ft=ft: lambda eng: eng.activation(
                            hs[:, ft * CH:(ft + 1) * CH], ph[j][:, :], AF.Relu))(),
                            reads=(bph[j], bhs), writes=(bhs,), tag=f"{tag} relu")
                    else:
                        em.op("act", (lambda j=j, ft=ft: lambda eng: eng.activation(
                            hs[:, ft * CH:(ft + 1) * CH], ph[j][:, :], AF.Relu,
                            scale=amask[:, ft:ft + 1]))(),
                            reads=(bph[j], bam, bhs), writes=(bhs,), tag=f"{tag} relum")
            NFT = len(ftl)
            for ct in range(DT if "xphase" not in _off else 0):
                i = ct % 2
                NQ = 4 if NFT >= 4 else 1
                FQ = NFT // NQ
                for q in range(NQ):
                    em.dma("qact", (lambda ct=ct, i=i, q=q, FQ=FQ: lambda eng: eng.dma_start(
                        out=wp[i][:, (ftl[0] + q * FQ) * 128:(ftl[0] + (q + 1) * FQ) * 128].rearrange("p (ft c) -> p ft c", ft=FQ),
                        in_=Wsc["WpT"].rearrange("(ft p) c -> p ft c", p=128)[:, ftl[0] + q * FQ: ftl[0] + (q + 1) * FQ, ct * 128:(ct + 1) * 128]))(),
                        reads=(bwp[i],), writes=(bwp[i],), tag=f"{tag} wp{q}")
                for fi, ft in enumerate(ftl):
                    em.op("pe", (lambda i=i, ft=ft: lambda eng: eng.matmul(
                        px[i][:, :], wp[i][:, ft * 128:(ft + 1) * 128],
                        hs[:, ft * CH:(ft + 1) * CH], start=(ft == ftl[0]), stop=(ft == ftl[-1])))(),
                        reads=(bwp[i], bhs), writes=(bpx[i],), tag=f"{tag} xmm")
                if "evf" not in _off:
                    em.op("dve", (lambda i=i, ct=ct: lambda eng: eng.tensor_copy(
                        xnf[:, ct * CH:(ct + 1) * CH], px[i][:, :]))(),
                        reads=(bpx[i], bxnf), writes=(bxnf,), tag=f"{tag} evf")
                if "evb" not in _off:
                    em.op("act", (lambda i=i, ct=ct: lambda eng: eng.copy(
                        xnb[:, ct * CH:(ct + 1) * CH], xnf[:, ct * CH:(ct + 1) * CH]))(),
                        reads=(bxnf, bxnb), writes=(bxnb,), tag=f"{tag} evb")
                if do_ck and "pool" not in _off:
                    em.op("dve", (lambda ct=ct: lambda eng: eng.tensor_reduce(
                        pooled[:, DT + ct:DT + ct + 1],
                        xnf[:, ct * CH:(ct + 1) * CH], AX.X, ALU.add))(),
                        reads=(bxnf, bpool), writes=(bpool,), tag=f"{tag} poolred")
                    em.op("dve", (lambda ct=ct: lambda eng: eng.tensor_add(
                        pooled[:, ct:ct + 1], pooled[:, ct:ct + 1],
                        pooled[:, DT + ct:DT + ct + 1]))(),
                        reads=(bpool,), writes=(bpool,), tag=f"{tag} pooladd")
            if "stf" not in _off and "xphase" not in _off:
                em.dma("qact", (lambda c=c: lambda eng: eng.dma_start(
                    out=xout_f32.rearrange("(dt p) l -> p dt l", p=128)[:, :, c * CH:(c + 1) * CH],
                    in_=xnf[:, :].rearrange("p (dt t) -> p dt t", dt=DT)))(),
                    reads=(bxnf,), writes=(), tag=f"{tag} stf")
            if "stb" not in _off and "xphase" not in _off:
                em.dma("qact", (lambda c=c: lambda eng: eng.dma_start(
                    out=xout_bf.rearrange("(dt p) l -> p dt l", p=128)[:, :, c * CH:(c + 1) * CH],
                    in_=xnb[:, :].rearrange("p (dt t) -> p dt t", dt=DT)))(),
                    reads=(bxnb,), writes=(), tag=f"{tag} stb")
        if do_ck and "pool" not in _off:
            em.op("dve", lambda eng: eng.tensor_scalar(pooled[:, 0:DT], pooled[:, 0:DT],
                                                       1.0 / Lt, None, ALU.mult),
                  reads=(bpool,), writes=(bpool,), tag=f"{tag} poolavg")
            em.dma("qact", lambda eng: eng.dma_start(out=pooled_dram, in_=pooled[:, 0:DT]),
                   reads=(bpool,), writes=(), tag=f"{tag} poolst")


def phase_l1ffn_ck(nc, em, C, pooled_dram, Wsc, mask_new_dram, hist_in_dram,
                   hist_out_dram, nov_out_dram, tag):
    with contextlib.ExitStack() as stk:
        sb = lambda name, shape, dt_: stk.enter_context(nc.sbuf_tensor(f"{tag}_{name}", shape, dt_))
        pooled = sb("pooled", [128, DT], F32)
        pooled_bf = sb("poolbf", [128, DT], BF16)
        wc1 = sb("wc1", [128, DT * 256], BF16)
        wc2 = sb("wc2", [128, 2 * (FS // 2)], BF16)
        ubf = sb("ubf", [128, 2], BF16)
        cks = sb("cks", [128, FT], F32)
        msk = sb("msk", [128, FT], F32)
        hist = sb("hist", [128, FT], F32)
        scr = sb("scr", [128, FT], F32)
        nov = sb("nov", [1, 1], F32)
        pu = stk.enter_context(nc.psum_tensor(f"{tag}_pu", [128, 2], F32))
        pck = stk.enter_context(nc.psum_tensor(f"{tag}_pck", [128, FT], F32))
        pnv = stk.enter_context(nc.psum_tensor(f"{tag}_pnv", [1, 1], F32))
        bpool, bpbf, bwc1, bwc2 = Buf("pool"), Buf("pbf"), Buf("wc1"), Buf("wc2")
        bu, bck, bmsk, bhist, bscr, bnov = Buf("u"), Buf("ck"), Buf("msk"), Buf("hist"), Buf("scr"), Buf("nov")
        bpu, bpck, bpnv = Buf("pu"), Buf("pck"), Buf("pnv")
        em.dma("qsp", lambda eng: eng.dma_start(out=pooled[:, :], in_=pooled_dram),
               reads=(), writes=(bpool,), tag=f"{tag} ldpool")
        em.op("dve", lambda eng: eng.tensor_copy(pooled_bf[:, :], pooled[:, :]),
              reads=(bpool,), writes=(bpbf,), tag=f"{tag} poolbf")
        em.dma("qsp", lambda eng: eng.dma_start(
            out=wc1[:, :].rearrange("p (ct r) -> p ct r", ct=DT),
            in_=Wsc["Wc1T"].rearrange("(ct p) r -> p ct r", p=128)),
            reads=(), writes=(bwc1,), tag=f"{tag} ldwc1")
        for rt in range(2):
            for ct in range(DT):
                em.op("pe", (lambda rt=rt, ct=ct: lambda eng: eng.matmul(
                    pu[:, rt:rt + 1], wc1[:, ct * 256 + rt * 128: ct * 256 + (rt + 1) * 128],
                    pooled_bf[:, ct:ct + 1],
                    start=(rt == 0 and ct == 0), stop=(rt == 1 and ct == DT - 1)))(),
                    reads=(bwc1, bpbf), writes=(bpu,), tag=f"{tag} umm")
        em.op("act", lambda eng: eng.activation(ubf[:, :], pu[:, :], AF.Relu),
              reads=(bpu,), writes=(bu,), tag=f"{tag} urelu")
        FH = FT // 2
        for half in range(2):
            em.dma("qsp", (lambda half=half: lambda eng: eng.dma_start(
                out=wc2[:, :].rearrange("p (rt f) -> p rt f", rt=2),
                in_=Wsc["Wc2T"].rearrange("(rt p) f -> p rt f", p=128)[:, :, half * (FS // 2):(half + 1) * (FS // 2)]))(),
                reads=(bwc2,), writes=(bwc2,), tag=f"{tag} ldwc2")
            for fi in range(FH):
                ft = half * FH + fi
                for rt in range(2):
                    em.op("pe", (lambda fi=fi, ft=ft, rt=rt, half=half: lambda eng: eng.matmul(
                        pck[:, ft:ft + 1], wc2[:, rt * (FS // 2) + fi * 128: rt * (FS // 2) + (fi + 1) * 128],
                        ubf[:, rt:rt + 1],
                        start=(ft == 0 and rt == 0), stop=(ft == FT - 1 and rt == 1)))(),
                        reads=(bwc2, bu), writes=(bpck,), tag=f"{tag} ckmm")
        em.op("dve", lambda eng: eng.tensor_copy(cks[:, :], pck[:, :]),
              reads=(bpck,), writes=(bck,), tag=f"{tag} ckev")
        emit_bisect(nc, em, C, stk, cks, bck, msk, bmsk, IS, tag)
        em.dma("qact", lambda eng: eng.dma_start(out=mask_new_dram, in_=msk[:, :]),
               reads=(bmsk,), writes=(), tag=f"{tag} stmask")
        if hist_in_dram is None:
            em.op("dve", lambda eng: eng.memset(hist[:, 0:BASE_FT], 1.0),
                  reads=(), writes=(bhist,), tag=f"{tag} hbase1")
            em.op("dve", lambda eng: eng.memset(hist[:, BASE_FT:FT], 0.0),
                  reads=(bhist,), writes=(bhist,), tag=f"{tag} hbase0")
        else:
            em.dma("qsp", lambda eng: eng.dma_start(out=hist[:, :], in_=hist_in_dram),
                   reads=(), writes=(bhist,), tag=f"{tag} ldhist")
        em.op("dve", lambda eng: eng.tensor_max(scr[:, :], hist[:, :], msk[:, :]),
              reads=(bhist, bmsk), writes=(bscr,), tag=f"{tag} comb")
        if hist_out_dram is not None:
            em.dma("qact", lambda eng: eng.dma_start(out=hist_out_dram, in_=scr[:, :]),
                   reads=(bscr,), writes=(), tag=f"{tag} sthist")
        em.op("dve", lambda eng: eng.tensor_sub(hist[:, :], scr[:, :], hist[:, :]),
              reads=(bscr, bhist), writes=(bhist,), tag=f"{tag} delta")
        em.op("dve", lambda eng: eng.tensor_scalar(hist[:, :], hist[:, :], 1.0, None,
                                                   ALU.mult, ALU.add, accum_out=scr[:, 0:1]),
              reads=(bhist, bscr), writes=(bhist, bscr), tag=f"{tag} dcount")
        em.op("pe", lambda eng: eng.matmul(pnv[:, :], C.ones_col[:, :], scr[:, 0:1],
                                           start=True, stop=True),
              reads=(bscr, C.buf, bpnv), writes=(bpnv,), tag=f"{tag} dsum")
        em.op("dve", lambda eng: eng.tensor_copy(nov[:, :], pnv[:, :]),
              reads=(bpnv, bnov), writes=(bnov,), tag=f"{tag} novev")
        em.dma("qact", lambda eng: eng.dma_start(out=nov_out_dram, in_=nov[:, :]),
               reads=(bnov,), writes=(), tag=f"{tag} stnov")


def phase_l1ffn_done(nc, em, nov_all_dram, n_scalars_per_step, d_dram, tag="done"):
    thr = RNOV * 16 * IS
    n = n_scalars_per_step
    with contextlib.ExitStack() as stk:
        novr = stk.enter_context(nc.sbuf_tensor(f"{tag}_n", [1, 2 * n], F32))
        dd = stk.enter_context(nc.sbuf_tensor(f"{tag}_d", [1, 4], F32))
        bn, bd = Buf("novr"), Buf("dd")
        em.dma("qsp", lambda eng: eng.dma_start(
            out=novr[:, :], in_=nov_all_dram.rearrange("a b -> (a b)").unsqueeze(0)),
            reads=(), writes=(bn,), tag=f"{tag} ld")
        em.op("dve", lambda eng: eng.tensor_reduce(dd[:, 0:1], novr[0:1, 0:n], AX.X, ALU.add),
              reads=(bn, bd), writes=(bd,), tag=f"{tag} s1")
        em.op("dve", lambda eng: eng.tensor_reduce(dd[:, 1:2], novr[0:1, n:2 * n], AX.X, ALU.add),
              reads=(bn, bd), writes=(bd,), tag=f"{tag} s2")
        em.op("dve", (lambda thr=thr: lambda eng: eng.tensor_scalar(
            dd[:, 0:2], dd[:, 0:2], float(thr), None, ALU.is_lt))(),
            reads=(bd,), writes=(bd,), tag=f"{tag} lt")
        em.op("dve", lambda eng: eng.tensor_max(dd[:, 1:2], dd[:, 1:2], dd[:, 0:1]),
              reads=(bd,), writes=(bd,), tag=f"{tag} or")
        em.dma("qact", lambda eng: eng.dma_start(out=d_dram, in_=dd[:, 0:2]),
               reads=(bd,), writes=(), tag=f"{tag} st")


def phase_l1ffn_active3(nc, em, C, mask1_dram, mask2_dram, d_dram, active3_dram, tag="act3"):
    with contextlib.ExitStack() as stk:
        m1 = stk.enter_context(nc.sbuf_tensor(f"{tag}_m1", [128, FT], F32))
        m2 = stk.enter_context(nc.sbuf_tensor(f"{tag}_m2", [128, FT], F32))
        dd = stk.enter_context(nc.sbuf_tensor(f"{tag}_d", [1, 2], F32))
        dcol = stk.enter_context(nc.sbuf_tensor(f"{tag}_dc", [128, 1], F32))
        pb = stk.enter_context(nc.psum_tensor(f"{tag}_pb", [128, 1], F32))
        bm1, bm2, bd, bdc, bpb = Buf("m1"), Buf("m2"), Buf("d"), Buf("dc"), Buf("pb")
        em.dma("qsp", lambda eng: eng.dma_start(out=m1[:, :], in_=mask1_dram),
               reads=(), writes=(bm1,), tag=f"{tag} l1")
        em.dma("qsp", lambda eng: eng.dma_start(out=m2[:, :], in_=mask2_dram),
               reads=(), writes=(bm2,), tag=f"{tag} l2")
        em.dma("qsp", lambda eng: eng.dma_start(out=dd[:, :], in_=d_dram),
               reads=(), writes=(bd,), tag=f"{tag} ld")
        em.op("pe", lambda eng: eng.matmul(pb[:, :], C.ones_row[0:1, 0:128], dd[0:1, 0:1],
                                           start=True, stop=True),
              reads=(bd, C.buf, bpb), writes=(bpb,), tag=f"{tag} bc")
        em.op("dve", lambda eng: eng.tensor_copy(dcol[:, :], pb[:, :]),
              reads=(bpb, bdc), writes=(bdc,), tag=f"{tag} dcol")
        em.op("dve", lambda eng: eng.tensor_sub(m1[:, :], m1[:, :], m2[:, :]),
              reads=(bm1, bm2), writes=(bm1,), tag=f"{tag} sub")
        em.op("dve", lambda eng: eng.tensor_scalar(m1[:, :], m1[:, :], dcol[:, :], None, ALU.mult),
              reads=(bm1, bdc), writes=(bm1,), tag=f"{tag} mul")
        em.op("dve", lambda eng: eng.tensor_add(m1[:, :], m1[:, :], m2[:, :]),
              reads=(bm1, bm2), writes=(bm1,), tag=f"{tag} add")
        em.dma("qact", lambda eng: eng.dma_start(out=active3_dram, in_=m1[:, :]),
               reads=(bm1,), writes=(), tag=f"{tag} st")


def phase_l1ffn_blend(nc, em, C, xn1, xn2, xn3, d_dram, xfin_f32, Lt=L, tag="blend"):
    CH = 512
    with contextlib.ExitStack() as stk:
        sb = lambda name, shape, dt_: stk.enter_context(nc.sbuf_tensor(f"{tag}_{name}", shape, dt_))
        t1 = sb("t1", [128, DT * CH], F32)
        t2 = sb("t2", [128, DT * CH], F32)
        t3 = sb("t3", [128, DT * CH], F32)
        dd = sb("d", [1, 2], F32)
        dcol = sb("dc", [128, 2], F32)
        pb = stk.enter_context(nc.psum_tensor(f"{tag}_pb", [128, 2], F32))
        b1, b2, b3, bd, bdc, bpb = Buf("t1"), Buf("t2"), Buf("t3"), Buf("d"), Buf("dc"), Buf("pb")
        em.dma("qsp", lambda eng: eng.dma_start(out=dd[:, :], in_=d_dram),
               reads=(), writes=(bd,), tag=f"{tag} ld")
        em.op("pe", lambda eng: eng.matmul(pb[:, 0:1], C.ones_row[0:1, 0:128], dd[0:1, 0:1],
                                           start=True, stop=False),
              reads=(bd, C.buf, bpb), writes=(bpb,), tag=f"{tag} bc1")
        em.op("pe", lambda eng: eng.matmul(pb[:, 1:2], C.ones_row[0:1, 0:128], dd[0:1, 1:2],
                                           start=False, stop=True),
              reads=(bd, C.buf, bpb), writes=(bpb,), tag=f"{tag} bc2")
        em.op("dve", lambda eng: eng.tensor_copy(dcol[:, :], pb[:, :]),
              reads=(bpb, bdc), writes=(bdc,), tag=f"{tag} dcol")
        for c in range(Lt // CH):
            for (t, bt, src) in ((t1, b1, xn1), (t2, b2, xn2), (t3, b3, xn3)):
                em.dma("qsp", (lambda t=t, src=src, c=c: lambda eng: eng.dma_start(
                    out=t[:, :].rearrange("p (dt w) -> p dt w", dt=DT),
                    in_=src.rearrange("(dt p) l -> p dt l", p=128)[:, :, c * CH:(c + 1) * CH]))(),
                    reads=(bt,), writes=(bt,), tag=f"{tag} ldx")
            em.op("dve", lambda eng: eng.tensor_sub(t1[:, :], t1[:, :], t2[:, :]),
                  reads=(b1, b2), writes=(b1,), tag=f"{tag} s1")
            em.op("dve", lambda eng: eng.tensor_scalar(t1[:, :], t1[:, :], dcol[:, 0:1], None, ALU.mult),
                  reads=(b1, bdc), writes=(b1,), tag=f"{tag} m1")
            em.op("dve", lambda eng: eng.tensor_add(t1[:, :], t1[:, :], t2[:, :]),
                  reads=(b1, b2), writes=(b1,), tag=f"{tag} a1")
            em.op("dve", lambda eng: eng.tensor_sub(t1[:, :], t1[:, :], t3[:, :]),
                  reads=(b1, b3), writes=(b1,), tag=f"{tag} s2")
            em.op("dve", lambda eng: eng.tensor_scalar(t1[:, :], t1[:, :], dcol[:, 1:2], None, ALU.mult),
                  reads=(b1, bdc), writes=(b1,), tag=f"{tag} m2")
            em.op("dve", lambda eng: eng.tensor_add(t1[:, :], t1[:, :], t3[:, :]),
                  reads=(b1, b3), writes=(b1,), tag=f"{tag} a2")
            em.dma("qact", (lambda c=c: lambda eng: eng.dma_start(
                out=xfin_f32.rearrange("(dt p) l -> p dt l", p=128)[:, :, c * CH:(c + 1) * CH],
                in_=t1[:, :].rearrange("p (dt w) -> p dt w", dt=DT)))(),
                reads=(b1,), writes=(), tag=f"{tag} st")



def l0_generalize_bisect():
    pass


def emit_bisect2(nc, em, C, stk, vals, bvals, mask_sb, bmask, P, W, k_count, tag, nit=45):
    sc = stk.enter_context(nc.sbuf_tensor(f"{tag}_bs", [1, 8], F32))
    colb = stk.enter_context(nc.sbuf_tensor(f"{tag}_bc", [128, 3], F32))
    scr = stk.enter_context(nc.sbuf_tensor(f"{tag}_bscr", [P, W], F32))
    ptr1 = stk.enter_context(nc.psum_tensor(f"{tag}_bp1", [1, 128], F32))
    pbc = stk.enter_context(nc.psum_tensor(f"{tag}_bp2", [128, 1], F32))
    pcnt = stk.enter_context(nc.psum_tensor(f"{tag}_bp3", [1, 1], F32))
    bs, bcol, bscr = Buf("bs"), Buf("bcol"), Buf("bscr")
    bp1, bp2, bp3 = Buf("bp1"), Buf("bp2"), Buf("bp3")
    em.op("dve", lambda eng: eng.memset(colb[:, :], 0.0),
          reads=(bcol,), writes=(bcol,), tag=f"{tag} colz")
    em.op("dve", lambda eng: eng.tensor_reduce(colb[0:P, 0:1], vals[:, :], AX.X, ALU.max),
          reads=(bvals, bcol), writes=(bcol,), tag=f"{tag} mx1")
    em.op("dve", lambda eng: eng.tensor_reduce(colb[0:P, 1:2], vals[:, :], AX.X, ALU.min),
          reads=(bvals, bcol), writes=(bcol,), tag=f"{tag} mn1")
    if P < 128:
        em.op("dve", (lambda: lambda eng: eng.tensor_copy(colb[P:128, 0:2], colb[0:1, 0:2].partition_broadcast(128 - P)))() if False else
              (lambda: lambda eng: eng.memset(colb[P:128, 0:1], -3e38))(),
              reads=(bcol,), writes=(bcol,), tag=f"{tag} padmx")
        em.op("dve", lambda eng: eng.memset(colb[P:128, 1:2], 3e38),
              reads=(bcol,), writes=(bcol,), tag=f"{tag} padmn")
    em.op("pe", lambda eng: eng.transpose(ptr1[0:1, :], colb[:, 0:1], C.ident32[:, :]),
          reads=(bcol, C.buf, bp1), writes=(bp1,), tag=f"{tag} tr1")
    em.op("dve", lambda eng: eng.tensor_reduce(sc[:, 1:2], ptr1[0:1, :], AX.X, ALU.max),
          reads=(bp1, bs), writes=(bs,), tag=f"{tag} mx2")
    em.op("pe", lambda eng: eng.transpose(ptr1[0:1, :], colb[:, 1:2], C.ident32[:, :]),
          reads=(bcol, C.buf, bp1, bs), writes=(bp1,), tag=f"{tag} tr2")
    em.op("dve", lambda eng: eng.tensor_reduce(sc[:, 0:1], ptr1[0:1, :], AX.X, ALU.min),
          reads=(bp1, bs), writes=(bs,), tag=f"{tag} mn2")
    em.op("dve", lambda eng: eng.tensor_scalar(sc[:, 0:1], sc[:, 0:1], 1.0, None, ALU.subtract),
          reads=(bs,), writes=(bs,), tag=f"{tag} lo-")
    em.op("dve", lambda eng: eng.tensor_scalar(sc[:, 1:2], sc[:, 1:2], 1.0, None, ALU.add),
          reads=(bs,), writes=(bs,), tag=f"{tag} hi+")
    for it in range(nit):
        em.op("dve", lambda eng: eng.tensor_add(sc[:, 2:3], sc[:, 0:1], sc[:, 1:2]),
              reads=(bs,), writes=(bs,), tag=f"{tag} mid1")
        em.op("dve", lambda eng: eng.tensor_scalar(sc[:, 2:3], sc[:, 2:3], 0.5, None, ALU.mult),
              reads=(bs,), writes=(bs,), tag=f"{tag} mid2")
        em.op("pe", lambda eng: eng.matmul(pbc[:, :], C.ones_row[0:1, 0:128], sc[0:1, 2:3],
                                           start=True, stop=True),
              reads=(bs, C.buf, bp2), writes=(bp2,), tag=f"{tag} bcmid")
        em.op("dve", lambda eng: eng.tensor_copy(colb[:, 2:3], pbc[:, :]),
              reads=(bp2, bcol), writes=(bcol,), tag=f"{tag} midcol")
        em.op("dve", lambda eng: eng.tensor_scalar(scr[:, :], vals[:, :], colb[0:P, 2:3], None,
                                                   ALU.is_ge, ALU.add, accum_out=colb[0:P, 0:1]),
              reads=(bvals, bcol, bscr), writes=(bscr, bcol), tag=f"{tag} cmp")
        if P < 128:
            em.op("dve", lambda eng: eng.memset(colb[P:128, 0:1], 0.0),
                  reads=(bcol,), writes=(bcol,), tag=f"{tag} padc")
        em.op("pe", lambda eng: eng.matmul(pcnt[:, :], C.ones_col[:, :], colb[:, 0:1],
                                           start=True, stop=True),
              reads=(bcol, C.buf, bp3), writes=(bp3,), tag=f"{tag} cnt")
        em.op("dve", (lambda k_count=k_count: lambda eng: eng.tensor_scalar(
            sc[:, 3:4], pcnt[0:1, 0:1], float(k_count), None, ALU.is_gt))(),
            reads=(bp3, bs), writes=(bs,), tag=f"{tag} gt")
        em.op("dve", lambda eng: eng.tensor_sub(sc[:, 4:5], sc[:, 2:3], sc[:, 0:1]),
              reads=(bs,), writes=(bs,), tag=f"{tag} lod")
        em.op("dve", lambda eng: eng.tensor_scalar(sc[:, 4:5], sc[:, 4:5], sc[0:1, 3:4], None, ALU.mult),
              reads=(bs,), writes=(bs,), tag=f"{tag} lom")
        em.op("dve", lambda eng: eng.tensor_add(sc[:, 0:1], sc[:, 0:1], sc[:, 4:5]),
              reads=(bs,), writes=(bs,), tag=f"{tag} lo")
        em.op("dve", lambda eng: eng.tensor_sub(sc[:, 4:5], sc[:, 1:2], sc[:, 2:3]),
              reads=(bs,), writes=(bs,), tag=f"{tag} hid")
        em.op("dve", lambda eng: eng.tensor_scalar(sc[:, 4:5], sc[:, 4:5], sc[0:1, 3:4], None, ALU.mult),
              reads=(bs,), writes=(bs,), tag=f"{tag} him")
        em.op("dve", lambda eng: eng.tensor_add(sc[:, 1:2], sc[:, 2:3], sc[:, 4:5]),
              reads=(bs,), writes=(bs,), tag=f"{tag} hi")
    em.op("pe", lambda eng: eng.matmul(pbc[:, :], C.ones_row[0:1, 0:128], sc[0:1, 1:2],
                                       start=True, stop=True),
          reads=(bs, C.buf, bp2), writes=(bp2,), tag=f"{tag} bchi")
    em.op("dve", lambda eng: eng.tensor_copy(colb[:, 2:3], pbc[:, :]),
          reads=(bp2, bcol), writes=(bcol,), tag=f"{tag} hicol")
    em.op("dve", lambda eng: eng.tensor_scalar(mask_sb[:, :], vals[:, :], colb[0:P, 2:3], None, ALU.is_ge),
          reads=(bvals, bcol, bmask), writes=(bmask,), tag=f"{tag} mask")


def phase_l0attn_A(nc, em, C, xs_f32, W, sample_idx, Mrows, nb=NB, tag="a0A"):
    NP = NH // 2
    with contextlib.ExitStack() as stk:
        sb = lambda name, shape, dt_: stk.enter_context(nc.sbuf_tensor(f"{tag}_{name}", shape, dt_))
        xch = sb("xch", [128, DT * 512], F32)
        xsamp = sb("xs", [128, DT * U], F32)
        wq = sb("wq", [128, DT * 128], F32)
        wk = sb("wk", [128, DT * 128], F32)
        bqk = sb("bqk", [1, 2 * D], F32)
        qfm = sb("qfm", [128, L], F32)
        ks = sb("ks", [128, U], F32)
        msb = sb("msb", [128, 32], F32)
        mcol = sb("mcol", [128, 2], F32)
        pq = [stk.enter_context(nc.psum_tensor(f"{tag}_pq{i}", [128, 512], F32)) for i in range(2)]
        pm = [stk.enter_context(nc.psum_tensor(f"{tag}_pm{i}", [128, 45], F32)) for i in range(2)]
        bxc, bxs, bwq, bwk, bbqk = Buf("xch"), Buf("xs"), Buf("wq"), Buf("wk"), Buf("bqk")
        bqfm, bks, bmsb, bmcol = Buf("qfm"), Buf("ks"), Buf("msb"), Buf("mcol")
        bpq = [Buf("pq0"), Buf("pq1")]
        bpm = [Buf("pm0"), Buf("pm1")]
        em.dma("qsp", lambda eng: eng.dma_start(out=bqk[:, 0:D], in_=W["bq"].unsqueeze(0)),
               reads=(), writes=(bbqk,), tag=f"{tag} bq")
        em.dma("qsp", lambda eng: eng.dma_start(out=bqk[:, D:2 * D], in_=W["bk"].unsqueeze(0)),
               reads=(), writes=(bbqk,), tag=f"{tag} bk")
        samp_by_chunk = {}
        for j, idx in enumerate(sample_idx):
            samp_by_chunk.setdefault(int(idx) // 512, []).append((j, int(idx)))
        for b in range(nb):
            for c in range(L // 512):
                em.dma("qsp", (lambda b=b, c=c: lambda eng: eng.dma_start(
                    out=xch[:, :].rearrange("p (dt w) -> p dt w", dt=DT),
                    in_=xs_f32[b].rearrange("(dt p) l -> p dt l", p=128)[:, :, c * 512:(c + 1) * 512]))(),
                    reads=(bxc,), writes=(bxc,), tag=f"{tag} gldx")
                for (j, idx) in samp_by_chunk.get(c, []):
                    em.op("dve", (lambda j=j, idx=idx, c=c: lambda eng: eng.tensor_copy(
                        xsamp[:, :].rearrange("p (dt u) -> p dt u", dt=DT)[:, :, j:j + 1],
                        xch[:, :].rearrange("p (dt w) -> p dt w", dt=DT)[:, :, idx - c * 512: idx - c * 512 + 1]))(),
                        reads=(bxc, bxs), writes=(bxs,), tag=f"{tag} xs{j}")
            for hp in range(NP):
                em.dma("qsp", (lambda hp=hp: lambda eng: eng.dma_start(
                    out=wq[:, :].rearrange("p (dt c) -> p dt c", dt=DT),
                    in_=W["Wq"].rearrange("(dt p) e -> p dt e", p=128)[:, :, hp * 128:(hp + 1) * 128]))(),
                    reads=(bwq,), writes=(bwq,), tag=f"{tag} wq")
                em.dma("qsp", (lambda hp=hp: lambda eng: eng.dma_start(
                    out=wk[:, :].rearrange("p (dt c) -> p dt c", dt=DT),
                    in_=W["Wk"].rearrange("(dt p) e -> p dt e", p=128)[:, :, hp * 128:(hp + 1) * 128]))(),
                    reads=(bwk,), writes=(bwk,), tag=f"{tag} wk")
                for c in range(L // 512):
                    i = c % 2
                    em.dma("qsp", (lambda b=b, c=c: lambda eng: eng.dma_start(
                        out=xch[:, :].rearrange("p (dt w) -> p dt w", dt=DT),
                        in_=xs_f32[b].rearrange("(dt p) l -> p dt l", p=128)[:, :, c * 512:(c + 1) * 512]))(),
                        reads=(bxc,), writes=(bxc,), tag=f"{tag} ldx")
                    for dt in range(DT):
                        em.op("pe", (lambda i=i, dt=dt: lambda eng: eng.matmul(
                            pq[i][:, :], wq[:, dt * 128:(dt + 1) * 128],
                            xch[:, dt * 512:(dt + 1) * 512],
                            start=(dt == 0), stop=False))(),
                            reads=(bwq, bxc), writes=(bpq[i],), tag=f"{tag} qmm")
                    em.op("pe", (lambda i=i, hp=hp: lambda eng: eng.matmul(
                        pq[i][:, :], bqk[0:1, hp * 128:(hp + 1) * 128], C.ones_row[0:1, :],
                        start=False, stop=True))(),
                        reads=(bbqk, C.buf), writes=(bpq[i],), tag=f"{tag} qb")
                    em.op("act", (lambda i=i, c=c: lambda eng: eng.copy(
                        qfm[:, c * 512:(c + 1) * 512], pq[i][:, :]))(),
                        reads=(bpq[i], bqfm), writes=(bqfm,), tag=f"{tag} qev")
                for dt in range(DT):
                    em.op("pe", (lambda dt=dt: lambda eng: eng.matmul(
                        pm[0][:, :], wk[:, dt * 128:(dt + 1) * 128],
                        xsamp[:, dt * U:(dt + 1) * U], start=(dt == 0), stop=False))(),
                        reads=(bwk, bxs), writes=(bpm[0],), tag=f"{tag} kmm")
                em.op("pe", (lambda hp=hp: lambda eng: eng.matmul(
                    pm[0][:, :], bqk[0:1, D + hp * 128: D + (hp + 1) * 128], C.ones_row[0:1, 0:U],
                    start=False, stop=True))(),
                    reads=(bbqk, C.buf), writes=(bpm[0],), tag=f"{tag} kb")
                em.op("act", lambda eng: eng.copy(ks[:, :], pm[0][:, :]),
                      reads=(bpm[0], bks), writes=(bks,), tag=f"{tag} kev")
                for h in range(2):
                    for lt in range(L // 128):
                        i = lt % 2
                        em.op("pe", (lambda i=i, h=h, lt=lt: lambda eng: eng.matmul(
                            pm[i][:, :], qfm[h * 64:(h + 1) * 64, lt * 128:(lt + 1) * 128],
                            ks[h * 64:(h + 1) * 64, :], start=True, stop=True))(),
                            reads=(bqfm, bks), writes=(bpm[i],), tag=f"{tag} qk")
                        em.op("dve", (lambda i=i, lt=lt: lambda eng: eng.tensor_reduce(
                            mcol[:, 0:1], pm[i][:, :], AX.X, ALU.max))(),
                            reads=(bpm[i], bmcol), writes=(bmcol,), tag=f"{tag} mmax")
                        em.op("dve", (lambda i=i, lt=lt: lambda eng: eng.tensor_reduce(
                            mcol[:, 1:2], pm[i][:, :], AX.X, ALU.add))(),
                            reads=(bpm[i], bmcol), writes=(bmcol,), tag=f"{tag} msum")
                        em.op("dve", (lambda lt=lt: lambda eng: eng.scalar_tensor_tensor(
                            msb[:, lt:lt + 1], mcol[:, 1:2], -1.0 / U, mcol[:, 0:1],
                            ALU.mult, ALU.add))(),
                            reads=(bmcol, bmsb), writes=(bmsb,), tag=f"{tag} mval")
                    bh = b * NH + hp * 2 + h
                    em.dma("qact", (lambda bh=bh: lambda eng: eng.dma_start(
                        out=Mrows[bh].rearrange("(p lt) -> p lt", p=128), in_=msb[:, :]))(),
                        reads=(bmsb,), writes=(), tag=f"{tag} mst")


def phase_l0attn_sel(nc, em, C, Mrows, RMrows, nrows, tag="a0S"):
    with contextlib.ExitStack() as stk:
        sb = lambda name, shape, dt_: stk.enter_context(nc.sbuf_tensor(f"{tag}_{name}", shape, dt_))
        mr = sb("mr", [nrows, L], F32)
        ones = sb("ones", [nrows, L], F32)
        msk = sb("msk", [nrows, L], F32)
        rk = sb("rk", [nrows, L], F32)
        his = sb("his", [128, 1], F32)
        pbc = stk.enter_context(nc.psum_tensor(f"{tag}_pbc", [128, 1], F32))
        bmr, bones, bmsk, brk, bhis, bpbc = Buf("mr"), Buf("ones"), Buf("msk"), Buf("rk"), Buf("his"), Buf("pbc")
        em.dma("qsp", lambda eng: eng.dma_start(out=mr[:, :], in_=Mrows),
               reads=(), writes=(bmr,), tag=f"{tag} ld")
        em.op("dve", lambda eng: eng.memset(ones[:, :], 1.0),
              reads=(), writes=(bones,), tag=f"{tag} ones")
        emit_bisect_rows(nc, em, C, stk, mr, bmr, msk, bmsk, his, bhis, pbc, bpbc, nrows, tag)
        em.op("dve", lambda eng: eng.tensor_tensor_scan(rk[:, :], ones[:, :], msk[:, :], 0.0,
                                                        ALU.mult, ALU.add),
              reads=(bones, bmsk, brk), writes=(brk,), tag=f"{tag} cumsum")
        em.op("dve", lambda eng: eng.tensor_mul(rk[:, :], rk[:, :], msk[:, :]),
              reads=(brk, bmsk), writes=(brk,), tag=f"{tag} rm")
        em.dma("qact", lambda eng: eng.dma_start(out=RMrows, in_=rk[:, :]),
               reads=(brk,), writes=(), tag=f"{tag} st")


def emit_bisect_rows(nc, em, C, stk, vals, bvals, mask_sb, bmask, his, bhis, pbc, bpbc, P, tag):
    cols = stk.enter_context(nc.sbuf_tensor(f"{tag}_cols", [128, 6], F32))
    scr = stk.enter_context(nc.sbuf_tensor(f"{tag}_scr", [P, L], F32))
    bcols, bscr = Buf("cols"), Buf("scr")
    em.op("dve", lambda eng: eng.tensor_reduce(cols[0:P, 1:2], vals[:, :], AX.X, ALU.max),
          reads=(bvals, bcols), writes=(bcols,), tag=f"{tag} mx")
    em.op("dve", lambda eng: eng.tensor_reduce(cols[0:P, 0:1], vals[:, :], AX.X, ALU.min),
          reads=(bvals, bcols), writes=(bcols,), tag=f"{tag} mn")
    em.op("dve", lambda eng: eng.tensor_scalar(cols[0:P, 0:1], cols[0:P, 0:1], 1.0, None, ALU.subtract),
          reads=(bcols,), writes=(bcols,), tag=f"{tag} lo-")
    em.op("dve", lambda eng: eng.tensor_scalar(cols[0:P, 1:2], cols[0:P, 1:2], 1.0, None, ALU.add),
          reads=(bcols,), writes=(bcols,), tag=f"{tag} hi+")
    for it in range(42):
        em.op("dve", lambda eng: eng.tensor_add(cols[0:P, 2:3], cols[0:P, 0:1], cols[0:P, 1:2]),
              reads=(bcols,), writes=(bcols,), tag=f"{tag} mid")
        em.op("dve", lambda eng: eng.tensor_scalar(cols[0:P, 2:3], cols[0:P, 2:3], 0.5, None, ALU.mult),
              reads=(bcols,), writes=(bcols,), tag=f"{tag} mid2")
        em.op("dve", lambda eng: eng.tensor_scalar(scr[:, :], vals[:, :], cols[0:P, 2:3], None,
                                                   ALU.is_ge, ALU.add, accum_out=cols[0:P, 3:4]),
              reads=(bvals, bcols, bscr), writes=(bscr, bcols), tag=f"{tag} cnt")
        em.op("dve", lambda eng: eng.tensor_scalar(cols[0:P, 4:5], cols[0:P, 3:4], float(U), None, ALU.is_gt),
              reads=(bcols,), writes=(bcols,), tag=f"{tag} gt")
        em.op("dve", lambda eng: eng.tensor_sub(cols[0:P, 5:6], cols[0:P, 2:3], cols[0:P, 0:1]),
              reads=(bcols,), writes=(bcols,), tag=f"{tag} lod")
        em.op("dve", lambda eng: eng.tensor_tensor(cols[0:P, 5:6], cols[0:P, 5:6], cols[0:P, 4:5], ALU.mult),
              reads=(bcols,), writes=(bcols,), tag=f"{tag} lom")
        em.op("dve", lambda eng: eng.tensor_add(cols[0:P, 0:1], cols[0:P, 0:1], cols[0:P, 5:6]),
              reads=(bcols,), writes=(bcols,), tag=f"{tag} lo")
        em.op("dve", lambda eng: eng.tensor_sub(cols[0:P, 5:6], cols[0:P, 1:2], cols[0:P, 2:3]),
              reads=(bcols,), writes=(bcols,), tag=f"{tag} hid")
        em.op("dve", lambda eng: eng.tensor_tensor(cols[0:P, 5:6], cols[0:P, 5:6], cols[0:P, 4:5], ALU.mult),
              reads=(bcols,), writes=(bcols,), tag=f"{tag} him")
        em.op("dve", lambda eng: eng.tensor_add(cols[0:P, 1:2], cols[0:P, 2:3], cols[0:P, 5:6]),
              reads=(bcols,), writes=(bcols,), tag=f"{tag} hi")
    em.op("dve", lambda eng: eng.tensor_scalar(mask_sb[:, :], vals[:, :], cols[0:P, 1:2], None, ALU.is_ge),
          reads=(bvals, bcols, bmask), writes=(bmask,), tag=f"{tag} mask")


def phase_l0attn_B(nc, em, C, xs_f32, W, RMrows, outs_f32, nb=NB, tag="a0B"):
    NP = NH // 2
    GP = 2
    OA = [nc.dram_tensor(f"{tag}_oa{b}", [D, L], BF16,
                         kind="ExternalOutput" if DBG_STAGES else "Internal").ap() for b in range(nb)]
    bOA = [Buf(f"{tag}_oaD{b}") for b in range(nb)]
    with contextlib.ExitStack() as stk:
        sb = lambda name, shape, dt_: stk.enter_context(nc.sbuf_tensor(f"{tag}_{name}", shape, dt_))
        xb = sb("xb", [128, DT * L], BF16)
        qtm = sb("qtm", [128, 32 * GP * 128], BF16)
        vtm = sb("vtm", [128, 32 * GP * 128], BF16)
        wpair = sb("wpair", [128, 3 * DT * 128], BF16)
        brow = sb("brow", [1, 4 * D], BF16)
        qfm = sb("qfm", [128, L], BF16)
        kfm = sb("kfm", [128, L], BF16)
        rmc = sb("rmc", [128, 32], F32)
        stt_ = sb("stt", [128, 32 * 45], BF16)
        ssb = sb("ssb", [45, L], BF16)
        scosb = sb("sco", [45, L], F32)
        attn = sb("attn", [45, L], BF16)
        attnT = sb("attnT", [128, 32 * 45], BF16)
        qred = sb("qred", [128, 45], BF16)
        vmr = sb("vmr", [1, 64], F32)
        vmrb = sb("vmrb", [1, 64], BF16)
        dsc = sb("dsc", [45, 64], F32)
        delta = sb("delta", [45, 64], BF16)
        rcol = sb("rcol", [45, 2], F32)
        oastage = sb("oast", [64, 512], BF16)
        pA = stk.enter_context(nc.psum_tensor(f"{tag}_pA", [128, 512], F32))
        pB = stk.enter_context(nc.psum_tensor(f"{tag}_pB", [128, 512], F32))
        pE = stk.enter_context(nc.psum_tensor(f"{tag}_pE", [128, 512], F32))
        pT = stk.enter_context(nc.psum_tensor(f"{tag}_pT", [128, 128], BF16))
        B_ = lambda n: Buf(f"{tag}{n}")
        bxb, bqtm, bvtm, bwp, bbr = B_("xb"), B_("qtm"), B_("vtm"), B_("wp"), B_("br")
        bqfm, bkfm, brm = B_("qfm"), B_("kfm"), B_("rm")
        bst, bss, bsco, batn, batnT = B_("st"), B_("ss"), B_("sco"), B_("atn"), B_("atnT")
        bqred, bvm, bdsc, bdelta, brc, boa = B_("qred"), B_("vm"), B_("dsc"), B_("delta"), B_("rc"), B_("oa")
        bpA, bpB, bpE, bpT = B_("pA"), B_("pB"), B_("pE"), B_("pT")
        for j, nm in enumerate(["bq", "bk", "bv", "bo"]):
            em.dma("qpool", (lambda j=j, nm=nm: lambda eng: eng.dma_start(
                out=brow[:, j * D:(j + 1) * D], in_=W[nm].unsqueeze(0)))(),
                reads=(), writes=(bbr,), tag=f"{tag} {nm}")
        for b in range(nb):
            for c in range(L // 512):
                em.dma("qpool", (lambda b=b, c=c: lambda eng: eng.dma_start(
                    out=xb[:, :].rearrange("p (dt l) -> p dt l", dt=DT)[:, :, c * 512:(c + 1) * 512],
                    in_=xs_f32[b].rearrange("(dt p) l -> p dt l", p=128)[:, :, c * 512:(c + 1) * 512]))(),
                    reads=(bxb,), writes=(bxb,), tag=f"{tag} ldx")
            for grp in range(NP // GP):
                GW = GP * 128
                for wsel, dst, bdst in (("Wq", qtm, bqtm), ("Wv", vtm, bvtm)):
                    em.dma("qpool", (lambda wsel=wsel, grp=grp, GW=GW: lambda eng: eng.dma_start(
                        out=wpair[:, 0:DT * GW].rearrange("p (dt c) -> p dt c", dt=DT),
                        in_=W[wsel].rearrange("(dt p) e -> p dt e", p=128)[:, :, grp * GW:(grp + 1) * GW]))(),
                        reads=(bwp,), writes=(bwp,), tag=f"{tag} w{wsel}")
                    for lt in range(L // 128):
                        i = lt % 2
                        ps = pA if i == 0 else pB
                        bps = bpA if i == 0 else bpB
                        for dt in range(DT):
                            em.op("pe", (lambda ps=ps, lt=lt, dt=dt, GW=GW: lambda eng: eng.matmul(
                                ps[:, 0:GW], xb[:, dt * L + lt * 128: dt * L + (lt + 1) * 128],
                                wpair[:, dt * GW:(dt + 1) * GW], start=(dt == 0), stop=False))(),
                                reads=(bxb, bwp), writes=(bps,), tag=f"{tag} tmm")
                        em.op("pe", (lambda ps=ps, wsel=wsel, grp=grp, GW=GW: lambda eng: eng.matmul(
                            ps[:, 0:GW], C.ones16[0:1, 0:128],
                            brow[0:1, (0 if wsel == "Wq" else 2) * D + grp * GW:
                                 (0 if wsel == "Wq" else 2) * D + (grp + 1) * GW],
                            start=False, stop=True))(),
                            reads=(bbr, C.buf), writes=(bps,), tag=f"{tag} tmb")
                        em.op("act", (lambda ps=ps, dst=dst, lt=lt, GW=GW: lambda eng: eng.copy(
                            dst[:, lt * GW:(lt + 1) * GW], ps[:, 0:GW]))(),
                            reads=(bps, bdst), writes=(bdst,), tag=f"{tag} tmev")
                for pp in range(GP):
                    hp = grp * GP + pp
                    em.dma("qpool", (lambda hp=hp: lambda eng: eng.dma_start(
                        out=wpair[:, 0:DT * 128].rearrange("p (dt c) -> p dt c", dt=DT),
                        in_=W["Wq"].rearrange("(dt p) e -> p dt e", p=128)[:, :, hp * 128:(hp + 1) * 128]))(),
                        reads=(bwp,), writes=(bwp,), tag=f"{tag} pwq")
                    em.dma("qpool", (lambda hp=hp: lambda eng: eng.dma_start(
                        out=wpair[:, DT * 128:2 * DT * 128].rearrange("p (dt c) -> p dt c", dt=DT),
                        in_=W["Wk"].rearrange("(dt p) e -> p dt e", p=128)[:, :, hp * 128:(hp + 1) * 128]))(),
                        reads=(bwp,), writes=(bwp,), tag=f"{tag} pwk")
                    for (area, wb, dst, bdst) in ((0, 0, qfm, bqfm), (1, 1, kfm, bkfm)):
                        for c in range(L // 512):
                            i = c % 2
                            ps = pA if i == 0 else pB
                            bps = bpA if i == 0 else bpB
                            for dt in range(DT):
                                em.op("pe", (lambda ps=ps, area=area, c=c, dt=dt: lambda eng: eng.matmul(
                                    ps[:, :], wpair[:, area * DT * 128 + dt * 128: area * DT * 128 + (dt + 1) * 128],
                                    xb[:, dt * L + c * 512: dt * L + (c + 1) * 512],
                                    start=(dt == 0), stop=False))(),
                                    reads=(bwp, bxb), writes=(bps,), tag=f"{tag} fmm")
                            em.op("pe", (lambda ps=ps, wb=wb, hp=hp: lambda eng: eng.matmul(
                                ps[:, :], brow[0:1, wb * D + hp * 128: wb * D + (hp + 1) * 128],
                                C.ones16[0:1, :], start=False, stop=True))(),
                                reads=(bbr, C.buf), writes=(bps,), tag=f"{tag} fmb")
                            em.op("act", (lambda ps=ps, dst=dst, c=c: lambda eng: eng.copy(
                                dst[:, c * 512:(c + 1) * 512], ps[:, :]))(),
                                reads=(bps, bdst), writes=(bdst,), tag=f"{tag} fmev")
                    for h in range(2):
                        hh = hp * 2 + h
                        bh = b * NH + hh
                        hc = pp * 128 + h * 64
                        em.dma("qsp", (lambda bh=bh: lambda eng: eng.dma_start(
                            out=rmc[:, :], in_=RMrows[bh].rearrange("(p lt) -> p lt", p=128)))(),
                            reads=(brm,), writes=(brm,), tag=f"{tag} rmc")
                        for lt in range(L // 128):
                            em.op("dve", (lambda lt=lt: lambda eng: eng.tensor_scalar(
                                stt_[:, lt * 45:(lt + 1) * 45], C.iotarow45[:, :],
                                rmc[:, lt:lt + 1], None, ALU.is_equal))(),
                                reads=(brm, C.buf, bst), writes=(bst,), tag=f"{tag} stb")
                        for lt in range(L // 128):
                            em.op("pe", (lambda lt=lt: lambda eng: eng.transpose(
                                pT[0:45, :], stt_[:, lt * 45:(lt + 1) * 45], C.ident16[:, :]))(),
                                reads=(bst, C.buf, bpT), writes=(bpT,), tag=f"{tag} str")
                            em.op("dve", (lambda lt=lt: lambda eng: eng.tensor_copy(
                                ssb[:, lt * 128:(lt + 1) * 128], pT[0:45, :]))(),
                                reads=(bpT, bss), writes=(bss,), tag=f"{tag} strev")
                        for lt in range(L // 128):
                            em.op("pe", (lambda lt=lt, hc=hc: lambda eng: eng.matmul(
                                pE[0:64, 0:45], qtm[:, lt * (GP * 128) + hc: lt * (GP * 128) + hc + 64],
                                stt_[:, lt * 45:(lt + 1) * 45],
                                start=(lt == 0), stop=(lt == 31)))(),
                                reads=(bqtm, bst, bpE), writes=(bpE,), tag=f"{tag} qred")
                        em.op("act", (lambda h=h: lambda eng: eng.copy(
                            qred[h * 64:(h + 1) * 64, :], pE[0:64, 0:45]))(),
                            reads=(bpE, bqred), writes=(bqred,), tag=f"{tag} qrev")
                        for c in range(L // 512):
                            i = c % 2
                            ps = pA if i == 0 else pB
                            bps = bpA if i == 0 else bpB
                            em.op("pe", (lambda ps=ps, h=h, c=c: lambda eng: eng.matmul(
                                ps[0:45, :], qred[h * 64:(h + 1) * 64, :],
                                kfm[h * 64:(h + 1) * 64, c * 512:(c + 1) * 512],
                                start=True, stop=True))(),
                                reads=(bqred, bkfm, bps), writes=(bps,), tag=f"{tag} sc")
                            em.op("dve", (lambda ps=ps, c=c: lambda eng: eng.tensor_copy(
                                scosb[:, c * 512:(c + 1) * 512], ps[0:45, :]))(),
                                reads=(bps, bsco), writes=(bsco,), tag=f"{tag} scev")
                        em.op("dve", lambda eng: eng.tensor_reduce(rcol[:, 0:1], scosb[:, :], AX.X, ALU.max),
                              reads=(bsco, brc), writes=(brc,), tag=f"{tag} smax")
                        em.op("dve", lambda eng: eng.tensor_scalar(rcol[:, 0:1], rcol[:, 0:1],
                                                                   -0.125, None, ALU.mult),
                              reads=(brc,), writes=(brc,), tag=f"{tag} snm")
                        em.op("act", lambda eng: eng.activation(attn[:, :], scosb[:, :], AF.Exp,
                                                                bias=rcol[:, 0:1], scale=0.125,
                                                                accum_out=rcol[:, 1:2]),
                              reads=(bsco, brc), writes=(batn, brc), tag=f"{tag} sexp")
                        em.op("dve", lambda eng: eng.reciprocal(rcol[:, 1:2], rcol[:, 1:2]),
                              reads=(brc,), writes=(brc,), tag=f"{tag} srec")
                        for lt in range(L // 128):
                            em.op("pe", (lambda lt=lt: lambda eng: eng.transpose(
                                pT[:, 0:45], attn[:, lt * 128:(lt + 1) * 128], C.ident16[0:45, 0:45]))(),
                                reads=(batn, C.buf, bpT), writes=(bpT,), tag=f"{tag} atr")
                            em.op("dve", (lambda lt=lt: lambda eng: eng.tensor_copy(
                                attnT[:, lt * 45:(lt + 1) * 45], pT[:, 0:45]))(),
                                reads=(bpT, batnT), writes=(batnT,), tag=f"{tag} atrev")
                        for lt in range(L // 128):
                            em.op("pe", (lambda lt=lt, hc=hc: lambda eng: eng.matmul(
                                pE[0:1, 0:64], C.ones_col16[:, :],
                                vtm[:, lt * (GP * 128) + hc: lt * (GP * 128) + hc + 64],
                                start=(lt == 0), stop=(lt == 31)))(),
                                reads=(bvtm, C.buf, bpE), writes=(bpE,), tag=f"{tag} vm")
                        em.op("dve", lambda eng: eng.tensor_scalar(vmr[:, :], pE[0:1, 0:64],
                                                                   1.0 / L, None, ALU.mult),
                              reads=(bpE, bvm), writes=(bvm,), tag=f"{tag} vmev")
                        em.op("act", lambda eng: eng.copy(vmrb[:, :], vmr[:, :]),
                              reads=(bvm,), writes=(bvm,), tag=f"{tag} vmbf")
                        for lt in range(L // 128):
                            em.op("pe", (lambda lt=lt, hc=hc: lambda eng: eng.matmul(
                                pE[0:45, 64:128], attnT[:, lt * 45:(lt + 1) * 45],
                                vtm[:, lt * (GP * 128) + hc: lt * (GP * 128) + hc + 64],
                                start=(lt == 0), stop=(lt == 31)))(),
                                reads=(batnT, bvtm, bpE), writes=(bpE,), tag=f"{tag} ored")
                        em.op("dve", lambda eng: eng.tensor_scalar(dsc[:, :], pE[0:45, 64:128],
                                                                   rcol[:, 1:2], None, ALU.mult),
                              reads=(bpE, brc, bdsc), writes=(bdsc,), tag=f"{tag} dnorm")
                        em.op("pe", lambda eng: eng.matmul(pE[0:45, 128:192], C.ones_row[0:1, 0:45],
                                                           vmr[:, :], start=True, stop=True),
                              reads=(bvm, C.buf, bpE), writes=(bpE,), tag=f"{tag} vbc")
                        em.op("dve", lambda eng: eng.tensor_sub(delta[:, :], dsc[:, :], pE[0:45, 128:192]),
                              reads=(bdsc, bpE, bdelta), writes=(bdelta,), tag=f"{tag} dsub")
                        for c in range(L // 512):
                            i = c % 2
                            ps = pA if i == 0 else pB
                            bps = bpA if i == 0 else bpB
                            em.op("pe", (lambda ps=ps, c=c: lambda eng: eng.matmul(
                                ps[0:64, :], delta[:, :], ssb[:, c * 512:(c + 1) * 512],
                                start=True, stop=False))(),
                                reads=(bdelta, bss, bps), writes=(bps,), tag=f"{tag} scat")
                            em.op("pe", (lambda ps=ps, c=c: lambda eng: eng.matmul(
                                ps[0:64, :], vmrb[:, :], C.ones16[0:1, :],
                                start=False, stop=True))(),
                                reads=(bvm, C.buf, bps), writes=(bps,), tag=f"{tag} fill")
                            em.op("act", (lambda ps=ps: lambda eng: eng.copy(
                                oastage[:, :], ps[0:64, :]))(),
                                reads=(bps, boa), writes=(boa,), tag=f"{tag} oaev")
                            em.dma("qact", (lambda b=b, hh=hh, c=c: lambda eng: eng.dma_start(
                                out=OA[b][hh * 64:(hh + 1) * 64, c * 512:(c + 1) * 512],
                                in_=oastage[:, :]))(),
                                reads=(boa,), writes=(bOA[b],), tag=f"{tag} oast")
        for b in range(nb):
            with contextlib.ExitStack() as stk2:
                oach = stk2.enter_context(nc.sbuf_tensor(f"{tag}_oach{b}", [128, DT * 512], BF16))
                wo = [stk2.enter_context(nc.sbuf_tensor(f"{tag}_wo{b}{i}", [128, DT * 128], BF16))
                      for i in range(2)]
                ost = stk2.enter_context(nc.sbuf_tensor(f"{tag}_ost{b}", [128, DT * 512], F32))
                boach, bost = B_("oach"), B_("ost")
                bwo = [B_("wo0"), B_("wo1")]
                for c in range(L // 512):
                    if c == 0:
                        em.barrier()
                    em.dma("qsp", (lambda b=b, c=c: lambda eng: eng.dma_start(
                        out=oach[:, :].rearrange("p (dt l) -> p dt l", dt=DT),
                        in_=OA[b].rearrange("(dt p) l -> p dt l", p=128)[:, :, c * 512:(c + 1) * 512]))(),
                        reads=(boach, bOA[b]), writes=(boach,), tag=f"{tag} oald")
                    for et in range(DT):
                        i = et % 2
                        ps = pA if i == 0 else pB
                        bps = bpA if i == 0 else bpB
                        em.dma("qpool", (lambda et=et, i=i: lambda eng: eng.dma_start(
                            out=wo[i][:, :].rearrange("p (dt c) -> p dt c", dt=DT),
                            in_=W["Wo"].rearrange("(dt p) e -> p dt e", p=128)[:, :, et * 128:(et + 1) * 128]))(),
                            reads=(bwo[i],), writes=(bwo[i],), tag=f"{tag} wold")
                        for dt in range(DT):
                            em.op("pe", (lambda ps=ps, i=i, dt=dt, c=c: lambda eng: eng.matmul(
                                ps[:, :], wo[i][:, dt * 128:(dt + 1) * 128],
                                oach[:, dt * 512:(dt + 1) * 512], start=(dt == 0), stop=False))(),
                                reads=(bwo[i], boach), writes=(bps,), tag=f"{tag} womm")
                        em.op("pe", (lambda ps=ps, et=et: lambda eng: eng.matmul(
                            ps[:, :], brow[0:1, 3 * D + et * 128: 3 * D + (et + 1) * 128],
                            C.ones16[0:1, :], start=False, stop=True))(),
                            reads=(bbr, C.buf), writes=(bps,), tag=f"{tag} wob")
                        em.op("act", (lambda ps=ps, et=et: lambda eng: eng.copy(
                            ost[:, et * 512:(et + 1) * 512], ps[:, :]))(),
                            reads=(bps, bost), writes=(bost,), tag=f"{tag} woev")
                    em.dma("qact", (lambda b=b, c=c: lambda eng: eng.dma_start(
                        out=outs_f32[b].rearrange("(dt p) l -> p dt l", p=128)[:, :, c * 512:(c + 1) * 512],
                        in_=ost[:, :].rearrange("p (dt l) -> p dt l", dt=DT)))(),
                        reads=(bost,), writes=(), tag=f"{tag} wost")



def phase_cast_bf16(nc, em, src_f32, dst_bf16, tag):
    with contextlib.ExitStack() as stk:
        t = stk.enter_context(nc.sbuf_tensor(f"{tag}_cb", [128, DT * 512], BF16))
        bt = Buf(f"{tag}cb")
        for c in range(L // 512):
            em.dma("qpool", (lambda c=c: lambda eng: eng.dma_start(
                out=t[:, :].rearrange("p (dt w) -> p dt w", dt=DT),
                in_=src_f32.rearrange("(dt p) l -> p dt l", p=128)[:, :, c * 512:(c + 1) * 512]))(),
                reads=(bt,), writes=(bt,), tag=f"{tag} in")
            em.dma("qsp", (lambda c=c: lambda eng: eng.dma_start(
                out=dst_bf16.rearrange("(dt p) l -> p dt l", p=128)[:, :, c * 512:(c + 1) * 512],
                in_=t[:, :].rearrange("p (dt w) -> p dt w", dt=DT)))(),
                reads=(bt,), writes=(), tag=f"{tag} out")


def phase_out_transpose(nc, em, C, src_f32, y_out, b, tag):
    with contextlib.ExitStack() as stk:
        fin = [stk.enter_context(nc.sbuf_tensor(f"{tag}_f{i}", [128, DT * 128], F32)) for i in range(2)]
        tmo = [stk.enter_context(nc.sbuf_tensor(f"{tag}_t{i}", [128, D], F32)) for i in range(2)]
        ps = [stk.enter_context(nc.psum_tensor(f"{tag}_p{i}", [128, 128], F32)) for i in range(2)]
        bf = [Buf("f0"), Buf("f1")]
        bt = [Buf("t0"), Buf("t1")]
        bp = [Buf("p0"), Buf("p1")]
        for lt in range(L // 128):
            i = lt % 2
            em.dma("qsp", (lambda lt=lt, i=i: lambda eng: eng.dma_start(
                out=fin[i][:, :].rearrange("p (dt c) -> p dt c", dt=DT),
                in_=src_f32.rearrange("(dt p) l -> p dt l", p=128)[:, :, lt * 128:(lt + 1) * 128]))(),
                reads=(bf[i], bt[i]), writes=(bf[i],), tag=f"{tag} ld")
            for dt in range(DT):
                j = dt % 2
                em.op("pe", (lambda i=i, j=j, dt=dt: lambda eng: eng.transpose(
                    ps[j][:, :], fin[i][:, dt * 128:(dt + 1) * 128], C.ident32[:, :]))(),
                    reads=(bf[i], C.buf, bp[j]), writes=(bp[j],), tag=f"{tag} tr")
                em.op("act", (lambda i=i, j=j, dt=dt: lambda eng: eng.copy(
                    tmo[i][:, dt * 128:(dt + 1) * 128], ps[j][:, :]))(),
                    reads=(bp[j], bt[i]), writes=(bt[i],), tag=f"{tag} ev")
            em.dma("qact", (lambda b=b, lt=lt, i=i: lambda eng: eng.dma_start(
                out=y_out[b, lt * 128:(lt + 1) * 128, :], in_=tmo[i][:, :]))(),
                reads=(bt[i],), writes=(), tag=f"{tag} st")



from concourse.bass_utils import run_bass_kernel_spmd

PNAMES = ["angle", "lnm", "scale", "pi_re", "pi_im", "lc_re", "lc_im"]


def _ext(nc, name, shape, dt_, kind="ExternalInput"):
    return nc.dram_tensor(name, list(shape), dt_, kind=kind).ap()


def _weight_tensors(nc, which):
    W = {}
    if "l0attn" in which:
        for n in ["Wq", "Wk", "Wv", "Wo"]:
            W[n] = _ext(nc, f"L0_attn_{n}", [D, D], F32)
        for n in ["bq", "bk", "bv", "bo"]:
            W[n] = _ext(nc, f"L0_attn_{n}", [D], F32)
    if "l0ffn" in which:
        W["gate"] = _ext(nc, "L0_ffn_gate", [D, E], F32)
        W["W1"] = _ext(nc, "L0_ffn_W1", [E, D, HID], F32)
        W["b1"] = _ext(nc, "L0_ffn_b1", [E, HID], F32)
        W["W2"] = _ext(nc, "L0_ffn_W2", [E, HID, D], F32)
        W["b2"] = _ext(nc, "L0_ffn_b2", [E, D], F32)
    if "l1attn" in which:
        for n in PNAMES:
            W[f"p_{n}"] = _ext(nc, f"p_{n}", [128, DT], F32)
    if "l1ffn" in which:
        W["Wm"] = _ext(nc, "L1_ffn_Wm", [FS, D], F32)
        W["Wp"] = _ext(nc, "L1_ffn_Wp", [D, FS], F32)
    if "l1ffn_c" in which:
        W["Wc1"] = _ext(nc, "L1_ffn_Wc1", [256, D], F32)
        W["Wc2"] = _ext(nc, "L1_ffn_Wc2", [FS, 256], F32)
    return W


def build_launch1(sample_idx, max_phases=99):
    import os
    max_phases = int(os.environ.get("MAXPH", max_phases))
    nc = bass.Bass("TRN2", target_bir_lowering=False)
    em = Em(nc)
    x_in = _ext(nc, "x_in", [NB, L, D], F32)
    W = _weight_tensors(nc, ("l0attn", "l0ffn", "l1attn", "l1ffn", "l1ffn_c"))
    X3 = _ext(nc, "X3", [NB * D, L], F32, "ExternalOutput")
    XN1 = _ext(nc, "XN1", [NB * D, L], F32, "ExternalOutput")
    XN2 = _ext(nc, "XN2", [NB * D, L], F32, "ExternalOutput")
    MASK1 = _ext(nc, "MASK1", [NB * 128, FT], F32, "ExternalOutput")
    MASK2 = _ext(nc, "MASK2", [NB * 128, FT], F32, "ExternalOutput")
    NOV = _ext(nc, "NOV", [2, NB], F32, "ExternalOutput")
    dbgk = "ExternalOutput" if DBG_STAGES else "Internal"
    X0 = nc.dram_tensor("X0", [NB * D, L], F32, kind=dbgk).ap()
    A0 = nc.dram_tensor("A0", [NB * D, L], F32, kind=dbgk).ap()
    X1 = nc.dram_tensor("X1", [NB * D, L], F32, kind=dbgk).ap()
    M1o = nc.dram_tensor("M1o", [NB * D, L], F32, kind=dbgk).ap()
    X2 = nc.dram_tensor("X2", [NB * D, L], F32, kind=dbgk).ap()
    C2 = nc.dram_tensor("C2", [NB * D, L], F32, kind=dbgk).ap()
    X3BF = nc.dram_tensor("X3BF", [NB * D, L], BF16, kind="Internal").ap()
    XN1BF = nc.dram_tensor("XN1BF", [NB * D, L], BF16, kind="Internal").ap()
    XN2BF = nc.dram_tensor("XN2BF", [NB * D, L], BF16, kind="Internal").ap()
    HIST = nc.dram_tensor("HIST", [NB * 128, FT], F32, kind="Internal").ap()
    POOL = nc.dram_tensor("POOL", [2 * NB * 128, DT], F32, kind="Internal").ap()
    Mrows = nc.dram_tensor("Mrows", [NB * NH, L], F32, kind="Internal").ap()
    RMrows = nc.dram_tensor("RMrows", [NB * NH, L], F32, kind="Internal").ap()
    IDX = nc.dram_tensor("IDX", [IS, 1], mybir.dt.int32, kind="Internal").ap()
    WGT = nc.dram_tensor("WGT", [D, IS], BF16, kind="Internal").ap()
    WPG = nc.dram_tensor("WPG", [IS, D], BF16, kind="Internal").ap()
    bWGT, bWPG = Buf("WGTd"), Buf("WPGd")
    bs = lambda T, b: T[b * D:(b + 1) * D, :]
    bm = lambda T, b: T[b * 128:(b + 1) * 128, :]
    with contextlib.ExitStack() as stk:
        C = Consts(nc, em, stk)
        sems = {k: stk.enter_context(nc.semaphore(f"sem_{k}")) for k in em.sem_keys()}
        block = stk.enter_context(nc.Block())
        for b in range(NB):
            phase_in_transpose(nc, em, C, x_in, bs(X0, b), b)
            em.barrier()
        if max_phases < 1:
            em.finalize(block, sems)
            return nc
        phase_l0attn_A(nc, em, C, [bs(X0, b) for b in range(NB)], W, sample_idx, Mrows)
        em.barrier()
        phase_l0attn_sel(nc, em, C, Mrows, RMrows, NB * NH)
        em.barrier()
        if max_phases < 2:
            em.finalize(block, sems)
            return nc
        phase_l0attn_B(nc, em, C, [bs(X0, b) for b in range(NB)], W, RMrows,
                       [bs(A0, b) for b in range(NB)])
        em.barrier()
        for b in range(NB):
            phase_ln(nc, em, C, bs(X0, b), bs(A0, b), bs(X1, b), b, tag=f"ln1b{b}")
            em.barrier()
        if max_phases < 3:
            em.finalize(block, sems)
            return nc
        w1bf, w2bf = phase_moe_prep(nc, em, W)
        em.barrier()
        if max_phases < 4:
            em.finalize(block, sems)
            return nc
        for b in range(NB):
            phase_moe(nc, em, C, bs(X1, b), W, w1bf, w2bf, bs(M1o, b), b)
            em.barrier()
            phase_ln(nc, em, C, bs(X1, b), bs(M1o, b), bs(X2, b), b, tag=f"ln2b{b}")
            em.barrier()
        if max_phases < 5:
            em.finalize(block, sems)
            return nc
        P = {n: W[f"p_{n}"] for n in PNAMES}
        phase_l1attn(nc, em, C, [bs(X2, b) for b in range(NB)], P,
                     [bs(C2, b) for b in range(NB)])
        em.barrier()
        for b in range(NB):
            phase_ln(nc, em, C, bs(X2, b), bs(C2, b), bs(X3, b), b, tag=f"ln3b{b}")
            em.barrier()
        if max_phases < 6:
            em.finalize(block, sems)
            return nc
        Wsc = phase_l1ffn_prep(nc, em, C, W)
        em.barrier()
        if max_phases < 7:
            em.finalize(block, sems)
            return nc
        for b in range(NB):
            phase_cast_bf16(nc, em, bs(X3, b), bs(X3BF, b), f"cx3b{b}")
            em.barrier()
            phase_l1ffn_step(nc, em, C, bs(X3BF, b), Wsc, bs(XN1BF, b), bs(XN1, b),
                             b, 1, pooled_dram=bm(POOL, b))
            em.barrier()
            phase_l1ffn_ck(nc, em, C, bm(POOL, b), Wsc, bm(MASK1, b), None,
                           bm(HIST, b), NOV[0:1, b:b + 1], f"ck1b{b}")
            em.barrier()
            if max_phases < 8:
                continue
            bidx = phase_l1ffn_idx(nc, em, C, bm(MASK1, b), IDX, f"idx2b{b}")
            em.barrier()
            phase_l1ffn_gather(nc, em, C, Wsc, IDX, bidx, WGT, WPG, bWGT, bWPG, f"g2b{b}")
            em.barrier()
            phase_l1ffn_step_sparse(nc, em, C, bs(XN1BF, b), WGT, bWGT, WPG, bWPG,
                                    bs(XN2BF, b), bs(XN2, b), b, 2,
                                    pooled_dram=bm(POOL, NB + b))
            em.barrier()
            phase_l1ffn_ck(nc, em, C, bm(POOL, NB + b), Wsc, bm(MASK2, b), bm(HIST, b),
                           None, NOV[1:2, b:b + 1], f"ck2b{b}")
            em.barrier()
        em.finalize(block, sems)
    return nc


def build_launch2():
    nc = bass.Bass("TRN2", target_bir_lowering=False)
    em = Em(nc)
    W = _weight_tensors(nc, ("l1ffn",))
    X3 = _ext(nc, "X3", [NB * D, L], F32)
    XN1 = _ext(nc, "XN1", [NB * D, L], F32)
    XN2 = _ext(nc, "XN2", [NB * D, L], F32)
    MASK1 = _ext(nc, "MASK1", [NB * 128, FT], F32)
    MASK2 = _ext(nc, "MASK2", [NB * 128, FT], F32)
    NOVALL = _ext(nc, "NOVALL", [2, 16], F32)
    y_out = _ext(nc, "y_out", [NB, L, D], F32, "ExternalOutput")
    XN2BF = nc.dram_tensor("XN2BF", [NB * D, L], BF16, kind="Internal").ap()
    XN3BF = nc.dram_tensor("XN3BF", [NB * D, L], BF16, kind="Internal").ap()
    XN3 = nc.dram_tensor("XN3", [NB * D, L], F32, kind="Internal").ap()
    XFIN = nc.dram_tensor("XFIN", [NB * D, L], F32, kind="Internal").ap()
    XOUT = nc.dram_tensor("XOUT", [NB * D, L], F32, kind="Internal").ap()
    DFLAG = nc.dram_tensor("DFLAG", [1, 2], F32, kind="Internal").ap()
    ACT3 = nc.dram_tensor("ACT3", [NB * 128, FT], F32, kind="Internal").ap()
    IDX = nc.dram_tensor("IDX", [IS, 1], mybir.dt.int32, kind="Internal").ap()
    WGT = nc.dram_tensor("WGT", [D, IS], BF16, kind="Internal").ap()
    WPG = nc.dram_tensor("WPG", [IS, D], BF16, kind="Internal").ap()
    bWGT, bWPG = Buf("WGTd"), Buf("WPGd")
    bs = lambda T, b: T[b * D:(b + 1) * D, :]
    bm = lambda T, b: T[b * 128:(b + 1) * 128, :]
    with contextlib.ExitStack() as stk:
        C = Consts(nc, em, stk)
        sems = {k: stk.enter_context(nc.semaphore(f"sem_{k}")) for k in em.sem_keys()}
        block = stk.enter_context(nc.Block())
        Wsc = phase_l1ffn_prep(nc, em, C, W, subset=("Wm", "Wp"))
        em.barrier()
        phase_l1ffn_done(nc, em, NOVALL, 16, DFLAG)
        em.barrier()
        for b in range(NB):
            phase_l1ffn_active3(nc, em, C, bm(MASK1, b), bm(MASK2, b), DFLAG,
                                bm(ACT3, b), tag=f"act3b{b}")
            em.barrier()
            phase_cast_bf16(nc, em, bs(XN2, b), bs(XN2BF, b), f"cx2b{b}")
            em.barrier()
            bidx = phase_l1ffn_idx(nc, em, C, bm(ACT3, b), IDX, f"idx3b{b}")
            em.barrier()
            phase_l1ffn_gather(nc, em, C, Wsc, IDX, bidx, WGT, WPG, bWGT, bWPG, f"g3b{b}")
            em.barrier()
            phase_l1ffn_step_sparse(nc, em, C, bs(XN2BF, b), WGT, bWGT, WPG, bWPG,
                                    bs(XN3BF, b), bs(XN3, b), b, 3)
            em.barrier()
            phase_l1ffn_blend(nc, em, C, bs(XN1, b), bs(XN2, b), bs(XN3, b), DFLAG,
                              bs(XFIN, b), tag=f"blendb{b}")
            em.barrier()
            phase_ln(nc, em, C, bs(X3, b), bs(XFIN, b), bs(XOUT, b), b, tag=f"ln4b{b}")
            em.barrier()
            phase_out_transpose(nc, em, C, bs(XOUT, b), y_out, b, f"otb{b}")
            em.barrier()
        em.finalize(block, sems)
    return nc


def kernel(**inputs):
    NCORES = 8
    x = np.ascontiguousarray(inputs["L0_attn_x"])
    sample_idx = [int(v) for v in np.asarray(inputs["L0_attn_sample_idx"])]
    consts = make_host_consts()
    wmap = {}
    for k, v in inputs.items():
        if k not in ("L0_attn_x", "L0_attn_sample_idx"):
            wmap[k] = np.ascontiguousarray(v)
    for n in PNAMES:
        key = {"angle": "L1_attn_phazor_angle", "lnm": "L1_attn_ln_minus_ln_phazor_abs",
               "scale": "L1_attn_ln_phazor_scale", "pi_re": "L1_attn_phazor_init_re",
               "pi_im": "L1_attn_phazor_init_im", "lc_re": "L1_attn_last_conv_re",
               "lc_im": "L1_attn_last_conv_im"}[n]
        wmap[f"p_{n}"] = np.ascontiguousarray(np.asarray(inputs[key]).reshape(DT, 128).T)
    for k in ["L1_attn_phazor_angle", "L1_attn_ln_minus_ln_phazor_abs",
              "L1_attn_ln_phazor_scale", "L1_attn_phazor_init_re",
              "L1_attn_phazor_init_im", "L1_attn_last_conv_re", "L1_attn_last_conv_im"]:
        wmap.pop(k, None)

    nc1 = build_launch1(sample_idx)
    in_maps1 = []
    for ci in range(NCORES):
        m = {"x_in": x[ci * NB:(ci + 1) * NB]}
        m.update(wmap)
        m.update(consts)
        in_maps1.append(m)
    res1 = run_bass_kernel_spmd(nc1, in_maps1, core_ids=list(range(NCORES)))
    nov_all = np.concatenate([res1.results[ci]["NOV"] for ci in range(NCORES)], axis=1)
    nov_all = np.ascontiguousarray(nov_all.astype(np.float32))

    nc2 = build_launch2()
    in_maps2 = []
    for ci in range(NCORES):
        r = res1.results[ci]
        m = {"X3": r["X3"], "XN1": r["XN1"], "XN2": r["XN2"],
             "MASK1": r["MASK1"], "MASK2": r["MASK2"], "NOVALL": nov_all,
             "L1_ffn_Wm": wmap["L1_ffn_Wm"], "L1_ffn_Wp": wmap["L1_ffn_Wp"]}
        m.update(consts)
        in_maps2.append(m)
    res2 = run_bass_kernel_spmd(nc2, in_maps2, core_ids=list(range(NCORES)))
    out = np.concatenate([res2.results[ci]["y_out"] for ci in range(NCORES)], axis=0)
    return out



def phase_l1ffn_idx(nc, em, C, mask_dram, IDX, tag):
    from concourse.bass_core_rust import IndirectOffsetOnAxis
    I32 = mybir.dt.int32
    with contextlib.ExitStack() as stk:
        sb = lambda name, shape, dt_: stk.enter_context(nc.sbuf_tensor(f"{tag}_{name}", shape, dt_))
        msk = sb("m", [128, FT], F32)
        ones = sb("o", [128, FT], F32)
        rank = sb("r", [128, FT], F32)
        pen = sb("pen", [128, FT], F32)
        ji = sb("ji", [128, FT], I32)
        zz = sb("z", [128, IS // 128], I32)
        ppre = stk.enter_context(nc.psum_tensor(f"{tag}_pp", [128, 1], F32))
        bm, bo, br, bpen, bji, bz, bpp = (Buf(x) for x in
                                          ("m", "o", "r", "pen", "ji", "z", "pp"))
        em.dma("qsp", lambda eng: eng.dma_start(out=msk[:, :], in_=mask_dram),
               reads=(), writes=(bm,), tag=f"{tag} ldm")
        em.op("dve", lambda eng: eng.memset(ones[:, :], 1.0),
              reads=(), writes=(bo,), tag=f"{tag} ones")
        em.op("dve", lambda eng: eng.memset(zz[:, :], 0),
              reads=(), writes=(bz,), tag=f"{tag} zz")
        em.op("dve", lambda eng: eng.tensor_tensor_scan(rank[:, :], ones[:, :], msk[:, :],
                                                        0.0, ALU.mult, ALU.add),
              reads=(bo, bm, br), writes=(br,), tag=f"{tag} scan")
        em.op("pe", lambda eng: eng.matmul(ppre[:, :], C.lowtri[:, :], rank[:, FT - 1:FT],
                                           start=True, stop=True),
              reads=(br, C.buf, bpp), writes=(bpp,), tag=f"{tag} pre")
        em.op("dve", lambda eng: eng.tensor_scalar(pen[:, :], msk[:, :], -1e9, 1e9, ALU.mult, ALU.add),
              reads=(bm, bpen), writes=(bpen,), tag=f"{tag} pen")
        em.op("dve", lambda eng: eng.tensor_copy(ones[:, 0:1], ppre[:, :]),
              reads=(bpp, bo), writes=(bo,), tag=f"{tag} precp")
        em.op("dve", lambda eng: eng.tensor_scalar(rank[:, :], rank[:, :], ones[:, 0:1], -1.0,
                                                   ALU.add, ALU.add),
              reads=(br, bo), writes=(br,), tag=f"{tag} jall")
        em.op("dve", lambda eng: eng.tensor_add(rank[:, :], rank[:, :], pen[:, :]),
              reads=(br, bpen), writes=(br,), tag=f"{tag} jm")
        em.op("dve", lambda eng: eng.tensor_copy(ji[:, :], rank[:, :]),
              reads=(br, bji), writes=(bji,), tag=f"{tag} jint")
        em.dma("qpool", lambda eng: eng.dma_start(
            out=IDX.rearrange("(p t) a -> p (t a)", p=128), in_=zz[:, :]),
            reads=(bz,), writes=(), tag=f"{tag} zero")
        bidx = Buf(f"{tag}_idxD")
        for ft in range(FT):
            em.dma("qpool", (lambda ft=ft: lambda eng: eng.indirect_dma_start(
                IDX, IndirectOffsetOnAxis(ap=ji[:, ft:ft + 1], axis=0),
                C.fid[:, ft:ft + 1], None))(),
                reads=(bji, C.buf), writes=(bidx,), tag=f"{tag} sc{ft}")
    return bidx


def phase_l1ffn_gather(nc, em, C, Wsc, IDX, bidx, WGT, WPG, bWGT, bWPG, tag):
    from concourse.bass_core_rust import IndirectOffsetOnAxis
    I32 = mybir.dt.int32
    NT = IS // 128
    with contextlib.ExitStack() as stk:
        sb = lambda name, shape, dt_: stk.enter_context(nc.sbuf_tensor(f"{tag}_{name}", shape, dt_))
        idxc = sb("idxc", [128, NT], I32)
        wg = [sb(f"wg{i}", [128, D], BF16) for i in range(2)]
        wt = [sb(f"wt{i}", [128, D], BF16) for i in range(2)]
        ps = [stk.enter_context(nc.psum_tensor(f"{tag}_ps{i}", [128, 128], BF16)) for i in range(2)]
        bic = Buf("idxc")
        bwg = [Buf("wg0"), Buf("wg1")]
        bwt = [Buf("wt0"), Buf("wt1")]
        bps = [Buf("ps0"), Buf("ps1")]
        em.dma("qsp", lambda eng: eng.dma_start(
            out=idxc[:, :], in_=IDX.rearrange("(p t) a -> p (t a)", p=128)),
            reads=(bidx,), writes=(bic,), tag=f"{tag} ldi")
        for t in range(NT):
            i = t % 2
            em.dma("qpool", (lambda t=t, i=i: lambda eng: eng.indirect_dma_start(
                wt[i][:, :], None, Wsc["WpT"], IndirectOffsetOnAxis(ap=idxc[:, t:t + 1], axis=0)))(),
                reads=(bic, bwt[i]), writes=(bwt[i],), tag=f"{tag} gwp")
            em.dma("qact", (lambda t=t, i=i: lambda eng: eng.dma_start(
                out=WPG[t * 128:(t + 1) * 128, :], in_=wt[i][:, :]))(),
                reads=(bwt[i],), writes=(bWPG,), tag=f"{tag} swp")
            em.dma("qpool", (lambda t=t, i=i: lambda eng: eng.indirect_dma_start(
                wg[i][:, :], None, Wsc["WmR"], IndirectOffsetOnAxis(ap=idxc[:, t:t + 1], axis=0)))(),
                reads=(bic, bwg[i]), writes=(bwg[i],), tag=f"{tag} gwm")
            for dt in range(DT):
                j = dt % 2
                em.op("pe", (lambda i=i, j=j, dt=dt: lambda eng: eng.transpose(
                    ps[j][:, :], wg[i][:, dt * 128:(dt + 1) * 128], C.ident16[:, :]))(),
                    reads=(bwg[i], C.buf, bps[j]), writes=(bps[j],), tag=f"{tag} tr")
                em.op("dve", (lambda i=i, j=j, dt=dt: lambda eng: eng.tensor_copy(
                    wt[i][:, dt * 128:(dt + 1) * 128], ps[j][:, :]))(),
                    reads=(bps[j], bwt[i]), writes=(bwt[i],), tag=f"{tag} trev")
            em.dma("qact", (lambda t=t, i=i: lambda eng: eng.dma_start(
                out=WGT.rearrange("(dt p) f -> p dt f", p=128)[:, :, t * 128:(t + 1) * 128],
                in_=wt[i][:, :].rearrange("p (dt c) -> p dt c", dt=DT)))(),
                reads=(bwt[i],), writes=(bWGT,), tag=f"{tag} swm")


def phase_l1ffn_step_sparse(nc, em, C, xin_bf, WGT, bWGT, WPG, bWPG, xout_bf, xout_f32,
                            b, step, pooled_dram=None, Lt=L, tag=None):
    tag = tag or f"l1sp{step}b{b}"
    CH = 512
    NCH = Lt // CH
    NT = IS // 128
    do_ck = pooled_dram is not None
    with contextlib.ExitStack() as stk:
        sb = lambda name, shape, dt_: stk.enter_context(nc.sbuf_tensor(f"{tag}_{name}", shape, dt_))
        xcb = sb("xcb", [128, DT * CH], BF16)
        hs = sb("hs", [128, NT * CH], BF16)
        wm = [sb(f"wm{i}", [128, DT * 512], BF16) for i in range(2)]
        wp = [sb(f"wp{i}", [128, NT * 128], BF16) for i in range(2)]
        xnf = sb("xnf", [128, DT * CH], F32)
        xnb = sb("xnb", [128, DT * CH], BF16)
        pooled = sb("pooled", [128, 2 * DT], F32)
        ph = [stk.enter_context(nc.psum_tensor(f"{tag}_ph{i}", [128, CH], F32)) for i in range(2)]
        px = [stk.enter_context(nc.psum_tensor(f"{tag}_px{i}", [128, CH], F32)) for i in range(2)]
        bxcb, bhs, bxnf, bxnb = Buf("xcb"), Buf("hs"), Buf("xnf"), Buf("xnb")
        bwm = [Buf("wm0"), Buf("wm1")]
        bwp = [Buf("wp0"), Buf("wp1")]
        bpool = Buf("pooled")
        bph = [Buf("ph0"), Buf("ph1")]
        bpx = [Buf("px0"), Buf("px1")]
        em.op("dve", lambda eng: eng.memset(pooled[:, :], 0.0),
              reads=(), writes=(bpool,), tag=f"{tag} pool0")
        for c in range(NCH):
            em.dma("qsp", (lambda c=c: lambda eng: eng.dma_start(
                out=xcb[:, :].rearrange("p (dt t) -> p dt t", dt=DT),
                in_=xin_bf.rearrange("(dt p) l -> p dt l", p=128)[:, :, c * CH:(c + 1) * CH]))(),
                reads=(bxcb,), writes=(bxcb,), tag=f"{tag} ldx")
            for fq in range(NT // 4):
                i = fq % 2
                em.dma("qsp", (lambda fq=fq, i=i: lambda eng: eng.dma_start(
                    out=wm[i][:, :].rearrange("p (dt f) -> p dt f", dt=DT),
                    in_=WGT.rearrange("(dt p) f -> p dt f", p=128)[:, :, fq * 512:(fq + 1) * 512]))(),
                    reads=(bwm[i], bWGT), writes=(bwm[i],), tag=f"{tag} wm")
                for fi in range(4):
                    ft = fq * 4 + fi
                    j = ft % 2
                    for dt in range(DT):
                        em.op("pe", (lambda i=i, j=j, fi=fi, dt=dt: lambda eng: eng.matmul(
                            ph[j][:, :], wm[i][:, dt * 512 + fi * 128: dt * 512 + (fi + 1) * 128],
                            xcb[:, dt * CH:(dt + 1) * CH], start=(dt == 0), stop=(dt == DT - 1)))(),
                            reads=(bwm[i], bxcb), writes=(bph[j],), tag=f"{tag} hmm")
                    em.op("act", (lambda j=j, ft=ft: lambda eng: eng.activation(
                        hs[:, ft * CH:(ft + 1) * CH], ph[j][:, :], AF.Relu))(),
                        reads=(bph[j], bhs), writes=(bhs,), tag=f"{tag} relu")
            for ct in range(DT):
                i = ct % 2
                em.dma("qact", (lambda ct=ct, i=i: lambda eng: eng.dma_start(
                    out=wp[i][:, :].rearrange("p (ft c) -> p ft c", ft=NT),
                    in_=WPG.rearrange("(ft p) c -> p ft c", p=128)[:, :, ct * 128:(ct + 1) * 128]))(),
                    reads=(bwp[i], bWPG), writes=(bwp[i],), tag=f"{tag} wp")
                for ft in range(NT):
                    em.op("pe", (lambda i=i, ft=ft: lambda eng: eng.matmul(
                        px[i][:, :], wp[i][:, ft * 128:(ft + 1) * 128],
                        hs[:, ft * CH:(ft + 1) * CH], start=(ft == 0), stop=(ft == NT - 1)))(),
                        reads=(bwp[i], bhs), writes=(bpx[i],), tag=f"{tag} xmm")
                em.op("dve", (lambda i=i, ct=ct: lambda eng: eng.tensor_copy(
                    xnf[:, ct * CH:(ct + 1) * CH], px[i][:, :]))(),
                    reads=(bpx[i], bxnf), writes=(bxnf,), tag=f"{tag} evf")
                em.op("act", (lambda i=i, ct=ct: lambda eng: eng.copy(
                    xnb[:, ct * CH:(ct + 1) * CH], xnf[:, ct * CH:(ct + 1) * CH]))(),
                    reads=(bxnf, bxnb), writes=(bxnb,), tag=f"{tag} evb")
                if do_ck:
                    em.op("dve", (lambda ct=ct: lambda eng: eng.tensor_reduce(
                        pooled[:, DT + ct:DT + ct + 1],
                        xnf[:, ct * CH:(ct + 1) * CH], AX.X, ALU.add))(),
                        reads=(bxnf, bpool), writes=(bpool,), tag=f"{tag} poolred")
                    em.op("dve", (lambda ct=ct: lambda eng: eng.tensor_add(
                        pooled[:, ct:ct + 1], pooled[:, ct:ct + 1],
                        pooled[:, DT + ct:DT + ct + 1]))(),
                        reads=(bpool,), writes=(bpool,), tag=f"{tag} pooladd")
            em.dma("qact", (lambda c=c: lambda eng: eng.dma_start(
                out=xout_f32.rearrange("(dt p) l -> p dt l", p=128)[:, :, c * CH:(c + 1) * CH],
                in_=xnf[:, :].rearrange("p (dt t) -> p dt t", dt=DT)))(),
                reads=(bxnf,), writes=(), tag=f"{tag} stf")
            em.dma("qact", (lambda c=c: lambda eng: eng.dma_start(
                out=xout_bf.rearrange("(dt p) l -> p dt l", p=128)[:, :, c * CH:(c + 1) * CH],
                in_=xnb[:, :].rearrange("p (dt t) -> p dt t", dt=DT)))(),
                reads=(bxnb,), writes=(), tag=f"{tag} stb")
        if do_ck:
            em.op("dve", (lambda Lt=Lt: lambda eng: eng.tensor_scalar(
                pooled[:, 0:DT], pooled[:, 0:DT], 1.0 / Lt, None, ALU.mult))(),
                reads=(bpool,), writes=(bpool,), tag=f"{tag} poolavg")
            em.dma("qact", lambda eng: eng.dma_start(out=pooled_dram, in_=pooled[:, 0:DT]),
                   reads=(bpool,), writes=(), tag=f"{tag} poolst")


# version 08 (final)
# speedup vs baseline: 1.1697x; 1.0113x over previous
import contextlib
import numpy as np
import ml_dtypes
import concourse.bass as bass
import concourse.mybir as mybir

F32 = mybir.dt.float32
BF16 = mybir.dt.bfloat16
AF = mybir.ActivationFunctionType
ALU = mybir.AluOpType
AX = mybir.AxisListType

D = 1024
L = 4096
NB = 2
DT = D // 128
NH = 16
DH = 64
U = 45
E = 8
HID = 2048
FS = 12288
IS = 4096
RNOV = 0.7
DBG_STAGES = False
SAME_ENGINE_EDGES = True



class Buf:
    __slots__ = ("name", "w", "readers")

    def __init__(self, name):
        self.name = name
        self.w = None
        self.readers = []


class _Op:
    __slots__ = ("stream", "fn", "waits", "ev", "inc_needed", "is_dma", "tag")

    def __init__(self, stream, fn, waits, ev, is_dma, tag):
        self.stream = stream
        self.fn = fn
        self.waits = waits
        self.ev = ev
        self.inc_needed = False
        self.is_dma = is_dma
        self.tag = tag


class Em:

    STREAMS = ("pe", "act", "dve", "pool", "sp")
    QUEUES = {"qsp": "sp", "qact": "act", "qpool": "pool"}
    NROT = 8

    def __init__(self, nc):
        self.nc = nc
        self.ops = {s: [] for s in self.STREAMS}
        self.counts = dict.fromkeys(self.STREAMS, 0)
        self.qcounts = dict.fromkeys(self.QUEUES, 0)
        self.ops_by_ev = {}
        self._last_ev = dict.fromkeys(self.STREAMS, None)

    def sem_keys(self):
        keys = list(self.STREAMS)
        for q in self.QUEUES:
            keys += [f"{q}{j}" for j in range(self.NROT)]
        return keys

    def _record(self, stream, fn, reads, writes, queue=None, tag=""):
        deps = set()
        for b in reads:
            if b.w is not None:
                deps.add(b.w)
        for b in writes:
            if b.w is not None:
                deps.add(b.w)
            for r in b.readers:
                deps.add(r)
        if queue is None:
            self.counts[stream] += 1
            ev = (stream, self.counts[stream])
        else:
            i = self.qcounts[queue]
            self.qcounts[queue] += 1
            j = i % self.NROT
            chain = i // self.NROT + 1
            ev = (f"{queue}{j}", 16 * chain)
            if chain > 1:
                deps.add((f"{queue}{j}", 16 * (chain - 1)))
        waits = []
        for (dstream, dval) in deps:
            if dstream == stream and queue is None and not SAME_ENGINE_EDGES:
                continue
            if dstream == stream == "pe" and queue is None:
                continue
            waits.append((dstream, dval))
            if (dstream, dval) in self.ops_by_ev:
                self.ops_by_ev[(dstream, dval)].inc_needed = True
        op = _Op(stream, fn, waits, ev, queue is not None, tag)
        self.ops[stream].append(op)
        self.ops_by_ev[ev] = op
        self._last_ev[stream] = ev
        for b in reads:
            b.readers.append(ev)
        for b in writes:
            b.w = ev
            b.readers = []
        return ev

    def op(self, stream, fn, reads=(), writes=(), tag=""):
        return self._record(stream, fn, reads, writes, None, tag)

    def dma(self, queue, fn, reads=(), writes=(), tag=""):
        return self._record(self.QUEUES[queue], fn, reads, writes, queue, tag)

    def barrier(self):
        lasts = {s: self._last_ev[s] for s in self.STREAMS}
        qlasts = []
        for q in self.QUEUES:
            n = self.qcounts[q]
            for j in range(self.NROT):
                cnt = (n - j + self.NROT - 1) // self.NROT
                if cnt > 0:
                    qlasts.append((f"{q}{j}", 16 * cnt))
        for s in self.STREAMS:
            deps = [ev for t, ev in lasts.items() if t != s and ev is not None]
            deps += qlasts
            for ev in deps:
                if ev in self.ops_by_ev:
                    self.ops_by_ev[ev].inc_needed = True
            op = _Op(s, lambda eng: None, [d for d in deps if d[0] != s], None, False, "barrier")
            self.ops[s].append(op)

    def finalize(self, block, sems):
        rank = {}
        for s in self.STREAMS:
            r = 0
            for op in self.ops[s]:
                if op.ev is not None and not op.is_dma:
                    if op.inc_needed:
                        r += 1
                    rank[op.ev] = r

        def mapval(ev):
            s, v = ev
            if s in self.STREAMS:
                return rank[ev]
            return v

        def emit_stream(eng, stream):
            waited = {}
            for op in self.ops[stream]:
                for dep in op.waits:
                    dval = mapval(dep)
                    dstream = dep[0]
                    if dval == 0 or waited.get(dstream, 0) >= dval:
                        continue
                    eng.wait_ge(sems[dstream], dval)
                    waited[dstream] = dval
                inst = op.fn(eng)
                if op.is_dma:
                    inst.then_inc(sems[op.ev[0]], 16)
                elif op.inc_needed:
                    if inst is None:
                        inst = eng.nop()
                    inst.then_inc(sems[op.ev[0]], 1)

        @block.tensor
        def _(t):
            emit_stream(t, "pe")

        @block.scalar
        def _(s):
            emit_stream(s, "act")

        @block.vector
        def _(v):
            emit_stream(v, "dve")

        @block.gpsimd
        def _(g):
            emit_stream(g, "pool")

        @block.sync
        def _(sy):
            emit_stream(sy, "sp")



def ceil_div(a, b):
    return -(-a // b)



def make_host_consts():
    return {
        "c_ones_col": np.ones((128, 1), np.float32),
        "c_ones_row": np.ones((1, 512), np.float32),
        "c_ident32": np.eye(128, dtype=np.float32),
        "c_ident16": np.eye(128).astype(ml_dtypes.bfloat16),
        "c_iota45": (np.arange(1, 46, dtype=np.float32)).reshape(45, 1),
        "c_ones16": np.ones((1, 512), ml_dtypes.bfloat16),
        "c_iotarow45": np.tile(np.arange(1, 46, dtype=np.float32), (128, 1)),
        "c_ones_col16": np.ones((128, 1), ml_dtypes.bfloat16),
        "c_lowtri": np.tril(np.ones((128, 128), np.float32), -1).T.copy(),
        "c_fid": (np.arange(FT)[None, :] * 128 + np.arange(128)[:, None]).astype(np.int32),
    }


class Consts:

    def __init__(self, nc, em, stk):
        self.ones_col = stk.enter_context(nc.sbuf_tensor("s_ones_col", [128, 1], F32))
        self.ones_row = stk.enter_context(nc.sbuf_tensor("s_ones_row", [1, 512], F32))
        self.ident32 = stk.enter_context(nc.sbuf_tensor("s_ident32", [128, 128], F32))
        self.ident16 = stk.enter_context(nc.sbuf_tensor("s_ident16", [128, 128], BF16))
        self.iota45 = stk.enter_context(nc.sbuf_tensor("s_iota45", [45, 1], F32))
        self.ones16 = stk.enter_context(nc.sbuf_tensor("s_ones16", [1, 512], BF16))
        self.iotarow45 = stk.enter_context(nc.sbuf_tensor("s_iotarow45", [128, 45], F32))
        self.ones_col16 = stk.enter_context(nc.sbuf_tensor("s_ones_col16", [128, 1], BF16))
        self.lowtri = stk.enter_context(nc.sbuf_tensor("s_lowtri", [128, 128], F32))
        self.fid = stk.enter_context(nc.sbuf_tensor("s_fid", [128, FT], mybir.dt.int32))
        self.buf = Buf("consts")
        self._cols = {}
        self._colbuf = stk.enter_context(nc.sbuf_tensor("s_colconsts", [128, 32], F32))
        self._ncols = 0
        names = ["c_ones_col", "c_ones_row", "c_ident32", "c_ident16", "c_iota45", "c_ones16",
                 "c_iotarow45", "c_ones_col16", "c_lowtri", "c_fid"]
        tiles = [self.ones_col, self.ones_row, self.ident32, self.ident16, self.iota45, self.ones16,
                 self.iotarow45, self.ones_col16, self.lowtri, self.fid]
        for name, tile in zip(names, tiles):
            ap = nc.dram_tensor(name, list(tile.ap().shape), tile.ap().dtype,
                                kind="ExternalInput").ap()
            em.dma(
                "qsp",
                (lambda t, a: lambda eng: eng.dma_start(out=t[:, :], in_=a[:, :]))(tile.ap(), ap),
                reads=(), writes=(self.buf,), tag=f"load {name}",
            )



def phase_in_transpose(nc, em, C, x_in, x0_f32, b):
    with contextlib.ExitStack() as stk:
        stage_tm = [stk.enter_context(nc.sbuf_tensor(f"p0tm{b}_{i}", [128, D], F32)) for i in range(2)]
        stage_fm = [stk.enter_context(nc.sbuf_tensor(f"p0fm{b}_{i}", [128, DT * 128], F32)) for i in range(2)]
        ps = [stk.enter_context(nc.psum_tensor(f"p0ps{b}_{i}", [128, 128], F32)) for i in range(2)]
        btm = [Buf("tm0"), Buf("tm1")]
        bfm = [Buf("fm0"), Buf("fm1")]
        bps = [Buf("ps0"), Buf("ps1")]
        for lt in range(L // 128):
            i = lt % 2
            em.dma("qsp", (lambda i=i, lt=lt: lambda eng: eng.dma_start(
                out=stage_tm[i][:, :], in_=x_in[b, lt * 128:(lt + 1) * 128, :]))(),
                reads=(), writes=(btm[i],), tag=f"p0 load {lt}")
            for dt in range(DT):
                j = dt % 2
                em.op("pe", (lambda i=i, j=j, dt=dt: lambda eng: eng.transpose(
                    ps[j][:, :], stage_tm[i][:, dt * 128:(dt + 1) * 128], C.ident32[:, :]))(),
                    reads=(btm[i], C.buf), writes=(bps[j],), tag="p0 tr")
                em.op("act", (lambda i=i, j=j, dt=dt: lambda eng: eng.copy(
                    stage_fm[i][:, dt * 128:(dt + 1) * 128], ps[j][:, :]))(),
                    reads=(bps[j],), writes=(bfm[i],), tag="p0 evac")
            em.dma("qact", (lambda i=i, lt=lt: lambda eng: eng.dma_start(
                out=x0_f32.rearrange("(dt p) l -> p dt l", p=128)[:, :, lt * 128:(lt + 1) * 128],
                in_=stage_fm[i][:, :].rearrange("p (dt c) -> p dt c", dt=DT)))(),
                reads=(bfm[i],), writes=(), tag=f"p0 store {lt}")



def phase_ln(nc, em, C, xprev, f_in, xout, b, tag="ln"):
    CH = 512
    NCH = L // CH
    with contextlib.ExitStack() as stk:
        zb = [stk.enter_context(nc.sbuf_tensor(f"{tag}z{i}", [128, DT * CH], F32)) for i in range(2)]
        fb = [stk.enter_context(nc.sbuf_tensor(f"{tag}f{i}", [128, DT * CH], F32)) for i in range(2)]
        z2 = stk.enter_context(nc.sbuf_tensor(f"{tag}z2", [128, DT * CH], F32))
        yb = [stk.enter_context(nc.sbuf_tensor(f"{tag}y{i}", [128, DT * CH], F32)) for i in range(2)]
        mrow = stk.enter_context(nc.sbuf_tensor(f"{tag}m", [1, CH], F32))
        vrow = stk.enter_context(nc.sbuf_tensor(f"{tag}v", [1, CH], F32))
        rrow = stk.enter_context(nc.sbuf_tensor(f"{tag}r", [1, CH], F32))
        m2row = stk.enter_context(nc.sbuf_tensor(f"{tag}m2", [1, CH], F32))
        pS = stk.enter_context(nc.psum_tensor(f"{tag}pS", [1, CH], F32))
        pQ = stk.enter_context(nc.psum_tensor(f"{tag}pQ", [1, CH], F32))
        pM = stk.enter_context(nc.psum_tensor(f"{tag}pM", [128, CH], F32))
        pR = stk.enter_context(nc.psum_tensor(f"{tag}pR", [128, CH], F32))
        bz = [Buf("z0"), Buf("z1")]
        bf_ = [Buf("f0"), Buf("f1")]
        bz2, by = Buf("z2"), [Buf("y0"), Buf("y1")]
        bm, bv, br, bm2 = Buf("m"), Buf("v"), Buf("r"), Buf("m2")
        bpS, bpQ, bpM, bpR = Buf("pS"), Buf("pQ"), Buf("pM"), Buf("pR")

        eps_ap = C.col(em, 1e-5)

        def ch_view(dram, c):
            return dram.rearrange("(dt p) l -> p dt l", p=128)[:, :, c * CH:(c + 1) * CH]

        for c in range(NCH):
            i = c % 2
            em.dma("qsp", (lambda i=i, c=c: lambda eng: eng.dma_start(
                out=zb[i][:, :].rearrange("p (dt c) -> p dt c", dt=DT), in_=ch_view(xprev, c)))(),
                reads=(), writes=(bz[i],), tag=f"{tag} ldx {c}")
            em.dma("qact", (lambda i=i, c=c: lambda eng: eng.dma_start(
                out=fb[i][:, :].rearrange("p (dt c) -> p dt c", dt=DT), in_=ch_view(f_in, c)))(),
                reads=(), writes=(bf_[i],), tag=f"{tag} ldf {c}")
            em.op("dve", (lambda i=i: lambda eng: eng.tensor_add(zb[i][:, :], zb[i][:, :], fb[i][:, :]))(),
                  reads=(bz[i], bf_[i]), writes=(bz[i],), tag=f"{tag} add")
            em.op("act", (lambda i=i: lambda eng: eng.activation(z2[:, :], zb[i][:, :], AF.Square))(),
                  reads=(bz[i],), writes=(bz2,), tag=f"{tag} sq")
            for dt in range(DT):
                em.op("pe", (lambda i=i, dt=dt: lambda eng: eng.matmul(
                    pS[:, :], C.ones_col[:, :], zb[i][:, dt * CH:(dt + 1) * CH],
                    start=(dt == 0), stop=(dt == DT - 1)))(),
                    reads=(bz[i], C.buf), writes=(bpS,), tag=f"{tag} mmS")
            for dt in range(DT):
                em.op("pe", (lambda dt=dt: lambda eng: eng.matmul(
                    pQ[:, :], C.ones_col[:, :], z2[:, dt * CH:(dt + 1) * CH],
                    start=(dt == 0), stop=(dt == DT - 1)))(),
                    reads=(bz2, C.buf), writes=(bpQ,), tag=f"{tag} mmQ")
            em.op("dve", lambda eng: eng.tensor_scalar(mrow[:, :], pS[:, :], 1.0 / D, None, ALU.mult),
                  reads=(bpS,), writes=(bm,), tag=f"{tag} m")
            em.op("dve", lambda eng: eng.tensor_scalar(vrow[:, :], pQ[:, :], 1.0 / D, None, ALU.mult),
                  reads=(bpQ,), writes=(bv,), tag=f"{tag} q")
            em.op("act", lambda eng: eng.activation(m2row[:, :], mrow[:, :], AF.Square),
                  reads=(bm,), writes=(bm2,), tag=f"{tag} msq")
            em.op("dve", lambda eng: eng.tensor_sub(vrow[:, :], vrow[:, :], m2row[:, :]),
                  reads=(bv, bm2), writes=(bv,), tag=f"{tag} var")
            em.op("act", lambda eng: eng.activation(rrow[:, :], vrow[:, :], AF.Sqrt, bias=eps_ap[0:1, :]),
                  reads=(bv, C.buf), writes=(br,), tag=f"{tag} sqrt")
            em.op("dve", lambda eng: eng.reciprocal(rrow[:, :], rrow[:, :]),
                  reads=(br,), writes=(br,), tag=f"{tag} recip")
            em.op("pe", lambda eng: eng.matmul(pM[:, :], C.ones_row[0:1, 0:128], mrow[:, :],
                                               start=True, stop=True),
                  reads=(bm, C.buf), writes=(bpM,), tag=f"{tag} bcm")
            em.op("pe", lambda eng: eng.matmul(pR[:, :], C.ones_row[0:1, 0:128], rrow[:, :],
                                               start=True, stop=True),
                  reads=(br, C.buf), writes=(bpR,), tag=f"{tag} bcr")
            for dt in range(DT):
                em.op("dve", (lambda i=i, dt=dt: lambda eng: eng.tensor_sub(
                    yb[i][:, dt * CH:(dt + 1) * CH], zb[i][:, dt * CH:(dt + 1) * CH], pM[:, :]))(),
                    reads=(bz[i], bpM), writes=(by[i],), tag=f"{tag} sub")
            for dt in range(DT):
                em.op("dve", (lambda i=i, dt=dt: lambda eng: eng.tensor_mul(
                    yb[i][:, dt * CH:(dt + 1) * CH], yb[i][:, dt * CH:(dt + 1) * CH], pR[:, :]))(),
                    reads=(by[i], bpR), writes=(by[i],), tag=f"{tag} mul")
            em.dma("qact", (lambda i=i, c=c: lambda eng: eng.dma_start(
                out=ch_view(xout, c), in_=yb[i][:, :].rearrange("p (dt c) -> p dt c", dt=DT)))(),
                reads=(by[i],), writes=(), tag=f"{tag} st {c}")





def _consts_col(self, em, value):
    if value in self._cols:
        return self._cols[value]
    i = self._ncols
    assert i < 32
    self._ncols += 1
    ap = self._colbuf[:, i:i + 1]
    em.op("dve", (lambda ap=ap, value=value: lambda eng: eng.memset(ap, float(value)))(),
          reads=(), writes=(self.buf,), tag=f"const col {value}")
    self._cols[value] = ap
    return ap

Consts.col = _consts_col



def phase_moe_prep(nc, em, W, tag="moeprep"):
    w1bf = nc.dram_tensor(f"{tag}_w1", [E, D, HID], BF16, kind="Internal").ap()
    w2bf = nc.dram_tensor(f"{tag}_w2", [E, HID, D], BF16, kind="Internal").ap()
    with contextlib.ExitStack() as stk:
        tmp = [stk.enter_context(nc.sbuf_tensor(f"{tag}t{i}", [128, 16384], BF16)) for i in range(2)]
        bt = [Buf("t0"), Buf("t1")]
        for e in range(E):
            i = e % 2
            em.dma("qpool", (lambda e=e, i=i: lambda eng: eng.dma_start(
                out=tmp[i][:, :].rearrange("p (dt h) -> p dt h", dt=DT),
                in_=W["W1"][e].rearrange("(dt p) h -> p dt h", p=128)))(),
                reads=(), writes=(bt[i],), tag=f"{tag} in1")
            em.dma("qsp", (lambda e=e, i=i: lambda eng: eng.dma_start(
                out=w1bf[e].rearrange("(dt p) h -> p dt h", p=128),
                in_=tmp[i][:, :].rearrange("p (dt h) -> p dt h", dt=DT)))(),
                reads=(bt[i],), writes=(), tag=f"{tag} out1")
            em.dma("qpool", (lambda e=e, i=i: lambda eng: eng.dma_start(
                out=tmp[i][:, :].rearrange("p (ht d) -> p ht d", ht=HID // 128),
                in_=W["W2"][e].rearrange("(ht p) d -> p ht d", p=128)))(),
                reads=(bt[i],), writes=(bt[i],), tag=f"{tag} in2")
            em.dma("qsp", (lambda e=e, i=i: lambda eng: eng.dma_start(
                out=w2bf[e].rearrange("(ht p) d -> p ht d", p=128),
                in_=tmp[i][:, :].rearrange("p (ht d) -> p ht d", ht=HID // 128)))(),
                reads=(bt[i],), writes=(), tag=f"{tag} out2")
    return w1bf, w2bf


def phase_moe(nc, em, C, x_f32, W, w1bf, w2bf, out_f32, b, Lt=L, tag="moe"):
    CH = 512
    MAC = min(512, Lt)
    NM = Lt // MAC
    NCH = MAC // CH
    HT = HID // 128
    gs = nc.dram_tensor(f"{tag}{b}_gs", [8, Lt], F32, kind="Internal").ap()
    bgs = Buf(f"{tag}gs")

    with contextlib.ExitStack() as stk:
        sb = lambda name, shape, dt_: stk.enter_context(nc.sbuf_tensor(f"{tag}{b}_{name}", shape, dt_))
        gw = sb("gw", [128, DT * E], F32)
        xbg = sb("xbg", [128, DT * 512], F32)
        grows = sb("grows", [8, 512], F32)
        nmx = sb("nmx", [128, 1], F32)
        prob = sb("prob", [128, 8], F32)
        ssum = sb("ssum", [128, 1], F32)
        top8 = sb("top8", [128, 8], F32)
        wtm = sb("wtm", [128, 8], F32)
        pglog = stk.enter_context(nc.psum_tensor(f"{tag}{b}_pgl", [128, 8], F32))
        pgtr = stk.enter_context(nc.psum_tensor(f"{tag}{b}_pgt", [128, 128], F32))
        B = lambda n: Buf(f"{tag}{n}")
        bgw, bxb, bgrows, bsc = B("gw"), B("xbg"), B("grows"), B("sc")
        bpgl, bpgtr = B("pgl"), B("pgtr")
        em.dma("qsp", lambda eng: eng.dma_start(
            out=gw[:, :].rearrange("p (dt e) -> p dt e", dt=DT),
            in_=W["gate"].rearrange("(dt p) e -> p dt e", p=128)),
            reads=(), writes=(bgw,), tag=f"{tag} gw")
        GM = min(512, Lt)
        for m in range(Lt // GM):
            em.dma("qsp", (lambda m=m: lambda eng: eng.dma_start(
                out=xbg[:, 0:DT * GM].rearrange("p (dt c) -> p dt c", dt=DT),
                in_=x_f32.rearrange("(dt p) l -> p dt l", p=128)[:, :, m * GM:(m + 1) * GM]))(),
                reads=(), writes=(bxb,), tag=f"{tag} ldxg")
            for tt in range(GM // 128):
                for dt in range(DT):
                    em.op("pe", (lambda tt=tt, dt=dt: lambda eng: eng.matmul(
                        pglog[:, :], xbg[:, dt * GM + tt * 128: dt * GM + (tt + 1) * 128],
                        gw[:, dt * E:(dt + 1) * E], start=(dt == 0), stop=(dt == DT - 1)))(),
                        reads=(bxb, bgw), writes=(bpgl,), tag=f"{tag} glog")
                em.op("dve", lambda eng: eng.tensor_reduce(nmx[:, :], pglog[:, :], AX.X, ALU.max, negate=True),
                      reads=(bpgl,), writes=(bsc,), tag=f"{tag} gmax")
                em.op("act", lambda eng: eng.activation(prob[:, :], pglog[:, :], AF.Exp,
                                                        bias=nmx[:, :], accum_out=ssum[:, :]),
                      reads=(bpgl, bsc), writes=(bsc,), tag=f"{tag} gexp")
                em.op("dve", lambda eng: eng.reciprocal(ssum[:, :], ssum[:, :]),
                      reads=(bsc,), writes=(bsc,), tag=f"{tag} grecip")
                em.op("dve", lambda eng: eng.tensor_scalar(prob[:, :], prob[:, :], ssum[:, :], None, ALU.mult),
                      reads=(bsc,), writes=(bsc,), tag=f"{tag} gnorm")
                em.op("dve", lambda eng: eng.max(top8[:, :], prob[:, :]),
                      reads=(bsc,), writes=(bsc,), tag=f"{tag} gtop8")
                em.op("dve", lambda eng: eng.tensor_scalar(wtm[:, :], prob[:, :], top8[:, 1:2], None, ALU.is_ge),
                      reads=(bsc,), writes=(bsc,), tag=f"{tag} gmask")
                em.op("dve", lambda eng: eng.tensor_tensor(wtm[:, :], wtm[:, :], prob[:, :], ALU.mult),
                      reads=(bsc,), writes=(bsc,), tag=f"{tag} gmul")
                em.op("dve", lambda eng: eng.tensor_scalar(wtm[:, :], wtm[:, :], 1.0, None,
                                                           ALU.mult, ALU.add, accum_out=ssum[:, :]),
                      reads=(bsc,), writes=(bsc,), tag=f"{tag} gden")
                em.op("dve", lambda eng: eng.tensor_scalar_max(ssum[:, :], ssum[:, :], 1e-12),
                      reads=(bsc,), writes=(bsc,), tag=f"{tag} gdmax")
                em.op("dve", lambda eng: eng.reciprocal(ssum[:, :], ssum[:, :]),
                      reads=(bsc,), writes=(bsc,), tag=f"{tag} gdrecip")
                em.op("dve", lambda eng: eng.tensor_scalar(wtm[:, :], wtm[:, :], ssum[:, :], None, ALU.mult),
                      reads=(bsc,), writes=(bsc,), tag=f"{tag} gwn")
                em.op("pe", lambda eng: eng.transpose(pgtr[0:8, :], wtm[:, :], C.ident32[:, :]),
                      reads=(bsc, C.buf), writes=(bpgtr,), tag=f"{tag} gtr")
                em.op("dve", (lambda tt=tt: lambda eng: eng.tensor_copy(
                    grows[:, tt * 128:(tt + 1) * 128], pgtr[0:8, :]))(),
                    reads=(bpgtr,), writes=(bgrows,), tag=f"{tag} gcp")
            em.dma("qact", (lambda m=m: lambda eng: eng.dma_start(
                out=gs[:, m * GM:(m + 1) * GM], in_=grows[:, 0:GM]))(),
                reads=(bgrows,), writes=(bgs,), tag=f"{tag} gst")
    em.barrier()

    with contextlib.ExitStack() as stk:
        sb = lambda name, shape, dt_: stk.enter_context(nc.sbuf_tensor(f"{tag}{b}_{name}", shape, dt_))
        xb = sb("xb", [128, DT * MAC], BF16)
        xge = [sb(f"xge{i}", [128, DT * CH], BF16) for i in range(2)]
        hsb = [sb(f"hsb{i}", [128, HT * CH], BF16) for i in range(2)]
        acc = sb("acc", [128, DT * MAC], F32)
        w1 = [sb(f"w1_{i}", [128, DT * HID], BF16) for i in range(2)]
        w2 = [sb("w2_0", [128, HT * D], BF16)] * 2
        b1e = [sb(f"b1e{i}", [1, HID], BF16) for i in range(2)]
        b2e = [sb(f"b2e{i}", [1, D], BF16) for i in range(2)]
        gee = [sb(f"ge{i}", [1, MAC], BF16) for i in range(2)]
        pg = stk.enter_context(nc.psum_tensor(f"{tag}{b}_pg", [128, CH], F32))
        ph = [stk.enter_context(nc.psum_tensor(f"{tag}{b}_ph{i}", [128, CH], F32)) for i in range(2)]
        py = [stk.enter_context(nc.psum_tensor(f"{tag}{b}_py{i}", [128, CH], F32)) for i in range(2)]
        B = lambda n: Buf(f"{tag}{n}")
        bxb, bacc = B("xb"), B("acc")
        bxge = [B("xgea"), B("xgeb")]
        bh = [B("ha"), B("hb")]
        bw1 = [B("w1a"), B("w1b")]
        bw2 = [B("w2a")] * 2
        bb1 = [B("b1a"), B("b1b")]
        bb2 = [B("b2a"), B("b2b")]
        bge = [B("gea"), B("geb")]
        bpg = B("pg")
        bph = [B("pha"), B("phb")]
        bpy = [B("pya"), B("pyb")]
        for m in range(NM):
            em.dma("qpool", (lambda m=m: lambda eng: eng.dma_start(
                out=xb[:, :].rearrange("p (dt c) -> p dt c", dt=DT),
                in_=x_f32.rearrange("(dt p) l -> p dt l", p=128)[:, :, m * MAC:(m + 1) * MAC]))(),
                reads=(), writes=(bxb,), tag=f"{tag} ldx2")
            em.op("dve", lambda eng: eng.memset(acc[:, :], 0.0),
                  reads=(), writes=(bacc,), tag=f"{tag} acc0")
            for e in range(E):
                i = e % 2
                em.dma("qsp", (lambda e=e, i=i: lambda eng: eng.dma_start(
                    out=w1[i][:, :].rearrange("p (dt h) -> p dt h", dt=DT),
                    in_=w1bf[e].rearrange("(dt p) h -> p dt h", p=128)))(),
                    reads=(), writes=(bw1[i],), tag=f"{tag} w1 {e}")
                em.dma("qsp", (lambda e=e, i=i: lambda eng: eng.dma_start(
                    out=w2[i][:, :].rearrange("p (ht d) -> p ht d", ht=HT),
                    in_=w2bf[e].rearrange("(ht p) d -> p ht d", p=128)))(),
                    reads=(), writes=(bw2[i],), tag=f"{tag} w2 {e}")
                em.dma("qpool", (lambda e=e, i=i: lambda eng: eng.dma_start(
                    out=b1e[i][:, :], in_=W["b1"][e].unsqueeze(0)))(),
                    reads=(), writes=(bb1[i],), tag=f"{tag} b1 {e}")
                em.dma("qpool", (lambda e=e, i=i: lambda eng: eng.dma_start(
                    out=b2e[i][:, :], in_=W["b2"][e].unsqueeze(0)))(),
                    reads=(), writes=(bb2[i],), tag=f"{tag} b2 {e}")
                em.dma("qpool", (lambda e=e, i=i, m=m: lambda eng: eng.dma_start(
                    out=gee[i][:, :], in_=gs[e:e + 1, m * MAC:(m + 1) * MAC]))(),
                    reads=(bgs,), writes=(bge[i],), tag=f"{tag} ge {e}")
                for cc in range(NCH):
                    k = (e * NCH + cc) % 2
                    em.op("pe", (lambda i=i, cc=cc: lambda eng: eng.matmul(
                        pg[:, :], C.ones16[0:1, 0:128], gee[i][0:1, cc * CH:(cc + 1) * CH],
                        start=True, stop=True))(),
                        reads=(bge[i], C.buf), writes=(bpg,), tag=f"{tag} gbc")
                    for dt in range(DT):
                        em.op("dve", (lambda k=k, cc=cc, dt=dt: lambda eng: eng.tensor_tensor(
                            xge[k][:, dt * CH:(dt + 1) * CH],
                            xb[:, dt * MAC + cc * CH: dt * MAC + (cc + 1) * CH],
                            pg[:, :], ALU.mult))(),
                            reads=(bxb, bpg), writes=(bxge[k],), tag=f"{tag} xge")
                    for ht in range(HT):
                        j = ht % 2
                        for dt in range(DT):
                            em.op("pe", (lambda i=i, j=j, k=k, ht=ht, dt=dt: lambda eng: eng.matmul(
                                ph[j][:, :], w1[i][:, dt * HID + ht * 128: dt * HID + (ht + 1) * 128],
                                xge[k][:, dt * CH:(dt + 1) * CH], start=(dt == 0), stop=False))(),
                                reads=(bw1[i], bxge[k]), writes=(bph[j],), tag=f"{tag} h mm")
                        em.op("pe", (lambda i=i, j=j, ht=ht, cc=cc: lambda eng: eng.matmul(
                            ph[j][:, :], b1e[i][0:1, ht * 128:(ht + 1) * 128],
                            gee[i][0:1, cc * CH:(cc + 1) * CH],
                            start=False, stop=True))(),
                            reads=(bb1[i], bge[i]), writes=(bph[j],), tag=f"{tag} h bias")
                        em.op("act", (lambda j=j, k=k, ht=ht: lambda eng: eng.activation(
                            hsb[k][:, ht * CH:(ht + 1) * CH], ph[j][:, :], AF.Relu))(),
                            reads=(bph[j], bh[k]), writes=(bh[k],), tag=f"{tag} h relu")
                    for dt in range(DT):
                        j = dt % 2
                        for ht in range(HT):
                            em.op("pe", (lambda i=i, j=j, k=k, dt=dt, ht=ht: lambda eng: eng.matmul(
                                py[j][:, :], w2[i][:, ht * D + dt * 128: ht * D + (dt + 1) * 128],
                                hsb[k][:, ht * CH:(ht + 1) * CH], start=(ht == 0), stop=False))(),
                                reads=(bw2[i], bh[k]), writes=(bpy[j],), tag=f"{tag} y mm")
                        em.op("pe", (lambda i=i, j=j, dt=dt, cc=cc: lambda eng: eng.matmul(
                            py[j][:, :], b2e[i][0:1, dt * 128:(dt + 1) * 128],
                            gee[i][0:1, cc * CH:(cc + 1) * CH],
                            start=False, stop=True))(),
                            reads=(bb2[i], bge[i]), writes=(bpy[j],), tag=f"{tag} y bias")
                        em.op("dve", (lambda j=j, dt=dt, cc=cc: lambda eng: eng.tensor_add(
                            acc[:, dt * MAC + cc * CH: dt * MAC + (cc + 1) * CH],
                            acc[:, dt * MAC + cc * CH: dt * MAC + (cc + 1) * CH], py[j][:, :]))(),
                            reads=(bpy[j], bacc), writes=(bacc,), tag=f"{tag} y acc")
            em.dma("qact", (lambda m=m: lambda eng: eng.dma_start(
                out=out_f32.rearrange("(dt p) l -> p dt l", p=128)[:, :, m * MAC:(m + 1) * MAC],
                in_=acc[:, :].rearrange("p (dt c) -> p dt c", dt=DT)))(),
                reads=(bacc,), writes=(), tag=f"{tag} store")



def phase_l1attn(nc, em, C, xs_f32, P, outs_f32, nb=NB, tag="phz"):
    TW = L + 1
    halfpi = C.col(em, float(np.pi / 2))
    with contextlib.ExitStack() as stk:
        sb = lambda name, shape, dt_: stk.enter_context(nc.sbuf_tensor(f"{tag}_{name}", shape, dt_))
        pcols = sb("pc", [128, 7 * DT], F32)
        dcols = sb("dc", [128, 16], F32)
        EC = sb("EC", [128, TW], F32)
        ESn = sb("ESn", [128, TW], F32)
        R = sb("R", [128, TW], F32)
        A = sb("A", [128, L], F32)
        Bt = sb("B", [128, L], F32)
        T = sb("T", [128, L], F32)
        rbc = sb("rbc", [128, L], F32)
        xr = sb("xr", [128, L], F32)
        ur = sb("ur", [128, L], F32)
        ui = sb("ui", [128, L], F32)
        vr = sb("vr", [128, L], F32)
        vi = sb("vi", [128, L], F32)
        bP, bD = Buf("pcols"), Buf("dcols")
        bEC, bES, bR = Buf("EC"), Buf("ESn"), Buf("R")
        bA, bB, bT, brbc = Buf("A"), Buf("B"), Buf("T"), Buf("rbc")
        bx, bur, bui, bvr, bvi = Buf("x"), Buf("ur"), Buf("ui"), Buf("vr"), Buf("vi")
        names = ["angle", "lnm", "scale", "pi_re", "pi_im", "lc_re", "lc_im"]
        for ip, pname in enumerate(names):
            em.dma("qsp", (lambda ip=ip, pname=pname: lambda eng: eng.dma_start(
                out=pcols[:, ip * DT:(ip + 1) * DT], in_=P[pname]))(),
                reads=(), writes=(bP,), tag=f"{tag} ld {pname}")
        coli = {n: ip for ip, n in enumerate(names)}

        def col(n, ct):
            return pcols[:, coli[n] * DT + ct: coli[n] * DT + ct + 1]

        cc = lambda j: dcols[:, j:j + 1]
        for ct in range(DT):
            em.op("act", (lambda ct=ct: lambda eng: eng.activation(cc(0), col("lnm", ct), AF.Exp))(),
                  reads=(bP, bD), writes=(bD,), tag=f"{tag} decay")
            em.op("dve", (lambda ct=ct: lambda eng: eng.tensor_tensor(cc(0), cc(0), col("scale", ct), ALU.mult))(),
                  reads=(bP, bD), writes=(bD,), tag=f"{tag} dscale")
            em.op("act", lambda eng: eng.activation(cc(1), cc(0), AF.Exp, scale=-1.0),
                  reads=(bD,), writes=(bD,), tag=f"{tag} r")
            em.op("dve", lambda eng: eng.tensor_scalar(cc(2), cc(1), -1.0, 1.0, ALU.mult, ALU.add),
                  reads=(bD,), writes=(bD,), tag=f"{tag} gate")
            em.op("dve", (lambda ct=ct: lambda eng: eng.tensor_tensor(cc(3), col("angle", ct), col("scale", ct), ALU.mult))(),
                  reads=(bP, bD), writes=(bD,), tag=f"{tag} theta")
            em.op("act", lambda eng: eng.activation(cc(4), cc(3), AF.Sin, bias=halfpi[0:128, :]),
                  reads=(bD, C.buf), writes=(bD,), tag=f"{tag} C1")
            em.op("act", lambda eng: eng.activation(cc(5), cc(3), AF.Sin),
                  reads=(bD,), writes=(bD,), tag=f"{tag} S1")
            em.op("dve", lambda eng: eng.memset(EC[:, 0:1], 1.0), reads=(bEC,), writes=(bEC,), tag=f"{tag} ec0")
            em.op("dve", lambda eng: eng.memset(ESn[:, 0:1], 0.0), reads=(bES,), writes=(bES,), tag=f"{tag} es0")
            em.op("dve", lambda eng: eng.memset(R[:, 0:1], 1.0), reads=(bR,), writes=(bR,), tag=f"{tag} r0")
            k = 1
            while k <= L:
                w = min(k, TW - k)
                em.op("dve", (lambda k=k: lambda eng: eng.tensor_tensor(cc(8), EC[:, k - 1:k], cc(4), ALU.mult))(),
                      reads=(bEC, bD), writes=(bD,), tag=f"{tag} ck1")
                em.op("dve", (lambda k=k: lambda eng: eng.scalar_tensor_tensor(
                    cc(8), ESn[:, k - 1:k], cc(5), cc(8), ALU.mult, ALU.add))(),
                    reads=(bES, bD), writes=(bD,), tag=f"{tag} ck2")
                em.op("dve", (lambda k=k: lambda eng: eng.tensor_tensor(cc(9), EC[:, k - 1:k], cc(5), ALU.mult))(),
                      reads=(bEC, bD), writes=(bD,), tag=f"{tag} sk1")
                em.op("dve", (lambda k=k: lambda eng: eng.scalar_tensor_tensor(
                    cc(11), ESn[:, k - 1:k], cc(4), cc(9), ALU.mult, ALU.subtract))(),
                    reads=(bES, bD), writes=(bD,), tag=f"{tag} sk2")
                em.op("dve", lambda eng: eng.tensor_scalar(cc(9), cc(11), -1.0, None, ALU.mult),
                      reads=(bD,), writes=(bD,), tag=f"{tag} sk3")
                em.op("dve", (lambda k=k: lambda eng: eng.tensor_tensor(cc(10), R[:, k - 1:k], cc(1), ALU.mult))(),
                      reads=(bR, bD), writes=(bD,), tag=f"{tag} rk")
                em.op("dve", (lambda k=k, w=w: lambda eng: eng.tensor_scalar(
                    EC[:, k:k + w], EC[:, 0:w], cc(8), None, ALU.mult))(),
                    reads=(bEC, bD), writes=(bEC,), tag=f"{tag} ecd1")
                em.op("dve", (lambda k=k, w=w: lambda eng: eng.scalar_tensor_tensor(
                    EC[:, k:k + w], ESn[:, 0:w], cc(9), EC[:, k:k + w], ALU.mult, ALU.add))(),
                    reads=(bES, bD, bEC), writes=(bEC,), tag=f"{tag} ecd2")
                em.op("dve", (lambda k=k, w=w: lambda eng: eng.tensor_scalar(
                    ESn[:, k:k + w], ESn[:, 0:w], cc(8), None, ALU.mult))(),
                    reads=(bES, bD), writes=(bES,), tag=f"{tag} esd1")
                em.op("dve", (lambda k=k, w=w: lambda eng: eng.scalar_tensor_tensor(
                    ESn[:, k:k + w], EC[:, 0:w], cc(9), ESn[:, k:k + w], ALU.mult, ALU.subtract))(),
                    reads=(bEC, bD, bES), writes=(bES,), tag=f"{tag} esd2")
                em.op("dve", (lambda k=k, w=w: lambda eng: eng.tensor_scalar(
                    ESn[:, k:k + w], ESn[:, k:k + w], -1.0, None, ALU.mult))(),
                    reads=(bES,), writes=(bES,), tag=f"{tag} esd3")
                em.op("dve", (lambda k=k, w=w: lambda eng: eng.tensor_scalar(
                    R[:, k:k + w], R[:, 0:w], cc(10), None, ALU.mult))(),
                    reads=(bR, bD), writes=(bR,), tag=f"{tag} rd")
                k *= 2
            em.op("dve", (lambda ct=ct: lambda eng: eng.tensor_scalar(
                A[:, :], EC[:, 0:L], col("pi_re", ct), None, ALU.mult))(),
                reads=(bEC, bP), writes=(bA,), tag=f"{tag} A1")
            em.op("dve", (lambda ct=ct: lambda eng: eng.scalar_tensor_tensor(
                A[:, :], ESn[:, 0:L], col("pi_im", ct), A[:, :], ALU.mult, ALU.add))(),
                reads=(bES, bP, bA), writes=(bA,), tag=f"{tag} A2")
            em.op("dve", (lambda ct=ct: lambda eng: eng.tensor_scalar(
                Bt[:, :], ESn[:, 0:L], col("pi_re", ct), None, ALU.mult))(),
                reads=(bES, bP), writes=(bB,), tag=f"{tag} B1")
            em.op("dve", (lambda ct=ct: lambda eng: eng.scalar_tensor_tensor(
                Bt[:, :], EC[:, 0:L], col("pi_im", ct), Bt[:, :], ALU.mult, ALU.subtract))(),
                reads=(bEC, bP, bB), writes=(bB,), tag=f"{tag} B2")
            em.op("dve", (lambda ct=ct: lambda eng: eng.tensor_scalar(
                T[:, :], EC[:, 1:TW], col("lc_re", ct), None, ALU.mult))(),
                reads=(bEC, bP), writes=(bT,), tag=f"{tag} T1")
            em.op("dve", (lambda ct=ct: lambda eng: eng.scalar_tensor_tensor(
                T[:, :], ESn[:, 1:TW], col("lc_im", ct), T[:, :], ALU.mult, ALU.add))(),
                reads=(bES, bP, bT), writes=(bT,), tag=f"{tag} T2")
            em.op("dve", lambda eng: eng.tensor_tensor(T[:, :], T[:, :], R[:, 1:TW], ALU.mult),
                  reads=(bR, bT), writes=(bT,), tag=f"{tag} T3")
            em.op("dve", lambda eng: eng.tensor_scalar(rbc[:, :], EC[:, 0:L], 0.0, cc(1), ALU.mult, ALU.add),
                  reads=(bEC, bD, brbc), writes=(brbc,), tag=f"{tag} rbc")

            for b in range(nb):
                em.dma("qsp", (lambda ct=ct, b=b: lambda eng: eng.dma_start(
                    out=xr[:, :], in_=xs_f32[b][ct * 128:(ct + 1) * 128, :]))(),
                    reads=(bx,), writes=(bx,), tag=f"{tag} ldx")
                em.op("dve", lambda eng: eng.tensor_mul(ur[:, :], xr[:, :], EC[:, 0:L]),
                      reads=(bx, bEC, bur), writes=(bur,), tag=f"{tag} ur")
                em.op("dve", lambda eng: eng.tensor_mul(ui[:, :], xr[:, :], ESn[:, 0:L]),
                      reads=(bx, bES, bui), writes=(bui,), tag=f"{tag} ui")
                em.op("dve", lambda eng: eng.tensor_tensor_scan(vr[:, :], rbc[:, :], ur[:, :], 0.0, ALU.mult, ALU.add),
                      reads=(brbc, bur, bvr), writes=(bvr,), tag=f"{tag} scanr")
                em.op("dve", lambda eng: eng.tensor_tensor_scan(vi[:, :], rbc[:, :], ui[:, :], 0.0, ALU.mult, ALU.add),
                      reads=(brbc, bui, bvi), writes=(bvi,), tag=f"{tag} scani")
                em.op("dve", lambda eng: eng.tensor_mul(ur[:, :], vr[:, :], A[:, :]),
                      reads=(bvr, bA, bur), writes=(bur,), tag=f"{tag} y1")
                em.op("dve", lambda eng: eng.tensor_mul(ui[:, :], vi[:, :], Bt[:, :]),
                      reads=(bvi, bB, bui), writes=(bui,), tag=f"{tag} y2")
                em.op("dve", lambda eng: eng.tensor_sub(ur[:, :], ur[:, :], ui[:, :]),
                      reads=(bur, bui), writes=(bur,), tag=f"{tag} y3")
                em.op("dve", lambda eng: eng.tensor_add(ur[:, :], ur[:, :], T[:, :]),
                      reads=(bur, bT), writes=(bur,), tag=f"{tag} y4")
                em.op("dve", lambda eng: eng.tensor_scalar(ur[:, :], ur[:, :], cc(2), None, ALU.mult),
                      reads=(bur, bD), writes=(bur,), tag=f"{tag} y5")
                em.dma("qact", (lambda ct=ct, b=b: lambda eng: eng.dma_start(
                    out=outs_f32[b][ct * 128:(ct + 1) * 128, :], in_=ur[:, :]))(),
                    reads=(bur,), writes=(), tag=f"{tag} sty")



FT = FS // 128
BASE_FT = IS // 128


def phase_l1ffn_prep(nc, em, C, W, tag="l1prep", subset=("Wm", "Wp", "Wc1", "Wc2")):
    WmR = nc.dram_tensor(f"{tag}_WmR", [FS, D], BF16, kind="Internal").ap()
    WmT = nc.dram_tensor(f"{tag}_WmT", [D, FS], BF16, kind="Internal").ap()
    WpT = nc.dram_tensor(f"{tag}_WpT", [FS, D], BF16, kind="Internal").ap()
    Wc1T = nc.dram_tensor(f"{tag}_Wc1T", [D, 256], BF16, kind="Internal").ap()
    Wc2T = nc.dram_tensor(f"{tag}_Wc2T", [256, FS], BF16, kind="Internal").ap()
    with contextlib.ExitStack() as stk:
        tin = [stk.enter_context(nc.sbuf_tensor(f"{tag}_i{i}", [128, FS], BF16)) for i in range(2)]
        tout = [stk.enter_context(nc.sbuf_tensor(f"{tag}_o{i}", [128, FS], BF16)) for i in range(2)]
        ps = [stk.enter_context(nc.psum_tensor(f"{tag}_p{i}", [128, 128], BF16)) for i in range(2)]
        bi = [Buf("i0"), Buf("i1")]
        bo = [Buf("o0"), Buf("o1")]
        bp = [Buf("p0"), Buf("p1")]

        def transpose_mat(src, rows, cols, dst, tg, rowcopy=None, nr_tr=None):
            nr, ncol = rows // 128, cols // 128
            for rt in range(nr):
                i = rt % 2
                em.dma("qpool", (lambda rt=rt, i=i: lambda eng: eng.dma_start(
                    out=tin[i][:, 0:cols], in_=src[rt * 128:(rt + 1) * 128, :]))(),
                    reads=(bo[i],), writes=(bi[i],), tag=f"{tg} in")
                for ct_ in range(ncol if (nr_tr is None or rt < nr_tr) else 0):
                    j = ct_ % 2
                    em.op("pe", (lambda i=i, j=j, ct_=ct_: lambda eng: eng.transpose(
                        ps[j][:, :], tin[i][:, ct_ * 128:(ct_ + 1) * 128], C.ident16[:, :]))(),
                        reads=(bi[i], bp[j], C.buf), writes=(bp[j],), tag=f"{tg} tr")
                    em.op("dve", (lambda i=i, j=j, ct_=ct_: lambda eng: eng.tensor_copy(
                        tout[i][:, ct_ * 128:(ct_ + 1) * 128], ps[j][:, :]))(),
                        reads=(bp[j], bo[i]), writes=(bo[i],), tag=f"{tg} ev")
                if nr_tr is None or rt < nr_tr:
                    em.dma("qact", (lambda rt=rt, i=i, ncol=ncol: lambda eng: eng.dma_start(
                        out=dst.rearrange("(ct p) r -> p ct r", p=128)[:, :, rt * 128:(rt + 1) * 128],
                        in_=tout[i][:, 0:cols].rearrange("p (ct c) -> p ct c", ct=ncol)))(),
                        reads=(bo[i],), writes=(), tag=f"{tg} out")
                if rowcopy is not None:
                    em.dma("qact", (lambda rt=rt, i=i, cols=cols: lambda eng: eng.dma_start(
                        out=rowcopy[rt * 128:(rt + 1) * 128, :], in_=tin[i][:, 0:cols]))(),
                        reads=(bi[i],), writes=(), tag=f"{tg} rowc")

        if "Wm" in subset:
            ntr = BASE_FT if "Wm32" in subset else (0 if "Wm0" in subset else None)
            transpose_mat(W["Wm"], FS, D, WmT, f"{tag}m", rowcopy=WmR, nr_tr=ntr)
        if "Wp" in subset:
            transpose_mat(W["Wp"], D, FS, WpT, f"{tag}p")
        if "Wc1" in subset:
            transpose_mat(W["Wc1"], 256, D, Wc1T, f"{tag}c1")
        if "Wc2" in subset:
            transpose_mat(W["Wc2"], FS, 256, Wc2T, f"{tag}c2")
    return dict(WmT=WmT, WpT=WpT, Wc1T=Wc1T, Wc2T=Wc2T, WmR=WmR)


def emit_bisect(nc, em, C, stk, ck_sb, bck, mask_sb, bmask, k_count, tag, nit=45):
    sc = stk.enter_context(nc.sbuf_tensor(f"{tag}_bs", [1, 8], F32))
    colb = stk.enter_context(nc.sbuf_tensor(f"{tag}_bc", [128, 3], F32))
    scr = stk.enter_context(nc.sbuf_tensor(f"{tag}_bscr", [128, FT], F32))
    ptr1 = stk.enter_context(nc.psum_tensor(f"{tag}_bp1", [1, 128], F32))
    pbc = stk.enter_context(nc.psum_tensor(f"{tag}_bp2", [128, 1], F32))
    pcnt = stk.enter_context(nc.psum_tensor(f"{tag}_bp3", [1, 1], F32))
    bs, bcol, bscr = Buf("bs"), Buf("bcol"), Buf("bscr")
    bp1, bp2, bp3 = Buf("bp1"), Buf("bp2"), Buf("bp3")
    em.op("dve", lambda eng: eng.tensor_reduce(colb[:, 0:1], ck_sb[:, :], AX.X, ALU.max),
          reads=(bck, bcol), writes=(bcol,), tag=f"{tag} mx1")
    em.op("dve", lambda eng: eng.tensor_reduce(colb[:, 1:2], ck_sb[:, :], AX.X, ALU.min),
          reads=(bck, bcol), writes=(bcol,), tag=f"{tag} mn1")
    em.op("pe", lambda eng: eng.transpose(ptr1[0:1, :], colb[:, 0:1], C.ident32[:, :]),
          reads=(bcol, C.buf, bp1), writes=(bp1,), tag=f"{tag} tr1")
    em.op("dve", lambda eng: eng.tensor_reduce(sc[:, 1:2], ptr1[0:1, :], AX.X, ALU.max),
          reads=(bp1, bs), writes=(bs,), tag=f"{tag} mx2")
    em.op("pe", lambda eng: eng.transpose(ptr1[0:1, :], colb[:, 1:2], C.ident32[:, :]),
          reads=(bcol, C.buf, bp1, bs), writes=(bp1,), tag=f"{tag} tr2")
    em.op("dve", lambda eng: eng.tensor_reduce(sc[:, 0:1], ptr1[0:1, :], AX.X, ALU.min),
          reads=(bp1, bs), writes=(bs,), tag=f"{tag} mn2")
    em.op("dve", lambda eng: eng.tensor_scalar(sc[:, 0:1], sc[:, 0:1], 1.0, None, ALU.subtract),
          reads=(bs,), writes=(bs,), tag=f"{tag} lo-")
    em.op("dve", lambda eng: eng.tensor_scalar(sc[:, 1:2], sc[:, 1:2], 1.0, None, ALU.add),
          reads=(bs,), writes=(bs,), tag=f"{tag} hi+")
    for it in range(nit):
        em.op("dve", lambda eng: eng.tensor_add(sc[:, 2:3], sc[:, 0:1], sc[:, 1:2]),
              reads=(bs,), writes=(bs,), tag=f"{tag} mid1")
        em.op("dve", lambda eng: eng.tensor_scalar(sc[:, 2:3], sc[:, 2:3], 0.5, None, ALU.mult),
              reads=(bs,), writes=(bs,), tag=f"{tag} mid2")
        em.op("pe", lambda eng: eng.matmul(pbc[:, :], C.ones_row[0:1, 0:128], sc[0:1, 2:3],
                                           start=True, stop=True),
              reads=(bs, C.buf, bp2), writes=(bp2,), tag=f"{tag} bcmid")
        em.op("dve", lambda eng: eng.tensor_copy(colb[:, 2:3], pbc[:, :]),
              reads=(bp2, bcol), writes=(bcol,), tag=f"{tag} midcol")
        em.op("dve", lambda eng: eng.tensor_scalar(scr[:, :], ck_sb[:, :], colb[:, 2:3], None,
                                                   ALU.is_ge, ALU.add, accum_out=colb[:, 0:1]),
              reads=(bck, bcol, bscr), writes=(bscr, bcol), tag=f"{tag} cmp")
        em.op("pe", lambda eng: eng.matmul(pcnt[:, :], C.ones_col[:, :], colb[:, 0:1],
                                           start=True, stop=True),
              reads=(bcol, C.buf, bp3), writes=(bp3,), tag=f"{tag} cnt")
        em.op("dve", (lambda k_count=k_count: lambda eng: eng.tensor_scalar(
            sc[:, 3:4], pcnt[0:1, 0:1], float(k_count), None, ALU.is_gt))(),
            reads=(bp3, bs), writes=(bs,), tag=f"{tag} gt")
        em.op("dve", lambda eng: eng.tensor_sub(sc[:, 4:5], sc[:, 2:3], sc[:, 0:1]),
              reads=(bs,), writes=(bs,), tag=f"{tag} lod")
        em.op("dve", lambda eng: eng.tensor_scalar(sc[:, 4:5], sc[:, 4:5], sc[0:1, 3:4], None, ALU.mult),
              reads=(bs,), writes=(bs,), tag=f"{tag} lom")
        em.op("dve", lambda eng: eng.tensor_add(sc[:, 0:1], sc[:, 0:1], sc[:, 4:5]),
              reads=(bs,), writes=(bs,), tag=f"{tag} lo")
        em.op("dve", lambda eng: eng.tensor_sub(sc[:, 4:5], sc[:, 1:2], sc[:, 2:3]),
              reads=(bs,), writes=(bs,), tag=f"{tag} hid")
        em.op("dve", lambda eng: eng.tensor_scalar(sc[:, 4:5], sc[:, 4:5], sc[0:1, 3:4], None, ALU.mult),
              reads=(bs,), writes=(bs,), tag=f"{tag} him")
        em.op("dve", lambda eng: eng.tensor_add(sc[:, 1:2], sc[:, 2:3], sc[:, 4:5]),
              reads=(bs,), writes=(bs,), tag=f"{tag} hi")
    em.op("pe", lambda eng: eng.matmul(pbc[:, :], C.ones_row[0:1, 0:128], sc[0:1, 1:2],
                                       start=True, stop=True),
          reads=(bs, C.buf, bp2), writes=(bp2,), tag=f"{tag} bchi")
    em.op("dve", lambda eng: eng.tensor_copy(colb[:, 2:3], pbc[:, :]),
          reads=(bp2, bcol), writes=(bcol,), tag=f"{tag} hicol")
    em.op("dve", lambda eng: eng.tensor_scalar(mask_sb[:, :], ck_sb[:, :], colb[:, 2:3], None, ALU.is_ge),
          reads=(bck, bcol, bmask), writes=(bmask,), tag=f"{tag} mask")


def phase_l1ffn_step(nc, em, C, xin_bf, Wsc, xout_bf, xout_f32, b, step,
                     active_dram=None, mask_new_dram=None, pooled_dram=None,
                     Lt=L, tag=None):
    import os
    _off = set(os.environ.get("L1F_OFF", "").split(","))
    """One scan step of L1 ffn for batch b.
    xin_bf: DRAM [D, L] bf16 input activations (f-major).
    xout_bf/xout_f32: DRAM [D, L] outputs (x_new).
    step==1: active = first IS features (tile-aligned); else active_dram [128, FT].
    If mask_new_dram: compute pooled->ck->bisect mask + novelty counts."""
    tag = tag or f"l1s{step}b{b}"
    CH = 512
    NCH = Lt // CH
    do_ck = pooled_dram is not None
    with contextlib.ExitStack() as stk:
        sb = lambda name, shape, dt_: stk.enter_context(nc.sbuf_tensor(f"{tag}_{name}", shape, dt_))
        xcb = sb("xcb", [128, DT * CH], BF16)
        hs = sb("hs", [128, FT * CH], BF16)
        wm = [sb(f"wm{i}", [128, DT * 512], BF16) for i in range(2)]
        wp = [sb(f"wp{i}", [128, FT * 128], BF16) for i in range(2)]
        xnf = sb("xnf", [128, DT * CH], F32)
        xnb = sb("xnb", [128, DT * CH], BF16)
        amask = sb("amask", [128, FT], F32)
        pooled = sb("pooled", [128, 2 * DT], F32)
        ph = [stk.enter_context(nc.psum_tensor(f"{tag}_ph{i}", [128, CH], F32)) for i in range(2)]
        px = [stk.enter_context(nc.psum_tensor(f"{tag}_px{i}", [128, CH], F32)) for i in range(2)]
        bxcb, bhs, bxnf, bxnb = Buf("xcb"), Buf("hs"), Buf("xnf"), Buf("xnb")
        bwm = [Buf("wm0"), Buf("wm1")]
        bwp = [Buf("wp0"), Buf("wp1")]
        bam, bpool = Buf("amask"), Buf("pooled")
        bph = [Buf("ph0"), Buf("ph1")]
        bpx = [Buf("px0"), Buf("px1")]
        if step == 1:
            ftl = list(range(BASE_FT))
        else:
            ftl = list(range(FT))
            em.dma("qsp", lambda eng: eng.dma_start(out=amask[:, :], in_=active_dram),
                   reads=(), writes=(bam,), tag=f"{tag} ldmask")
        em.op("dve", lambda eng: eng.memset(pooled[:, :], 0.0),
              reads=(), writes=(bpool,), tag=f"{tag} pool0")
        for c in range(NCH):
            em.dma("qsp", (lambda c=c: lambda eng: eng.dma_start(
                out=xcb[:, :].rearrange("p (dt t) -> p dt t", dt=DT),
                in_=xin_bf.rearrange("(dt p) l -> p dt l", p=128)[:, :, c * CH:(c + 1) * CH]))(),
                reads=(bxcb,), writes=(bxcb,), tag=f"{tag} ldx")
            for fqi, fq in enumerate(range(0, len(ftl), 4)):
                i = fqi % 2
                f0 = ftl[fq]
                em.dma("qsp", (lambda f0=f0, i=i: lambda eng: eng.dma_start(
                    out=wm[i][:, :].rearrange("p (dt f) -> p dt f", dt=DT),
                    in_=Wsc["WmT"].rearrange("(dt p) f -> p dt f", p=128)[:, :, f0 * 128:(f0 + 4) * 128]))(),
                    reads=(bwm[i],), writes=(bwm[i],), tag=f"{tag} wm")
                for fi in range(4):
                    ft = ftl[fq + fi]
                    j = ft % 2
                    for dt in range(DT):
                        em.op("pe", (lambda i=i, j=j, fi=fi, dt=dt: lambda eng: eng.matmul(
                            ph[j][:, :], wm[i][:, dt * 512 + fi * 128: dt * 512 + (fi + 1) * 128],
                            xcb[:, dt * CH:(dt + 1) * CH], start=(dt == 0), stop=(dt == DT - 1)))(),
                            reads=(bwm[i], bxcb), writes=(bph[j],), tag=f"{tag} hmm")
                    if step == 1:
                        em.op("act", (lambda j=j, ft=ft: lambda eng: eng.activation(
                            hs[:, ft * CH:(ft + 1) * CH], ph[j][:, :], AF.Relu))(),
                            reads=(bph[j], bhs), writes=(bhs,), tag=f"{tag} relu")
                    else:
                        em.op("act", (lambda j=j, ft=ft: lambda eng: eng.activation(
                            hs[:, ft * CH:(ft + 1) * CH], ph[j][:, :], AF.Relu,
                            scale=amask[:, ft:ft + 1]))(),
                            reads=(bph[j], bam, bhs), writes=(bhs,), tag=f"{tag} relum")
            NFT = len(ftl)
            for ct in range(DT if "xphase" not in _off else 0):
                i = ct % 2
                NQ = 4 if NFT >= 4 else 1
                FQ = NFT // NQ
                for q in range(NQ):
                    em.dma("qact", (lambda ct=ct, i=i, q=q, FQ=FQ: lambda eng: eng.dma_start(
                        out=wp[i][:, (ftl[0] + q * FQ) * 128:(ftl[0] + (q + 1) * FQ) * 128].rearrange("p (ft c) -> p ft c", ft=FQ),
                        in_=Wsc["WpT"].rearrange("(ft p) c -> p ft c", p=128)[:, ftl[0] + q * FQ: ftl[0] + (q + 1) * FQ, ct * 128:(ct + 1) * 128]))(),
                        reads=(bwp[i],), writes=(bwp[i],), tag=f"{tag} wp{q}")
                for fi, ft in enumerate(ftl):
                    em.op("pe", (lambda i=i, ft=ft: lambda eng: eng.matmul(
                        px[i][:, :], wp[i][:, ft * 128:(ft + 1) * 128],
                        hs[:, ft * CH:(ft + 1) * CH], start=(ft == ftl[0]), stop=(ft == ftl[-1])))(),
                        reads=(bwp[i], bhs), writes=(bpx[i],), tag=f"{tag} xmm")
                if "evf" not in _off:
                    em.op("dve", (lambda i=i, ct=ct: lambda eng: eng.tensor_copy(
                        xnf[:, ct * CH:(ct + 1) * CH], px[i][:, :]))(),
                        reads=(bpx[i], bxnf), writes=(bxnf,), tag=f"{tag} evf")
                if "evb" not in _off:
                    em.op("act", (lambda i=i, ct=ct: lambda eng: eng.copy(
                        xnb[:, ct * CH:(ct + 1) * CH], xnf[:, ct * CH:(ct + 1) * CH]))(),
                        reads=(bxnf, bxnb), writes=(bxnb,), tag=f"{tag} evb")
                if do_ck and "pool" not in _off:
                    em.op("dve", (lambda ct=ct: lambda eng: eng.tensor_reduce(
                        pooled[:, DT + ct:DT + ct + 1],
                        xnf[:, ct * CH:(ct + 1) * CH], AX.X, ALU.add))(),
                        reads=(bxnf, bpool), writes=(bpool,), tag=f"{tag} poolred")
                    em.op("dve", (lambda ct=ct: lambda eng: eng.tensor_add(
                        pooled[:, ct:ct + 1], pooled[:, ct:ct + 1],
                        pooled[:, DT + ct:DT + ct + 1]))(),
                        reads=(bpool,), writes=(bpool,), tag=f"{tag} pooladd")
            if "stf" not in _off and "xphase" not in _off:
                em.dma("qact", (lambda c=c: lambda eng: eng.dma_start(
                    out=xout_f32.rearrange("(dt p) l -> p dt l", p=128)[:, :, c * CH:(c + 1) * CH],
                    in_=xnf[:, :].rearrange("p (dt t) -> p dt t", dt=DT)))(),
                    reads=(bxnf,), writes=(), tag=f"{tag} stf")
            if "stb" not in _off and "xphase" not in _off:
                em.dma("qact", (lambda c=c: lambda eng: eng.dma_start(
                    out=xout_bf.rearrange("(dt p) l -> p dt l", p=128)[:, :, c * CH:(c + 1) * CH],
                    in_=xnb[:, :].rearrange("p (dt t) -> p dt t", dt=DT)))(),
                    reads=(bxnb,), writes=(), tag=f"{tag} stb")
        if do_ck and "pool" not in _off:
            em.op("dve", lambda eng: eng.tensor_scalar(pooled[:, 0:DT], pooled[:, 0:DT],
                                                       1.0 / Lt, None, ALU.mult),
                  reads=(bpool,), writes=(bpool,), tag=f"{tag} poolavg")
            em.dma("qact", lambda eng: eng.dma_start(out=pooled_dram, in_=pooled[:, 0:DT]),
                   reads=(bpool,), writes=(), tag=f"{tag} poolst")


def phase_l1ffn_ck(nc, em, C, pooled_dram, Wsc, mask_new_dram, hist_in_dram,
                   hist_out_dram, nov_out_dram, tag):
    with contextlib.ExitStack() as stk:
        sb = lambda name, shape, dt_: stk.enter_context(nc.sbuf_tensor(f"{tag}_{name}", shape, dt_))
        pooled = sb("pooled", [128, DT], F32)
        pooled_bf = sb("poolbf", [128, DT], BF16)
        wc1 = sb("wc1", [128, DT * 256], BF16)
        wc2 = sb("wc2", [128, 2 * (FS // 2)], BF16)
        ubf = sb("ubf", [128, 2], BF16)
        cks = sb("cks", [128, FT], F32)
        msk = sb("msk", [128, FT], F32)
        hist = sb("hist", [128, FT], F32)
        scr = sb("scr", [128, FT], F32)
        nov = sb("nov", [1, 1], F32)
        pu = stk.enter_context(nc.psum_tensor(f"{tag}_pu", [128, 2], F32))
        pck = stk.enter_context(nc.psum_tensor(f"{tag}_pck", [128, FT], F32))
        pnv = stk.enter_context(nc.psum_tensor(f"{tag}_pnv", [1, 1], F32))
        bpool, bpbf, bwc1, bwc2 = Buf("pool"), Buf("pbf"), Buf("wc1"), Buf("wc2")
        bu, bck, bmsk, bhist, bscr, bnov = Buf("u"), Buf("ck"), Buf("msk"), Buf("hist"), Buf("scr"), Buf("nov")
        bpu, bpck, bpnv = Buf("pu"), Buf("pck"), Buf("pnv")
        em.dma("qsp", lambda eng: eng.dma_start(out=pooled[:, :], in_=pooled_dram),
               reads=(), writes=(bpool,), tag=f"{tag} ldpool")
        em.op("dve", lambda eng: eng.tensor_copy(pooled_bf[:, :], pooled[:, :]),
              reads=(bpool,), writes=(bpbf,), tag=f"{tag} poolbf")
        em.dma("qsp", lambda eng: eng.dma_start(
            out=wc1[:, :].rearrange("p (ct r) -> p ct r", ct=DT),
            in_=Wsc["Wc1T"].rearrange("(ct p) r -> p ct r", p=128)),
            reads=(), writes=(bwc1,), tag=f"{tag} ldwc1")
        for rt in range(2):
            for ct in range(DT):
                em.op("pe", (lambda rt=rt, ct=ct: lambda eng: eng.matmul(
                    pu[:, rt:rt + 1], wc1[:, ct * 256 + rt * 128: ct * 256 + (rt + 1) * 128],
                    pooled_bf[:, ct:ct + 1],
                    start=(rt == 0 and ct == 0), stop=(rt == 1 and ct == DT - 1)))(),
                    reads=(bwc1, bpbf), writes=(bpu,), tag=f"{tag} umm")
        em.op("act", lambda eng: eng.activation(ubf[:, :], pu[:, :], AF.Relu),
              reads=(bpu,), writes=(bu,), tag=f"{tag} urelu")
        FH = FT // 2
        for half in range(2):
            em.dma("qsp", (lambda half=half: lambda eng: eng.dma_start(
                out=wc2[:, :].rearrange("p (rt f) -> p rt f", rt=2),
                in_=Wsc["Wc2T"].rearrange("(rt p) f -> p rt f", p=128)[:, :, half * (FS // 2):(half + 1) * (FS // 2)]))(),
                reads=(bwc2,), writes=(bwc2,), tag=f"{tag} ldwc2")
            for fi in range(FH):
                ft = half * FH + fi
                for rt in range(2):
                    em.op("pe", (lambda fi=fi, ft=ft, rt=rt, half=half: lambda eng: eng.matmul(
                        pck[:, ft:ft + 1], wc2[:, rt * (FS // 2) + fi * 128: rt * (FS // 2) + (fi + 1) * 128],
                        ubf[:, rt:rt + 1],
                        start=(ft == 0 and rt == 0), stop=(ft == FT - 1 and rt == 1)))(),
                        reads=(bwc2, bu), writes=(bpck,), tag=f"{tag} ckmm")
        em.op("dve", lambda eng: eng.tensor_copy(cks[:, :], pck[:, :]),
              reads=(bpck,), writes=(bck,), tag=f"{tag} ckev")
        emit_bisect(nc, em, C, stk, cks, bck, msk, bmsk, IS, tag)
        em.dma("qact", lambda eng: eng.dma_start(out=mask_new_dram, in_=msk[:, :]),
               reads=(bmsk,), writes=(), tag=f"{tag} stmask")
        if hist_in_dram is None:
            em.op("dve", lambda eng: eng.memset(hist[:, 0:BASE_FT], 1.0),
                  reads=(), writes=(bhist,), tag=f"{tag} hbase1")
            em.op("dve", lambda eng: eng.memset(hist[:, BASE_FT:FT], 0.0),
                  reads=(bhist,), writes=(bhist,), tag=f"{tag} hbase0")
        else:
            em.dma("qsp", lambda eng: eng.dma_start(out=hist[:, :], in_=hist_in_dram),
                   reads=(), writes=(bhist,), tag=f"{tag} ldhist")
        em.op("dve", lambda eng: eng.tensor_max(scr[:, :], hist[:, :], msk[:, :]),
              reads=(bhist, bmsk), writes=(bscr,), tag=f"{tag} comb")
        if hist_out_dram is not None:
            em.dma("qact", lambda eng: eng.dma_start(out=hist_out_dram, in_=scr[:, :]),
                   reads=(bscr,), writes=(), tag=f"{tag} sthist")
        em.op("dve", lambda eng: eng.tensor_sub(hist[:, :], scr[:, :], hist[:, :]),
              reads=(bscr, bhist), writes=(bhist,), tag=f"{tag} delta")
        em.op("dve", lambda eng: eng.tensor_scalar(hist[:, :], hist[:, :], 1.0, None,
                                                   ALU.mult, ALU.add, accum_out=scr[:, 0:1]),
              reads=(bhist, bscr), writes=(bhist, bscr), tag=f"{tag} dcount")
        em.op("pe", lambda eng: eng.matmul(pnv[:, :], C.ones_col[:, :], scr[:, 0:1],
                                           start=True, stop=True),
              reads=(bscr, C.buf, bpnv), writes=(bpnv,), tag=f"{tag} dsum")
        em.op("dve", lambda eng: eng.tensor_copy(nov[:, :], pnv[:, :]),
              reads=(bpnv, bnov), writes=(bnov,), tag=f"{tag} novev")
        em.dma("qact", lambda eng: eng.dma_start(out=nov_out_dram, in_=nov[:, :]),
               reads=(bnov,), writes=(), tag=f"{tag} stnov")


def phase_l1ffn_done(nc, em, nov_all_dram, n_scalars_per_step, d_dram, tag="done"):
    thr = RNOV * 16 * IS
    n = n_scalars_per_step
    with contextlib.ExitStack() as stk:
        novr = stk.enter_context(nc.sbuf_tensor(f"{tag}_n", [1, 2 * n], F32))
        dd = stk.enter_context(nc.sbuf_tensor(f"{tag}_d", [1, 4], F32))
        bn, bd = Buf("novr"), Buf("dd")
        em.dma("qsp", lambda eng: eng.dma_start(
            out=novr[:, :], in_=nov_all_dram.rearrange("a b -> (a b)").unsqueeze(0)),
            reads=(), writes=(bn,), tag=f"{tag} ld")
        em.op("dve", lambda eng: eng.tensor_reduce(dd[:, 0:1], novr[0:1, 0:n], AX.X, ALU.add),
              reads=(bn, bd), writes=(bd,), tag=f"{tag} s1")
        em.op("dve", lambda eng: eng.tensor_reduce(dd[:, 1:2], novr[0:1, n:2 * n], AX.X, ALU.add),
              reads=(bn, bd), writes=(bd,), tag=f"{tag} s2")
        em.op("dve", (lambda thr=thr: lambda eng: eng.tensor_scalar(
            dd[:, 0:2], dd[:, 0:2], float(thr), None, ALU.is_lt))(),
            reads=(bd,), writes=(bd,), tag=f"{tag} lt")
        em.op("dve", lambda eng: eng.tensor_max(dd[:, 1:2], dd[:, 1:2], dd[:, 0:1]),
              reads=(bd,), writes=(bd,), tag=f"{tag} or")
        em.dma("qact", lambda eng: eng.dma_start(out=d_dram, in_=dd[:, 0:2]),
               reads=(bd,), writes=(), tag=f"{tag} st")


def phase_l1ffn_active3(nc, em, C, mask1_dram, mask2_dram, d_dram, active3_dram, tag="act3"):
    with contextlib.ExitStack() as stk:
        m1 = stk.enter_context(nc.sbuf_tensor(f"{tag}_m1", [128, FT], F32))
        m2 = stk.enter_context(nc.sbuf_tensor(f"{tag}_m2", [128, FT], F32))
        dd = stk.enter_context(nc.sbuf_tensor(f"{tag}_d", [1, 2], F32))
        dcol = stk.enter_context(nc.sbuf_tensor(f"{tag}_dc", [128, 1], F32))
        pb = stk.enter_context(nc.psum_tensor(f"{tag}_pb", [128, 1], F32))
        bm1, bm2, bd, bdc, bpb = Buf("m1"), Buf("m2"), Buf("d"), Buf("dc"), Buf("pb")
        em.dma("qsp", lambda eng: eng.dma_start(out=m1[:, :], in_=mask1_dram),
               reads=(), writes=(bm1,), tag=f"{tag} l1")
        em.dma("qsp", lambda eng: eng.dma_start(out=m2[:, :], in_=mask2_dram),
               reads=(), writes=(bm2,), tag=f"{tag} l2")
        em.dma("qsp", lambda eng: eng.dma_start(out=dd[:, :], in_=d_dram),
               reads=(), writes=(bd,), tag=f"{tag} ld")
        em.op("pe", lambda eng: eng.matmul(pb[:, :], C.ones_row[0:1, 0:128], dd[0:1, 0:1],
                                           start=True, stop=True),
              reads=(bd, C.buf, bpb), writes=(bpb,), tag=f"{tag} bc")
        em.op("dve", lambda eng: eng.tensor_copy(dcol[:, :], pb[:, :]),
              reads=(bpb, bdc), writes=(bdc,), tag=f"{tag} dcol")
        em.op("dve", lambda eng: eng.tensor_sub(m1[:, :], m1[:, :], m2[:, :]),
              reads=(bm1, bm2), writes=(bm1,), tag=f"{tag} sub")
        em.op("dve", lambda eng: eng.tensor_scalar(m1[:, :], m1[:, :], dcol[:, :], None, ALU.mult),
              reads=(bm1, bdc), writes=(bm1,), tag=f"{tag} mul")
        em.op("dve", lambda eng: eng.tensor_add(m1[:, :], m1[:, :], m2[:, :]),
              reads=(bm1, bm2), writes=(bm1,), tag=f"{tag} add")
        em.dma("qact", lambda eng: eng.dma_start(out=active3_dram, in_=m1[:, :]),
               reads=(bm1,), writes=(), tag=f"{tag} st")


def phase_l1ffn_blend(nc, em, C, xn1, xn2, xn3, d_dram, xfin_f32, Lt=L, tag="blend"):
    CH = 512
    with contextlib.ExitStack() as stk:
        sb = lambda name, shape, dt_: stk.enter_context(nc.sbuf_tensor(f"{tag}_{name}", shape, dt_))
        t1 = sb("t1", [128, DT * CH], F32)
        t2 = sb("t2", [128, DT * CH], F32)
        t3 = sb("t3", [128, DT * CH], F32)
        dd = sb("d", [1, 2], F32)
        dcol = sb("dc", [128, 2], F32)
        pb = stk.enter_context(nc.psum_tensor(f"{tag}_pb", [128, 2], F32))
        b1, b2, b3, bd, bdc, bpb = Buf("t1"), Buf("t2"), Buf("t3"), Buf("d"), Buf("dc"), Buf("pb")
        em.dma("qsp", lambda eng: eng.dma_start(out=dd[:, :], in_=d_dram),
               reads=(), writes=(bd,), tag=f"{tag} ld")
        em.op("pe", lambda eng: eng.matmul(pb[:, 0:1], C.ones_row[0:1, 0:128], dd[0:1, 0:1],
                                           start=True, stop=False),
              reads=(bd, C.buf, bpb), writes=(bpb,), tag=f"{tag} bc1")
        em.op("pe", lambda eng: eng.matmul(pb[:, 1:2], C.ones_row[0:1, 0:128], dd[0:1, 1:2],
                                           start=False, stop=True),
              reads=(bd, C.buf, bpb), writes=(bpb,), tag=f"{tag} bc2")
        em.op("dve", lambda eng: eng.tensor_copy(dcol[:, :], pb[:, :]),
              reads=(bpb, bdc), writes=(bdc,), tag=f"{tag} dcol")
        for c in range(Lt // CH):
            for (t, bt, src) in ((t1, b1, xn1), (t2, b2, xn2), (t3, b3, xn3)):
                em.dma("qsp", (lambda t=t, src=src, c=c: lambda eng: eng.dma_start(
                    out=t[:, :].rearrange("p (dt w) -> p dt w", dt=DT),
                    in_=src.rearrange("(dt p) l -> p dt l", p=128)[:, :, c * CH:(c + 1) * CH]))(),
                    reads=(bt,), writes=(bt,), tag=f"{tag} ldx")
            em.op("dve", lambda eng: eng.tensor_sub(t1[:, :], t1[:, :], t2[:, :]),
                  reads=(b1, b2), writes=(b1,), tag=f"{tag} s1")
            em.op("dve", lambda eng: eng.tensor_scalar(t1[:, :], t1[:, :], dcol[:, 0:1], None, ALU.mult),
                  reads=(b1, bdc), writes=(b1,), tag=f"{tag} m1")
            em.op("dve", lambda eng: eng.tensor_add(t1[:, :], t1[:, :], t2[:, :]),
                  reads=(b1, b2), writes=(b1,), tag=f"{tag} a1")
            em.op("dve", lambda eng: eng.tensor_sub(t1[:, :], t1[:, :], t3[:, :]),
                  reads=(b1, b3), writes=(b1,), tag=f"{tag} s2")
            em.op("dve", lambda eng: eng.tensor_scalar(t1[:, :], t1[:, :], dcol[:, 1:2], None, ALU.mult),
                  reads=(b1, bdc), writes=(b1,), tag=f"{tag} m2")
            em.op("dve", lambda eng: eng.tensor_add(t1[:, :], t1[:, :], t3[:, :]),
                  reads=(b1, b3), writes=(b1,), tag=f"{tag} a2")
            em.dma("qact", (lambda c=c: lambda eng: eng.dma_start(
                out=xfin_f32.rearrange("(dt p) l -> p dt l", p=128)[:, :, c * CH:(c + 1) * CH],
                in_=t1[:, :].rearrange("p (dt w) -> p dt w", dt=DT)))(),
                reads=(b1,), writes=(), tag=f"{tag} st")



def l0_generalize_bisect():
    pass


def emit_bisect2(nc, em, C, stk, vals, bvals, mask_sb, bmask, P, W, k_count, tag, nit=45):
    sc = stk.enter_context(nc.sbuf_tensor(f"{tag}_bs", [1, 8], F32))
    colb = stk.enter_context(nc.sbuf_tensor(f"{tag}_bc", [128, 3], F32))
    scr = stk.enter_context(nc.sbuf_tensor(f"{tag}_bscr", [P, W], F32))
    ptr1 = stk.enter_context(nc.psum_tensor(f"{tag}_bp1", [1, 128], F32))
    pbc = stk.enter_context(nc.psum_tensor(f"{tag}_bp2", [128, 1], F32))
    pcnt = stk.enter_context(nc.psum_tensor(f"{tag}_bp3", [1, 1], F32))
    bs, bcol, bscr = Buf("bs"), Buf("bcol"), Buf("bscr")
    bp1, bp2, bp3 = Buf("bp1"), Buf("bp2"), Buf("bp3")
    em.op("dve", lambda eng: eng.memset(colb[:, :], 0.0),
          reads=(bcol,), writes=(bcol,), tag=f"{tag} colz")
    em.op("dve", lambda eng: eng.tensor_reduce(colb[0:P, 0:1], vals[:, :], AX.X, ALU.max),
          reads=(bvals, bcol), writes=(bcol,), tag=f"{tag} mx1")
    em.op("dve", lambda eng: eng.tensor_reduce(colb[0:P, 1:2], vals[:, :], AX.X, ALU.min),
          reads=(bvals, bcol), writes=(bcol,), tag=f"{tag} mn1")
    if P < 128:
        em.op("dve", (lambda: lambda eng: eng.tensor_copy(colb[P:128, 0:2], colb[0:1, 0:2].partition_broadcast(128 - P)))() if False else
              (lambda: lambda eng: eng.memset(colb[P:128, 0:1], -3e38))(),
              reads=(bcol,), writes=(bcol,), tag=f"{tag} padmx")
        em.op("dve", lambda eng: eng.memset(colb[P:128, 1:2], 3e38),
              reads=(bcol,), writes=(bcol,), tag=f"{tag} padmn")
    em.op("pe", lambda eng: eng.transpose(ptr1[0:1, :], colb[:, 0:1], C.ident32[:, :]),
          reads=(bcol, C.buf, bp1), writes=(bp1,), tag=f"{tag} tr1")
    em.op("dve", lambda eng: eng.tensor_reduce(sc[:, 1:2], ptr1[0:1, :], AX.X, ALU.max),
          reads=(bp1, bs), writes=(bs,), tag=f"{tag} mx2")
    em.op("pe", lambda eng: eng.transpose(ptr1[0:1, :], colb[:, 1:2], C.ident32[:, :]),
          reads=(bcol, C.buf, bp1, bs), writes=(bp1,), tag=f"{tag} tr2")
    em.op("dve", lambda eng: eng.tensor_reduce(sc[:, 0:1], ptr1[0:1, :], AX.X, ALU.min),
          reads=(bp1, bs), writes=(bs,), tag=f"{tag} mn2")
    em.op("dve", lambda eng: eng.tensor_scalar(sc[:, 0:1], sc[:, 0:1], 1.0, None, ALU.subtract),
          reads=(bs,), writes=(bs,), tag=f"{tag} lo-")
    em.op("dve", lambda eng: eng.tensor_scalar(sc[:, 1:2], sc[:, 1:2], 1.0, None, ALU.add),
          reads=(bs,), writes=(bs,), tag=f"{tag} hi+")
    for it in range(nit):
        em.op("dve", lambda eng: eng.tensor_add(sc[:, 2:3], sc[:, 0:1], sc[:, 1:2]),
              reads=(bs,), writes=(bs,), tag=f"{tag} mid1")
        em.op("dve", lambda eng: eng.tensor_scalar(sc[:, 2:3], sc[:, 2:3], 0.5, None, ALU.mult),
              reads=(bs,), writes=(bs,), tag=f"{tag} mid2")
        em.op("pe", lambda eng: eng.matmul(pbc[:, :], C.ones_row[0:1, 0:128], sc[0:1, 2:3],
                                           start=True, stop=True),
              reads=(bs, C.buf, bp2), writes=(bp2,), tag=f"{tag} bcmid")
        em.op("dve", lambda eng: eng.tensor_copy(colb[:, 2:3], pbc[:, :]),
              reads=(bp2, bcol), writes=(bcol,), tag=f"{tag} midcol")
        em.op("dve", lambda eng: eng.tensor_scalar(scr[:, :], vals[:, :], colb[0:P, 2:3], None,
                                                   ALU.is_ge, ALU.add, accum_out=colb[0:P, 0:1]),
              reads=(bvals, bcol, bscr), writes=(bscr, bcol), tag=f"{tag} cmp")
        if P < 128:
            em.op("dve", lambda eng: eng.memset(colb[P:128, 0:1], 0.0),
                  reads=(bcol,), writes=(bcol,), tag=f"{tag} padc")
        em.op("pe", lambda eng: eng.matmul(pcnt[:, :], C.ones_col[:, :], colb[:, 0:1],
                                           start=True, stop=True),
              reads=(bcol, C.buf, bp3), writes=(bp3,), tag=f"{tag} cnt")
        em.op("dve", (lambda k_count=k_count: lambda eng: eng.tensor_scalar(
            sc[:, 3:4], pcnt[0:1, 0:1], float(k_count), None, ALU.is_gt))(),
            reads=(bp3, bs), writes=(bs,), tag=f"{tag} gt")
        em.op("dve", lambda eng: eng.tensor_sub(sc[:, 4:5], sc[:, 2:3], sc[:, 0:1]),
              reads=(bs,), writes=(bs,), tag=f"{tag} lod")
        em.op("dve", lambda eng: eng.tensor_scalar(sc[:, 4:5], sc[:, 4:5], sc[0:1, 3:4], None, ALU.mult),
              reads=(bs,), writes=(bs,), tag=f"{tag} lom")
        em.op("dve", lambda eng: eng.tensor_add(sc[:, 0:1], sc[:, 0:1], sc[:, 4:5]),
              reads=(bs,), writes=(bs,), tag=f"{tag} lo")
        em.op("dve", lambda eng: eng.tensor_sub(sc[:, 4:5], sc[:, 1:2], sc[:, 2:3]),
              reads=(bs,), writes=(bs,), tag=f"{tag} hid")
        em.op("dve", lambda eng: eng.tensor_scalar(sc[:, 4:5], sc[:, 4:5], sc[0:1, 3:4], None, ALU.mult),
              reads=(bs,), writes=(bs,), tag=f"{tag} him")
        em.op("dve", lambda eng: eng.tensor_add(sc[:, 1:2], sc[:, 2:3], sc[:, 4:5]),
              reads=(bs,), writes=(bs,), tag=f"{tag} hi")
    em.op("pe", lambda eng: eng.matmul(pbc[:, :], C.ones_row[0:1, 0:128], sc[0:1, 1:2],
                                       start=True, stop=True),
          reads=(bs, C.buf, bp2), writes=(bp2,), tag=f"{tag} bchi")
    em.op("dve", lambda eng: eng.tensor_copy(colb[:, 2:3], pbc[:, :]),
          reads=(bp2, bcol), writes=(bcol,), tag=f"{tag} hicol")
    em.op("dve", lambda eng: eng.tensor_scalar(mask_sb[:, :], vals[:, :], colb[0:P, 2:3], None, ALU.is_ge),
          reads=(bvals, bcol, bmask), writes=(bmask,), tag=f"{tag} mask")


def phase_l0attn_A(nc, em, C, xs_f32, W, sample_idx, Mrows, nb=NB, tag="a0A"):
    NP = NH // 2
    with contextlib.ExitStack() as stk:
        sb = lambda name, shape, dt_: stk.enter_context(nc.sbuf_tensor(f"{tag}_{name}", shape, dt_))
        xch = sb("xch", [128, DT * 512], F32)
        xsamp = sb("xs", [128, DT * U], F32)
        wq = sb("wq", [128, DT * 128], F32)
        wk = sb("wk", [128, DT * 128], F32)
        bqk = sb("bqk", [1, 2 * D], F32)
        qfm = sb("qfm", [128, L], F32)
        ks = sb("ks", [128, U], F32)
        msb = sb("msb", [128, 32], F32)
        mcol = sb("mcol", [128, 2], F32)
        pq = [stk.enter_context(nc.psum_tensor(f"{tag}_pq{i}", [128, 512], F32)) for i in range(2)]
        pm = [stk.enter_context(nc.psum_tensor(f"{tag}_pm{i}", [128, 45], F32)) for i in range(2)]
        bxc, bxs, bwq, bwk, bbqk = Buf("xch"), Buf("xs"), Buf("wq"), Buf("wk"), Buf("bqk")
        bqfm, bks, bmsb, bmcol = Buf("qfm"), Buf("ks"), Buf("msb"), Buf("mcol")
        bpq = [Buf("pq0"), Buf("pq1")]
        bpm = [Buf("pm0"), Buf("pm1")]
        em.dma("qsp", lambda eng: eng.dma_start(out=bqk[:, 0:D], in_=W["bq"].unsqueeze(0)),
               reads=(), writes=(bbqk,), tag=f"{tag} bq")
        em.dma("qsp", lambda eng: eng.dma_start(out=bqk[:, D:2 * D], in_=W["bk"].unsqueeze(0)),
               reads=(), writes=(bbqk,), tag=f"{tag} bk")
        samp_by_chunk = {}
        for j, idx in enumerate(sample_idx):
            samp_by_chunk.setdefault(int(idx) // 512, []).append((j, int(idx)))
        for b in range(nb):
            for c in range(L // 512):
                em.dma("qsp", (lambda b=b, c=c: lambda eng: eng.dma_start(
                    out=xch[:, :].rearrange("p (dt w) -> p dt w", dt=DT),
                    in_=xs_f32[b].rearrange("(dt p) l -> p dt l", p=128)[:, :, c * 512:(c + 1) * 512]))(),
                    reads=(bxc,), writes=(bxc,), tag=f"{tag} gldx")
                for (j, idx) in samp_by_chunk.get(c, []):
                    em.op("dve", (lambda j=j, idx=idx, c=c: lambda eng: eng.tensor_copy(
                        xsamp[:, :].rearrange("p (dt u) -> p dt u", dt=DT)[:, :, j:j + 1],
                        xch[:, :].rearrange("p (dt w) -> p dt w", dt=DT)[:, :, idx - c * 512: idx - c * 512 + 1]))(),
                        reads=(bxc, bxs), writes=(bxs,), tag=f"{tag} xs{j}")
            for hp in range(NP):
                em.dma("qsp", (lambda hp=hp: lambda eng: eng.dma_start(
                    out=wq[:, :].rearrange("p (dt c) -> p dt c", dt=DT),
                    in_=W["Wq"].rearrange("(dt p) e -> p dt e", p=128)[:, :, hp * 128:(hp + 1) * 128]))(),
                    reads=(bwq,), writes=(bwq,), tag=f"{tag} wq")
                em.dma("qsp", (lambda hp=hp: lambda eng: eng.dma_start(
                    out=wk[:, :].rearrange("p (dt c) -> p dt c", dt=DT),
                    in_=W["Wk"].rearrange("(dt p) e -> p dt e", p=128)[:, :, hp * 128:(hp + 1) * 128]))(),
                    reads=(bwk,), writes=(bwk,), tag=f"{tag} wk")
                for c in range(L // 512):
                    i = c % 2
                    em.dma("qsp", (lambda b=b, c=c: lambda eng: eng.dma_start(
                        out=xch[:, :].rearrange("p (dt w) -> p dt w", dt=DT),
                        in_=xs_f32[b].rearrange("(dt p) l -> p dt l", p=128)[:, :, c * 512:(c + 1) * 512]))(),
                        reads=(bxc,), writes=(bxc,), tag=f"{tag} ldx")
                    for dt in range(DT):
                        em.op("pe", (lambda i=i, dt=dt: lambda eng: eng.matmul(
                            pq[i][:, :], wq[:, dt * 128:(dt + 1) * 128],
                            xch[:, dt * 512:(dt + 1) * 512],
                            start=(dt == 0), stop=False))(),
                            reads=(bwq, bxc), writes=(bpq[i],), tag=f"{tag} qmm")
                    em.op("pe", (lambda i=i, hp=hp: lambda eng: eng.matmul(
                        pq[i][:, :], bqk[0:1, hp * 128:(hp + 1) * 128], C.ones_row[0:1, :],
                        start=False, stop=True))(),
                        reads=(bbqk, C.buf), writes=(bpq[i],), tag=f"{tag} qb")
                    em.op("act", (lambda i=i, c=c: lambda eng: eng.copy(
                        qfm[:, c * 512:(c + 1) * 512], pq[i][:, :]))(),
                        reads=(bpq[i], bqfm), writes=(bqfm,), tag=f"{tag} qev")
                for dt in range(DT):
                    em.op("pe", (lambda dt=dt: lambda eng: eng.matmul(
                        pm[0][:, :], wk[:, dt * 128:(dt + 1) * 128],
                        xsamp[:, dt * U:(dt + 1) * U], start=(dt == 0), stop=False))(),
                        reads=(bwk, bxs), writes=(bpm[0],), tag=f"{tag} kmm")
                em.op("pe", (lambda hp=hp: lambda eng: eng.matmul(
                    pm[0][:, :], bqk[0:1, D + hp * 128: D + (hp + 1) * 128], C.ones_row[0:1, 0:U],
                    start=False, stop=True))(),
                    reads=(bbqk, C.buf), writes=(bpm[0],), tag=f"{tag} kb")
                em.op("act", lambda eng: eng.copy(ks[:, :], pm[0][:, :]),
                      reads=(bpm[0], bks), writes=(bks,), tag=f"{tag} kev")
                for h in range(2):
                    for lt in range(L // 128):
                        i = lt % 2
                        em.op("pe", (lambda i=i, h=h, lt=lt: lambda eng: eng.matmul(
                            pm[i][:, :], qfm[h * 64:(h + 1) * 64, lt * 128:(lt + 1) * 128],
                            ks[h * 64:(h + 1) * 64, :], start=True, stop=True))(),
                            reads=(bqfm, bks), writes=(bpm[i],), tag=f"{tag} qk")
                        em.op("dve", (lambda i=i, lt=lt: lambda eng: eng.tensor_reduce(
                            mcol[:, 0:1], pm[i][:, :], AX.X, ALU.max))(),
                            reads=(bpm[i], bmcol), writes=(bmcol,), tag=f"{tag} mmax")
                        em.op("dve", (lambda i=i, lt=lt: lambda eng: eng.tensor_reduce(
                            mcol[:, 1:2], pm[i][:, :], AX.X, ALU.add))(),
                            reads=(bpm[i], bmcol), writes=(bmcol,), tag=f"{tag} msum")
                        em.op("dve", (lambda lt=lt: lambda eng: eng.scalar_tensor_tensor(
                            msb[:, lt:lt + 1], mcol[:, 1:2], -1.0 / U, mcol[:, 0:1],
                            ALU.mult, ALU.add))(),
                            reads=(bmcol, bmsb), writes=(bmsb,), tag=f"{tag} mval")
                    bh = b * NH + hp * 2 + h
                    em.dma("qact", (lambda bh=bh: lambda eng: eng.dma_start(
                        out=Mrows[bh].rearrange("(p lt) -> p lt", p=128), in_=msb[:, :]))(),
                        reads=(bmsb,), writes=(), tag=f"{tag} mst")


def phase_l0attn_sel(nc, em, C, Mrows, RMrows, nrows, tag="a0S"):
    with contextlib.ExitStack() as stk:
        sb = lambda name, shape, dt_: stk.enter_context(nc.sbuf_tensor(f"{tag}_{name}", shape, dt_))
        mr = sb("mr", [nrows, L], F32)
        ones = sb("ones", [nrows, L], F32)
        msk = sb("msk", [nrows, L], F32)
        rk = sb("rk", [nrows, L], F32)
        his = sb("his", [128, 1], F32)
        pbc = stk.enter_context(nc.psum_tensor(f"{tag}_pbc", [128, 1], F32))
        bmr, bones, bmsk, brk, bhis, bpbc = Buf("mr"), Buf("ones"), Buf("msk"), Buf("rk"), Buf("his"), Buf("pbc")
        em.dma("qsp", lambda eng: eng.dma_start(out=mr[:, :], in_=Mrows),
               reads=(), writes=(bmr,), tag=f"{tag} ld")
        em.op("dve", lambda eng: eng.memset(ones[:, :], 1.0),
              reads=(), writes=(bones,), tag=f"{tag} ones")
        emit_bisect_rows(nc, em, C, stk, mr, bmr, msk, bmsk, his, bhis, pbc, bpbc, nrows, tag)
        em.op("dve", lambda eng: eng.tensor_tensor_scan(rk[:, :], ones[:, :], msk[:, :], 0.0,
                                                        ALU.mult, ALU.add),
              reads=(bones, bmsk, brk), writes=(brk,), tag=f"{tag} cumsum")
        em.op("dve", lambda eng: eng.tensor_mul(rk[:, :], rk[:, :], msk[:, :]),
              reads=(brk, bmsk), writes=(brk,), tag=f"{tag} rm")
        em.dma("qact", lambda eng: eng.dma_start(out=RMrows, in_=rk[:, :]),
               reads=(brk,), writes=(), tag=f"{tag} st")


def emit_bisect_rows(nc, em, C, stk, vals, bvals, mask_sb, bmask, his, bhis, pbc, bpbc, P, tag):
    cols = stk.enter_context(nc.sbuf_tensor(f"{tag}_cols", [128, 6], F32))
    scr = stk.enter_context(nc.sbuf_tensor(f"{tag}_scr", [P, L], F32))
    bcols, bscr = Buf("cols"), Buf("scr")
    em.op("dve", lambda eng: eng.tensor_reduce(cols[0:P, 1:2], vals[:, :], AX.X, ALU.max),
          reads=(bvals, bcols), writes=(bcols,), tag=f"{tag} mx")
    em.op("dve", lambda eng: eng.tensor_reduce(cols[0:P, 0:1], vals[:, :], AX.X, ALU.min),
          reads=(bvals, bcols), writes=(bcols,), tag=f"{tag} mn")
    em.op("dve", lambda eng: eng.tensor_scalar(cols[0:P, 0:1], cols[0:P, 0:1], 1.0, None, ALU.subtract),
          reads=(bcols,), writes=(bcols,), tag=f"{tag} lo-")
    em.op("dve", lambda eng: eng.tensor_scalar(cols[0:P, 1:2], cols[0:P, 1:2], 1.0, None, ALU.add),
          reads=(bcols,), writes=(bcols,), tag=f"{tag} hi+")
    for it in range(42):
        em.op("dve", lambda eng: eng.tensor_add(cols[0:P, 2:3], cols[0:P, 0:1], cols[0:P, 1:2]),
              reads=(bcols,), writes=(bcols,), tag=f"{tag} mid")
        em.op("dve", lambda eng: eng.tensor_scalar(cols[0:P, 2:3], cols[0:P, 2:3], 0.5, None, ALU.mult),
              reads=(bcols,), writes=(bcols,), tag=f"{tag} mid2")
        em.op("dve", lambda eng: eng.tensor_scalar(scr[:, :], vals[:, :], cols[0:P, 2:3], None,
                                                   ALU.is_ge, ALU.add, accum_out=cols[0:P, 3:4]),
              reads=(bvals, bcols, bscr), writes=(bscr, bcols), tag=f"{tag} cnt")
        em.op("dve", lambda eng: eng.tensor_scalar(cols[0:P, 4:5], cols[0:P, 3:4], float(U), None, ALU.is_gt),
              reads=(bcols,), writes=(bcols,), tag=f"{tag} gt")
        em.op("dve", lambda eng: eng.tensor_sub(cols[0:P, 5:6], cols[0:P, 2:3], cols[0:P, 0:1]),
              reads=(bcols,), writes=(bcols,), tag=f"{tag} lod")
        em.op("dve", lambda eng: eng.tensor_tensor(cols[0:P, 5:6], cols[0:P, 5:6], cols[0:P, 4:5], ALU.mult),
              reads=(bcols,), writes=(bcols,), tag=f"{tag} lom")
        em.op("dve", lambda eng: eng.tensor_add(cols[0:P, 0:1], cols[0:P, 0:1], cols[0:P, 5:6]),
              reads=(bcols,), writes=(bcols,), tag=f"{tag} lo")
        em.op("dve", lambda eng: eng.tensor_sub(cols[0:P, 5:6], cols[0:P, 1:2], cols[0:P, 2:3]),
              reads=(bcols,), writes=(bcols,), tag=f"{tag} hid")
        em.op("dve", lambda eng: eng.tensor_tensor(cols[0:P, 5:6], cols[0:P, 5:6], cols[0:P, 4:5], ALU.mult),
              reads=(bcols,), writes=(bcols,), tag=f"{tag} him")
        em.op("dve", lambda eng: eng.tensor_add(cols[0:P, 1:2], cols[0:P, 2:3], cols[0:P, 5:6]),
              reads=(bcols,), writes=(bcols,), tag=f"{tag} hi")
    em.op("dve", lambda eng: eng.tensor_scalar(mask_sb[:, :], vals[:, :], cols[0:P, 1:2], None, ALU.is_ge),
          reads=(bvals, bcols, bmask), writes=(bmask,), tag=f"{tag} mask")


def phase_l0attn_B(nc, em, C, xs_f32, W, RMrows, outs_f32, nb=NB, tag="a0B"):
    NP = NH // 2
    GP = 2
    OA = [nc.dram_tensor(f"{tag}_oa{b}", [D, L], BF16,
                         kind="ExternalOutput" if DBG_STAGES else "Internal").ap() for b in range(nb)]
    bOA = [Buf(f"{tag}_oaD{b}") for b in range(nb)]
    with contextlib.ExitStack() as stk:
        sb = lambda name, shape, dt_: stk.enter_context(nc.sbuf_tensor(f"{tag}_{name}", shape, dt_))
        xb = sb("xb", [128, DT * L], BF16)
        qtm = sb("qtm", [128, 32 * GP * 128], BF16)
        vtm = sb("vtm", [128, 32 * GP * 128], BF16)
        wpair = sb("wpair", [128, 3 * DT * 128], BF16)
        brow = sb("brow", [1, 4 * D], BF16)
        qfm = sb("qfm", [128, L], BF16)
        kfm = sb("kfm", [128, L], BF16)
        rmc = sb("rmc", [128, 32], F32)
        stt_ = sb("stt", [128, 32 * 45], BF16)
        ssb = sb("ssb", [45, L], BF16)
        scosb = sb("sco", [45, L], F32)
        attn = sb("attn", [45, L], BF16)
        attnT = sb("attnT", [128, 32 * 45], BF16)
        qred = sb("qred", [128, 45], BF16)
        vmr = sb("vmr", [1, 64], F32)
        vmrb = sb("vmrb", [1, 64], BF16)
        dsc = sb("dsc", [45, 64], F32)
        delta = sb("delta", [45, 64], BF16)
        rcol = sb("rcol", [45, 2], F32)
        oastage = sb("oast", [64, 512], BF16)
        pA = stk.enter_context(nc.psum_tensor(f"{tag}_pA", [128, 512], F32))
        pB = stk.enter_context(nc.psum_tensor(f"{tag}_pB", [128, 512], F32))
        pE = stk.enter_context(nc.psum_tensor(f"{tag}_pE", [128, 512], F32))
        pT = stk.enter_context(nc.psum_tensor(f"{tag}_pT", [128, 128], BF16))
        B_ = lambda n: Buf(f"{tag}{n}")
        bxb, bqtm, bvtm, bwp, bbr = B_("xb"), B_("qtm"), B_("vtm"), B_("wp"), B_("br")
        bqfm, bkfm, brm = B_("qfm"), B_("kfm"), B_("rm")
        bst, bss, bsco, batn, batnT = B_("st"), B_("ss"), B_("sco"), B_("atn"), B_("atnT")
        bqred, bvm, bdsc, bdelta, brc, boa = B_("qred"), B_("vm"), B_("dsc"), B_("delta"), B_("rc"), B_("oa")
        bpA, bpB, bpE, bpT = B_("pA"), B_("pB"), B_("pE"), B_("pT")
        for j, nm in enumerate(["bq", "bk", "bv", "bo"]):
            em.dma("qpool", (lambda j=j, nm=nm: lambda eng: eng.dma_start(
                out=brow[:, j * D:(j + 1) * D], in_=W[nm].unsqueeze(0)))(),
                reads=(), writes=(bbr,), tag=f"{tag} {nm}")
        for b in range(nb):
            for c in range(L // 512):
                em.dma("qpool", (lambda b=b, c=c: lambda eng: eng.dma_start(
                    out=xb[:, :].rearrange("p (dt l) -> p dt l", dt=DT)[:, :, c * 512:(c + 1) * 512],
                    in_=xs_f32[b].rearrange("(dt p) l -> p dt l", p=128)[:, :, c * 512:(c + 1) * 512]))(),
                    reads=(bxb,), writes=(bxb,), tag=f"{tag} ldx")
            for grp in range(NP // GP):
                GW = GP * 128
                for wsel, dst, bdst in (("Wq", qtm, bqtm), ("Wv", vtm, bvtm)):
                    em.dma("qpool", (lambda wsel=wsel, grp=grp, GW=GW: lambda eng: eng.dma_start(
                        out=wpair[:, 0:DT * GW].rearrange("p (dt c) -> p dt c", dt=DT),
                        in_=W[wsel].rearrange("(dt p) e -> p dt e", p=128)[:, :, grp * GW:(grp + 1) * GW]))(),
                        reads=(bwp,), writes=(bwp,), tag=f"{tag} w{wsel}")
                    for lt in range(L // 128):
                        i = lt % 2
                        ps = pA if i == 0 else pB
                        bps = bpA if i == 0 else bpB
                        for dt in range(DT):
                            em.op("pe", (lambda ps=ps, lt=lt, dt=dt, GW=GW: lambda eng: eng.matmul(
                                ps[:, 0:GW], xb[:, dt * L + lt * 128: dt * L + (lt + 1) * 128],
                                wpair[:, dt * GW:(dt + 1) * GW], start=(dt == 0), stop=False))(),
                                reads=(bxb, bwp), writes=(bps,), tag=f"{tag} tmm")
                        em.op("pe", (lambda ps=ps, wsel=wsel, grp=grp, GW=GW: lambda eng: eng.matmul(
                            ps[:, 0:GW], C.ones16[0:1, 0:128],
                            brow[0:1, (0 if wsel == "Wq" else 2) * D + grp * GW:
                                 (0 if wsel == "Wq" else 2) * D + (grp + 1) * GW],
                            start=False, stop=True))(),
                            reads=(bbr, C.buf), writes=(bps,), tag=f"{tag} tmb")
                        em.op("act", (lambda ps=ps, dst=dst, lt=lt, GW=GW: lambda eng: eng.copy(
                            dst[:, lt * GW:(lt + 1) * GW], ps[:, 0:GW]))(),
                            reads=(bps, bdst), writes=(bdst,), tag=f"{tag} tmev")
                for pp in range(GP):
                    hp = grp * GP + pp
                    em.dma("qpool", (lambda hp=hp: lambda eng: eng.dma_start(
                        out=wpair[:, 0:DT * 128].rearrange("p (dt c) -> p dt c", dt=DT),
                        in_=W["Wq"].rearrange("(dt p) e -> p dt e", p=128)[:, :, hp * 128:(hp + 1) * 128]))(),
                        reads=(bwp,), writes=(bwp,), tag=f"{tag} pwq")
                    em.dma("qpool", (lambda hp=hp: lambda eng: eng.dma_start(
                        out=wpair[:, DT * 128:2 * DT * 128].rearrange("p (dt c) -> p dt c", dt=DT),
                        in_=W["Wk"].rearrange("(dt p) e -> p dt e", p=128)[:, :, hp * 128:(hp + 1) * 128]))(),
                        reads=(bwp,), writes=(bwp,), tag=f"{tag} pwk")
                    for (area, wb, dst, bdst) in ((0, 0, qfm, bqfm), (1, 1, kfm, bkfm)):
                        for c in range(L // 512):
                            i = c % 2
                            ps = pA if i == 0 else pB
                            bps = bpA if i == 0 else bpB
                            for dt in range(DT):
                                em.op("pe", (lambda ps=ps, area=area, c=c, dt=dt: lambda eng: eng.matmul(
                                    ps[:, :], wpair[:, area * DT * 128 + dt * 128: area * DT * 128 + (dt + 1) * 128],
                                    xb[:, dt * L + c * 512: dt * L + (c + 1) * 512],
                                    start=(dt == 0), stop=False))(),
                                    reads=(bwp, bxb), writes=(bps,), tag=f"{tag} fmm")
                            em.op("pe", (lambda ps=ps, wb=wb, hp=hp: lambda eng: eng.matmul(
                                ps[:, :], brow[0:1, wb * D + hp * 128: wb * D + (hp + 1) * 128],
                                C.ones16[0:1, :], start=False, stop=True))(),
                                reads=(bbr, C.buf), writes=(bps,), tag=f"{tag} fmb")
                            em.op("act", (lambda ps=ps, dst=dst, c=c: lambda eng: eng.copy(
                                dst[:, c * 512:(c + 1) * 512], ps[:, :]))(),
                                reads=(bps, bdst), writes=(bdst,), tag=f"{tag} fmev")
                    for h in range(2):
                        hh = hp * 2 + h
                        bh = b * NH + hh
                        hc = pp * 128 + h * 64
                        em.dma("qsp", (lambda bh=bh: lambda eng: eng.dma_start(
                            out=rmc[:, :], in_=RMrows[bh].rearrange("(p lt) -> p lt", p=128)))(),
                            reads=(brm,), writes=(brm,), tag=f"{tag} rmc")
                        for lt in range(L // 128):
                            em.op("dve", (lambda lt=lt: lambda eng: eng.tensor_scalar(
                                stt_[:, lt * 45:(lt + 1) * 45], C.iotarow45[:, :],
                                rmc[:, lt:lt + 1], None, ALU.is_equal))(),
                                reads=(brm, C.buf, bst), writes=(bst,), tag=f"{tag} stb")
                        for lt in range(L // 128):
                            em.op("pe", (lambda lt=lt: lambda eng: eng.transpose(
                                pT[0:45, :], stt_[:, lt * 45:(lt + 1) * 45], C.ident16[:, :]))(),
                                reads=(bst, C.buf, bpT), writes=(bpT,), tag=f"{tag} str")
                            em.op("dve", (lambda lt=lt: lambda eng: eng.tensor_copy(
                                ssb[:, lt * 128:(lt + 1) * 128], pT[0:45, :]))(),
                                reads=(bpT, bss), writes=(bss,), tag=f"{tag} strev")
                        for lt in range(L // 128):
                            em.op("pe", (lambda lt=lt, hc=hc: lambda eng: eng.matmul(
                                pE[0:64, 0:45], qtm[:, lt * (GP * 128) + hc: lt * (GP * 128) + hc + 64],
                                stt_[:, lt * 45:(lt + 1) * 45],
                                start=(lt == 0), stop=(lt == 31)))(),
                                reads=(bqtm, bst, bpE), writes=(bpE,), tag=f"{tag} qred")
                        em.op("act", (lambda h=h: lambda eng: eng.copy(
                            qred[h * 64:(h + 1) * 64, :], pE[0:64, 0:45]))(),
                            reads=(bpE, bqred), writes=(bqred,), tag=f"{tag} qrev")
                        for c in range(L // 512):
                            i = c % 2
                            ps = pA if i == 0 else pB
                            bps = bpA if i == 0 else bpB
                            em.op("pe", (lambda ps=ps, h=h, c=c: lambda eng: eng.matmul(
                                ps[0:45, :], qred[h * 64:(h + 1) * 64, :],
                                kfm[h * 64:(h + 1) * 64, c * 512:(c + 1) * 512],
                                start=True, stop=True))(),
                                reads=(bqred, bkfm, bps), writes=(bps,), tag=f"{tag} sc")
                            em.op("dve", (lambda ps=ps, c=c: lambda eng: eng.tensor_copy(
                                scosb[:, c * 512:(c + 1) * 512], ps[0:45, :]))(),
                                reads=(bps, bsco), writes=(bsco,), tag=f"{tag} scev")
                        em.op("dve", lambda eng: eng.tensor_reduce(rcol[:, 0:1], scosb[:, :], AX.X, ALU.max),
                              reads=(bsco, brc), writes=(brc,), tag=f"{tag} smax")
                        em.op("dve", lambda eng: eng.tensor_scalar(rcol[:, 0:1], rcol[:, 0:1],
                                                                   -0.125, None, ALU.mult),
                              reads=(brc,), writes=(brc,), tag=f"{tag} snm")
                        em.op("act", lambda eng: eng.activation(attn[:, :], scosb[:, :], AF.Exp,
                                                                bias=rcol[:, 0:1], scale=0.125,
                                                                accum_out=rcol[:, 1:2]),
                              reads=(bsco, brc), writes=(batn, brc), tag=f"{tag} sexp")
                        em.op("dve", lambda eng: eng.reciprocal(rcol[:, 1:2], rcol[:, 1:2]),
                              reads=(brc,), writes=(brc,), tag=f"{tag} srec")
                        for lt in range(L // 128):
                            em.op("pe", (lambda lt=lt: lambda eng: eng.transpose(
                                pT[:, 0:45], attn[:, lt * 128:(lt + 1) * 128], C.ident16[0:45, 0:45]))(),
                                reads=(batn, C.buf, bpT), writes=(bpT,), tag=f"{tag} atr")
                            em.op("dve", (lambda lt=lt: lambda eng: eng.tensor_copy(
                                attnT[:, lt * 45:(lt + 1) * 45], pT[:, 0:45]))(),
                                reads=(bpT, batnT), writes=(batnT,), tag=f"{tag} atrev")
                        for lt in range(L // 128):
                            em.op("pe", (lambda lt=lt, hc=hc: lambda eng: eng.matmul(
                                pE[0:1, 0:64], C.ones_col16[:, :],
                                vtm[:, lt * (GP * 128) + hc: lt * (GP * 128) + hc + 64],
                                start=(lt == 0), stop=(lt == 31)))(),
                                reads=(bvtm, C.buf, bpE), writes=(bpE,), tag=f"{tag} vm")
                        em.op("dve", lambda eng: eng.tensor_scalar(vmr[:, :], pE[0:1, 0:64],
                                                                   1.0 / L, None, ALU.mult),
                              reads=(bpE, bvm), writes=(bvm,), tag=f"{tag} vmev")
                        em.op("act", lambda eng: eng.copy(vmrb[:, :], vmr[:, :]),
                              reads=(bvm,), writes=(bvm,), tag=f"{tag} vmbf")
                        for lt in range(L // 128):
                            em.op("pe", (lambda lt=lt, hc=hc: lambda eng: eng.matmul(
                                pE[0:45, 64:128], attnT[:, lt * 45:(lt + 1) * 45],
                                vtm[:, lt * (GP * 128) + hc: lt * (GP * 128) + hc + 64],
                                start=(lt == 0), stop=(lt == 31)))(),
                                reads=(batnT, bvtm, bpE), writes=(bpE,), tag=f"{tag} ored")
                        em.op("dve", lambda eng: eng.tensor_scalar(dsc[:, :], pE[0:45, 64:128],
                                                                   rcol[:, 1:2], None, ALU.mult),
                              reads=(bpE, brc, bdsc), writes=(bdsc,), tag=f"{tag} dnorm")
                        em.op("pe", lambda eng: eng.matmul(pE[0:45, 128:192], C.ones_row[0:1, 0:45],
                                                           vmr[:, :], start=True, stop=True),
                              reads=(bvm, C.buf, bpE), writes=(bpE,), tag=f"{tag} vbc")
                        em.op("dve", lambda eng: eng.tensor_sub(delta[:, :], dsc[:, :], pE[0:45, 128:192]),
                              reads=(bdsc, bpE, bdelta), writes=(bdelta,), tag=f"{tag} dsub")
                        for c in range(L // 512):
                            i = c % 2
                            ps = pA if i == 0 else pB
                            bps = bpA if i == 0 else bpB
                            em.op("pe", (lambda ps=ps, c=c: lambda eng: eng.matmul(
                                ps[0:64, :], delta[:, :], ssb[:, c * 512:(c + 1) * 512],
                                start=True, stop=False))(),
                                reads=(bdelta, bss, bps), writes=(bps,), tag=f"{tag} scat")
                            em.op("pe", (lambda ps=ps, c=c: lambda eng: eng.matmul(
                                ps[0:64, :], vmrb[:, :], C.ones16[0:1, :],
                                start=False, stop=True))(),
                                reads=(bvm, C.buf, bps), writes=(bps,), tag=f"{tag} fill")
                            em.op("act", (lambda ps=ps: lambda eng: eng.copy(
                                oastage[:, :], ps[0:64, :]))(),
                                reads=(bps, boa), writes=(boa,), tag=f"{tag} oaev")
                            em.dma("qact", (lambda b=b, hh=hh, c=c: lambda eng: eng.dma_start(
                                out=OA[b][hh * 64:(hh + 1) * 64, c * 512:(c + 1) * 512],
                                in_=oastage[:, :]))(),
                                reads=(boa,), writes=(bOA[b],), tag=f"{tag} oast")
        for b in range(nb):
            with contextlib.ExitStack() as stk2:
                oach = stk2.enter_context(nc.sbuf_tensor(f"{tag}_oach{b}", [128, DT * 512], BF16))
                wo = [stk2.enter_context(nc.sbuf_tensor(f"{tag}_wo{b}{i}", [128, DT * 128], BF16))
                      for i in range(2)]
                ost = stk2.enter_context(nc.sbuf_tensor(f"{tag}_ost{b}", [128, DT * 512], F32))
                boach, bost = B_("oach"), B_("ost")
                bwo = [B_("wo0"), B_("wo1")]
                for c in range(L // 512):
                    if c == 0:
                        em.barrier()
                    em.dma("qsp", (lambda b=b, c=c: lambda eng: eng.dma_start(
                        out=oach[:, :].rearrange("p (dt l) -> p dt l", dt=DT),
                        in_=OA[b].rearrange("(dt p) l -> p dt l", p=128)[:, :, c * 512:(c + 1) * 512]))(),
                        reads=(boach, bOA[b]), writes=(boach,), tag=f"{tag} oald")
                    for et in range(DT):
                        i = et % 2
                        ps = pA if i == 0 else pB
                        bps = bpA if i == 0 else bpB
                        em.dma("qpool", (lambda et=et, i=i: lambda eng: eng.dma_start(
                            out=wo[i][:, :].rearrange("p (dt c) -> p dt c", dt=DT),
                            in_=W["Wo"].rearrange("(dt p) e -> p dt e", p=128)[:, :, et * 128:(et + 1) * 128]))(),
                            reads=(bwo[i],), writes=(bwo[i],), tag=f"{tag} wold")
                        for dt in range(DT):
                            em.op("pe", (lambda ps=ps, i=i, dt=dt, c=c: lambda eng: eng.matmul(
                                ps[:, :], wo[i][:, dt * 128:(dt + 1) * 128],
                                oach[:, dt * 512:(dt + 1) * 512], start=(dt == 0), stop=False))(),
                                reads=(bwo[i], boach), writes=(bps,), tag=f"{tag} womm")
                        em.op("pe", (lambda ps=ps, et=et: lambda eng: eng.matmul(
                            ps[:, :], brow[0:1, 3 * D + et * 128: 3 * D + (et + 1) * 128],
                            C.ones16[0:1, :], start=False, stop=True))(),
                            reads=(bbr, C.buf), writes=(bps,), tag=f"{tag} wob")
                        em.op("act", (lambda ps=ps, et=et: lambda eng: eng.copy(
                            ost[:, et * 512:(et + 1) * 512], ps[:, :]))(),
                            reads=(bps, bost), writes=(bost,), tag=f"{tag} woev")
                    em.dma("qact", (lambda b=b, c=c: lambda eng: eng.dma_start(
                        out=outs_f32[b].rearrange("(dt p) l -> p dt l", p=128)[:, :, c * 512:(c + 1) * 512],
                        in_=ost[:, :].rearrange("p (dt l) -> p dt l", dt=DT)))(),
                        reads=(bost,), writes=(), tag=f"{tag} wost")



def phase_cast_bf16(nc, em, src_f32, dst_bf16, tag):
    with contextlib.ExitStack() as stk:
        t = stk.enter_context(nc.sbuf_tensor(f"{tag}_cb", [128, DT * 512], BF16))
        bt = Buf(f"{tag}cb")
        for c in range(L // 512):
            em.dma("qpool", (lambda c=c: lambda eng: eng.dma_start(
                out=t[:, :].rearrange("p (dt w) -> p dt w", dt=DT),
                in_=src_f32.rearrange("(dt p) l -> p dt l", p=128)[:, :, c * 512:(c + 1) * 512]))(),
                reads=(bt,), writes=(bt,), tag=f"{tag} in")
            em.dma("qsp", (lambda c=c: lambda eng: eng.dma_start(
                out=dst_bf16.rearrange("(dt p) l -> p dt l", p=128)[:, :, c * 512:(c + 1) * 512],
                in_=t[:, :].rearrange("p (dt w) -> p dt w", dt=DT)))(),
                reads=(bt,), writes=(), tag=f"{tag} out")


def phase_out_transpose(nc, em, C, src_f32, y_out, b, tag):
    with contextlib.ExitStack() as stk:
        fin = [stk.enter_context(nc.sbuf_tensor(f"{tag}_f{i}", [128, DT * 128], F32)) for i in range(2)]
        tmo = [stk.enter_context(nc.sbuf_tensor(f"{tag}_t{i}", [128, D], F32)) for i in range(2)]
        ps = [stk.enter_context(nc.psum_tensor(f"{tag}_p{i}", [128, 128], F32)) for i in range(2)]
        bf = [Buf("f0"), Buf("f1")]
        bt = [Buf("t0"), Buf("t1")]
        bp = [Buf("p0"), Buf("p1")]
        for lt in range(L // 128):
            i = lt % 2
            em.dma("qsp", (lambda lt=lt, i=i: lambda eng: eng.dma_start(
                out=fin[i][:, :].rearrange("p (dt c) -> p dt c", dt=DT),
                in_=src_f32.rearrange("(dt p) l -> p dt l", p=128)[:, :, lt * 128:(lt + 1) * 128]))(),
                reads=(bf[i], bt[i]), writes=(bf[i],), tag=f"{tag} ld")
            for dt in range(DT):
                j = dt % 2
                em.op("pe", (lambda i=i, j=j, dt=dt: lambda eng: eng.transpose(
                    ps[j][:, :], fin[i][:, dt * 128:(dt + 1) * 128], C.ident32[:, :]))(),
                    reads=(bf[i], C.buf, bp[j]), writes=(bp[j],), tag=f"{tag} tr")
                em.op("act", (lambda i=i, j=j, dt=dt: lambda eng: eng.copy(
                    tmo[i][:, dt * 128:(dt + 1) * 128], ps[j][:, :]))(),
                    reads=(bp[j], bt[i]), writes=(bt[i],), tag=f"{tag} ev")
            em.dma("qact", (lambda b=b, lt=lt, i=i: lambda eng: eng.dma_start(
                out=y_out[b, lt * 128:(lt + 1) * 128, :], in_=tmo[i][:, :]))(),
                reads=(bt[i],), writes=(), tag=f"{tag} st")



from concourse.bass_utils import run_bass_kernel_spmd

PNAMES = ["angle", "lnm", "scale", "pi_re", "pi_im", "lc_re", "lc_im"]


def _ext(nc, name, shape, dt_, kind="ExternalInput"):
    return nc.dram_tensor(name, list(shape), dt_, kind=kind).ap()


def _weight_tensors(nc, which):
    W = {}
    if "l0attn" in which:
        for n in ["Wq", "Wk", "Wv", "Wo"]:
            W[n] = _ext(nc, f"L0_attn_{n}", [D, D], F32)
        for n in ["bq", "bk", "bv", "bo"]:
            W[n] = _ext(nc, f"L0_attn_{n}", [D], F32)
    if "l0ffn" in which:
        W["gate"] = _ext(nc, "L0_ffn_gate", [D, E], F32)
        W["W1"] = _ext(nc, "L0_ffn_W1", [E, D, HID], F32)
        W["b1"] = _ext(nc, "L0_ffn_b1", [E, HID], F32)
        W["W2"] = _ext(nc, "L0_ffn_W2", [E, HID, D], F32)
        W["b2"] = _ext(nc, "L0_ffn_b2", [E, D], F32)
    if "l1attn" in which:
        for n in PNAMES:
            W[f"p_{n}"] = _ext(nc, f"p_{n}", [128, DT], F32)
    if "l1ffn" in which:
        W["Wm"] = _ext(nc, "L1_ffn_Wm", [FS, D], F32)
        W["Wp"] = _ext(nc, "L1_ffn_Wp", [D, FS], F32)
    if "l1ffn_c" in which:
        W["Wc1"] = _ext(nc, "L1_ffn_Wc1", [256, D], F32)
        W["Wc2"] = _ext(nc, "L1_ffn_Wc2", [FS, 256], F32)
    return W


def build_launch1(sample_idx, max_phases=99):
    import os
    max_phases = int(os.environ.get("MAXPH", max_phases))
    nc = bass.Bass("TRN2", target_bir_lowering=False)
    em = Em(nc)
    x_in = _ext(nc, "x_in", [NB, L, D], F32)
    W = _weight_tensors(nc, ("l0attn", "l0ffn", "l1attn", "l1ffn", "l1ffn_c"))
    X3 = _ext(nc, "X3", [NB * D, L], F32, "ExternalOutput")
    XN1 = _ext(nc, "XN1", [NB * D, L], F32, "ExternalOutput")
    XN2 = _ext(nc, "XN2", [NB * D, L], F32, "ExternalOutput")
    MASK1 = _ext(nc, "MASK1", [NB * 128, FT], F32, "ExternalOutput")
    MASK2 = _ext(nc, "MASK2", [NB * 128, FT], F32, "ExternalOutput")
    NOV = _ext(nc, "NOV", [2, NB], F32, "ExternalOutput")
    dbgk = "ExternalOutput" if DBG_STAGES else "Internal"
    X0 = nc.dram_tensor("X0", [NB * D, L], F32, kind=dbgk).ap()
    A0 = nc.dram_tensor("A0", [NB * D, L], F32, kind=dbgk).ap()
    X1 = nc.dram_tensor("X1", [NB * D, L], F32, kind=dbgk).ap()
    M1o = nc.dram_tensor("M1o", [NB * D, L], F32, kind=dbgk).ap()
    X2 = nc.dram_tensor("X2", [NB * D, L], F32, kind=dbgk).ap()
    C2 = nc.dram_tensor("C2", [NB * D, L], F32, kind=dbgk).ap()
    X3BF = nc.dram_tensor("X3BF", [NB * D, L], BF16, kind="Internal").ap()
    XN1BF = nc.dram_tensor("XN1BF", [NB * D, L], BF16, kind="Internal").ap()
    XN2BF = nc.dram_tensor("XN2BF", [NB * D, L], BF16, kind="Internal").ap()
    HIST = nc.dram_tensor("HIST", [NB * 128, FT], F32, kind="Internal").ap()
    POOL = nc.dram_tensor("POOL", [2 * NB * 128, DT], F32, kind="Internal").ap()
    Mrows = nc.dram_tensor("Mrows", [NB * NH, L], F32, kind="Internal").ap()
    RMrows = nc.dram_tensor("RMrows", [NB * NH, L], F32, kind="Internal").ap()
    IDX = nc.dram_tensor("IDX", [IS, 1], mybir.dt.int32, kind="Internal").ap()
    WGT = nc.dram_tensor("WGT", [D, IS], BF16, kind="Internal").ap()
    WPG = nc.dram_tensor("WPG", [IS, D], BF16, kind="Internal").ap()
    bWGT, bWPG = Buf("WGTd"), Buf("WPGd")
    bs = lambda T, b: T[b * D:(b + 1) * D, :]
    bm = lambda T, b: T[b * 128:(b + 1) * 128, :]
    with contextlib.ExitStack() as stk:
        C = Consts(nc, em, stk)
        sems = {k: stk.enter_context(nc.semaphore(f"sem_{k}")) for k in em.sem_keys()}
        block = stk.enter_context(nc.Block())
        for b in range(NB):
            phase_in_transpose(nc, em, C, x_in, bs(X0, b), b)
            em.barrier()
        if max_phases < 1:
            em.finalize(block, sems)
            return nc
        phase_l0attn_A(nc, em, C, [bs(X0, b) for b in range(NB)], W, sample_idx, Mrows)
        em.barrier()
        phase_l0attn_sel(nc, em, C, Mrows, RMrows, NB * NH)
        em.barrier()
        if max_phases < 2:
            em.finalize(block, sems)
            return nc
        phase_l0attn_B(nc, em, C, [bs(X0, b) for b in range(NB)], W, RMrows,
                       [bs(A0, b) for b in range(NB)])
        em.barrier()
        for b in range(NB):
            phase_ln(nc, em, C, bs(X0, b), bs(A0, b), bs(X1, b), b, tag=f"ln1b{b}")
            em.barrier()
        if max_phases < 3:
            em.finalize(block, sems)
            return nc
        w1bf, w2bf = phase_moe_prep(nc, em, W)
        em.barrier()
        if max_phases < 4:
            em.finalize(block, sems)
            return nc
        for b in range(NB):
            phase_moe(nc, em, C, bs(X1, b), W, w1bf, w2bf, bs(M1o, b), b)
            em.barrier()
            phase_ln(nc, em, C, bs(X1, b), bs(M1o, b), bs(X2, b), b, tag=f"ln2b{b}")
            em.barrier()
        if max_phases < 5:
            em.finalize(block, sems)
            return nc
        P = {n: W[f"p_{n}"] for n in PNAMES}
        phase_l1attn(nc, em, C, [bs(X2, b) for b in range(NB)], P,
                     [bs(C2, b) for b in range(NB)])
        em.barrier()
        for b in range(NB):
            phase_ln(nc, em, C, bs(X2, b), bs(C2, b), bs(X3, b), b, tag=f"ln3b{b}")
            em.barrier()
        if max_phases < 6:
            em.finalize(block, sems)
            return nc
        Wsc = phase_l1ffn_prep(nc, em, C, W)
        em.barrier()
        if max_phases < 7:
            em.finalize(block, sems)
            return nc
        for b in range(NB):
            phase_cast_bf16(nc, em, bs(X3, b), bs(X3BF, b), f"cx3b{b}")
            em.barrier()
            phase_l1ffn_step(nc, em, C, bs(X3BF, b), Wsc, bs(XN1BF, b), bs(XN1, b),
                             b, 1, pooled_dram=bm(POOL, b))
            em.barrier()
            phase_l1ffn_ck(nc, em, C, bm(POOL, b), Wsc, bm(MASK1, b), None,
                           bm(HIST, b), NOV[0:1, b:b + 1], f"ck1b{b}")
            em.barrier()
            if max_phases < 8:
                continue
            bidx = phase_l1ffn_idx(nc, em, C, bm(MASK1, b), IDX, f"idx2b{b}")
            em.barrier()
            phase_l1ffn_gather(nc, em, C, Wsc, IDX, bidx, WGT, WPG, bWGT, bWPG, f"g2b{b}")
            em.barrier()
            phase_l1ffn_step_sparse(nc, em, C, bs(XN1BF, b), WGT, bWGT, WPG, bWPG,
                                    bs(XN2BF, b), bs(XN2, b), b, 2,
                                    pooled_dram=bm(POOL, NB + b))
            em.barrier()
            phase_l1ffn_ck(nc, em, C, bm(POOL, NB + b), Wsc, bm(MASK2, b), bm(HIST, b),
                           None, NOV[1:2, b:b + 1], f"ck2b{b}")
            em.barrier()
        em.finalize(block, sems)
    return nc


def build_launch2():
    nc = bass.Bass("TRN2", target_bir_lowering=False)
    em = Em(nc)
    W = _weight_tensors(nc, ("l1ffn",))
    X3 = _ext(nc, "X3", [NB * D, L], F32)
    XN1 = _ext(nc, "XN1", [NB * D, L], F32)
    XN2 = _ext(nc, "XN2", [NB * D, L], F32)
    MASK1 = _ext(nc, "MASK1", [NB * 128, FT], F32)
    MASK2 = _ext(nc, "MASK2", [NB * 128, FT], F32)
    NOVALL = _ext(nc, "NOVALL", [2, 16], F32)
    y_out = _ext(nc, "y_out", [NB, L, D], F32, "ExternalOutput")
    XN2BF = nc.dram_tensor("XN2BF", [NB * D, L], BF16, kind="Internal").ap()
    XN3BF = nc.dram_tensor("XN3BF", [NB * D, L], BF16, kind="Internal").ap()
    XN3 = nc.dram_tensor("XN3", [NB * D, L], F32, kind="Internal").ap()
    XFIN = nc.dram_tensor("XFIN", [NB * D, L], F32, kind="Internal").ap()
    XOUT = nc.dram_tensor("XOUT", [NB * D, L], F32, kind="Internal").ap()
    DFLAG = nc.dram_tensor("DFLAG", [1, 2], F32, kind="Internal").ap()
    ACT3 = nc.dram_tensor("ACT3", [NB * 128, FT], F32, kind="Internal").ap()
    IDX = nc.dram_tensor("IDX", [IS, 1], mybir.dt.int32, kind="Internal").ap()
    WGT = nc.dram_tensor("WGT", [D, IS], BF16, kind="Internal").ap()
    WPG = nc.dram_tensor("WPG", [IS, D], BF16, kind="Internal").ap()
    bWGT, bWPG = Buf("WGTd"), Buf("WPGd")
    bs = lambda T, b: T[b * D:(b + 1) * D, :]
    bm = lambda T, b: T[b * 128:(b + 1) * 128, :]
    with contextlib.ExitStack() as stk:
        C = Consts(nc, em, stk)
        sems = {k: stk.enter_context(nc.semaphore(f"sem_{k}")) for k in em.sem_keys()}
        block = stk.enter_context(nc.Block())
        Wsc = phase_l1ffn_prep(nc, em, C, W, subset=("Wm", "Wm0", "Wp"))
        em.barrier()
        phase_l1ffn_done(nc, em, NOVALL, 16, DFLAG)
        em.barrier()
        for b in range(NB):
            phase_l1ffn_active3(nc, em, C, bm(MASK1, b), bm(MASK2, b), DFLAG,
                                bm(ACT3, b), tag=f"act3b{b}")
            em.barrier()
            phase_cast_bf16(nc, em, bs(XN2, b), bs(XN2BF, b), f"cx2b{b}")
            em.barrier()
            bidx = phase_l1ffn_idx(nc, em, C, bm(ACT3, b), IDX, f"idx3b{b}")
            em.barrier()
            phase_l1ffn_gather(nc, em, C, Wsc, IDX, bidx, WGT, WPG, bWGT, bWPG, f"g3b{b}")
            em.barrier()
            phase_l1ffn_step_sparse(nc, em, C, bs(XN2BF, b), WGT, bWGT, WPG, bWPG,
                                    bs(XN3BF, b), bs(XN3, b), b, 3)
            em.barrier()
            phase_l1ffn_blend(nc, em, C, bs(XN1, b), bs(XN2, b), bs(XN3, b), DFLAG,
                              bs(XFIN, b), tag=f"blendb{b}")
            em.barrier()
            phase_ln(nc, em, C, bs(X3, b), bs(XFIN, b), bs(XOUT, b), b, tag=f"ln4b{b}")
            em.barrier()
            phase_out_transpose(nc, em, C, bs(XOUT, b), y_out, b, f"otb{b}")
            em.barrier()
        em.finalize(block, sems)
    return nc


def kernel(**inputs):
    NCORES = 8
    x = np.ascontiguousarray(inputs["L0_attn_x"])
    sample_idx = [int(v) for v in np.asarray(inputs["L0_attn_sample_idx"])]
    consts = make_host_consts()
    wmap = {}
    for k, v in inputs.items():
        if k not in ("L0_attn_x", "L0_attn_sample_idx"):
            wmap[k] = np.ascontiguousarray(v)
    for n in PNAMES:
        key = {"angle": "L1_attn_phazor_angle", "lnm": "L1_attn_ln_minus_ln_phazor_abs",
               "scale": "L1_attn_ln_phazor_scale", "pi_re": "L1_attn_phazor_init_re",
               "pi_im": "L1_attn_phazor_init_im", "lc_re": "L1_attn_last_conv_re",
               "lc_im": "L1_attn_last_conv_im"}[n]
        wmap[f"p_{n}"] = np.ascontiguousarray(np.asarray(inputs[key]).reshape(DT, 128).T)
    for k in ["L1_attn_phazor_angle", "L1_attn_ln_minus_ln_phazor_abs",
              "L1_attn_ln_phazor_scale", "L1_attn_phazor_init_re",
              "L1_attn_phazor_init_im", "L1_attn_last_conv_re", "L1_attn_last_conv_im"]:
        wmap.pop(k, None)

    nc1 = build_launch1(sample_idx)
    in_maps1 = []
    for ci in range(NCORES):
        m = {"x_in": x[ci * NB:(ci + 1) * NB]}
        m.update(wmap)
        m.update(consts)
        in_maps1.append(m)
    res1 = run_bass_kernel_spmd(nc1, in_maps1, core_ids=list(range(NCORES)))
    nov_all = np.concatenate([res1.results[ci]["NOV"] for ci in range(NCORES)], axis=1)
    nov_all = np.ascontiguousarray(nov_all.astype(np.float32))

    nc2 = build_launch2()
    in_maps2 = []
    for ci in range(NCORES):
        r = res1.results[ci]
        m = {"X3": r["X3"], "XN1": r["XN1"], "XN2": r["XN2"],
             "MASK1": r["MASK1"], "MASK2": r["MASK2"], "NOVALL": nov_all,
             "L1_ffn_Wm": wmap["L1_ffn_Wm"], "L1_ffn_Wp": wmap["L1_ffn_Wp"]}
        m.update(consts)
        in_maps2.append(m)
    res2 = run_bass_kernel_spmd(nc2, in_maps2, core_ids=list(range(NCORES)))
    out = np.concatenate([res2.results[ci]["y_out"] for ci in range(NCORES)], axis=0)
    return out



def phase_l1ffn_idx(nc, em, C, mask_dram, IDX, tag):
    from concourse.bass_core_rust import IndirectOffsetOnAxis
    I32 = mybir.dt.int32
    with contextlib.ExitStack() as stk:
        sb = lambda name, shape, dt_: stk.enter_context(nc.sbuf_tensor(f"{tag}_{name}", shape, dt_))
        msk = sb("m", [128, FT], F32)
        ones = sb("o", [128, FT], F32)
        rank = sb("r", [128, FT], F32)
        pen = sb("pen", [128, FT], F32)
        ji = sb("ji", [128, FT], I32)
        zz = sb("z", [128, IS // 128], I32)
        ppre = stk.enter_context(nc.psum_tensor(f"{tag}_pp", [128, 1], F32))
        bm, bo, br, bpen, bji, bz, bpp = (Buf(x) for x in
                                          ("m", "o", "r", "pen", "ji", "z", "pp"))
        em.dma("qsp", lambda eng: eng.dma_start(out=msk[:, :], in_=mask_dram),
               reads=(), writes=(bm,), tag=f"{tag} ldm")
        em.op("dve", lambda eng: eng.memset(ones[:, :], 1.0),
              reads=(), writes=(bo,), tag=f"{tag} ones")
        em.op("dve", lambda eng: eng.memset(zz[:, :], 0),
              reads=(), writes=(bz,), tag=f"{tag} zz")
        em.op("dve", lambda eng: eng.tensor_tensor_scan(rank[:, :], ones[:, :], msk[:, :],
                                                        0.0, ALU.mult, ALU.add),
              reads=(bo, bm, br), writes=(br,), tag=f"{tag} scan")
        em.op("pe", lambda eng: eng.matmul(ppre[:, :], C.lowtri[:, :], rank[:, FT - 1:FT],
                                           start=True, stop=True),
              reads=(br, C.buf, bpp), writes=(bpp,), tag=f"{tag} pre")
        em.op("dve", lambda eng: eng.tensor_scalar(pen[:, :], msk[:, :], -1e9, 1e9, ALU.mult, ALU.add),
              reads=(bm, bpen), writes=(bpen,), tag=f"{tag} pen")
        em.op("dve", lambda eng: eng.tensor_copy(ones[:, 0:1], ppre[:, :]),
              reads=(bpp, bo), writes=(bo,), tag=f"{tag} precp")
        em.op("dve", lambda eng: eng.tensor_scalar(rank[:, :], rank[:, :], ones[:, 0:1], -1.0,
                                                   ALU.add, ALU.add),
              reads=(br, bo), writes=(br,), tag=f"{tag} jall")
        em.op("dve", lambda eng: eng.tensor_add(rank[:, :], rank[:, :], pen[:, :]),
              reads=(br, bpen), writes=(br,), tag=f"{tag} jm")
        em.op("dve", lambda eng: eng.tensor_copy(ji[:, :], rank[:, :]),
              reads=(br, bji), writes=(bji,), tag=f"{tag} jint")
        em.dma("qpool", lambda eng: eng.dma_start(
            out=IDX.rearrange("(p t) a -> p (t a)", p=128), in_=zz[:, :]),
            reads=(bz,), writes=(), tag=f"{tag} zero")
        bidx = Buf(f"{tag}_idxD")
        for ft in range(FT):
            em.dma("qpool", (lambda ft=ft: lambda eng: eng.indirect_dma_start(
                IDX, IndirectOffsetOnAxis(ap=ji[:, ft:ft + 1], axis=0),
                C.fid[:, ft:ft + 1], None))(),
                reads=(bji, C.buf), writes=(bidx,), tag=f"{tag} sc{ft}")
    return bidx


def phase_l1ffn_gather(nc, em, C, Wsc, IDX, bidx, WGT, WPG, bWGT, bWPG, tag):
    from concourse.bass_core_rust import IndirectOffsetOnAxis
    I32 = mybir.dt.int32
    NT = IS // 128
    with contextlib.ExitStack() as stk:
        sb = lambda name, shape, dt_: stk.enter_context(nc.sbuf_tensor(f"{tag}_{name}", shape, dt_))
        idxc = sb("idxc", [128, NT], I32)
        wg = [sb(f"wg{i}", [128, D], BF16) for i in range(2)]
        wt = [sb(f"wt{i}", [128, D], BF16) for i in range(2)]
        ps = [stk.enter_context(nc.psum_tensor(f"{tag}_ps{i}", [128, 128], BF16)) for i in range(2)]
        bic = Buf("idxc")
        bwg = [Buf("wg0"), Buf("wg1")]
        bwt = [Buf("wt0"), Buf("wt1")]
        bps = [Buf("ps0"), Buf("ps1")]
        em.dma("qsp", lambda eng: eng.dma_start(
            out=idxc[:, :], in_=IDX.rearrange("(p t) a -> p (t a)", p=128)),
            reads=(bidx,), writes=(bic,), tag=f"{tag} ldi")
        for t in range(NT):
            i = t % 2
            em.dma("qpool", (lambda t=t, i=i: lambda eng: eng.indirect_dma_start(
                wt[i][:, :], None, Wsc["WpT"], IndirectOffsetOnAxis(ap=idxc[:, t:t + 1], axis=0)))(),
                reads=(bic, bwt[i]), writes=(bwt[i],), tag=f"{tag} gwp")
            em.dma("qact", (lambda t=t, i=i: lambda eng: eng.dma_start(
                out=WPG[t * 128:(t + 1) * 128, :], in_=wt[i][:, :]))(),
                reads=(bwt[i],), writes=(bWPG,), tag=f"{tag} swp")
            em.dma("qpool", (lambda t=t, i=i: lambda eng: eng.indirect_dma_start(
                wg[i][:, :], None, Wsc["WmR"], IndirectOffsetOnAxis(ap=idxc[:, t:t + 1], axis=0)))(),
                reads=(bic, bwg[i]), writes=(bwg[i],), tag=f"{tag} gwm")
            for dt in range(DT):
                j = dt % 2
                em.op("pe", (lambda i=i, j=j, dt=dt: lambda eng: eng.transpose(
                    ps[j][:, :], wg[i][:, dt * 128:(dt + 1) * 128], C.ident16[:, :]))(),
                    reads=(bwg[i], C.buf, bps[j]), writes=(bps[j],), tag=f"{tag} tr")
                em.op("dve", (lambda i=i, j=j, dt=dt: lambda eng: eng.tensor_copy(
                    wt[i][:, dt * 128:(dt + 1) * 128], ps[j][:, :]))(),
                    reads=(bps[j], bwt[i]), writes=(bwt[i],), tag=f"{tag} trev")
            em.dma("qact", (lambda t=t, i=i: lambda eng: eng.dma_start(
                out=WGT.rearrange("(dt p) f -> p dt f", p=128)[:, :, t * 128:(t + 1) * 128],
                in_=wt[i][:, :].rearrange("p (dt c) -> p dt c", dt=DT)))(),
                reads=(bwt[i],), writes=(bWGT,), tag=f"{tag} swm")


def phase_l1ffn_step_sparse(nc, em, C, xin_bf, WGT, bWGT, WPG, bWPG, xout_bf, xout_f32,
                            b, step, pooled_dram=None, Lt=L, tag=None):
    tag = tag or f"l1sp{step}b{b}"
    CH = 512
    NCH = Lt // CH
    NT = IS // 128
    do_ck = pooled_dram is not None
    with contextlib.ExitStack() as stk:
        sb = lambda name, shape, dt_: stk.enter_context(nc.sbuf_tensor(f"{tag}_{name}", shape, dt_))
        xcb = sb("xcb", [128, DT * CH], BF16)
        hs = sb("hs", [128, NT * CH], BF16)
        wm = [sb(f"wm{i}", [128, DT * 512], BF16) for i in range(2)]
        wp = [sb(f"wp{i}", [128, NT * 128], BF16) for i in range(2)]
        xnf = sb("xnf", [128, DT * CH], F32)
        xnb = sb("xnb", [128, DT * CH], BF16)
        pooled = sb("pooled", [128, 2 * DT], F32)
        ph = [stk.enter_context(nc.psum_tensor(f"{tag}_ph{i}", [128, CH], F32)) for i in range(2)]
        px = [stk.enter_context(nc.psum_tensor(f"{tag}_px{i}", [128, CH], F32)) for i in range(2)]
        bxcb, bhs, bxnf, bxnb = Buf("xcb"), Buf("hs"), Buf("xnf"), Buf("xnb")
        bwm = [Buf("wm0"), Buf("wm1")]
        bwp = [Buf("wp0"), Buf("wp1")]
        bpool = Buf("pooled")
        bph = [Buf("ph0"), Buf("ph1")]
        bpx = [Buf("px0"), Buf("px1")]
        em.op("dve", lambda eng: eng.memset(pooled[:, :], 0.0),
              reads=(), writes=(bpool,), tag=f"{tag} pool0")
        for c in range(NCH):
            em.dma("qsp", (lambda c=c: lambda eng: eng.dma_start(
                out=xcb[:, :].rearrange("p (dt t) -> p dt t", dt=DT),
                in_=xin_bf.rearrange("(dt p) l -> p dt l", p=128)[:, :, c * CH:(c + 1) * CH]))(),
                reads=(bxcb,), writes=(bxcb,), tag=f"{tag} ldx")
            for fq in range(NT // 4):
                i = fq % 2
                em.dma("qsp", (lambda fq=fq, i=i: lambda eng: eng.dma_start(
                    out=wm[i][:, :].rearrange("p (dt f) -> p dt f", dt=DT),
                    in_=WGT.rearrange("(dt p) f -> p dt f", p=128)[:, :, fq * 512:(fq + 1) * 512]))(),
                    reads=(bwm[i], bWGT), writes=(bwm[i],), tag=f"{tag} wm")
                for fi in range(4):
                    ft = fq * 4 + fi
                    j = ft % 2
                    for dt in range(DT):
                        em.op("pe", (lambda i=i, j=j, fi=fi, dt=dt: lambda eng: eng.matmul(
                            ph[j][:, :], wm[i][:, dt * 512 + fi * 128: dt * 512 + (fi + 1) * 128],
                            xcb[:, dt * CH:(dt + 1) * CH], start=(dt == 0), stop=(dt == DT - 1)))(),
                            reads=(bwm[i], bxcb), writes=(bph[j],), tag=f"{tag} hmm")
                    em.op("act", (lambda j=j, ft=ft: lambda eng: eng.activation(
                        hs[:, ft * CH:(ft + 1) * CH], ph[j][:, :], AF.Relu))(),
                        reads=(bph[j], bhs), writes=(bhs,), tag=f"{tag} relu")
            for ct in range(DT):
                i = ct % 2
                em.dma("qact", (lambda ct=ct, i=i: lambda eng: eng.dma_start(
                    out=wp[i][:, :].rearrange("p (ft c) -> p ft c", ft=NT),
                    in_=WPG.rearrange("(ft p) c -> p ft c", p=128)[:, :, ct * 128:(ct + 1) * 128]))(),
                    reads=(bwp[i], bWPG), writes=(bwp[i],), tag=f"{tag} wp")
                for ft in range(NT):
                    em.op("pe", (lambda i=i, ft=ft: lambda eng: eng.matmul(
                        px[i][:, :], wp[i][:, ft * 128:(ft + 1) * 128],
                        hs[:, ft * CH:(ft + 1) * CH], start=(ft == 0), stop=(ft == NT - 1)))(),
                        reads=(bwp[i], bhs), writes=(bpx[i],), tag=f"{tag} xmm")
                em.op("dve", (lambda i=i, ct=ct: lambda eng: eng.tensor_copy(
                    xnf[:, ct * CH:(ct + 1) * CH], px[i][:, :]))(),
                    reads=(bpx[i], bxnf), writes=(bxnf,), tag=f"{tag} evf")
                em.op("act", (lambda i=i, ct=ct: lambda eng: eng.copy(
                    xnb[:, ct * CH:(ct + 1) * CH], xnf[:, ct * CH:(ct + 1) * CH]))(),
                    reads=(bxnf, bxnb), writes=(bxnb,), tag=f"{tag} evb")
                if do_ck:
                    em.op("dve", (lambda ct=ct: lambda eng: eng.tensor_reduce(
                        pooled[:, DT + ct:DT + ct + 1],
                        xnf[:, ct * CH:(ct + 1) * CH], AX.X, ALU.add))(),
                        reads=(bxnf, bpool), writes=(bpool,), tag=f"{tag} poolred")
                    em.op("dve", (lambda ct=ct: lambda eng: eng.tensor_add(
                        pooled[:, ct:ct + 1], pooled[:, ct:ct + 1],
                        pooled[:, DT + ct:DT + ct + 1]))(),
                        reads=(bpool,), writes=(bpool,), tag=f"{tag} pooladd")
            em.dma("qact", (lambda c=c: lambda eng: eng.dma_start(
                out=xout_f32.rearrange("(dt p) l -> p dt l", p=128)[:, :, c * CH:(c + 1) * CH],
                in_=xnf[:, :].rearrange("p (dt t) -> p dt t", dt=DT)))(),
                reads=(bxnf,), writes=(), tag=f"{tag} stf")
            em.dma("qact", (lambda c=c: lambda eng: eng.dma_start(
                out=xout_bf.rearrange("(dt p) l -> p dt l", p=128)[:, :, c * CH:(c + 1) * CH],
                in_=xnb[:, :].rearrange("p (dt t) -> p dt t", dt=DT)))(),
                reads=(bxnb,), writes=(), tag=f"{tag} stb")
        if do_ck:
            em.op("dve", (lambda Lt=Lt: lambda eng: eng.tensor_scalar(
                pooled[:, 0:DT], pooled[:, 0:DT], 1.0 / Lt, None, ALU.mult))(),
                reads=(bpool,), writes=(bpool,), tag=f"{tag} poolavg")
            em.dma("qact", lambda eng: eng.dma_start(out=pooled_dram, in_=pooled[:, 0:DT]),
                   reads=(bpool,), writes=(), tag=f"{tag} poolst")
